# version 02
# baseline (speedup 1.0000x reference)
import math
import numpy as np
import ml_dtypes
from contextlib import ExitStack
import concourse.bass as bass
import concourse.mybir as mybir
from concourse.bass_utils import run_bass_kernel_spmd

F32 = mybir.dt.float32
BF16 = mybir.dt.bfloat16
AF = mybir.ActivationFunctionType
ALU = mybir.AluOpType
AX = mybir.AxisListType
BF = ml_dtypes.bfloat16

T = 2048
D = 2048
NK = D // 128
NTB = T // 512
NTT = T // 128

ENGINES = ("pe", "act", "dve", "pool", "sp")


class Buf:
    __slots__ = ("writers", "readers", "name")
    def __init__(self, name=""):
        self.writers = []
        self.readers = []
        self.name = name

class Op:
    __slots__ = ("eng", "fn", "deps", "inc_no", "is_dma", "name", "needed",
                 "waits", "post_drain")
    def __init__(self, eng, fn, deps, is_dma, name):
        self.eng = eng
        self.fn = fn
        self.deps = deps
        self.is_dma = is_dma
        self.name = name
        self.inc_no = None
        self.needed = False
        self.waits = ()
        self.post_drain = False

class Sched:
    def __init__(self):
        self.ops = {e: [] for e in ENGINES}
        self.n = 0

    def op(self, eng, fn, reads=(), writes=(), name="", dma=False,
           post_drain=False, extra=()):
        deps = set(extra)
        for b in reads:
            deps.update(b.writers)
        for b in writes:
            deps.update(b.writers)
            deps.update(b.readers)
        o = Op(eng, fn, deps, dma, name)
        o.post_drain = post_drain
        for b in reads:
            b.readers.append(o)
        for b in writes:
            b.writers = [o]
            b.readers = []
        self.ops[eng].append(o)
        self.n += 1
        return o

    def dma(self, queue, out_ap, in_ap, reads=(), writes=(), name=""):
        return self.op(queue, lambda e: e.dma_start(out=out_ap, in_=in_ap),
                       reads=reads, writes=writes, name=name, dma=True)

    M_DMA = 8

    def finalize(self):
        for e in ENGINES:
            for o in self.ops[e]:
                for d in o.deps:
                    d.needed = True
        M = self.M_DMA
        for e in ENGINES:
            cnt = dcnt = 0
            for o in self.ops[e]:
                if o.is_dma:
                    o.inc_no = dcnt
                    dcnt += 1
                elif o.needed:
                    cnt += 1
                    o.inc_no = cnt
        for e in ENGINES:
            watermark = {}
            for o in self.ops[e]:
                waits = {}
                for d in o.deps:
                    if d.is_dma:
                        key = ("q", d.eng, d.inc_no % M)
                        val = 16 * (d.inc_no // M + 1)
                    else:
                        if d.eng == e and e == "pe":
                            continue
                        key = ("e", d.eng)
                        val = d.inc_no
                    if waits.get(key, -1) < val:
                        waits[key] = val
                if o.is_dma:
                    key = ("q", o.eng, o.inc_no % M)
                    gen = o.inc_no // M
                    if gen > 0 and waits.get(key, -1) < 16 * gen:
                        waits[key] = 16 * gen
                o.waits = tuple((k, v) for k, v in waits.items()
                                if watermark.get(k, 0) < v)
                for k, v in o.waits:
                    watermark[k] = v


def _lightning_consts():
    H, C = 16, 128
    start = 2.0 ** (-2.0 ** (-(math.log2(H) - 3)))
    slopes = np.array([start ** (i + 1) for i in range(H)], dtype=np.float32)
    lam = np.exp(-slopes).astype(np.float32)
    pos = np.arange(C, dtype=np.float32)
    decay_q = lam[:, None] ** (pos + 1.0)
    decay_k = lam[:, None] ** (C - 1.0 - pos)
    diff = pos[:, None] - pos[None, :]
    dmask = np.where(diff >= 0, lam[:, None, None] ** np.maximum(diff, 0), 0.0)
    dmaskT = dmask.transpose(0, 2, 1)
    lam_c = (lam ** C).astype(np.float32)
    dqb = np.broadcast_to(decay_q[:, None, :], (H, C, C)).transpose(1, 0, 2)
    return {
        "lt_dmaskT": np.ascontiguousarray(dmaskT.transpose(1, 0, 2)
                                          ).reshape(C, H * C).astype(BF),
        "lt_dq_bcast": np.ascontiguousarray(dqb).reshape(C, H * C).astype(BF),
        "lt_dk_cols": np.ascontiguousarray(decay_k.T).astype(np.float32),
    }, [float(v) for v in lam_c]

LT_CONSTS, LT_LAMC = _lightning_consts()

def col_chunks(v):
    return np.ascontiguousarray(np.asarray(v).reshape(-1, 128).T).astype(np.float32)

def host_prepare(inputs):
    g = {}
    f32 = np.float32
    g["ident"] = np.eye(128, dtype=f32)
    g["ident_bf"] = np.eye(128, dtype=f32).astype(BF)
    g["ones_row"] = np.ones((1, 128), dtype=f32)
    g["ones_row_bf"] = np.ones((1, 128), dtype=f32).astype(BF)
    g["ones_col_bf"] = np.ones((128, 1), dtype=f32).astype(BF)
    onesel = np.zeros((128, 16), dtype=f32)
    for blk in range(4):
        onesel[:, blk * 4 + blk] = 1.0
    g["onesel"] = onesel.astype(BF)
    tri = np.tril(np.ones((128, 128), bool))
    g["diag_mask"] = np.where(tri, 0.0, -1e30).astype(BF)
    g["diag_maskT"] = np.where(tri.T, 0.0, -1e30).astype(BF)
    g.update(LT_CONSTS)

    i = inputs
    sw = np.asarray(i["L0_ffn_select_weight"])
    lu = np.asarray(i["L0_ffn_lora_update"])
    ld = np.asarray(i["L0_ffn_lora_degen"])
    g["L0f_WallT"] = np.ascontiguousarray(sw.transpose(1, 0, 2).reshape(2048, 64))
    g["L0f_smdT"] = np.ascontiguousarray(np.asarray(i["L0_ffn_select_mean_diff"]).T)
    g["L0f_updT"] = np.ascontiguousarray(lu.transpose(1, 0, 2).reshape(2048, 64))
    g["L0f_dgnT"] = np.ascontiguousarray(ld.transpose(0, 2, 1).reshape(64, 2048))
    g["L0f_biasw"] = np.ascontiguousarray(np.asarray(i["L0_ffn_bias_w"]))
    g["L0f_dwT"] = np.ascontiguousarray(np.asarray(i["L0_ffn_debias_w"]).T)
    g["L0f_slope"] = np.asarray(i["L0_ffn_imp_slope"]).reshape(8, 1).astype(f32)
    g["L0f_center"] = np.asarray(i["L0_ffn_imp_center"]).reshape(8, 1).astype(f32)
    E8 = np.zeros((64, 8), f32)
    for n in range(8):
        E8[n * 8:(n + 1) * 8, n] = 1.0
    g["E8mask"] = E8
    g["E8bf"] = E8.astype(BF)
    g["P8bf"] = np.ascontiguousarray(E8.T).astype(BF)
    g["pen8"] = (np.arange(8, dtype=f32) * 1e-9).reshape(8, 1)
    g["L1f_WqT"] = np.ascontiguousarray(np.asarray(i["L1_ffn_Wq"]).T)
    g["L1f_WkT"] = np.ascontiguousarray(np.asarray(i["L1_ffn_Wk"]).T)
    g["L1f_WvT"] = np.ascontiguousarray(np.asarray(i["L1_ffn_Wv"]).T)
    g["L1f_Wo"] = np.ascontiguousarray(np.asarray(i["L1_ffn_Wo"]))
    g["L1f_A"] = np.ascontiguousarray(np.asarray(i["L1_ffn_A"]))
    g["L1f_B"] = np.ascontiguousarray(np.asarray(i["L1_ffn_B"]))
    g["L1f_WinT"] = np.ascontiguousarray(np.asarray(i["L1_ffn_W_in_base"]).T)
    g["L1f_WupT"] = np.ascontiguousarray(np.asarray(i["L1_ffn_W_up_base"]).T)
    g["L1f_WdnT"] = np.ascontiguousarray(np.asarray(i["L1_ffn_W_down"]).T)
    g["L1f_g1"] = col_chunks(i["L1_ffn_ln1_w"])
    g["L1f_g2"] = col_chunks(i["L1_ffn_ln2_w"])
    g["L2a_Wq"] = np.ascontiguousarray(np.asarray(i["L2_attn_Wq"]))
    g["L2a_Wkv"] = np.ascontiguousarray(np.asarray(i["L2_attn_Wkv"]))
    g["L2a_Wout"] = np.ascontiguousarray(np.asarray(i["L2_attn_Wout"]))
    rotP = np.zeros((128, 128), f32)
    for ii in range(64):
        rotP[ii, 64 + ii] = -1.0
        rotP[64 + ii, ii] = 1.0
    g["l2_rotP"] = rotP.astype(BF)
    inv_freq = 1.0 / 10000.0 ** (np.arange(0, 128, 2, dtype=f32) / 128.0)
    fr = np.arange(T, dtype=f32)[:, None] * inv_freq[None, :]
    posf = np.concatenate([fr, fr], -1)
    g["l2_cos"] = np.ascontiguousarray(np.cos(posf).T).astype(f32)
    g["l2_sin"] = np.ascontiguousarray(np.sin(posf).T).astype(f32)
    g["L2f_WcT"] = np.ascontiguousarray(np.asarray(i["L2_ffn_Wc"]).T)
    g["L2f_genes"] = np.ascontiguousarray(np.asarray(i["L2_ffn_genes"]))
    g["L2f_WdT"] = np.ascontiguousarray(np.asarray(i["L2_ffn_Wd"]).T)
    g["L2f_WuT"] = np.ascontiguousarray(np.asarray(i["L2_ffn_Wu"]).T)
    g["L2f_gw"] = col_chunks(i["L2_ffn_norm_w"])
    g["L2f_temp"] = np.asarray(i["L2_ffn_temperature"]).reshape(1, 1).astype(f32)
    g["L2f_scale"] = np.asarray(i["L2_ffn_scale"]).reshape(1, 1).astype(f32)
    for nm, key in (("L3a_wqd", "L3_attn_wq_down"), ("L3a_wqn", "L3_attn_wq_nope"),
                    ("L3a_wqr", "L3_attn_wq_rope"), ("L3a_wkvd", "L3_attn_wkv_down"),
                    ("L3a_wvu", "L3_attn_wv_up"), ("L3a_wkn", "L3_attn_wk_nope"),
                    ("L3a_wkr", "L3_attn_wk_rope"), ("L3a_wo", "L3_attn_wo")):
        g[nm] = np.ascontiguousarray(np.asarray(i[key]))
    g["L3f_WgT"] = np.ascontiguousarray(np.asarray(i["L3_ffn_w_gate"]).T)
    g["L3f_WuT"] = np.ascontiguousarray(np.asarray(i["L3_ffn_w_up"]).T)
    g["L3f_WdT"] = np.ascontiguousarray(np.asarray(i["L3_ffn_w_down"]).T)
    rotP3 = np.zeros((128, 128), f32)
    for ii in range(0, 128, 2):
        rotP3[ii, ii + 1] = 1.0
        rotP3[ii + 1, ii] = 1.0
    g["l3_rotP"] = rotP3.astype(BF)
    DR = 64
    inv3 = 1.0 / 10000.0 ** (np.arange(0, DR, 2, dtype=f32) / DR)
    fr3 = np.arange(T, dtype=f32)[:, None] * inv3[None, :]
    cos3 = np.cos(fr3); sin3 = np.sin(fr3)
    crow = np.repeat(cos3, 2, axis=1)
    srow = np.empty_like(crow)
    srow[:, 0::2] = -sin3
    srow[:, 1::2] = sin3
    g["l3_cos"] = np.ascontiguousarray(np.tile(crow, (1, 2)).T).astype(f32)
    g["l3_sin"] = np.ascontiguousarray(np.tile(srow, (1, 2)).T).astype(f32)
    for L in ("L0", "L1"):
        g[f"{L}a_wqkvT"] = np.ascontiguousarray(np.asarray(i[f"{L}_attn_w_qkv"]).T)
        g[f"{L}a_bqkv"] = col_chunks(i[f"{L}_attn_b_qkv"])
        g[f"{L}a_woutT"] = np.ascontiguousarray(np.asarray(i[f"{L}_attn_w_out"]).T)
        g[f"{L}a_bout"] = col_chunks(i[f"{L}_attn_b_out"])
    return g


class Builder:
    def __init__(self, nc, ctx, host):
        self.nc = nc
        self.ctx = ctx
        self.host = host
        self.S = Sched()
        E = ctx.enter_context
        self.used_inputs = {}
        self.in_bufs = {}
        self.sb_cache = {}

        self.xT_in = self.inp("xT", (D, T), F32)
        self.yT = nc.dram_tensor("yT", [D, T], F32, kind="ExternalOutput").ap()
        self.yT_b = Buf("yT")
        self.xhbm = [nc.dram_tensor(f"xs{j}", [D, T], F32, kind="Internal").ap()
                     for j in range(2)]
        self.xhbm_b = [[Buf(f"xs{j}_{k}") for k in range(NK)] for j in range(2)]
        self.cur = 0
        self.cur_ap = self.xT_in
        self.cur_b = [self.in_bufs["xT"]] * NK

        self.biga = E(nc.sbuf_tensor("biga", [128, 49152], BF16))
        self.biga_b = [Buf(f"bg{k}") for k in range(24)]
        self.fwA = E(nc.sbuf_tensor("fwA", [128, 4096], F32))
        self.fw = [self.fwA[:, 0:2048], self.fwA[:, 2048:4096]]
        self.fw_b = [Buf(f"fw{j}") for j in range(2)]
        self.SR = E(nc.sbuf_tensor("SR", [80, 10240], F32))
        self.rows_b = [Buf(f"row{j}") for j in range(5)]
        self.sr32_b = [Buf(f"sr32_{j}") for j in range(5)]
        self.sr64_b = [Buf(f"sr64_{j}") for j in range(5)]
        self.pqs = [E(nc.sbuf_tensor(f"pqs{j}", [128, 512], F32)) for j in range(3)]
        self.pqs_b = [Buf(f"pqs{j}") for j in range(3)]
        self.cols = E(nc.sbuf_tensor("colsa", [128, 512], F32))
        self.cols_b = [Buf(f"cols{j}") for j in range(4)]
        self.s4 = E(nc.sbuf_tensor("s4", [4, 3584], F32))
        self.s4_b = [Buf(f"s4_{j}") for j in range(8)]
        self.wslab = [E(nc.sbuf_tensor(f"wsl{j}", [128, 2048], BF16)) for j in range(2)]
        self.wslab_b = [Buf(f"wsl{j}") for j in range(2)]
        self.stg = [E(nc.sbuf_tensor(f"stg{j}", [128, 512], BF16)) for j in range(4)]
        self.stg_b = [Buf(f"stg{j}") for j in range(4)]
        self.tmp = [E(nc.sbuf_tensor(f"tmp{j}", [128, 512], F32)) for j in range(2)]
        self.tmp_b = [Buf(f"tmp{j}") for j in range(2)]
        self.small = [E(nc.sbuf_tensor(f"sm{j}", [128, 128], F32)) for j in range(4)]
        self.small_b = [Buf(f"sm{j}") for j in range(4)]
        self.smallb = [E(nc.sbuf_tensor(f"sb{j}", [128, 128], BF16)) for j in range(10)]
        self.dmy = E(nc.sbuf_tensor("dmy", [1, 8], F32))
        self.dmy_b = Buf("dmy")
        self.smallb_b = [Buf(f"sb{j}") for j in range(10)]

        self.pb = [E(nc.psum_tensor(f"pb{j}", [128, 512], F32)) for j in range(6)]
        self.pb_b = [Buf(f"pb{j}") for j in range(6)]
        self.pbf = [E(nc.psum_tensor(f"pbf{j}", [128, 1024], BF16)) for j in range(2)]
        self.pbf_b = [Buf(f"pbf{j}") for j in range(2)]

        self.sems = {e: E(nc.semaphore(f"sem_{e}")) for e in ("pe", "act", "dve", "pool")}
        self.qsems = {(e, j): E(nc.semaphore(f"q_{e}{j}"))
                      for e in ("sp", "act", "pool") for j in range(Sched.M_DMA)}

        self.scr_row = [nc.dram_tensor(f"scrr{j}", [1, T], F32, kind="Internal").ap()
                        for j in range(4)]
        self.scr_row_b = [Buf(f"scrr{j}") for j in range(4)]
        self.qkvT = nc.dram_tensor("qkvT", [3 * D, T], BF16, kind="Internal").ap()
        self.qkvT_b = Buf("qkvT")
        self.attnT = nc.dram_tensor("attnT", [D, T], BF16, kind="Internal").ap()
        self.attnT_b = Buf("attnT")
        self.hidT = nc.dram_tensor("hidT", [8192, T], BF16, kind="Internal").ap()
        self.hidT_b = Buf("hidT")
        self.vT3 = nc.dram_tensor("vT3", [3072, T], BF16, kind="Internal").ap()
        self.vT3_b = Buf("vT3")
        self.attnT3 = nc.dram_tensor("attnT3", [3072, T], BF16, kind="Internal").ap()
        self.attnT3_b = Buf("attnT3")
        self.W2sb = E(nc.sbuf_tensor("W2sb", [128, 256], BF16))
        self.W2sb_b = Buf("W2sb")

    def inp(self, name, shape, dtype):
        ap = self.nc.dram_tensor(name, list(shape), dtype, kind="ExternalInput").ap()
        self.used_inputs[name] = ap
        self.in_bufs[name] = Buf(name)
        return ap

    def win(self, name):
        if name in self.used_inputs:
            return self.used_inputs[name], self.in_bufs[name]
        arr = self.host[name]
        dt = BF16 if arr.dtype == BF else F32
        ap = self.inp(name, arr.shape, dt)
        return ap, self.in_bufs[name]

    def sbconst(self, name):
        if name in self.sb_cache:
            return self.sb_cache[name]
        ap, b = self.win(name)
        arr = self.host[name]
        dt = BF16 if arr.dtype == BF else F32
        tile = self.ctx.enter_context(
            self.nc.sbuf_tensor("c_" + name, list(arr.shape), dt))
        tb = Buf("c_" + name)
        self.S.dma("sp", tile[:], ap, reads=(b,), writes=(tb,), name="ld_" + name)
        self.sb_cache[name] = (tile, tb)
        return tile, tb

    def dump(self, key, ap, bufs, shape, dtype):
        import kernel as _k
        if key not in DEBUG_DUMPS:
            return
        out = self.nc.dram_tensor("dbg_" + key, list(shape), dtype,
                                  kind="ExternalOutput").ap()
        ob = Buf("dbg_" + key)
        self.S.dma("sp", out, ap, reads=tuple(bufs), writes=(ob,),
                   name="dump_" + key)
        self.S.op("sp", lambda e: e.nop(), reads=(ob,), name="dumpw_" + key)

    def next_x(self, last):
        if last:
            return self.yT, [self.yT_b] * NK
        j = 1 - self.cur
        return self.xhbm[j], self.xhbm_b[j]

    def advance_x(self, dst_ap, dst_bufs):
        self.cur = 1 - self.cur
        self.cur_ap = dst_ap
        self.cur_b = dst_bufs

    def biga_slot(self, k):
        return self.biga[:, k * 2048:(k + 1) * 2048], self.biga_b[k]

    def row(self, j):
        return self.SR[0:1, j * 2048:(j + 1) * 2048], self.rows_b[j]

    def sr32(self, j, p=16):
        return self.SR[32:32 + p, j * 2048:(j + 1) * 2048], self.sr32_b[j]

    def sr64(self, j, p=16):
        return self.SR[64:64 + p, j * 2048:(j + 1) * 2048], self.sr64_b[j]

    def cols_to_row(self, cols_ap, cols_buf, n, row_idx, tag):
        S = self.S
        identf, identf_b = self.sbconst("ident")
        dst, dst_b = self.row(row_idx)
        pbk, pbk_b = self.pb[4], self.pb_b[4]
        for g in range(0, n, 4):
            for tt in range(g, min(g + 4, n)):
                S.op("pe", lambda e, tt=tt: e.transpose(
                        pbk[0:1, (tt % 4) * 128:(tt % 4) * 128 + 128],
                        cols_ap[:, tt:tt + 1], identf[:]),
                     reads=(cols_buf, identf_b), writes=(pbk_b,),
                     name=f"{tag}c2r{tt}")
            w = min(g + 4, n) - g
            S.op("dve", lambda e, g=g, w=w: e.tensor_copy(
                    dst[0:1, g * 128:(g + w) * 128], pbk[0:1, 0:w * 128]),
                 reads=(pbk_b,), writes=(dst_b,), name=f"{tag}c2re{g}")

    def s4s(self, j):
        return self.s4[0:4, j * 512:(j + 1) * 512], self.s4_b[j]

    def stats_pass(self, tag):
        S = self.S
        onesel, onesel_b = self.sbconst("onesel")
        psum_s, psum_s_b = self.pb[4], self.pb_b[4]
        psum_q, psum_q_b = self.pb[5], self.pb_b[5]
        for k in range(NK):
            xs, xs_b = self.fw[k % 2], self.fw_b[k % 2]
            S.dma("sp", xs, self.cur_ap[k * 128:(k + 1) * 128, :],
                  reads=(self.cur_b[k],), writes=(xs_b,), name=f"{tag}ldx{k}")
            xbf, xbf_b = self.biga_slot(k)
            S.op("pool", lambda e, xs=xs, xbf=xbf: e.tensor_copy(xbf, xs),
                 reads=(xs_b,), writes=(xbf_b,), name=f"{tag}cast{k}")
            sqs, sqs_b = self.biga_slot(16 + (k % 2))
            S.op("act", lambda e, xs=xs, sqs=sqs: e.activation(sqs, xs, AF.Square),
                 reads=(xs_b,), writes=(sqs_b,), name=f"{tag}sq{k}")
            for blk in range(4):
                st = (k == 0 and blk == 0)
                sp = (k == NK - 1 and blk == 3)
                S.op("pe", lambda e, blk=blk, xbf=xbf, st=st, sp=sp, o=onesel:
                     e.matmul(psum_s[0:4, :], o[:, blk * 4:blk * 4 + 4],
                              xbf[:, blk * 512:(blk + 1) * 512], start=st, stop=sp),
                     reads=(xbf_b, onesel_b), writes=(psum_s_b,), name=f"{tag}ms{k}")
                S.op("pe", lambda e, blk=blk, sqs=sqs, st=st, sp=sp, o=onesel:
                     e.matmul(psum_q[0:4, :], o[:, blk * 4:blk * 4 + 4],
                              sqs[:, blk * 512:(blk + 1) * 512], start=st, stop=sp),
                     reads=(sqs_b, onesel_b), writes=(psum_q_b,), name=f"{tag}mq{k}")
        sm, sm_b = self.s4s(0)
        sq, sq_b = self.s4s(1)
        S.op("dve", lambda e, sm=sm: e.tensor_copy(sm, psum_s[0:4, :]),
             reads=(psum_s_b,), writes=(sm_b,), name=f"{tag}evs")
        S.op("dve", lambda e, sq=sq: e.tensor_copy(sq, psum_q[0:4, :]),
             reads=(psum_q_b,), writes=(sq_b,), name=f"{tag}evq")

    def derive_mu_r(self, tag):
        S = self.S
        mu, mu_b = self.s4s(2)
        var, var_b = self.s4s(3)
        r, r_b = self.s4s(4)
        t0, t0_b = self.s4s(5)
        sm, sm_b = self.s4s(0)
        sq, sq_b = self.s4s(1)
        S.op("dve", lambda e: e.tensor_scalar_mul(mu, sm, 1.0 / D),
             reads=(sm_b,), writes=(mu_b,), name=f"{tag}mu")
        S.op("dve", lambda e: e.tensor_scalar_mul(t0, sq, 1.0 / D),
             reads=(sq_b,), writes=(t0_b,), name=f"{tag}ex2")
        S.op("dve", lambda e: e.tensor_mul(var, mu, mu),
             reads=(mu_b,), writes=(var_b,), name=f"{tag}mu2")
        S.op("dve", lambda e: e.tensor_sub(var, t0, var),
             reads=(t0_b, var_b), writes=(var_b,), name=f"{tag}var")
        S.op("dve", lambda e: e.tensor_scalar_add(t0, var, 1e-5),
             reads=(var_b,), writes=(t0_b,), name=f"{tag}ve")
        S.op("act", lambda e: e.activation(t0, t0, AF.Sqrt),
             reads=(t0_b,), writes=(t0_b,), name=f"{tag}sqrt")
        S.op("dve", lambda e: e.reciprocal(r, t0),
             reads=(t0_b,), writes=(r_b,), name=f"{tag}r")

    def to_row(self, s4_idx, row_idx, tag):
        S = self.S
        src, src_b = self.s4s(s4_idx)
        scr, scr_b = self.scr_row[row_idx], self.scr_row_b[row_idx]
        dst, dst_b = self.row(row_idx)
        scr4 = scr.rearrange("o (a b) -> (o a) b", a=4, b=512)
        S.dma("sp", scr4, src, reads=(src_b,), writes=(scr_b,), name=f"{tag}r2d{row_idx}")
        S.dma("sp", dst, scr, reads=(scr_b,), writes=(dst_b,), name=f"{tag}d2r{row_idx}")

    def make_s_ms_rows(self, tag, kind):
        S = self.S
        self.derive_mu_r(tag)
        mu, mu_b = self.s4s(2)
        r, r_b = self.s4s(4)
        t1, t1_b = self.s4s(6)
        if kind == "ln":
            S.op("dve", lambda e: e.tensor_mul(t1, mu, r),
                 reads=(mu_b, r_b), writes=(t1_b,), name=f"{tag}ms")
            self.to_row(4, 0, tag)
            self.to_row(6, 1, tag)
        else:
            raise NotImplementedError(kind)

    def xhat_inplace(self, tag):
        S = self.S
        ones_row, ones_row_b = self.sbconst("ones_row")
        srow, srow_b = self.row(0)
        msrow, msrow_b = self.row(1)
        for tb in range(NTB):
            bs, bs_b = self.pb[4], self.pb_b[4]
            bm, bm_b = self.pb[5], self.pb_b[5]
            sl = slice(tb * 512, (tb + 1) * 512)
            S.op("pe", lambda e, sl=sl, o=ones_row: e.matmul(
                    bs[:, :], o[:], srow[0:1, sl], start=True, stop=True),
                 reads=(srow_b, ones_row_b), writes=(bs_b,), name=f"{tag}bcs{tb}")
            S.op("pe", lambda e, sl=sl, o=ones_row: e.matmul(
                    bm[:, :], o[:], msrow[0:1, sl], start=True, stop=True),
                 reads=(msrow_b, ones_row_b), writes=(bm_b,), name=f"{tag}bcm{tb}")
            for k in range(NK):
                xbf, xbf_b = self.biga_slot(k)
                tj, tj_b = self.tmp[k % 2], self.tmp_b[k % 2]
                S.op("dve", lambda e, xbf=xbf, sl=sl, tj=tj: e.tensor_mul(
                        tj[:], xbf[:, sl], bs[:, :]),
                     reads=(xbf_b, bs_b), writes=(tj_b,), name=f"{tag}xh1_{k}_{tb}")
                S.op("dve", lambda e, xbf=xbf, sl=sl, tj=tj: e.tensor_sub(
                        xbf[:, sl], tj[:], bm[:, :]),
                     reads=(tj_b, bm_b), writes=(xbf_b,), name=f"{tag}xh2_{k}_{tb}")

    def proj(self, tag, wT_name, o_lo, o_hi, K_total, evac, rhs_slot=None,
             after_oc=None, extra_mm=None, o_w_off=0):
        S = self.S
        wT, wT_b = self.win(wT_name)
        KT = K_total // 128
        if rhs_slot is None:
            rhs_slot = self.biga_slot
        wview = wT.rearrange("(c p) o -> p c o", p=128)
        nslab = (KT + 15) // 16
        for oc in range(o_lo, o_hi):
            slabs = []
            for si in range(nslab):
                j = (oc * nslab + si) % 2
                wsl, wsl_b = self.wslab[j], self.wslab_b[j]
                c0, c1 = si * 16, min((si + 1) * 16, KT)
                dst = wsl[:].rearrange("p (c o) -> p c o", o=128)[:, 0:c1 - c0, :]
                src = wview[:, c0:c1,
                            (oc - o_w_off) * 128:(oc - o_w_off + 1) * 128]
                S.dma("pool", dst, src, reads=(wT_b,), writes=(wsl_b,),
                      name=f"{tag}wld{oc}_{si}")
                slabs.append((wsl, wsl_b, c0, c1))
            for half in range(2):
                g = (oc * 2 + half) % 2
                banks = [(self.pb[2 * g], self.pb_b[2 * g]),
                         (self.pb[2 * g + 1], self.pb_b[2 * g + 1])]
                for (wsl, wsl_b, c0, c1) in slabs:
                    for k in range(c0, c1):
                        lhsT = wsl[:, (k - c0) * 128:(k - c0 + 1) * 128]
                        rap, rb = rhs_slot(k)
                        for t2 in range(2):
                            tb = half * 2 + t2
                            st = (k == 0)
                            sp = (k == KT - 1 and extra_mm is None)
                            pa, pab = banks[t2]
                            S.op("pe", lambda e, lhsT=lhsT, rap=rap, tb=tb, st=st,
                                 sp=sp, pa=pa: e.matmul(
                                     pa[:, :], lhsT, rap[:, tb * 512:(tb + 1) * 512],
                                     start=st, stop=sp),
                                 reads=(wsl_b, rb), writes=(pab,),
                                 name=f"{tag}mm{oc}_{k}_{tb}")
                for t2 in range(2):
                    if extra_mm is not None:
                        extra_mm(oc, half * 2 + t2, banks[t2][0], banks[t2][1])
                    evac(oc, half * 2 + t2, banks[t2][0], banks[t2][1])
            if after_oc is not None:
                after_oc(oc)

    def slot_lightning(self, L, last):
        tag = f"{L}a"
        S = self.S
        nc = self.nc
        dst_ap, dst_b = self.next_x(last)
        self.stats_pass(tag)
        self.make_s_ms_rows(tag, "ln")
        self.xhat_inplace(tag)

        bq, bq_b = self.sbconst(f"{L}a_bqkv")
        def evac_qkv(oc, tb, pa, pab):
            stj, stj_b = self.stg[tb % 2], self.stg_b[tb % 2]
            S.op("dve", lambda e, oc=oc, pa=pa, stj=stj: e.tensor_scalar(
                    stj[:], pa[:, :], bq[:, oc:oc + 1], None, ALU.add),
                 reads=(pab, bq_b), writes=(stj_b,), name=f"{tag}eq{oc}_{tb}")
            S.dma("sp", self.qkvT[oc * 128:(oc + 1) * 128, tb * 512:(tb + 1) * 512],
                  stj[:], reads=(stj_b,), writes=(self.qkvT_b,),
                  name=f"{tag}sq{oc}_{tb}")

        dmaskT, dmaskT_b = self.sbconst("lt_dmaskT")
        dqb_c, dqb_b = self.sbconst("lt_dq_bcast")
        dkc, dkc_b = self.sbconst("lt_dk_cols")
        ones_row, ones_row_b = self.sbconst("ones_row")
        identb, identb_b = self.sbconst("ident_bf")

        def scan_head_ops(h):
            j = h % 2
            QT, QT_b = self.biga_slot(16 + j * 3 + 0)
            KT_, KT_b = self.biga_slot(16 + j * 3 + 1)
            VT, VT_b = self.biga_slot(16 + j * 3 + 2)
            for t_, tb_, row0 in ((QT, QT_b, h), (KT_, KT_b, 16 + h), (VT, VT_b, 32 + h)):
                S.dma("sp", t_, self.qkvT[row0 * 128:(row0 + 1) * 128, :],
                      reads=(self.qkvT_b,), writes=(tb_,), name=f"{tag}h{h}ld")
            pbx, pbx_b = self.pb[4 + j], self.pb_b[4 + j]
            S32, S32_b = self.small[2 + j], self.small_b[2 + j]
            Sbf, Sbf_b = self.smallb[8 + j], self.smallb_b[8 + j]
            S.op("dve", lambda e, S32=S32: e.memset(S32[:], 0.0),
                 writes=(S32_b,), name=f"{tag}h{h}s0")
            atn, atn_b = self.biga_slot(22 + j)
            pv, pv_b = self.pbf[j], self.pbf_b[j]
            for c in range(NTT):
                csl = slice(c * 128, (c + 1) * 128)
                def step1(c=c, csl=csl):
                    S.op("pe", lambda e: e.matmul(
                            pbx[:, 0:128], KT_[:, csl], QT[:, csl],
                            start=True, stop=True),
                         reads=(QT_b, KT_b), writes=(pbx_b,), name=f"{tag}h{h}c{c}sc")
                    S.op("pe", lambda e: e.transpose(
                            pv[:, 0:128], VT[:, csl], identb[:]),
                         reads=(VT_b, identb_b), writes=(pv_b,), name=f"{tag}h{h}c{c}vt")
                    sm_t, sm_b = self.smallb[4 * j], self.smallb_b[4 * j]
                    S.op("dve", lambda e: e.tensor_mul(
                            sm_t[:], pbx[:, 0:128], dmaskT[:, h * 128:(h + 1) * 128]),
                         reads=(pbx_b, dmaskT_b), writes=(sm_b,), name=f"{tag}h{h}c{c}sm")
                    vb_t, vb_b = self.smallb[4 * j + 1], self.smallb_b[4 * j + 1]
                    S.op("dve", lambda e: e.tensor_copy(vb_t[:], pv[:, 0:128]),
                         reads=(pv_b,), writes=(vb_b,), name=f"{tag}h{h}c{c}ve")
                def step2(c=c, csl=csl):
                    sm_t, sm_b = self.smallb[4 * j], self.smallb_b[4 * j]
                    vb_t, vb_b = self.smallb[4 * j + 1], self.smallb_b[4 * j + 1]
                    S.op("pe", lambda e: e.transpose(
                            pv[:, 128:256], KT_[:, csl], identb[:]),
                         reads=(KT_b, identb_b), writes=(pv_b,), name=f"{tag}h{h}c{c}kt")
                    ksg, ksg_b = self.smallb[4 * j + 2], self.smallb_b[4 * j + 2]
                    S.op("dve", lambda e: e.tensor_scalar(
                            ksg[:], pv[:, 128:256], dkc[:, h:h + 1], None, ALU.mult),
                         reads=(pv_b, dkc_b), writes=(ksg_b,), name=f"{tag}h{h}c{c}ke")
                    qs, qs_b = self.smallb[4 * j + 3], self.smallb_b[4 * j + 3]
                    S.op("dve", lambda e: e.tensor_mul(
                            qs[:], QT[:, csl], dqb_c[:, h * 128:(h + 1) * 128]),
                         reads=(QT_b, dqb_b), writes=(qs_b,), name=f"{tag}h{h}c{c}qs")
                def step3(c=c, csl=csl):
                    sm_t, sm_b = self.smallb[4 * j], self.smallb_b[4 * j]
                    vb_t, vb_b = self.smallb[4 * j + 1], self.smallb_b[4 * j + 1]
                    ksg, ksg_b = self.smallb[4 * j + 2], self.smallb_b[4 * j + 2]
                    qs, qs_b = self.smallb[4 * j + 3], self.smallb_b[4 * j + 3]
                    S.op("pe", lambda e: e.matmul(pbx[:, 128:256], vb_t[:], sm_t[:],
                                                  start=True, stop=(c == 0)),
                         reads=(vb_b, sm_b), writes=(pbx_b,), name=f"{tag}h{h}c{c}o1")
                    if c > 0:
                        S.op("pe", lambda e: e.matmul(pbx[:, 128:256], Sbf[:], qs[:],
                                                      start=False, stop=True),
                             reads=(qs_b, Sbf_b), writes=(pbx_b,), name=f"{tag}h{h}c{c}o2")
                    S.op("pe", lambda e: e.matmul(pbx[:, 256:384], ksg[:], vb_t[:],
                                                  start=True, stop=True),
                         reads=(ksg_b, vb_b), writes=(pbx_b,), name=f"{tag}h{h}c{c}su")
                def step4(c=c, csl=csl):
                    S.op("dve", lambda e: e.tensor_copy(atn[:, csl], pbx[:, 128:256]),
                         reads=(pbx_b,), writes=(atn_b,), name=f"{tag}h{h}c{c}oe")
                    S.op("dve", lambda e: e.scalar_tensor_tensor(
                            S32[:], S32[:], LT_LAMC[h], pbx[:, 256:384],
                            ALU.mult, ALU.add),
                         reads=(S32_b, pbx_b), writes=(S32_b,), name=f"{tag}h{h}c{c}sd")
                    if c < NTT - 1:
                        S.op("act", lambda e: e.activation(Sbf[:], S32[:], AF.Copy),
                             reads=(S32_b,), writes=(Sbf_b,), name=f"{tag}h{h}c{c}sb")
                yield step1
                yield step2
                yield step3
                yield step4
            def final():
                S.dma("sp", self.attnT[h * 128:(h + 1) * 128, :], atn,
                      reads=(atn_b,), writes=(self.attnT_b,), name=f"{tag}h{h}st")
                if h == 0 and "atn0" in DEBUG_DUMPS:
                    out0 = self.nc.dram_tensor("dbg_atn0", [128, T], BF16,
                                               kind="ExternalOutput").ap()
                    ob0 = Buf("dbg_atn0")
                    S.dma("act", out0, atn, reads=(atn_b,), writes=(ob0,),
                          name="dump_atn0")
                    S.op("sp", lambda e: e.nop(), reads=(ob0,), name="dump_atn0w")
            yield final

        def scan_pair(h0):
            gens = [scan_head_ops(h0), scan_head_ops(h0 + 1)]
            done = [False, False]
            while not all(done):
                for i in (0, 1):
                    if not done[i]:
                        try:
                            next(gens[i])()
                        except StopIteration:
                            done[i] = True

        def after_oc(oc):
            if oc >= 33 and (oc - 33) % 2 == 0:
                scan_pair(oc - 33)

        self.dump("xhat", self.biga[:, 0:32768], self.biga_b[0:16],
                  (128, 32768), BF16)
        self.dump("rows", self.SR[0:1, 0:4096], self.rows_b[0:2], (1, 4096), F32)
        self.proj(tag + "qkv", f"{L}a_wqkvT", 0, 48, D, evac_qkv, after_oc=after_oc)
        self.dump("qkvT", self.qkvT, [self.qkvT_b], (3 * D, T), BF16)
        self.dump("attnT", self.attnT, [self.attnT_b], (D, T), BF16)

        def attn_slot(k):
            ap, b = self.biga_slot(k)
            return ap, b
        for k in range(NK):
            ap, b = self.biga_slot(k)
            S.dma("sp", ap, self.attnT[k * 128:(k + 1) * 128, :],
                  reads=(self.attnT_b,), writes=(b,), name=f"{tag}lda{k}")

        bo, bo_b = self.sbconst(f"{L}a_bout")
        xloaded = {}
        def evac_wout(oc, tb, pa, pab):
            if oc not in xloaded:
                xs, xs_b = self.fw[oc % 2], self.fw_b[oc % 2]
                S.dma("sp", xs, self.cur_ap[oc * 128:(oc + 1) * 128, :],
                      reads=(self.cur_b[oc],), writes=(xs_b,), name=f"{tag}rx{oc}")
                xloaded[oc] = (xs, xs_b)
            xs, xs_b = xloaded[oc]
            sl = slice(tb * 512, (tb + 1) * 512)
            S.op("dve", lambda e, pa=pa, oc=oc, sl=sl, xs=xs: e.scalar_tensor_tensor(
                    xs[:, sl], pa[:, :], bo[:, oc:oc + 1], xs[:, sl], ALU.add, ALU.add),
                 reads=(pab, bo_b, xs_b), writes=(xs_b,), name=f"{tag}ro{oc}_{tb}")
            if tb == 3:
                S.dma("sp", dst_ap[oc * 128:(oc + 1) * 128, :], xs,
                      reads=(xs_b,), writes=(dst_b[oc],), name=f"{tag}wx{oc}")

        self.proj(tag + "wo", f"{L}a_woutT", 0, 16, D, evac_wout)
        self.advance_x(dst_ap, dst_b)


    def slot_l0ffn(self, last):
        tag = "L0f"
        S = self.S
        dst_ap, dst_b = self.next_x(last)
        self.stats_pass(tag)
        self.make_s_ms_rows(tag, "ln")
        var, var_b = self.s4s(3)
        r, r_b = self.s4s(4)
        t2, t2_b = self.s4s(6)
        S.op("dve", lambda e: e.tensor_mul(t2, var, r),
             reads=(var_b, r_b), writes=(t2_b,), name=f"{tag}n1")
        S.op("dve", lambda e: e.tensor_mul(t2, t2, r),
             reads=(t2_b, r_b), writes=(t2_b,), name=f"{tag}n2")
        S.op("dve", lambda e: e.tensor_scalar_mul(t2, t2, float(D)),
             reads=(t2_b,), writes=(t2_b,), name=f"{tag}n3")
        self.to_row(6, 2, tag)
        self.xhat_inplace(tag)

        wsl0, wsl0_b = self.wslab[0], self.wslab_b[0]
        wsl1, wsl1_b = self.wslab[1], self.wslab_b[1]
        WallT, WallT_b = self.win("L0f_WallT")
        updT, updT_b = self.win("L0f_updT")
        smdT, smdT_b = self.win("L0f_smdT")
        dwT, dwT_b = self.win("L0f_dwT")
        dgnT, dgnT_b = self.win("L0f_dgnT")
        biasw, biasw_b = self.win("L0f_biasw")
        def slabload(dst, dstb, src, srcb, o, nm):
            sv = src.rearrange("(c p) o -> p c o", p=128)
            S.dma("pool", dst.rearrange("p (c o) -> p c o", o=o), sv,
                  reads=(srcb,), writes=(dstb,), name=nm)
        slabload(wsl0[:, 0:1024], wsl0_b, WallT, WallT_b, 64, f"{tag}lw")
        slabload(wsl0[:, 1024:2048], wsl0_b, updT, updT_b, 64, f"{tag}lu")
        smd_sb, smd_sb_b = self.biga_slot(16)
        slabload(smd_sb[:, 0:128], smd_sb_b, smdT, smdT_b, 8, f"{tag}ls")
        dw_sb, dw_sb_b = self.biga_slot(17)
        slabload(dw_sb[:, 0:128], dw_sb_b, dwT, dwT_b, 8, f"{tag}ld")
        dgn_sb, dgn_sb_b = self.biga_slot(18)
        S.dma("pool", dgn_sb[0:64, :], dgnT, reads=(dgnT_b,), writes=(dgn_sb_b,),
              name=f"{tag}lg")
        bw_sb, bw_sb_b = self.biga_slot(19)
        S.dma("pool", bw_sb[0:8, :], biasw, reads=(biasw_b,), writes=(bw_sb_b,),
              name=f"{tag}lb")
        oh8bf, oh8bf_b = self.biga_slot(20)
        z_sb, z_sb_b = self.biga_slot(21)
        E8m, E8m_b = self.sbconst("E8mask")
        E8b, E8b_b = self.sbconst("E8bf")
        P8b, P8b_b = self.sbconst("P8bf")
        pen8, pen8_b = self.sbconst("pen8")
        onesc, onesc_b = self.sbconst("ones_col_bf")
        ones_row, ones_row_b = self.sbconst("ones_row")
        identf, identf_b = self.sbconst("ident")

        pb0, pb0_b = self.pb[0], self.pb_b[0]
        pb1, pb1_b = self.pb[1], self.pb_b[1]
        pb2, pb2_b = self.pb[2], self.pb_b[2]
        pb3, pb3_b = self.pb[3], self.pb_b[3]
        pb4, pb4_b = self.pb[4], self.pb_b[4]
        pb5, pb5_b = self.pb[5], self.pb_b[5]

        sq_sb, sq_sb_b = self.stg[1], self.stg_b[1]
        S.op("act", lambda e: e.activation(sq_sb[:, 0:128], smd_sb[:, 0:128],
                                           AF.Square),
             reads=(smd_sb_b,), writes=(sq_sb_b,), name=f"{tag}sqm")
        for c in range(NK):
            S.op("pe", lambda e, c=c: e.matmul(
                    pb4[0:8, 0:1], sq_sb[:, c * 8:(c + 1) * 8], onesc[:],
                    start=(c == 0), stop=(c == NK - 1)),
                 reads=(sq_sb_b, onesc_b), writes=(pb4_b,), name=f"{tag}mq{c}")
        musq_col = self.small[1]
        musq_col_b = self.small_b[1]
        S.op("dve", lambda e: e.tensor_copy(musq_col[0:8, 2:3], pb4[0:8, 0:1]),
             reads=(pb4_b,), writes=(musq_col_b,), name=f"{tag}mqe")
        slp, slp_b = self.small[0], self.small_b[0]
        S.dma("sp", slp[0:8, 4:5], self.win("L0f_slope")[0],
              reads=(slope_cb if False else self.in_bufs["L0f_slope"],),
              writes=(slp_b,), name=f"{tag}lsl")
        S.dma("sp", slp[0:8, 5:6], self.win("L0f_center")[0],
              reads=(self.in_bufs["L0f_center"],), writes=(slp_b,), name=f"{tag}lce")
        S.op("dve", lambda e: e.tensor_mul(slp[0:8, 6:7], slp[0:8, 4:5],
                                           slp[0:8, 5:6]),
             reads=(slp_b,), writes=(slp_b,), name=f"{tag}gb1")
        S.op("dve", lambda e: e.tensor_scalar_mul(slp[0:8, 6:7], slp[0:8, 6:7], -1.0),
             reads=(slp_b,), writes=(slp_b,), name=f"{tag}gb2")

        def diag64(lhs_sl, rhs_sl, rhs_b, out_sl, nm):
            for c in range(NK):
                S.op("pe", lambda e, c=c: e.matmul(
                        pb5[0:64, 0:8], wsl0[:, lhs_sl + c * 64: lhs_sl + (c + 1) * 64],
                        rhs_sl[:, c * 8:(c + 1) * 8],
                        start=(c == 0), stop=(c == NK - 1)),
                     reads=(wsl0_b, rhs_b), writes=(pb5_b,), name=f"{nm}{c}")
            S.op("dve", lambda e: e.tensor_mul(self.tmp[0][0:64, 0:8],
                                               pb5[0:64, 0:8], E8m[:]),
                 reads=(pb5_b, E8m_b), writes=(self.tmp_b[0],), name=f"{nm}m")
            S.op("dve", lambda e: e.reduce_sum(musq_col[0:64, out_sl:out_sl + 1],
                                               self.tmp[0][0:64, 0:8], axis=AX.X),
                 reads=(self.tmp_b[0],), writes=(musq_col_b,), name=f"{nm}r")
        diag64(0, smd_sb, smd_sb_b, 0, f"{tag}muw")
        diag64(1024, dw_sb, dw_sb_b, 1, f"{tag}mdb")

        umu, umu_b = self.sr32(0, 8)
        for tb in range(NTB):
            for c in range(NK):
                xh, xh_b = self.biga_slot(c)
                S.op("pe", lambda e, c=c, tb=tb, xh=xh: e.matmul(
                        pb4[0:8, :], smd_sb[:, c * 8:(c + 1) * 8],
                        xh[:, tb * 512:(tb + 1) * 512],
                        start=(c == 0), stop=(c == NK - 1)),
                     reads=(smd_sb_b, xh_b), writes=(pb4_b,), name=f"{tag}um{tb}_{c}")
            S.op("dve", lambda e, tb=tb: e.tensor_copy(
                    umu[0:8, tb * 512:(tb + 1) * 512], pb4[0:8, :]),
                 reads=(pb4_b,), writes=(umu_b,), name=f"{tag}ume{tb}")

        nsqr, nsqr_b = self.row(2)
        d2, d2_b = self.sr32(1, 8)
        score, score_b = self.sr32(2, 8)
        for tb in range(NTB):
            sl = slice(tb * 512, (tb + 1) * 512)
            for c in range(NK):
                xh, xh_b = self.biga_slot(c)
                S.op("pe", lambda e, c=c, sl=sl, xh=xh: e.matmul(
                        pb5[0:64, :], wsl0[:, c * 64:(c + 1) * 64], xh[:, sl],
                        start=(c == 0), stop=(c == NK - 1)),
                     reads=(wsl0_b, xh_b), writes=(pb5_b,), name=f"{tag}pj{tb}_{c}")
            S.op("dve", lambda e: e.tensor_scalar(
                    self.tmp[0][0:64, :], pb5[0:64, :], musq_col[0:64, 0:1],
                    None, ALU.subtract),
                 reads=(pb5_b, musq_col_b), writes=(self.tmp_b[0],),
                 name=f"{tag}pc{tb}")
            S.op("act", lambda e: e.activation(self.stg[0][0:64, :],
                                               self.tmp[0][0:64, :], AF.Square),
                 reads=(self.tmp_b[0],), writes=(self.stg_b[0],), name=f"{tag}ps{tb}")
            S.op("pe", lambda e: e.matmul(pb4[0:8, :], E8b[:], self.stg[0][0:64, :],
                                          start=True, stop=True),
                 reads=(E8b_b, self.stg_b[0]), writes=(pb4_b,), name=f"{tag}pg{tb}")
            S.op("pe", lambda e, sl=sl: e.matmul(
                    pb3[0:8, :], ones_row[0:1, 0:8], nsqr[0:1, sl],
                    start=True, stop=True),
                 reads=(ones_row_b, nsqr_b), writes=(pb3_b,), name=f"{tag}nb{tb}")
            S.op("dve", lambda e, sl=sl: e.scalar_tensor_tensor(
                    d2[0:8, sl], umu[0:8, sl], -2.0, pb3[0:8, :], ALU.mult, ALU.add),
                 reads=(umu_b, pb3_b), writes=(d2_b,), name=f"{tag}d2{tb}")
            S.op("dve", lambda e, sl=sl: e.tensor_scalar(
                    d2[0:8, sl], d2[0:8, sl], musq_col[0:8, 2:3], None, ALU.add),
                 reads=(d2_b, musq_col_b), writes=(d2_b,), name=f"{tag}d2b{tb}")
            S.op("dve", lambda e, sl=sl: e.reciprocal(d2[0:8, sl], d2[0:8, sl]),
                 reads=(d2_b,), writes=(d2_b,), name=f"{tag}rd{tb}")
            S.op("dve", lambda e, sl=sl: e.tensor_mul(
                    score[0:8, sl], pb4[0:8, :], d2[0:8, sl]),
                 reads=(pb4_b, d2_b), writes=(score_b,), name=f"{tag}sc{tb}")
        g8, g8_b = self.sr32(3, 8)
        S.op("act", lambda e: e.activation(g8[0:8, :], score[0:8, :], AF.Sigmoid,
                                           bias=slp[0:8, 6:7], scale=slp[0:8, 4:5]),
             reads=(score_b, slp_b), writes=(g8_b,), name=f"{tag}gate")
        S.op("dve", lambda e: e.tensor_scalar(g8[0:8, :], g8[0:8, :], pen8[:],
                                              None, ALU.subtract),
             reads=(g8_b, pen8_b), writes=(g8_b,), name=f"{tag}pen")
        GT = self.cols[:, 0:128]
        OH = self.cols[:, 128:256]
        SEL = self.cols[:, 256:384]
        GT_b, OH_b, SEL_b = self.cols_b[0], self.cols_b[1], self.cols_b[2]
        for tt in range(NTT):
            S.op("pe", lambda e, tt=tt: e.transpose(
                    pb4[0:128, 0:8], g8[0:8, tt * 128:(tt + 1) * 128],
                    identf[32:40, 32:40]),
                 reads=(g8_b, identf_b), writes=(pb4_b,), name=f"{tag}gt{tt}")
            S.op("dve", lambda e, tt=tt: e.tensor_copy(
                    GT[:, tt * 8:(tt + 1) * 8], pb4[0:128, 0:8]),
                 reads=(pb4_b,), writes=(GT_b,), name=f"{tag}gte{tt}")
            S.op("dve", lambda e, tt=tt: e.reduce_max(
                    SEL[:, tt:tt + 1], GT[:, tt * 8:(tt + 1) * 8], axis=AX.X),
                 reads=(GT_b,), writes=(SEL_b,), name=f"{tag}mx{tt}")
            S.op("dve", lambda e, tt=tt: e.tensor_scalar(
                    OH[:, tt * 8:(tt + 1) * 8], GT[:, tt * 8:(tt + 1) * 8],
                    SEL[:, tt:tt + 1], None, ALU.is_ge),
                 reads=(GT_b, SEL_b), writes=(OH_b,), name=f"{tag}oh{tt}")
        self.cols_to_row(SEL, SEL_b, 16, 2, tag + "sel")
        selr, selr_b = self.row(2)
        for tt in range(NTT):
            S.op("pe", lambda e, tt=tt: e.transpose(
                    pb4[0:8, 0:128], OH[:, tt * 8:(tt + 1) * 8], identf[:]),
                 reads=(OH_b, identf_b), writes=(pb4_b,), name=f"{tag}oht{tt}")
            S.op("act", lambda e, tt=tt: e.activation(
                    oh8bf[0:8, tt * 128:(tt + 1) * 128], pb4[0:8, 0:128], AF.Copy),
                 reads=(pb4_b,), writes=(oh8bf_b,), name=f"{tag}ohe{tt}")
        srow, srow_b = self.row(0)
        msrow, msrow_b = self.row(1)
        Prow, Prow_b = self.row(3)
        Qrow, Qrow_b = self.row(4)
        S.op("dve", lambda e: e.tensor_scalar(Prow, selr, -1.0, 1.0,
                                              ALU.mult, ALU.add),
             reads=(selr_b,), writes=(Prow_b,), name=f"{tag}p1")
        S.op("dve", lambda e: e.tensor_mul(Qrow, msrow, Prow),
             reads=(msrow_b, Prow_b), writes=(Qrow_b,), name=f"{tag}q1")
        S.op("dve", lambda e: e.tensor_scalar_mul(Qrow, Qrow, -1.0),
             reads=(Qrow_b,), writes=(Qrow_b,), name=f"{tag}q2")
        S.op("dve", lambda e: e.tensor_mul(Prow, srow, Prow),
             reads=(srow_b, Prow_b), writes=(Prow_b,), name=f"{tag}p2")
        S.op("dve", lambda e: e.tensor_scalar_add(Prow, Prow, 1.0),
             reads=(Prow_b,), writes=(Prow_b,), name=f"{tag}p3")

        for tb in range(NTB):
            sl = slice(tb * 512, (tb + 1) * 512)
            for c in range(NK):
                xh, xh_b = self.biga_slot(c)
                S.op("pe", lambda e, c=c, sl=sl, xh=xh: e.matmul(
                        pb5[0:64, :], wsl0[:, 1024 + c * 64:1024 + (c + 1) * 64],
                        xh[:, sl], start=(c == 0), stop=(c == NK - 1)),
                     reads=(wsl0_b, xh_b), writes=(pb5_b,), name=f"{tag}md{tb}_{c}")
            S.op("pe", lambda e, sl=sl: e.matmul(
                    pb4[0:64, :], P8b[:], oh8bf[0:8, sl], start=True, stop=True),
                 reads=(P8b_b, oh8bf_b), writes=(pb4_b,), name=f"{tag}ob{tb}")
            S.op("dve", lambda e: e.tensor_scalar(
                    self.tmp[0][0:64, :], pb5[0:64, :], musq_col[0:64, 1:2],
                    None, ALU.subtract),
                 reads=(pb5_b, musq_col_b), writes=(self.tmp_b[0],),
                 name=f"{tag}mz{tb}")
            S.op("dve", lambda e, sl=sl: e.tensor_mul(
                    z_sb[0:64, sl], self.tmp[0][0:64, :], pb4[0:64, :]),
                 reads=(self.tmp_b[0], pb4_b), writes=(z_sb_b,), name=f"{tag}zz{tb}")

        self.dump("l0f_sr", self.SR[0:80, :],
                  [umu_b, d2_b, score_b, g8_b, selr_b, Prow_b, Qrow_b,
                   srow_b, msrow_b, nsqr_b], (80, 10240), F32)
        self.dump("l0f_oh", oh8bf, [oh8bf_b], (128, 2048), BF16)
        self.dump("l0f_z", z_sb, [z_sb_b], (128, 2048), BF16)
        self.dump("l0f_cols", self.cols[:], [GT_b, OH_b, SEL_b], (128, 512), F32)
        self.dump("l0f_mc", musq_col[:], [musq_col_b], (128, 128), F32)
        for tb in range(NTB):
            sl = slice(tb * 512, (tb + 1) * 512)
            for ri, rowsrc in enumerate(((Prow, Prow_b), (Qrow, Qrow_b),
                                         (selr, selr_b))):
                ra, rb = rowsrc
                S.op("pe", lambda e, ra=ra, sl=sl: e.matmul(
                        pb2[:, :], ones_row[:], ra[0:1, sl], start=True, stop=True),
                     reads=(rb, ones_row_b), writes=(pb2_b,), name=f"{tag}fb{tb}_{ri}")
                S.op("dve", lambda e, ri=ri: e.tensor_copy(self.pqs[ri][:], pb2[:, :]),
                     reads=(pb2_b,), writes=(self.pqs_b[ri],), name=f"{tag}fe{tb}_{ri}")
            for oc in range(NK):
                bank, bank_b = (pb0, pb0_b) if oc % 2 == 0 else (pb1, pb1_b)
                S.op("pe", lambda e, oc=oc, sl=sl, bank=bank: e.matmul(
                        bank[:, :], dgn_sb[0:64, oc * 128:(oc + 1) * 128],
                        z_sb[0:64, sl], start=True, stop=False),
                     reads=(dgn_sb_b, z_sb_b), writes=(bank_b,), name=f"{tag}fm{tb}_{oc}")
                S.op("pe", lambda e, oc=oc, sl=sl, bank=bank: e.matmul(
                        bank[:, :], bw_sb[0:8, oc * 128:(oc + 1) * 128],
                        oh8bf[0:8, sl], start=False, stop=True),
                     reads=(bw_sb_b, oh8bf_b), writes=(bank_b,),
                     name=f"{tag}fn{tb}_{oc}")
                hw = (oc >> 1) & 1
                xs = self.fw[oc % 2][:, hw * 512:(hw + 1) * 512]
                xs_b = self.fw_b[oc % 2]
                S.dma("sp", xs, self.cur_ap[oc * 128:(oc + 1) * 128, sl],
                      reads=(self.cur_b[oc],), writes=(xs_b,), name=f"{tag}fx{tb}_{oc}")
                S.op("dve", lambda e, xs=xs: e.tensor_mul(
                        self.tmp[0][:, :], xs, self.pqs[0][:]),
                     reads=(xs_b, self.pqs_b[0]), writes=(self.tmp_b[0],),
                     name=f"{tag}f1{tb}_{oc}")
                S.op("dve", lambda e: e.tensor_add(
                        self.tmp[0][:, :], self.tmp[0][:, :], self.pqs[1][:]),
                     reads=(self.tmp_b[0], self.pqs_b[1]), writes=(self.tmp_b[0],),
                     name=f"{tag}f2{tb}_{oc}")
                S.op("dve", lambda e, bank=bank: e.tensor_mul(
                        self.tmp[1][:, :], bank[:, :], self.pqs[2][:]),
                     reads=(bank_b, self.pqs_b[2]), writes=(self.tmp_b[1],),
                     name=f"{tag}f3{tb}_{oc}")
                S.op("dve", lambda e, xs=xs: e.tensor_add(
                        xs, self.tmp[0][:, :], self.tmp[1][:, :]),
                     reads=(self.tmp_b[0], self.tmp_b[1]), writes=(xs_b,),
                     name=f"{tag}f4{tb}_{oc}")
                S.dma("sp", dst_ap[oc * 128:(oc + 1) * 128, sl], xs,
                      reads=(xs_b,), writes=(dst_b[oc],), name=f"{tag}fw{tb}_{oc}")
        self.advance_x(dst_ap, dst_b)

    def final_combine(self, tag, dst_ap, dst_b, Arow, Arow_b, Qrow, Qrow_b,
                      mlp_bank_fn):
        S = self.S
        ones_row, ones_row_b = self.sbconst("ones_row")
        pbc, pbc_b = self.pb[4], self.pb_b[4]
        for tb in range(NTB):
            sl = slice(tb * 512, (tb + 1) * 512)
            for ri, (ra, rb) in enumerate(((Arow, Arow_b), (Qrow, Qrow_b))):
                S.op("pe", lambda e, ra=ra, sl=sl: e.matmul(
                        pbc[:, :], ones_row[:], ra[0:1, sl], start=True, stop=True),
                     reads=(rb, ones_row_b), writes=(pbc_b,), name=f"{tag}cb{tb}_{ri}")
                S.op("dve", lambda e, ri=ri: e.tensor_copy(self.pqs[ri][:], pbc[:, :]),
                     reads=(pbc_b,), writes=(self.pqs_b[ri],), name=f"{tag}ce{tb}_{ri}")
            for oc in range(NK):
                bank, bank_b = mlp_bank_fn(tb, oc)
                hw = (oc >> 1) & 1
                xs = self.fw[oc % 2][:, hw * 512:(hw + 1) * 512]
                xs_b = self.fw_b[oc % 2]
                S.dma("sp", xs, self.cur_ap[oc * 128:(oc + 1) * 128, sl],
                      reads=(self.cur_b[oc],), writes=(xs_b,), name=f"{tag}cx{tb}_{oc}")
                S.op("dve", lambda e, xs=xs: e.tensor_mul(
                        self.tmp[0][:, :], xs, self.pqs[0][:]),
                     reads=(xs_b, self.pqs_b[0]), writes=(self.tmp_b[0],),
                     name=f"{tag}c1{tb}_{oc}")
                S.op("dve", lambda e: e.tensor_add(
                        self.tmp[0][:, :], self.tmp[0][:, :], self.pqs[1][:]),
                     reads=(self.tmp_b[0], self.pqs_b[1]), writes=(self.tmp_b[0],),
                     name=f"{tag}c2{tb}_{oc}")
                S.op("dve", lambda e, xs=xs, bank=bank: e.tensor_add(
                        xs, self.tmp[0][:, :], bank[:, :]),
                     reads=(self.tmp_b[0], bank_b), writes=(xs_b,),
                     name=f"{tag}c3{tb}_{oc}")
                S.dma("sp", dst_ap[oc * 128:(oc + 1) * 128, sl], xs,
                      reads=(xs_b,), writes=(dst_b[oc],), name=f"{tag}cw{tb}_{oc}")

    def slot_l1ffn(self, last):
        tag = "L1f"
        S = self.S
        dst_ap, dst_b = self.next_x(last)
        self.stats_pass(tag)
        self.derive_mu_r(tag)
        mu, mu_b = self.s4s(2)
        var, var_b = self.s4s(3)
        r, r_b = self.s4s(4)
        t5, t5_b = self.s4s(5)
        t6, t6_b = self.s4s(6)
        S.op("dve", lambda e: e.tensor_mul(t5, var, r), reads=(var_b, r_b),
             writes=(t5_b,), name=f"{tag}s1")
        S.op("dve", lambda e: e.tensor_mul(t5, t5, r), reads=(t5_b, r_b),
             writes=(t5_b,), name=f"{tag}s2")
        S.op("dve", lambda e: e.tensor_scalar_add(t5, t5, 1e-6), reads=(t5_b,),
             writes=(t5_b,), name=f"{tag}s3")
        S.op("act", lambda e: e.activation(t5, t5, AF.Sqrt), reads=(t5_b,),
             writes=(t5_b,), name=f"{tag}s4")
        S.op("dve", lambda e: e.reciprocal(t5, t5), reads=(t5_b,),
             writes=(t5_b,), name=f"{tag}s5")
        S.op("dve", lambda e: e.tensor_mul(t5, t5, r), reads=(t5_b, r_b),
             writes=(t5_b,), name=f"{tag}s6")
        S.op("dve", lambda e: e.tensor_mul(t6, mu, t5), reads=(mu_b, t5_b),
             writes=(t6_b,), name=f"{tag}s7")
        self.to_row(5, 0, tag)
        self.to_row(6, 1, tag)
        self.xhat_inplace(tag)
        g1, g1_b = self.sbconst("L1f_g1")
        g2, g2_b = self.sbconst("L1f_g2")
        for k in range(NK):
            xh, xh_b = self.biga_slot(k)
            S.op("act", lambda e, xh=xh, k=k: e.activation(
                    xh, xh, AF.Copy, scale=g1[:, k:k + 1]),
                 reads=(xh_b, g1_b), writes=(xh_b,), name=f"{tag}g1_{k}")

        qscale = float(128.0 ** -0.5)
        def evac_qkv(oc, tb, pa, pab):
            stj, stj_b = self.stg[tb % 2], self.stg_b[tb % 2]
            sc = qscale if oc < 16 else 1.0
            S.op("dve", lambda e, pa=pa, stj=stj, sc=sc: e.tensor_scalar_mul(
                    stj[:], pa[:, :], sc),
                 reads=(pab,), writes=(stj_b,), name=f"{tag}eq{oc}_{tb}")
            S.dma("sp", self.qkvT[oc * 128:(oc + 1) * 128, tb * 512:(tb + 1) * 512],
                  stj[:], reads=(stj_b,), writes=(self.qkvT_b,),
                  name=f"{tag}sq{oc}_{tb}")
        self.proj(tag + "q", "L1f_WqT", 0, 16, D, evac_qkv)
        self.proj(tag + "k", "L1f_WkT", 16, 32, D, evac_qkv, o_w_off=16)
        self.proj(tag + "v", "L1f_WvT", 32, 48, D, evac_qkv, o_w_off=32)

        self.dump("l1f_attnT_pre", self.attnT, [self.attnT_b], (D, T), BF16)
        A_ap, A_b = self.win("L1f_A")
        Asb, Asb_b = self.biga_slot(16)
        av = A_ap.rearrange("(c p) o -> p c o", p=128)
        S.dma("pool", Asb[:, 0:256].rearrange("p (c o) -> p c o", o=16), av,
              reads=(A_b,), writes=(Asb_b,), name=f"{tag}lA")
        Wo_ap, Wo_b = self.win("L1f_Wo")
        wov = Wo_ap.rearrange("(c p) o -> p c o", p=128)
        pW, pW_b = self.pb[4], self.pb_b[4]
        for f in range(16):
            j = f % 2
            wsl, wsl_b = self.wslab[j], self.wslab_b[j]
            S.dma("pool", wsl[:].rearrange("p (c o) -> p c o", o=128),
                  wov[:, :, f * 128:(f + 1) * 128],
                  reads=(Wo_b,), writes=(wsl_b,), name=f"{tag}w2l{f}")
            for o in range(16):
                S.op("pe", lambda e, o=o, wsl=wsl: e.matmul(
                        pW[0:128, 0:16], wsl[:, o * 128:(o + 1) * 128],
                        Asb[:, o * 16:(o + 1) * 16],
                        start=(o == 0), stop=(o == 15)),
                     reads=(wsl_b, Asb_b), writes=(pW_b,), name=f"{tag}w2m{f}_{o}")
            S.op("act", lambda e, f=f: e.activation(
                    self.W2sb[:, f * 16:(f + 1) * 16], pW[0:128, 0:16], AF.Copy),
                 reads=(pW_b,), writes=(self.W2sb_b,), name=f"{tag}w2e{f}")

        dmk, dmk_b = self.sbconst("diag_mask")
        dmkT, dmkT_b = self.sbconst("diag_maskT")
        identb, identb_b = self.sbconst("ident_bf")
        ones_row_bf, ones_row_bf_b = self.sbconst("ones_row_bf")

        def attn_head_ops(h):
            j = h % 2
            QT, QT_b = self.biga_slot(16 + j * 3 + 0)
            KT_, KT_b = self.biga_slot(16 + j * 3 + 1)
            VT, VT_b = self.biga_slot(16 + j * 3 + 2)
            for t_, tb_, row0 in ((QT, QT_b, h), (KT_, KT_b, 16 + h),
                                  (VT, VT_b, 32 + h)):
                S.dma("sp", t_, self.qkvT[row0 * 128:(row0 + 1) * 128, :],
                      reads=(self.qkvT_b,), writes=(tb_,), name=f"{tag}h{h}ld")
            psc, psc_b = (self.pb[0], self.pb_b[0]) if j == 0 else (self.pb[2], self.pb_b[2])
            pvx, pvx_b = (self.pb[1], self.pb_b[1]) if j == 0 else (self.pb[3], self.pb_b[3])
            pvt, pvt_b = self.pbf[j], self.pbf_b[j]
            fwj, fwj_b = self.fw[j], self.fw_b[j]
            negm = self.cols[:, j * 128 + 0:j * 128 + 16]
            Zc = self.cols[:, j * 128 + 16:j * 128 + 32]
            lnZ = self.cols[:, j * 128 + 32:j * 128 + 48]
            bcol = self.cols[:, j * 128 + 48:j * 128 + 64]
            colsb = self.cols_b[j]
            atn, atn_b = self.biga_slot(22 + j)
            for c in range(NTT):
                def vstep(c=c):
                    S.op("pe", lambda e, c=c: e.transpose(
                            pvt[:, 0:128], VT[:, c * 128:(c + 1) * 128], identb[:]),
                         reads=(VT_b, identb_b), writes=(pvt_b,), name=f"{tag}h{h}v{c}t")
                    S.op("dve", lambda e, c=c: e.tensor_copy(
                            VT[:, c * 128:(c + 1) * 128], pvt[:, 0:128]),
                         reads=(pvt_b,), writes=(VT_b,), name=f"{tag}h{h}v{c}e")
                yield vstep
            for qt in range(NTT):
                def astep(qt=qt):
                    L = (qt + 1) * 128
                    nb = qt // 4 + 1
                    for jb in range(nb):
                        w = 512 if jb < qt // 4 else (qt % 4 + 1) * 128
                        S.op("pe", lambda e, qt=qt, jb=jb, w=w: e.matmul(
                                psc[:, 0:w], QT[:, qt * 128:(qt + 1) * 128],
                                KT_[:, jb * 512:jb * 512 + w],
                                start=True, stop=True),
                             reads=(QT_b, KT_b), writes=(psc_b,),
                             name=f"{tag}h{h}a{qt}_{jb}")
                        if jb < qt // 4:
                            S.op("dve", lambda e, jb=jb, w=w: e.tensor_copy(
                                    fwj[:, jb * 512:jb * 512 + w], psc[:, 0:w]),
                                 reads=(psc_b,), writes=(fwj_b,),
                                 name=f"{tag}h{h}ae{qt}_{jb}")
                        else:
                            if w > 128:
                                S.op("dve", lambda e, jb=jb, w=w: e.tensor_copy(
                                        fwj[:, jb * 512:jb * 512 + w - 128],
                                        psc[:, 0:w - 128]),
                                     reads=(psc_b,), writes=(fwj_b,),
                                     name=f"{tag}h{h}ap{qt}_{jb}")
                            S.op("dve", lambda e, qt=qt, w=w: e.tensor_add(
                                    fwj[:, qt * 128:(qt + 1) * 128],
                                    psc[:, w - 128:w], dmk[:]),
                                 reads=(psc_b, dmk_b), writes=(fwj_b,),
                                 name=f"{tag}h{h}ad{qt}")
                    S.op("dve", lambda e, qt=qt, L=L: e.reduce_max(
                            negm[:, qt:qt + 1], fwj[:, 0:L], axis=AX.X, negate=True),
                         reads=(fwj_b,), writes=(colsb,), name=f"{tag}h{h}am{qt}")
                    S.op("act", lambda e, qt=qt, L=L: e.activation(
                            fwj[:, 0:L], fwj[:, 0:L], AF.Exp,
                            bias=negm[:, qt:qt + 1], accum_out=Zc[:, qt:qt + 1]),
                         reads=(fwj_b, colsb), writes=(fwj_b, colsb),
                         name=f"{tag}h{h}az{qt}")
                yield astep
            def brows():
                S.op("act", lambda e: e.activation(lnZ, Zc, AF.Ln),
                     reads=(colsb,), writes=(colsb,), name=f"{tag}h{h}ln")
                S.op("dve", lambda e: e.tensor_sub(bcol, negm, lnZ),
                     reads=(colsb,), writes=(colsb,), name=f"{tag}h{h}bc")
                self.cols_to_row(bcol, colsb, 16, 2 if j == 0 else 1, tag + f"h{h}")
                brow, brow_b = self.row(2 if j == 0 else 1)
                hilo = self.SR[0:1, (3 if j == 0 else 4) * 2048:
                               (4 if j == 0 else 5) * 2048].bitcast(BF16)
                hilo_b = self.rows_b[3 if j == 0 else 4]
                S.op("act", lambda e: e.activation(hilo[0:1, 0:2048], brow, AF.Copy),
                     reads=(brow_b,), writes=(hilo_b,), name=f"{tag}h{h}hi")
                S.op("dve", lambda e: e.tensor_sub(self.tmp[j][0:1, 0:512],
                                                   brow[0:1, 0:512],
                                                   hilo[0:1, 0:512]),
                     reads=(brow_b, hilo_b), writes=(self.tmp_b[j],),
                     name=f"{tag}h{h}lo0")
                for pc in range(4):
                    if pc > 0:
                        S.op("dve", lambda e, pc=pc: e.tensor_sub(
                                self.tmp[j][0:1, 0:512],
                                brow[0:1, pc * 512:(pc + 1) * 512],
                                hilo[0:1, pc * 512:(pc + 1) * 512]),
                             reads=(brow_b, hilo_b), writes=(self.tmp_b[j],),
                             name=f"{tag}h{h}lo{pc}")
                    S.op("act", lambda e, pc=pc: e.activation(
                            hilo[0:1, 2048 + pc * 512:2048 + (pc + 1) * 512],
                            self.tmp[j][0:1, 0:512], AF.Copy),
                         reads=(self.tmp_b[j],), writes=(hilo_b,),
                         name=f"{tag}h{h}loc{pc}")
            yield brows
            hilo = self.SR[0:1, (3 if j == 0 else 4) * 2048:
                           (4 if j == 0 else 5) * 2048].bitcast(BF16)
            hilo_b = self.rows_b[3 if j == 0 else 4]
            for jb in range(4):
                for c in range(4 * jb + 4):
                    def bstep(jb=jb, c=c):
                        qsl = slice(jb * 512, (jb + 1) * 512)
                        S.op("pe", lambda e, c=c, qsl=qsl: e.matmul(
                                psc[:, 0:512], KT_[:, c * 128:(c + 1) * 128],
                                QT[:, qsl], start=True, stop=False),
                             reads=(QT_b, KT_b), writes=(psc_b,),
                             name=f"{tag}h{h}b{jb}_{c}")
                        S.op("pe", lambda e, jb=jb: e.matmul(
                                psc[:, 0:512], ones_row_bf[:],
                                hilo[0:1, jb * 512:(jb + 1) * 512],
                                start=False, stop=False),
                             reads=(hilo_b, ones_row_bf_b), writes=(psc_b,),
                             name=f"{tag}h{h}bh{jb}_{c}")
                        S.op("pe", lambda e, jb=jb: e.matmul(
                                psc[:, 0:512], ones_row_bf[:],
                                hilo[0:1, 2048 + jb * 512:2048 + (jb + 1) * 512],
                                start=False, stop=True),
                             reads=(hilo_b, ones_row_bf_b), writes=(psc_b,),
                             name=f"{tag}h{h}bl{jb}_{c}")
                        d = c - 4 * jb
                        pT, pT_b = self.stg[2 * j + c % 2], self.stg_b[2 * j + c % 2]
                        if d >= 0:
                            S.op("dve", lambda e, d=d: e.tensor_add(
                                    psc[:, d * 128:(d + 1) * 128],
                                    psc[:, d * 128:(d + 1) * 128], dmkT[:]),
                                 reads=(psc_b, dmkT_b), writes=(psc_b,),
                                 name=f"{tag}h{h}bm{jb}_{c}")
                            if d > 0:
                                S.op("dve", lambda e, d=d, pT=pT: e.memset(
                                        pT[:, 0:d * 128], 0.0),
                                     writes=(pT_b,), name=f"{tag}h{h}bz{jb}_{c}")
                            S.op("act", lambda e, d=d, pT=pT: e.activation(
                                    pT[:, d * 128:512], psc[:, d * 128:512], AF.Exp),
                                 reads=(psc_b,), writes=(pT_b,),
                                 name=f"{tag}h{h}be{jb}_{c}")
                        else:
                            S.op("act", lambda e, pT=pT: e.activation(
                                    pT[:], psc[:, 0:512], AF.Exp),
                                 reads=(psc_b,), writes=(pT_b,),
                                 name=f"{tag}h{h}be{jb}_{c}")
                        S.op("pe", lambda e, c=c, jb=jb, pT=pT: e.matmul(
                                pvx[:, 0:512], VT[:, c * 128:(c + 1) * 128], pT[:],
                                start=(c == 0), stop=(c == 4 * jb + 3)),
                             reads=(VT_b, pT_b), writes=(pvx_b,),
                             name=f"{tag}h{h}bp{jb}_{c}")
                    yield bstep
                def oevac(jb=jb):
                    S.op("dve", lambda e, jb=jb: e.tensor_copy(
                            atn[:, jb * 512:(jb + 1) * 512], pvx[:, 0:512]),
                         reads=(pvx_b,), writes=(atn_b,), name=f"{tag}h{h}oe{jb}")
                yield oevac
            def final():
                S.dma("sp", self.attnT[h * 128:(h + 1) * 128, :], atn,
                      reads=(atn_b,), writes=(self.attnT_b,), name=f"{tag}h{h}st")
            yield final

        def run_pair(h0):
            gens = [attn_head_ops(h0), attn_head_ops(h0 + 1)]
            done = [False, False]
            while not all(done):
                for i2 in (0, 1):
                    if not done[i2]:
                        try:
                            next(gens[i2])()
                        except StopIteration:
                            done[i2] = True
        for h0 in range(0, 16, 2):
            run_pair(h0)

        rat, rat_b = self.small[3], self.small_b[3]
        S.op("dve", lambda e: e.reciprocal(rat[:, 0:16], g1[:]),
             reads=(g1_b,), writes=(rat_b,), name=f"{tag}rt1")
        S.op("dve", lambda e: e.tensor_mul(rat[:, 0:16], rat[:, 0:16], g2[:]),
             reads=(rat_b, g2_b), writes=(rat_b,), name=f"{tag}rt2")
        for k in range(NK):
            xh, xh_b = self.biga_slot(k)
            S.op("act", lambda e, xh=xh, k=k: e.activation(
                    xh, xh, AF.Copy, scale=rat[:, k:k + 1]),
                 reads=(xh_b, rat_b), writes=(xh_b,), name=f"{tag}g2_{k}")

        Asb, Asb_b = self.biga_slot(16)
        S.dma("pool", Asb[:, 0:256].rearrange("p (c o) -> p c o", o=16), av,
              reads=(A_b,), writes=(Asb_b,), name=f"{tag}lA2")
        modt, modt_b = self.sr32(0, 16)
        hAt, hAt_b = self.sr32(1, 16)
        pW4, pW4_b = self.pb[4], self.pb_b[4]
        pW5, pW5_b = self.pb[5], self.pb_b[5]
        for tb in range(NTB):
            sl = slice(tb * 512, (tb + 1) * 512)
            for c in range(NK):
                at, at_b2 = self.biga_slot(17 + (c % 2) * 3)
                S.dma("sp", at[:, 0:512], self.attnT[c * 128:(c + 1) * 128, sl],
                      reads=(self.attnT_b,), writes=(at_b2,), name=f"{tag}ml{tb}_{c}")
                S.op("pe", lambda e, c=c, at=at: e.matmul(
                        pW4[0:16, 0:512], self.W2sb[:, c * 16:(c + 1) * 16],
                        at[:, 0:512], start=(c == 0), stop=(c == NK - 1)),
                     reads=(self.W2sb_b, at_b2), writes=(pW4_b,),
                     name=f"{tag}mm{tb}_{c}")
                xh, xh_b = self.biga_slot(c)
                S.op("pe", lambda e, c=c, sl=sl, xh=xh: e.matmul(
                        pW5[0:16, 0:512], Asb[:, c * 16:(c + 1) * 16], xh[:, sl],
                        start=(c == 0), stop=(c == NK - 1)),
                     reads=(Asb_b, xh_b), writes=(pW5_b,), name=f"{tag}ha{tb}_{c}")
            S.op("act", lambda e, sl=sl: e.activation(
                    modt[0:16, sl], pW4[0:16, 0:512], AF.Tanh),
                 reads=(pW4_b,), writes=(modt_b,), name=f"{tag}mt{tb}")
            S.op("dve", lambda e, sl=sl: e.tensor_copy(hAt[0:16, sl], pW5[0:16, 0:512]),
                 reads=(pW5_b,), writes=(hAt_b,), name=f"{tag}he{tb}")
        mB, mB_b = self.biga_slot(23)
        S.op("dve", lambda e: e.tensor_scalar_mul(modt[0:16, :], modt[0:16, :], 0.1),
             reads=(modt_b,), writes=(modt_b,), name=f"{tag}msc")
        S.op("dve", lambda e: e.tensor_mul(mB[0:16, :], modt[0:16, :], hAt[0:16, :]),
             reads=(modt_b, hAt_b), writes=(mB_b,), name=f"{tag}mb")

        self.dump("l1f_mB", mB, [mB_b], (128, 2048), BF16)
        self.dump("l1f_mod", self.SR[0:80, :], [modt_b, hAt_b], (80, 10240), F32)
        WinT, WinT_b = self.win("L1f_WinT")
        WupT, WupT_b = self.win("L1f_WupT")
        B_ap, B_b = self.win("L1f_B")
        winv = WinT.rearrange("(c p) o -> p c o", p=128)
        wupv = WupT.rearrange("(c p) o -> p c o", p=128)
        for oc in range(64):
            wsl0, wsl0_b = self.wslab[0], self.wslab_b[0]
            wsl1, wsl1_b = self.wslab[1], self.wslab_b[1]
            S.dma("pool", wsl0[:].rearrange("p (c o) -> p c o", o=128),
                  winv[:, :, oc * 128:(oc + 1) * 128],
                  reads=(WinT_b,), writes=(wsl0_b,), name=f"{tag}gl{oc}")
            S.dma("pool", wsl1[:].rearrange("p (c o) -> p c o", o=128),
                  wupv[:, :, oc * 128:(oc + 1) * 128],
                  reads=(WupT_b,), writes=(wsl1_b,), name=f"{tag}ul{oc}")
            Bs, Bs_b = self.smallb[6 + oc % 2], self.smallb_b[6 + oc % 2]
            S.dma("pool", Bs[0:16, :], B_ap[:, oc * 128:(oc + 1) * 128],
                  reads=(B_b,), writes=(Bs_b,), name=f"{tag}bl{oc}")
            for half in range(2):
                gbanks = [(self.pb[0], self.pb_b[0]), (self.pb[1], self.pb_b[1])]
                ubanks = [(self.pb[2], self.pb_b[2]), (self.pb[3], self.pb_b[3])]
                for k in range(NK):
                    for t2 in range(2):
                        tb = half * 2 + t2
                        sl = slice(tb * 512, (tb + 1) * 512)
                        xh, xh_b = self.biga_slot(k)
                        gb, gb_b = gbanks[t2]
                        ub, ub_b = ubanks[t2]
                        S.op("pe", lambda e, k=k, sl=sl, xh=xh, gb=gb: e.matmul(
                                gb[:, :], wsl0[:, k * 128:(k + 1) * 128], xh[:, sl],
                                start=(k == 0), stop=False),
                             reads=(wsl0_b, xh_b), writes=(gb_b,),
                             name=f"{tag}gm{oc}_{tb}_{k}")
                        S.op("pe", lambda e, k=k, sl=sl, xh=xh, ub=ub: e.matmul(
                                ub[:, :], wsl1[:, k * 128:(k + 1) * 128], xh[:, sl],
                                start=(k == 0), stop=(k == NK - 1)),
                             reads=(wsl1_b, xh_b), writes=(ub_b,),
                             name=f"{tag}um{oc}_{tb}_{k}")
                for t2 in range(2):
                    tb = half * 2 + t2
                    sl = slice(tb * 512, (tb + 1) * 512)
                    gb, gb_b = gbanks[t2]
                    ub, ub_b = ubanks[t2]
                    S.op("pe", lambda e, sl=sl, gb=gb, Bs=Bs: e.matmul(
                            gb[:, :], Bs[0:16, :], mB[0:16, sl],
                            start=False, stop=True),
                         reads=(Bs_b, mB_b), writes=(gb_b,), name=f"{tag}gb{oc}_{tb}")
                    sg, sg_b = self.stg[t2], self.stg_b[t2]
                    hd, hd_b = self.stg[2 + t2], self.stg_b[2 + t2]
                    S.op("act", lambda e, gb=gb, sg=sg: e.activation(
                            sg[:], gb[:, :], AF.Silu),
                         reads=(gb_b,), writes=(sg_b,), name=f"{tag}si{oc}_{tb}")
                    S.op("dve", lambda e, ub=ub, sg=sg, hd=hd: e.tensor_mul(
                            hd[:], ub[:, :], sg[:]),
                         reads=(ub_b, sg_b), writes=(hd_b,), name=f"{tag}hd{oc}_{tb}")
                    S.dma("sp", self.hidT[oc * 128:(oc + 1) * 128, sl], hd[:],
                          reads=(hd_b,), writes=(self.hidT_b,),
                          name=f"{tag}hs{oc}_{tb}")

        S.op("dve", lambda e: e.tensor_mul(t6, mu, r), reads=(mu_b, r_b),
             writes=(t6_b,), name=f"{tag}qf1")
        S.op("dve", lambda e: e.tensor_scalar_mul(t6, t6, -1.0), reads=(t6_b,),
             writes=(t6_b,), name=f"{tag}qf2")
        self.to_row(4, 0, tag + "A")
        self.to_row(6, 1, tag + "Q")
        Arow, Arow_b = self.row(0)
        Qrow, Qrow_b = self.row(1)
        S.op("dve", lambda e: e.tensor_scalar_add(Arow, Arow, 1.0),
             reads=(Arow_b,), writes=(Arow_b,), name=f"{tag}A1")
        WdnT, WdnT_b = self.win("L1f_WdnT")
        wdv = WdnT.rearrange("(c p) o -> p c o", p=128)

        def hid_chunk(k):
            sslot, sb = self.biga_slot(k // 4)
            return sslot[:, (k % 4) * 512:(k % 4 + 1) * 512], sb

        def mlp_bank(tb, oc):
            sl = slice(tb * 512, (tb + 1) * 512)
            if oc == 0:
                for sslot in range(NK):
                    ba, bb = self.biga_slot(sslot)
                    src = self.hidT.rearrange("(c p) t -> p c t", p=128)[
                        :, sslot * 4:(sslot + 1) * 4, sl]
                    S.dma("sp", ba.rearrange("p (c t) -> p c t", t=512)[:, 0:4, :],
                          src, reads=(self.hidT_b,), writes=(bb,),
                          name=f"{tag}dl{tb}_{sslot}")
            bank, bank_b = (self.pb[0], self.pb_b[0]) if oc % 2 == 0 else \
                           (self.pb[1], self.pb_b[1])
            for si in range(4):
                j2 = (tb * 64 + oc * 4 + si) % 2
                wsl, wsl_b = self.wslab[j2], self.wslab_b[j2]
                S.dma("pool", wsl[:].rearrange("p (c o) -> p c o", o=128),
                      wdv[:, si * 16:(si + 1) * 16, oc * 128:(oc + 1) * 128],
                      reads=(WdnT_b,), writes=(wsl_b,), name=f"{tag}dw{tb}_{oc}_{si}")
                for kk in range(16):
                    k = si * 16 + kk
                    ha, hb = hid_chunk(k)
                    S.op("pe", lambda e, kk=kk, k=k, ha=ha, wsl=wsl, bank=bank:
                         e.matmul(bank[:, :], wsl[:, kk * 128:(kk + 1) * 128], ha,
                                  start=(k == 0), stop=(k == 63)),
                         reads=(wsl_b, hb), writes=(bank_b,),
                         name=f"{tag}dm{tb}_{oc}_{k}")
            return bank, bank_b

        self.final_combine(tag, dst_ap, dst_b, Arow, Arow_b, Qrow, Qrow_b, mlp_bank)
        self.advance_x(dst_ap, dst_b)


    def rope_evac(self, tag, oc, tb, pa, pab, cosv, cosv_b, sinv, sinv_b,
                  rotP, rotP_b, scale, dst_rows):
        S = self.S
        sl = slice(tb * 512, (tb + 1) * 512)
        q_sb, q_sb_b = self.stg[0 + tb % 2], self.stg_b[0 + tb % 2]
        S.op("dve", lambda e: e.tensor_scalar_mul(q_sb[:], pa[:, :], scale),
             reads=(pab,), writes=(q_sb_b,), name=f"{tag}re{oc}_{tb}")
        prp, prp_b = self.pb[4 + tb % 2], self.pb_b[4 + tb % 2]
        S.op("pe", lambda e: e.matmul(prp[:, 0:512], rotP[:], q_sb[:],
                                      start=True, stop=True),
             reads=(rotP_b, q_sb_b), writes=(prp_b,), name=f"{tag}rp{oc}_{tb}")
        t0, t0_b = self.tmp[tb % 2], self.tmp_b[tb % 2]
        S.op("dve", lambda e: e.tensor_mul(t0[:], q_sb[:], cosv[:, sl]),
             reads=(q_sb_b, cosv_b), writes=(t0_b,), name=f"{tag}rc{oc}_{tb}")
        hd, hd_b = self.stg[2 + tb % 2], self.stg_b[2 + tb % 2]
        S.op("dve", lambda e: e.tensor_mul(hd[:], prp[:, 0:512], sinv[:, sl]),
             reads=(prp_b, sinv_b), writes=(hd_b,), name=f"{tag}rs{oc}_{tb}")
        S.op("dve", lambda e: e.tensor_add(hd[:], t0[:], hd[:]),
             reads=(t0_b, hd_b), writes=(hd_b,), name=f"{tag}ra{oc}_{tb}")
        S.dma("sp", self.qkvT[dst_rows * 128:(dst_rows + 1) * 128, sl], hd[:],
              reads=(hd_b,), writes=(self.qkvT_b,), name=f"{tag}rw{oc}_{tb}")

    def slot_l2attn(self, last):
        tag = "L2a"
        S = self.S
        dst_ap, dst_b = self.next_x(last)
        self.stats_pass(tag)
        self.make_s_ms_rows(tag, "ln")
        self.xhat_inplace(tag)
        cosv = self.biga[:, 16 * 2048:18 * 2048].bitcast(F32)
        sinv = self.biga[:, 18 * 2048:20 * 2048].bitcast(F32)
        cosv_b, sinv_b = self.biga_b[16], self.biga_b[18]
        lc, lc_b = self.win("l2_cos")
        ls, ls_b = self.win("l2_sin")
        S.dma("sp", cosv, lc, reads=(lc_b,), writes=(cosv_b,), name=f"{tag}lc")
        S.dma("sp", sinv, ls, reads=(ls_b,), writes=(sinv_b,), name=f"{tag}ls")
        rotP, rotP_b = self.sbconst("l2_rotP")
        qsc = float(128.0 ** -0.5)

        def evac_q(oc, tb, pa, pab):
            self.rope_evac(tag, oc, tb, pa, pab, cosv, cosv_b, sinv, sinv_b,
                           rotP, rotP_b, qsc, oc)
        self.proj(tag + "q", "L2a_Wq", 0, 16, D, evac_q)

        def evac_kv(oc, tb, pa, pab):
            if oc == 16:
                self.rope_evac(tag, oc, tb, pa, pab, cosv, cosv_b, sinv, sinv_b,
                               rotP, rotP_b, 1.0, 16)
            else:
                stj, stj_b = self.stg[tb % 2], self.stg_b[tb % 2]
                S.op("dve", lambda e, pa=pa, stj=stj: e.tensor_copy(stj[:], pa[:, :]),
                     reads=(pab,), writes=(stj_b,), name=f"{tag}ev{oc}_{tb}")
                S.dma("sp", self.qkvT[17 * 128:18 * 128,
                                      tb * 512:(tb + 1) * 512], stj[:],
                      reads=(stj_b,), writes=(self.qkvT_b,), name=f"{tag}sv{oc}_{tb}")
        self.proj(tag + "kv", "L2a_Wkv", 16, 18, D, evac_kv, o_w_off=16)

        dmk, dmk_b = self.sbconst("diag_mask")
        dmkT, dmkT_b = self.sbconst("diag_maskT")
        identb, identb_b = self.sbconst("ident_bf")
        KT_, KT_b = self.biga_slot(20)
        VT, VT_b = self.biga_slot(21)
        S.dma("sp", KT_, self.qkvT[16 * 128:17 * 128, :],
              reads=(self.qkvT_b,), writes=(KT_b,), name=f"{tag}ldk")
        S.dma("sp", VT, self.qkvT[17 * 128:18 * 128, :],
              reads=(self.qkvT_b,), writes=(VT_b,), name=f"{tag}ldv")
        pvt0, pvt0_b = self.pbf[0], self.pbf_b[0]
        for c in range(NTT):
            S.op("pe", lambda e, c=c: e.transpose(
                    pvt0[:, 0:128], VT[:, c * 128:(c + 1) * 128], identb[:]),
                 reads=(VT_b, identb_b), writes=(pvt0_b,), name=f"{tag}v{c}t")
            S.op("dve", lambda e, c=c: e.tensor_copy(
                    VT[:, c * 128:(c + 1) * 128], pvt0[:, 0:128]),
                 reads=(pvt0_b,), writes=(VT_b,), name=f"{tag}v{c}e")

        def attn_head_ops(h):
            j = h % 2
            QT, QT_b = self.biga_slot(16 + j)
            S.dma("sp", QT, self.qkvT[h * 128:(h + 1) * 128, :],
                  reads=(self.qkvT_b,), writes=(self.biga_b[16 + j],),
                  name=f"{tag}h{h}ld")
            QT_b = self.biga_b[16 + j]
            psc, psc_b = (self.pb[0], self.pb_b[0]) if j == 0 else (self.pb[2], self.pb_b[2])
            pvx, pvx_b = (self.pb[1], self.pb_b[1]) if j == 0 else (self.pb[3], self.pb_b[3])
            fwj, fwj_b = self.fw[j], self.fw_b[j]
            negm = self.cols[:, j * 128 + 0:j * 128 + 16]
            Zc = self.cols[:, j * 128 + 16:j * 128 + 32]
            lnZ = self.cols[:, j * 128 + 32:j * 128 + 48]
            bcol = self.cols[:, j * 128 + 48:j * 128 + 64]
            colsb = self.cols_b[j]
            atn, atn_b = self.biga_slot(22 + j)
            for qt in range(NTT):
                def astep(qt=qt):
                    L = (qt + 1) * 128
                    nb = qt // 4 + 1
                    for jb in range(nb):
                        w = 512 if jb < qt // 4 else (qt % 4 + 1) * 128
                        S.op("pe", lambda e, qt=qt, jb=jb, w=w: e.matmul(
                                psc[:, 0:w], QT[:, qt * 128:(qt + 1) * 128],
                                KT_[:, jb * 512:jb * 512 + w],
                                start=True, stop=True),
                             reads=(QT_b, KT_b), writes=(psc_b,),
                             name=f"{tag}h{h}a{qt}_{jb}")
                        if jb < qt // 4:
                            S.op("dve", lambda e, jb=jb, w=w: e.tensor_copy(
                                    fwj[:, jb * 512:jb * 512 + w], psc[:, 0:w]),
                                 reads=(psc_b,), writes=(fwj_b,),
                                 name=f"{tag}h{h}ae{qt}_{jb}")
                        else:
                            if w > 128:
                                S.op("dve", lambda e, jb=jb, w=w: e.tensor_copy(
                                        fwj[:, jb * 512:jb * 512 + w - 128],
                                        psc[:, 0:w - 128]),
                                     reads=(psc_b,), writes=(fwj_b,),
                                     name=f"{tag}h{h}ap{qt}_{jb}")
                            S.op("dve", lambda e, qt=qt, w=w: e.tensor_add(
                                    fwj[:, qt * 128:(qt + 1) * 128],
                                    psc[:, w - 128:w], dmk[:]),
                                 reads=(psc_b, dmk_b), writes=(fwj_b,),
                                 name=f"{tag}h{h}ad{qt}")
                    S.op("dve", lambda e, qt=qt, L=L: e.reduce_max(
                            negm[:, qt:qt + 1], fwj[:, 0:L], axis=AX.X, negate=True),
                         reads=(fwj_b,), writes=(colsb,), name=f"{tag}h{h}am{qt}")
                    S.op("act", lambda e, qt=qt, L=L: e.activation(
                            fwj[:, 0:L], fwj[:, 0:L], AF.Exp,
                            bias=negm[:, qt:qt + 1], accum_out=Zc[:, qt:qt + 1]),
                         reads=(fwj_b, colsb), writes=(fwj_b, colsb),
                         name=f"{tag}h{h}az{qt}")
                yield astep
            def brows():
                S.op("act", lambda e: e.activation(lnZ, Zc, AF.Ln),
                     reads=(colsb,), writes=(colsb,), name=f"{tag}h{h}ln")
                S.op("dve", lambda e: e.tensor_sub(bcol, negm, lnZ),
                     reads=(colsb,), writes=(colsb,), name=f"{tag}h{h}bc")
                self.cols_to_row(bcol, colsb, 16, 2 if j == 0 else 1, tag + f"h{h}")
                brow, brow_b = self.row(2 if j == 0 else 1)
                hilo = self.SR[0:1, (3 if j == 0 else 4) * 2048:
                               (4 if j == 0 else 5) * 2048].bitcast(BF16)
                hilo_b = self.rows_b[3 if j == 0 else 4]
                S.op("act", lambda e: e.activation(hilo[0:1, 0:2048], brow, AF.Copy),
                     reads=(brow_b,), writes=(hilo_b,), name=f"{tag}h{h}hi")
                for pc in range(4):
                    S.op("dve", lambda e, pc=pc: e.tensor_sub(
                            self.tmp[j][0:1, 0:512],
                            brow[0:1, pc * 512:(pc + 1) * 512],
                            hilo[0:1, pc * 512:(pc + 1) * 512]),
                         reads=(brow_b, hilo_b), writes=(self.tmp_b[j],),
                         name=f"{tag}h{h}lo{pc}")
                    S.op("act", lambda e, pc=pc: e.activation(
                            hilo[0:1, 2048 + pc * 512:2048 + (pc + 1) * 512],
                            self.tmp[j][0:1, 0:512], AF.Copy),
                         reads=(self.tmp_b[j],), writes=(hilo_b,),
                         name=f"{tag}h{h}loc{pc}")
            yield brows
            hilo = self.SR[0:1, (3 if j == 0 else 4) * 2048:
                           (4 if j == 0 else 5) * 2048].bitcast(BF16)
            hilo_b = self.rows_b[3 if j == 0 else 4]
            ones_row_bf, ones_row_bf_b = self.sbconst("ones_row_bf")
            for jb in range(4):
                for c in range(4 * jb + 4):
                    def bstep(jb=jb, c=c):
                        qsl = slice(jb * 512, (jb + 1) * 512)
                        S.op("pe", lambda e, c=c, qsl=qsl: e.matmul(
                                psc[:, 0:512], KT_[:, c * 128:(c + 1) * 128],
                                QT[:, qsl], start=True, stop=False),
                             reads=(QT_b, KT_b), writes=(psc_b,),
                             name=f"{tag}h{h}b{jb}_{c}")
                        S.op("pe", lambda e, jb=jb: e.matmul(
                                psc[:, 0:512], ones_row_bf[:],
                                hilo[0:1, jb * 512:(jb + 1) * 512],
                                start=False, stop=False),
                             reads=(hilo_b, ones_row_bf_b), writes=(psc_b,),
                             name=f"{tag}h{h}bh{jb}_{c}")
                        S.op("pe", lambda e, jb=jb: e.matmul(
                                psc[:, 0:512], ones_row_bf[:],
                                hilo[0:1, 2048 + jb * 512:2048 + (jb + 1) * 512],
                                start=False, stop=True),
                             reads=(hilo_b, ones_row_bf_b), writes=(psc_b,),
                             name=f"{tag}h{h}bl{jb}_{c}")
                        d = c - 4 * jb
                        pT, pT_b = self.stg[2 * j + c % 2], self.stg_b[2 * j + c % 2]
                        if d >= 0:
                            S.op("dve", lambda e, d=d: e.tensor_add(
                                    psc[:, d * 128:(d + 1) * 128],
                                    psc[:, d * 128:(d + 1) * 128], dmkT[:]),
                                 reads=(psc_b, dmkT_b), writes=(psc_b,),
                                 name=f"{tag}h{h}bm{jb}_{c}")
                            if d > 0:
                                S.op("dve", lambda e, d=d, pT=pT: e.memset(
                                        pT[:, 0:d * 128], 0.0),
                                     writes=(pT_b,), name=f"{tag}h{h}bz{jb}_{c}")
                            S.op("act", lambda e, d=d, pT=pT: e.activation(
                                    pT[:, d * 128:512], psc[:, d * 128:512], AF.Exp),
                                 reads=(psc_b,), writes=(pT_b,),
                                 name=f"{tag}h{h}be{jb}_{c}")
                        else:
                            S.op("act", lambda e, pT=pT: e.activation(
                                    pT[:], psc[:, 0:512], AF.Exp),
                                 reads=(psc_b,), writes=(pT_b,),
                                 name=f"{tag}h{h}be{jb}_{c}")
                        S.op("pe", lambda e, c=c, jb=jb, pT=pT: e.matmul(
                                pvx[:, 0:512], VT[:, c * 128:(c + 1) * 128], pT[:],
                                start=(c == 0), stop=(c == 4 * jb + 3)),
                             reads=(VT_b, pT_b), writes=(pvx_b,),
                             name=f"{tag}h{h}bp{jb}_{c}")
                    yield bstep
                def oevac(jb=jb):
                    S.op("dve", lambda e, jb=jb: e.tensor_copy(
                            atn[:, jb * 512:(jb + 1) * 512], pvx[:, 0:512]),
                         reads=(pvx_b,), writes=(atn_b,), name=f"{tag}h{h}oe{jb}")
                yield oevac
            def final():
                S.dma("sp", self.attnT[h * 128:(h + 1) * 128, :], atn,
                      reads=(atn_b,), writes=(self.attnT_b,), name=f"{tag}h{h}st")
            yield final

        def run_pair(h0):
            gens = [attn_head_ops(h0), attn_head_ops(h0 + 1)]
            done = [False, False]
            while not all(done):
                for i2 in (0, 1):
                    if not done[i2]:
                        try:
                            next(gens[i2])()
                        except StopIteration:
                            done[i2] = True
        for h0 in range(0, 16, 2):
            run_pair(h0)

        for k in range(NK):
            ap, b2 = self.biga_slot(k)
            S.dma("sp", ap, self.attnT[k * 128:(k + 1) * 128, :],
                  reads=(self.attnT_b,), writes=(b2,), name=f"{tag}lda{k}")
        xloaded = {}
        def evac_wout(oc, tb, pa, pab):
            if oc not in xloaded:
                xs, xs_b = self.fw[oc % 2], self.fw_b[oc % 2]
                S.dma("sp", xs, self.cur_ap[oc * 128:(oc + 1) * 128, :],
                      reads=(self.cur_b[oc],), writes=(xs_b,), name=f"{tag}rx{oc}")
                xloaded[oc] = (xs, xs_b)
            xs, xs_b = xloaded[oc]
            sl = slice(tb * 512, (tb + 1) * 512)
            S.op("dve", lambda e, pa=pa, sl=sl, xs=xs: e.tensor_add(
                    xs[:, sl], pa[:, :], xs[:, sl]),
                 reads=(pab, xs_b), writes=(xs_b,), name=f"{tag}ro{oc}_{tb}")
            if tb == 3:
                S.dma("sp", dst_ap[oc * 128:(oc + 1) * 128, :], xs,
                      reads=(xs_b,), writes=(dst_b[oc],), name=f"{tag}wx{oc}")
        self.proj(tag + "wo", "L2a_Wout", 0, 16, D, evac_wout)
        self.advance_x(dst_ap, dst_b)


    def make_rms_rows(self, tag, eps):
        S = self.S
        self.derive_mu_r(tag)
        mu, mu_b = self.s4s(2)
        var, var_b = self.s4s(3)
        r, r_b = self.s4s(4)
        t5, t5_b = self.s4s(5)
        t6, t6_b = self.s4s(6)
        S.op("dve", lambda e: e.tensor_mul(t5, var, r), reads=(var_b, r_b),
             writes=(t5_b,), name=f"{tag}s1")
        S.op("dve", lambda e: e.tensor_mul(t5, t5, r), reads=(t5_b, r_b),
             writes=(t5_b,), name=f"{tag}s2")
        S.op("dve", lambda e, eps=eps: e.tensor_scalar_add(t5, t5, eps),
             reads=(t5_b,), writes=(t5_b,), name=f"{tag}s3")
        S.op("act", lambda e: e.activation(t5, t5, AF.Sqrt), reads=(t5_b,),
             writes=(t5_b,), name=f"{tag}s4")
        S.op("dve", lambda e: e.reciprocal(t5, t5), reads=(t5_b,),
             writes=(t5_b,), name=f"{tag}s5")
        S.op("dve", lambda e: e.tensor_mul(t5, t5, r), reads=(t5_b, r_b),
             writes=(t5_b,), name=f"{tag}s6")
        S.op("dve", lambda e: e.tensor_mul(t6, mu, t5), reads=(mu_b, t5_b),
             writes=(t6_b,), name=f"{tag}s7")
        self.to_row(5, 0, tag)
        self.to_row(6, 1, tag)

    def bval_col(self, tag, name, dstcol, dstcol_b, fn=None):
        S = self.S
        ap, b2 = self.win(name)
        ones_row, ones_row_b = self.sbconst("ones_row")
        sc = self.small[0]
        sc_b = self.small_b[0]
        S.dma("sp", sc[0:1, 12:13], ap, reads=(b2,), writes=(sc_b,),
              name=f"{tag}bv{name}")
        if fn is not None:
            fn(sc, sc_b)
        pbc, pbc_b = self.pb[4], self.pb_b[4]
        S.op("pe", lambda e: e.matmul(pbc[:, 0:1], ones_row[:], sc[0:1, 12:13],
                                      start=True, stop=True),
             reads=(sc_b, ones_row_b), writes=(pbc_b,), name=f"{tag}bm{name}")
        S.op("dve", lambda e: e.tensor_copy(dstcol, pbc[:, 0:1]),
             reads=(pbc_b,), writes=(dstcol_b,), name=f"{tag}be{name}")

    def slot_l2ffn(self, last):
        tag = "L2f"
        S = self.S
        dst_ap, dst_b = self.next_x(last)
        self.stats_pass(tag)
        self.make_rms_rows(tag, float(np.finfo(np.float32).eps))
        self.xhat_inplace(tag)
        gw, gw_b = self.sbconst("L2f_gw")
        for k in range(NK):
            xh, xh_b = self.biga_slot(k)
            S.op("act", lambda e, xh=xh, k=k: e.activation(
                    xh, xh, AF.Copy, scale=gw[:, k:k + 1]),
                 reads=(xh_b, gw_b), writes=(xh_b,), name=f"{tag}gw{k}")
        rt_col = self.small[2][:, 4:5]
        sc_col = self.small[2][:, 5:6]
        sm2_b = self.small_b[2]
        def mktemp(sc, sc_b):
            S.op("dve", lambda e: e.tensor_scalar_max(sc[0:1, 12:13],
                                                      sc[0:1, 12:13], 0.1),
                 reads=(sc_b,), writes=(sc_b,), name=f"{tag}tmx")
            S.op("dve", lambda e: e.reciprocal(sc[0:1, 12:13], sc[0:1, 12:13]),
                 reads=(sc_b,), writes=(sc_b,), name=f"{tag}trc")
        self.bval_col(tag, "L2f_temp", rt_col, sm2_b, fn=mktemp)
        self.bval_col(tag, "L2f_scale", sc_col, sm2_b)

        def evac_lg(oc, tb, pa, pab):
            stj, stj_b = self.stg[tb % 2], self.stg_b[tb % 2]
            S.op("dve", lambda e, pa=pa, stj=stj: e.tensor_scalar(
                    stj[:], pa[:, :], rt_col, None, ALU.mult),
                 reads=(pab, sm2_b), writes=(stj_b,), name=f"{tag}el{oc}_{tb}")
            S.dma("sp", self.hidT[oc * 128:(oc + 1) * 128,
                                  tb * 512:(tb + 1) * 512], stj[:],
                  reads=(stj_b,), writes=(self.hidT_b,), name=f"{tag}sl{oc}_{tb}")
        self.proj(tag + "lg", "L2f_WcT", 0, 32, D, evac_lg)

        genes_ap, genes_b = self.win("L2f_genes")
        identb, identb_b = self.sbconst("ident_bf")
        lg = self.fwA[:, 0:2048].bitcast(BF16)
        zap = self.fwA[:, 2048:4096].bitcast(BF16)
        lg_b, zap_b = self.fw_b[0], self.fw_b[1]
        exA, exA_b = self.biga_slot(16)
        exB, exB_b = self.biga_slot(17)
        m8 = self.smallb[6][:, 0:8]
        m8_b = self.smallb_b[6]
        negm = self.small[2][:, 6:7]
        Zc = self.small[2][:, 7:8]
        for tt in range(NTT):
            for hf in range(2):
                S.op("sp", lambda e, tt=tt, hf=hf: e.dma_start_transpose(
                        lg[:, hf * 2048:(hf + 1) * 2048],
                        self.hidT[hf * 2048:(hf + 1) * 2048,
                                  tt * 128:(tt + 1) * 128]),
                     reads=(self.hidT_b,), writes=(lg_b,), dma=True,
                     name=f"{tag}tt{tt}_{hf}")
            for rnd in range(4):
                src, src_b = (lg, lg_b) if rnd == 0 else (zap, zap_b)
                S.op("dve", lambda e, src=src: e.max(out=m8, in_=src),
                     reads=(src_b,), writes=(m8_b,), name=f"{tag}mx{tt}_{rnd}")
                if rnd == 0:
                    S.op("dve", lambda e: e.tensor_scalar(
                            negm, m8[:, 0:1], -1.0, None, ALU.mult),
                         reads=(m8_b,), writes=(sm2_b,), name=f"{tag}nm{tt}")
                S.op("dve", lambda e, src=src: e.match_replace(
                        out=zap, in_to_replace=m8, in_values=src,
                        imm_value=-1e30),
                     reads=(src_b, m8_b), writes=(zap_b,), name=f"{tag}mr{tt}_{rnd}")
            S.op("act", lambda e: e.activation(lg[:], lg[:], AF.Exp, bias=negm),
                 reads=(lg_b, sm2_b), writes=(lg_b,), name=f"{tag}ex{tt}")
            S.op("dve", lambda e: e.tensor_scalar(zap, zap, -1e29, None, ALU.is_le),
                 reads=(zap_b,), writes=(zap_b,), name=f"{tag}mk{tt}")
            S.op("dve", lambda e: e.scalar_tensor_tensor(
                    lg[:], lg[:], 1.0, zap, ALU.mult, ALU.mult, accum_out=Zc),
                 reads=(lg_b, zap_b), writes=(lg_b, sm2_b), name=f"{tag}wz{tt}")
            S.op("dve", lambda e: e.reciprocal(Zc, Zc),
                 reads=(sm2_b,), writes=(sm2_b,), name=f"{tag}iz{tt}")
            S.op("dve", lambda e: e.tensor_scalar(lg[:], lg[:], Zc, None, ALU.mult),
                 reads=(lg_b, sm2_b), writes=(lg_b,), name=f"{tag}pn{tt}")
            pbf0, pbf0_b = self.pbf[0], self.pbf_b[0]
            pA, pA_b = self.pb[0], self.pb_b[0]
            pB, pB_b = self.pb[1], self.pb_b[1]
            for go in range(32):
                gsl, gsl_b = self.wslab[go % 2], self.wslab_b[go % 2]
                S.dma("pool", gsl[:, 0:256],
                      genes_ap[go * 128:(go + 1) * 128, :],
                      reads=(genes_b,), writes=(gsl_b,), name=f"{tag}gl{tt}_{go}")
                S.op("pe", lambda e, go=go: e.transpose(
                        pbf0[:, 0:128], lg[:, go * 128:(go + 1) * 128], identb[:]),
                     reads=(lg_b, identb_b), writes=(pbf0_b,),
                     name=f"{tag}pt{tt}_{go}")
                pTs, pTs_b = self.smallb[8 + go % 2], self.smallb_b[8 + go % 2]
                S.op("dve", lambda e, pTs=pTs: e.tensor_copy(pTs[:], pbf0[:, 0:128]),
                     reads=(pbf0_b,), writes=(pTs_b,), name=f"{tag}pe{tt}_{go}")
                S.op("pe", lambda e, go=go, gsl=gsl, pTs=pTs: e.matmul(
                        pA[:, 0:128], gsl[:, 0:128], pTs[:],
                        start=(go == 0), stop=(go == 31)),
                     reads=(gsl_b, pTs_b), writes=(pA_b,), name=f"{tag}ea{tt}_{go}")
                S.op("pe", lambda e, go=go, gsl=gsl, pTs=pTs: e.matmul(
                        pB[:, 0:128], gsl[:, 128:256], pTs[:],
                        start=(go == 0), stop=(go == 31)),
                     reads=(gsl_b, pTs_b), writes=(pB_b,), name=f"{tag}eb{tt}_{go}")
            S.op("dve", lambda e, tt=tt: e.tensor_copy(
                    exA[:, tt * 128:(tt + 1) * 128], pA[:, 0:128]),
                 reads=(pA_b,), writes=(exA_b,), name=f"{tag}xa{tt}")
            S.op("dve", lambda e, tt=tt: e.tensor_copy(
                    exB[:, tt * 128:(tt + 1) * 128], pB[:, 0:128]),
                 reads=(pB_b,), writes=(exB_b,), name=f"{tag}xb{tt}")

        WdT, WdT_b = self.win("L2f_WdT")
        WuT, WuT_b = self.win("L2f_WuT")
        wdv = WdT.rearrange("(c p) o -> p c o", p=128)
        wuv = WuT.rearrange("(c p) o -> p c o", p=128)
        for oc in range(NK):
            wsl0, wsl0_b = self.wslab[0], self.wslab_b[0]
            wsl1, wsl1_b = self.wslab[1], self.wslab_b[1]
            S.dma("pool", wsl0[:, 0:256].rearrange("p (c o) -> p c o", o=128),
                  wdv[:, :, oc * 128:(oc + 1) * 128],
                  reads=(WdT_b,), writes=(wsl0_b,), name=f"{tag}dl{oc}")
            S.dma("pool", wsl1[:, 0:256].rearrange("p (c o) -> p c o", o=128),
                  wuv[:, :, oc * 128:(oc + 1) * 128],
                  reads=(WuT_b,), writes=(wsl1_b,), name=f"{tag}ul{oc}")
            for tb in range(NTB):
                sl = slice(tb * 512, (tb + 1) * 512)
                pd, pd_b = self.pb[2], self.pb_b[2]
                pu, pu_b = self.pb[3], self.pb_b[3]
                for ci, (src, src_b) in enumerate(((exA, exA_b), (exB, exB_b))):
                    S.op("pe", lambda e, ci=ci, sl=sl, src=src: e.matmul(
                            pd[:, :], wsl0[:, ci * 128:(ci + 1) * 128], src[:, sl],
                            start=(ci == 0), stop=(ci == 1)),
                         reads=(wsl0_b, src_b), writes=(pd_b,),
                         name=f"{tag}dm{oc}_{tb}_{ci}")
                    S.op("pe", lambda e, ci=ci, sl=sl, src=src: e.matmul(
                            pu[:, :], wsl1[:, ci * 128:(ci + 1) * 128], src[:, sl],
                            start=(ci == 0), stop=(ci == 1)),
                         reads=(wsl1_b, src_b), writes=(pu_b,),
                         name=f"{tag}um{oc}_{tb}_{ci}")
                S.op("act", lambda e: e.activation(self.tmp[0][:], pd[:, :], AF.Tanh),
                     reads=(pd_b,), writes=(self.tmp_b[0],), name=f"{tag}th{oc}_{tb}")
                xh, xh_b = self.biga_slot(oc)
                S.op("dve", lambda e, sl=sl, xh=xh: e.tensor_mul(
                        self.tmp[0][:], self.tmp[0][:], xh[:, sl]),
                     reads=(self.tmp_b[0], xh_b), writes=(self.tmp_b[0],),
                     name=f"{tag}p1{oc}_{tb}")
                S.op("dve", lambda e: e.tensor_mul(
                        self.tmp[1][:], self.tmp[0][:], pu[:, :]),
                     reads=(self.tmp_b[0], pu_b), writes=(self.tmp_b[1],),
                     name=f"{tag}p2{oc}_{tb}")
                hw = (oc >> 1) & 1
                xs = self.fw[oc % 2][:, hw * 512:(hw + 1) * 512]
                xs_b = self.fw_b[oc % 2]
                S.dma("sp", xs, self.cur_ap[oc * 128:(oc + 1) * 128, sl],
                      reads=(self.cur_b[oc],), writes=(xs_b,), name=f"{tag}x{oc}_{tb}")
                S.op("dve", lambda e, xs=xs: e.scalar_tensor_tensor(
                        xs, self.tmp[1][:], sc_col, xs, ALU.mult, ALU.add),
                     reads=(self.tmp_b[1], sm2_b, xs_b), writes=(xs_b,),
                     name=f"{tag}p3{oc}_{tb}")
                S.dma("sp", dst_ap[oc * 128:(oc + 1) * 128, sl], xs,
                      reads=(xs_b,), writes=(dst_b[oc],), name=f"{tag}w{oc}_{tb}")
        self.advance_x(dst_ap, dst_b)


    def slot_l3attn(self, last):
        tag = "L3a"
        S = self.S
        dst_ap, dst_b = self.next_x(last)
        self.stats_pass(tag)
        self.make_s_ms_rows(tag, "ln")
        self.xhat_inplace(tag)
        def evac_lat(oc, tb, pa, pab):
            stj, stj_b = self.stg[tb % 2], self.stg_b[tb % 2]
            S.op("dve", lambda e, pa=pa, stj=stj: e.tensor_copy(stj[:], pa[:, :]),
                 reads=(pab,), writes=(stj_b,), name=f"{tag}el{oc}_{tb}")
            S.dma("sp", self.hidT[oc * 128:(oc + 1) * 128,
                                  tb * 512:(tb + 1) * 512], stj[:],
                  reads=(stj_b,), writes=(self.hidT_b,), name=f"{tag}sl{oc}_{tb}")
        self.proj(tag + "ql", "L3a_wqd", 0, 12, D, evac_lat)
        self.proj(tag + "kv", "L3a_wkvd", 12, 16, D, evac_lat, o_w_off=12)
        for k in range(NK):
            ap, b2 = self.biga_slot(k)
            S.dma("sp", ap, self.hidT[k * 128:(k + 1) * 128, :],
                  reads=(self.hidT_b,), writes=(b2,), name=f"{tag}ll{k}")
        cosv = self.biga[:, 16 * 2048:18 * 2048].bitcast(F32)
        sinv = self.biga[:, 18 * 2048:20 * 2048].bitcast(F32)
        cosv_b, sinv_b = self.biga_b[16], self.biga_b[18]
        lc, lc_b = self.win("l3_cos")
        ls, ls_b = self.win("l3_sin")
        S.dma("sp", cosv, lc, reads=(lc_b,), writes=(cosv_b,), name=f"{tag}lc")
        S.dma("sp", sinv, ls, reads=(ls_b,), writes=(sinv_b,), name=f"{tag}ls")
        rotP, rotP_b = self.sbconst("l3_rotP")
        qsc = float(192.0 ** -0.5)
        ql_slot = lambda k: self.biga_slot(k)
        kv_slot = lambda k: self.biga_slot(12 + k)

        def evac_qc(oc, tb, pa, pab):
            stj, stj_b = self.stg[tb % 2], self.stg_b[tb % 2]
            S.op("dve", lambda e, pa=pa, stj=stj: e.tensor_scalar_mul(
                    stj[:], pa[:, :], qsc),
                 reads=(pab,), writes=(stj_b,), name=f"{tag}eqc{oc}_{tb}")
            S.dma("sp", self.qkvT[oc * 128:(oc + 1) * 128,
                                  tb * 512:(tb + 1) * 512], stj[:],
                  reads=(stj_b,), writes=(self.qkvT_b,), name=f"{tag}sqc{oc}_{tb}")
        self.proj(tag + "qc", "L3a_wqn", 0, 16, 1536, evac_qc, rhs_slot=ql_slot)

        def evac_qr(oc, tb, pa, pab):
            self.rope_evac(tag + "qr", oc, tb, pa, pab, cosv, cosv_b, sinv, sinv_b,
                           rotP, rotP_b, qsc, oc)
        self.proj(tag + "qr", "L3a_wqr", 16, 24, 1536, evac_qr,
                  rhs_slot=ql_slot, o_w_off=16)

        def evac_kc(oc, tb, pa, pab):
            stj, stj_b = self.stg[tb % 2], self.stg_b[tb % 2]
            S.op("dve", lambda e, pa=pa, stj=stj: e.tensor_copy(stj[:], pa[:, :]),
                 reads=(pab,), writes=(stj_b,), name=f"{tag}ekc{oc}_{tb}")
            S.dma("sp", self.qkvT[oc * 128:(oc + 1) * 128,
                                  tb * 512:(tb + 1) * 512], stj[:],
                  reads=(stj_b,), writes=(self.qkvT_b,), name=f"{tag}skc{oc}_{tb}")
        self.proj(tag + "kc", "L3a_wkn", 24, 40, 512, evac_kc,
                  rhs_slot=kv_slot, o_w_off=24)

        def evac_kr(oc, tb, pa, pab):
            self.rope_evac(tag + "kr", oc, tb, pa, pab, cosv, cosv_b, sinv, sinv_b,
                           rotP, rotP_b, 1.0, oc)
        self.proj(tag + "kr", "L3a_wkr", 40, 48, 512, evac_kr,
                  rhs_slot=kv_slot, o_w_off=40)

        def evac_v(oc, tb, pa, pab):
            stj, stj_b = self.stg[tb % 2], self.stg_b[tb % 2]
            S.op("dve", lambda e, pa=pa, stj=stj: e.tensor_copy(stj[:], pa[:, :]),
                 reads=(pab,), writes=(stj_b,), name=f"{tag}ev{oc}_{tb}")
            S.dma("sp", self.vT3[oc * 128:(oc + 1) * 128,
                                 tb * 512:(tb + 1) * 512], stj[:],
                  reads=(stj_b,), writes=(self.vT3_b,), name=f"{tag}sv{oc}_{tb}")
        self.proj(tag + "v", "L3a_wvu", 0, 24, 512, evac_v, rhs_slot=kv_slot)

        dmk, dmk_b = self.sbconst("diag_mask")
        dmkT, dmkT_b = self.sbconst("diag_maskT")
        identb, identb_b = self.sbconst("ident_bf")
        ones_row_bf, ones_row_bf_b = self.sbconst("ones_row_bf")
        for h in range(16):
            QTc, QTc_b = self.biga_slot(0)
            QTr, QTr_b = self.biga_slot(1)
            KTc, KTc_b = self.biga_slot(2)
            KTr, KTr_b = self.biga_slot(3)
            VTc, VTc_b = self.biga_slot(4)
            VTr, VTr_b = self.biga_slot(5)
            VTrm, VTrm_b = self.biga_slot(6)
            atnc, atnc_b = self.biga_slot(7)
            atnr, atnr_b = self.biga_slot(8)
            S.dma("sp", QTc, self.qkvT[h * 128:(h + 1) * 128, :],
                  reads=(self.qkvT_b,), writes=(QTc_b,), name=f"{tag}h{h}l1")
            S.dma("sp", QTr[0:64, :], self.qkvT[2048 + h * 64:2048 + (h + 1) * 64, :],
                  reads=(self.qkvT_b,), writes=(QTr_b,), name=f"{tag}h{h}l2")
            S.dma("sp", KTc, self.qkvT[3072 + h * 128:3072 + (h + 1) * 128, :],
                  reads=(self.qkvT_b,), writes=(KTc_b,), name=f"{tag}h{h}l3")
            S.dma("sp", KTr[0:64, :], self.qkvT[5120 + h * 64:5120 + (h + 1) * 64, :],
                  reads=(self.qkvT_b,), writes=(KTr_b,), name=f"{tag}h{h}l4")
            S.dma("sp", VTc, self.vT3[h * 192:h * 192 + 128, :],
                  reads=(self.vT3_b,), writes=(VTc_b,), name=f"{tag}h{h}l5")
            S.dma("sp", VTr[0:64, :], self.vT3[h * 192 + 128:h * 192 + 192, :],
                  reads=(self.vT3_b,), writes=(VTr_b,), name=f"{tag}h{h}l6")
            pvt0, pvt0_b = self.pbf[0], self.pbf_b[0]
            for c in range(NTT):
                S.op("pe", lambda e, c=c: e.transpose(
                        pvt0[:, 0:128], VTc[:, c * 128:(c + 1) * 128], identb[:]),
                     reads=(VTc_b, identb_b), writes=(pvt0_b,), name=f"{tag}h{h}vc{c}")
                S.op("dve", lambda e, c=c: e.tensor_copy(
                        VTc[:, c * 128:(c + 1) * 128], pvt0[:, 0:128]),
                     reads=(pvt0_b,), writes=(VTc_b,), name=f"{tag}h{h}ve{c}")
                S.op("pe", lambda e, c=c: e.transpose(
                        pvt0[:, 128:192], VTr[0:64, c * 128:(c + 1) * 128],
                        identb[0:64, 0:64]),
                     reads=(VTr_b, identb_b), writes=(pvt0_b,), name=f"{tag}h{h}vr{c}")
                S.op("dve", lambda e, c=c: e.tensor_copy(
                        VTrm[:, c * 64:(c + 1) * 64], pvt0[:, 128:192]),
                     reads=(pvt0_b,), writes=(VTrm_b,), name=f"{tag}h{h}vf{c}")
            psc, psc_b = self.pb[0], self.pb_b[0]
            pvC, pvC_b = self.pb[1], self.pb_b[1]
            pvR, pvR_b = self.pb[2], self.pb_b[2]
            fwj, fwj_b = self.fw[0], self.fw_b[0]
            negm = self.cols[:, 0:16]
            Zc = self.cols[:, 16:32]
            lnZ = self.cols[:, 32:48]
            bcol = self.cols[:, 48:64]
            colsb = self.cols_b[0]
            for qt in range(NTT):
                L = (qt + 1) * 128
                nb = qt // 4 + 1
                for jb in range(nb):
                    w = 512 if jb < qt // 4 else (qt % 4 + 1) * 128
                    S.op("pe", lambda e, qt=qt, jb=jb, w=w: e.matmul(
                            psc[:, 0:w], QTc[:, qt * 128:(qt + 1) * 128],
                            KTc[:, jb * 512:jb * 512 + w], start=True, stop=False),
                         reads=(QTc_b, KTc_b), writes=(psc_b,),
                         name=f"{tag}h{h}a{qt}_{jb}")
                    S.op("pe", lambda e, qt=qt, jb=jb, w=w: e.matmul(
                            psc[:, 0:w], QTr[0:64, qt * 128:(qt + 1) * 128],
                            KTr[0:64, jb * 512:jb * 512 + w], start=False, stop=True),
                         reads=(QTr_b, KTr_b), writes=(psc_b,),
                         name=f"{tag}h{h}a2{qt}_{jb}")
                    if jb < qt // 4:
                        S.op("dve", lambda e, jb=jb, w=w: e.tensor_copy(
                                fwj[:, jb * 512:jb * 512 + w], psc[:, 0:w]),
                             reads=(psc_b,), writes=(fwj_b,),
                             name=f"{tag}h{h}ae{qt}_{jb}")
                    else:
                        if w > 128:
                            S.op("dve", lambda e, jb=jb, w=w: e.tensor_copy(
                                    fwj[:, jb * 512:jb * 512 + w - 128],
                                    psc[:, 0:w - 128]),
                                 reads=(psc_b,), writes=(fwj_b,),
                                 name=f"{tag}h{h}ap{qt}_{jb}")
                        S.op("dve", lambda e, qt=qt, w=w: e.tensor_add(
                                fwj[:, qt * 128:(qt + 1) * 128],
                                psc[:, w - 128:w], dmk[:]),
                             reads=(psc_b, dmk_b), writes=(fwj_b,),
                             name=f"{tag}h{h}ad{qt}")
                S.op("dve", lambda e, qt=qt, L=L: e.reduce_max(
                        negm[:, qt:qt + 1], fwj[:, 0:L], axis=AX.X, negate=True),
                     reads=(fwj_b,), writes=(colsb,), name=f"{tag}h{h}am{qt}")
                S.op("act", lambda e, qt=qt, L=L: e.activation(
                        fwj[:, 0:L], fwj[:, 0:L], AF.Exp,
                        bias=negm[:, qt:qt + 1], accum_out=Zc[:, qt:qt + 1]),
                     reads=(fwj_b, colsb), writes=(fwj_b, colsb),
                     name=f"{tag}h{h}az{qt}")
            S.op("act", lambda e: e.activation(lnZ, Zc, AF.Ln),
                 reads=(colsb,), writes=(colsb,), name=f"{tag}h{h}ln")
            S.op("dve", lambda e: e.tensor_sub(bcol, negm, lnZ),
                 reads=(colsb,), writes=(colsb,), name=f"{tag}h{h}bc")
            self.cols_to_row(bcol, colsb, 16, 2, tag + f"h{h}")
            brow, brow_b = self.row(2)
            hilo = self.SR[0:1, 3 * 2048:4 * 2048].bitcast(BF16)
            hilo_b = self.rows_b[3]
            S.op("act", lambda e: e.activation(hilo[0:1, 0:2048], brow, AF.Copy),
                 reads=(brow_b,), writes=(hilo_b,), name=f"{tag}h{h}hi")
            for pc in range(4):
                S.op("dve", lambda e, pc=pc: e.tensor_sub(
                        self.tmp[0][0:1, 0:512],
                        brow[0:1, pc * 512:(pc + 1) * 512],
                        hilo[0:1, pc * 512:(pc + 1) * 512]),
                     reads=(brow_b, hilo_b), writes=(self.tmp_b[0],),
                     name=f"{tag}h{h}lo{pc}")
                S.op("act", lambda e, pc=pc: e.activation(
                        hilo[0:1, 2048 + pc * 512:2048 + (pc + 1) * 512],
                        self.tmp[0][0:1, 0:512], AF.Copy),
                     reads=(self.tmp_b[0],), writes=(hilo_b,),
                     name=f"{tag}h{h}loc{pc}")
            for jb in range(4):
                for c in range(4 * jb + 4):
                    qsl = slice(jb * 512, (jb + 1) * 512)
                    S.op("pe", lambda e, c=c, qsl=qsl: e.matmul(
                            psc[:, 0:512], KTc[:, c * 128:(c + 1) * 128],
                            QTc[:, qsl], start=True, stop=False),
                         reads=(QTc_b, KTc_b), writes=(psc_b,),
                         name=f"{tag}h{h}b{jb}_{c}")
                    S.op("pe", lambda e, c=c, qsl=qsl: e.matmul(
                            psc[:, 0:512], KTr[0:64, c * 128:(c + 1) * 128],
                            QTr[0:64, qsl], start=False, stop=False),
                         reads=(QTr_b, KTr_b), writes=(psc_b,),
                         name=f"{tag}h{h}b2{jb}_{c}")
                    S.op("pe", lambda e, jb=jb: e.matmul(
                            psc[:, 0:512], ones_row_bf[:],
                            hilo[0:1, jb * 512:(jb + 1) * 512],
                            start=False, stop=False),
                         reads=(hilo_b, ones_row_bf_b), writes=(psc_b,),
                         name=f"{tag}h{h}bh{jb}_{c}")
                    S.op("pe", lambda e, jb=jb: e.matmul(
                            psc[:, 0:512], ones_row_bf[:],
                            hilo[0:1, 2048 + jb * 512:2048 + (jb + 1) * 512],
                            start=False, stop=True),
                         reads=(hilo_b, ones_row_bf_b), writes=(psc_b,),
                         name=f"{tag}h{h}bl{jb}_{c}")
                    d = c - 4 * jb
                    pT, pT_b = self.stg[c % 2], self.stg_b[c % 2]
                    if d >= 0:
                        S.op("dve", lambda e, d=d: e.tensor_add(
                                psc[:, d * 128:(d + 1) * 128],
                                psc[:, d * 128:(d + 1) * 128], dmkT[:]),
                             reads=(psc_b, dmkT_b), writes=(psc_b,),
                             name=f"{tag}h{h}bm{jb}_{c}")
                        if d > 0:
                            S.op("dve", lambda e, d=d, pT=pT: e.memset(
                                    pT[:, 0:d * 128], 0.0),
                                 writes=(pT_b,), name=f"{tag}h{h}bz{jb}_{c}")
                        S.op("act", lambda e, d=d, pT=pT: e.activation(
                                pT[:, d * 128:512], psc[:, d * 128:512], AF.Exp),
                             reads=(psc_b,), writes=(pT_b,), name=f"{tag}h{h}be{jb}_{c}")
                    else:
                        S.op("act", lambda e, pT=pT: e.activation(
                                pT[:], psc[:, 0:512], AF.Exp),
                             reads=(psc_b,), writes=(pT_b,), name=f"{tag}h{h}be{jb}_{c}")
                    S.op("pe", lambda e, c=c, jb=jb, pT=pT: e.matmul(
                            pvC[:, 0:512], VTc[:, c * 128:(c + 1) * 128], pT[:],
                            start=(c == 0), stop=(c == 4 * jb + 3)),
                         reads=(VTc_b, pT_b), writes=(pvC_b,),
                         name=f"{tag}h{h}bp{jb}_{c}")
                    S.op("pe", lambda e, c=c, jb=jb, pT=pT: e.matmul(
                            pvR[0:64, 0:512], VTrm[:, c * 64:(c + 1) * 64], pT[:],
                            start=(c == 0), stop=(c == 4 * jb + 3)),
                         reads=(VTrm_b, pT_b), writes=(pvR_b,),
                         name=f"{tag}h{h}bq{jb}_{c}")
                S.op("dve", lambda e, jb=jb: e.tensor_copy(
                        atnc[:, jb * 512:(jb + 1) * 512], pvC[:, 0:512]),
                     reads=(pvC_b,), writes=(atnc_b,), name=f"{tag}h{h}oe{jb}")
                S.op("dve", lambda e, jb=jb: e.tensor_copy(
                        atnr[0:64, jb * 512:(jb + 1) * 512], pvR[0:64, 0:512]),
                     reads=(pvR_b,), writes=(atnr_b,), name=f"{tag}h{h}of{jb}")
            S.dma("sp", self.attnT3[h * 192:h * 192 + 128, :], atnc,
                  reads=(atnc_b,), writes=(self.attnT3_b,), name=f"{tag}h{h}s1")
            S.dma("sp", self.attnT3[h * 192 + 128:h * 192 + 192, :], atnr[0:64, :],
                  reads=(atnr_b,), writes=(self.attnT3_b,), name=f"{tag}h{h}s2")

        for k in range(24):
            ap, b2 = self.biga_slot(k)
            S.dma("sp", ap, self.attnT3[k * 128:(k + 1) * 128, :],
                  reads=(self.attnT3_b,), writes=(b2,), name=f"{tag}lda{k}")
        xloaded = {}
        def evac_wo(oc, tb, pa, pab):
            if oc not in xloaded:
                xs, xs_b = self.fw[oc % 2], self.fw_b[oc % 2]
                S.dma("sp", xs, self.cur_ap[oc * 128:(oc + 1) * 128, :],
                      reads=(self.cur_b[oc],), writes=(xs_b,), name=f"{tag}rx{oc}")
                xloaded[oc] = (xs, xs_b)
            xs, xs_b = xloaded[oc]
            sl = slice(tb * 512, (tb + 1) * 512)
            S.op("dve", lambda e, pa=pa, sl=sl, xs=xs: e.tensor_add(
                    xs[:, sl], pa[:, :], xs[:, sl]),
                 reads=(pab, xs_b), writes=(xs_b,), name=f"{tag}ro{oc}_{tb}")
            if tb == 3:
                S.dma("sp", dst_ap[oc * 128:(oc + 1) * 128, :], xs,
                      reads=(xs_b,), writes=(dst_b[oc],), name=f"{tag}wx{oc}")
        self.proj(tag + "wo", "L3a_wo", 0, 16, 3072, evac_wo,
                  rhs_slot=lambda k: self.biga_slot(k))
        self.advance_x(dst_ap, dst_b)

    def slot_l3ffn(self, last):
        tag = "L3f"
        S = self.S
        dst_ap, dst_b = self.next_x(last)
        self.stats_pass(tag)
        self.make_s_ms_rows(tag, "ln")
        self.xhat_inplace(tag)
        WgT, WgT_b = self.win("L3f_WgT")
        WuT, WuT_b = self.win("L3f_WuT")
        wgv = WgT.rearrange("(c p) o -> p c o", p=128)
        wuv = WuT.rearrange("(c p) o -> p c o", p=128)
        for oc in range(64):
            wsl0, wsl0_b = self.wslab[0], self.wslab_b[0]
            wsl1, wsl1_b = self.wslab[1], self.wslab_b[1]
            S.dma("pool", wsl0[:].rearrange("p (c o) -> p c o", o=128),
                  wgv[:, :, oc * 128:(oc + 1) * 128],
                  reads=(WgT_b,), writes=(wsl0_b,), name=f"{tag}gl{oc}")
            S.dma("pool", wsl1[:].rearrange("p (c o) -> p c o", o=128),
                  wuv[:, :, oc * 128:(oc + 1) * 128],
                  reads=(WuT_b,), writes=(wsl1_b,), name=f"{tag}ul{oc}")
            for half in range(2):
                gbanks = [(self.pb[0], self.pb_b[0]), (self.pb[1], self.pb_b[1])]
                ubanks = [(self.pb[2], self.pb_b[2]), (self.pb[3], self.pb_b[3])]
                for k in range(NK):
                    for t2 in range(2):
                        tb = half * 2 + t2
                        sl = slice(tb * 512, (tb + 1) * 512)
                        xh, xh_b = self.biga_slot(k)
                        gb, gb_b = gbanks[t2]
                        ub, ub_b = ubanks[t2]
                        S.op("pe", lambda e, k=k, sl=sl, xh=xh, gb=gb: e.matmul(
                                gb[:, :], wsl0[:, k * 128:(k + 1) * 128], xh[:, sl],
                                start=(k == 0), stop=(k == NK - 1)),
                             reads=(wsl0_b, xh_b), writes=(gb_b,),
                             name=f"{tag}gm{oc}_{tb}_{k}")
                        S.op("pe", lambda e, k=k, sl=sl, xh=xh, ub=ub: e.matmul(
                                ub[:, :], wsl1[:, k * 128:(k + 1) * 128], xh[:, sl],
                                start=(k == 0), stop=(k == NK - 1)),
                             reads=(wsl1_b, xh_b), writes=(ub_b,),
                             name=f"{tag}um{oc}_{tb}_{k}")
                for t2 in range(2):
                    tb = half * 2 + t2
                    sl = slice(tb * 512, (tb + 1) * 512)
                    gb, gb_b = gbanks[t2]
                    ub, ub_b = ubanks[t2]
                    sg, sg_b = self.stg[t2], self.stg_b[t2]
                    hd, hd_b = self.stg[2 + t2], self.stg_b[2 + t2]
                    S.op("act", lambda e, gb=gb, sg=sg: e.activation(
                            sg[:], gb[:, :], AF.Silu),
                         reads=(gb_b,), writes=(sg_b,), name=f"{tag}si{oc}_{tb}")
                    S.op("dve", lambda e, ub=ub, sg=sg, hd=hd: e.tensor_mul(
                            hd[:], ub[:, :], sg[:]),
                         reads=(ub_b, sg_b), writes=(hd_b,), name=f"{tag}hd{oc}_{tb}")
                    S.dma("sp", self.hidT[oc * 128:(oc + 1) * 128, sl], hd[:],
                          reads=(hd_b,), writes=(self.hidT_b,),
                          name=f"{tag}hs{oc}_{tb}")
        Arow, Arow_b = self.row(0)
        Qrow, Qrow_b = self.row(1)
        S.op("dve", lambda e: e.memset(Arow, 1.0), writes=(Arow_b,), name=f"{tag}A")
        S.op("dve", lambda e: e.memset(Qrow, 0.0), writes=(Qrow_b,), name=f"{tag}Q")
        WdT, WdT_b = self.win("L3f_WdT")
        wdv = WdT.rearrange("(c p) o -> p c o", p=128)

        def hid_chunk(k):
            sslot, sb2 = self.biga_slot(k // 4)
            return sslot[:, (k % 4) * 512:(k % 4 + 1) * 512], sb2

        def mlp_bank(tb, oc):
            sl = slice(tb * 512, (tb + 1) * 512)
            if oc == 0:
                for sslot in range(NK):
                    ba, bb = self.biga_slot(sslot)
                    src = self.hidT.rearrange("(c p) t -> p c t", p=128)[
                        :, sslot * 4:(sslot + 1) * 4, sl]
                    S.dma("sp", ba.rearrange("p (c t) -> p c t", t=512)[:, 0:4, :],
                          src, reads=(self.hidT_b,), writes=(bb,),
                          name=f"{tag}dl{tb}_{sslot}")
            bank, bank_b = (self.pb[0], self.pb_b[0]) if oc % 2 == 0 else                            (self.pb[1], self.pb_b[1])
            for si in range(4):
                j2 = (tb * 64 + oc * 4 + si) % 2
                wsl, wsl_b = self.wslab[j2], self.wslab_b[j2]
                S.dma("pool", wsl[:].rearrange("p (c o) -> p c o", o=128),
                      wdv[:, si * 16:(si + 1) * 16, oc * 128:(oc + 1) * 128],
                      reads=(WdT_b,), writes=(wsl_b,), name=f"{tag}dw{tb}_{oc}_{si}")
                for kk in range(16):
                    k = si * 16 + kk
                    ha, hb = hid_chunk(k)
                    S.op("pe", lambda e, kk=kk, k=k, ha=ha, wsl=wsl, bank=bank:
                         e.matmul(bank[:, :], wsl[:, kk * 128:(kk + 1) * 128], ha,
                                  start=(k == 0), stop=(k == 63)),
                         reads=(wsl_b, hb), writes=(bank_b,),
                         name=f"{tag}dm{tb}_{oc}_{k}")
            return bank, bank_b

        self.final_combine(tag, dst_ap, dst_b, Arow, Arow_b, Qrow, Qrow_b, mlp_bank)
        self.advance_x(dst_ap, dst_b)



SLOT_ORDER = ["L0_attn", "L0_ffn", "L1_attn", "L1_ffn",
              "L2_attn", "L2_ffn", "L3_attn", "L3_ffn"]
IMPLEMENTED = SLOT_ORDER[:]
DEBUG_DUMPS = []

def build_program(host, slots):
    nc = bass.Bass("TRN2", target_bir_lowering=False)
    with ExitStack() as ctx:
        b = Builder(nc, ctx, host)
        for i, s in enumerate(slots):
            last = (i == len(slots) - 1)
            if s in ("L0_attn", "L1_attn"):
                b.slot_lightning(s[:2], last)
            elif s == "L0_ffn":
                b.slot_l0ffn(last)
            elif s == "L1_ffn":
                b.slot_l1ffn(last)
            elif s == "L2_attn":
                b.slot_l2attn(last)
            elif s == "L2_ffn":
                b.slot_l2ffn(last)
            elif s == "L3_attn":
                b.slot_l3attn(last)
            elif s == "L3_ffn":
                b.slot_l3ffn(last)
            else:
                raise NotImplementedError(s)
        b.S.op("sp", lambda e: e.nop(), reads=(b.yT_b,), name="final")
        b.S.finalize()
        sems, qsems = b.sems, b.qsems
        S = b.S

        def emitf(name, eng):
            for o in S.ops[name]:
                for kv, val in o.waits:
                    if kv[0] == "q":
                        eng.wait_ge(qsems[(kv[1], kv[2])], val)
                    else:
                        eng.wait_ge(sems[kv[1]], val)
                ins = o.fn(eng)
                if o.is_dma:
                    ins.then_inc(qsems[(o.eng, o.inc_no % Sched.M_DMA)], 16)
                elif o.needed:
                    ins.then_inc(sems[o.eng], 1)

        with nc.Block() as blk:
            @blk.tensor
            def _(t):
                emitf("pe", t)
            @blk.scalar
            def _(s):
                emitf("act", s)
            @blk.vector
            def _(v):
                emitf("dve", v)
            @blk.gpsimd
            def _(g):
                emitf("pool", g)
            @blk.sync
            def _(s):
                emitf("sp", s)
        return nc, list(b.used_inputs.keys())


def kernel(**inputs):
    host = host_prepare(inputs)
    slots = [s for s in SLOT_ORDER if s in IMPLEMENTED]
    nc, needed = build_program(host, slots)
    x = np.asarray(inputs["L0_attn_x"])
    in_maps = []
    for c in range(8):
        m = {}
        for nm in needed:
            if nm == "xT":
                m[nm] = np.ascontiguousarray(x[c].T)
            else:
                m[nm] = host[nm]
        in_maps.append(m)
    res = run_bass_kernel_spmd(nc, in_maps, core_ids=list(range(8)))
    out = np.stack([np.ascontiguousarray(r["yT"].T) for r in res.results])
    return out


# version 04
# speedup vs baseline: 1.0051x; 1.0051x over previous
import math
import numpy as np
import ml_dtypes
from contextlib import ExitStack
import concourse.bass as bass
import concourse.mybir as mybir
from concourse.bass_utils import run_bass_kernel_spmd

F32 = mybir.dt.float32
BF16 = mybir.dt.bfloat16
AF = mybir.ActivationFunctionType
ALU = mybir.AluOpType
AX = mybir.AxisListType
BF = ml_dtypes.bfloat16

T = 2048
D = 2048
NK = D // 128
NTB = T // 512
NTT = T // 128

ENGINES = ("pe", "act", "dve", "pool", "sp")


class Buf:
    __slots__ = ("writers", "readers", "name")
    def __init__(self, name=""):
        self.writers = []
        self.readers = []
        self.name = name

class Op:
    __slots__ = ("eng", "fn", "deps", "inc_no", "is_dma", "name", "needed",
                 "waits", "post_drain")
    def __init__(self, eng, fn, deps, is_dma, name):
        self.eng = eng
        self.fn = fn
        self.deps = deps
        self.is_dma = is_dma
        self.name = name
        self.inc_no = None
        self.needed = False
        self.waits = ()
        self.post_drain = False

class Sched:
    def __init__(self):
        self.ops = {e: [] for e in ENGINES}
        self.n = 0

    def op(self, eng, fn, reads=(), writes=(), name="", dma=False,
           post_drain=False, extra=()):
        deps = set(extra)
        for b in reads:
            deps.update(b.writers)
        for b in writes:
            deps.update(b.writers)
            deps.update(b.readers)
        o = Op(eng, fn, deps, dma, name)
        o.post_drain = post_drain
        for b in reads:
            b.readers.append(o)
        for b in writes:
            b.writers = [o]
            b.readers = []
        self.ops[eng].append(o)
        self.n += 1
        return o

    def dma(self, queue, out_ap, in_ap, reads=(), writes=(), name=""):
        return self.op(queue, lambda e: e.dma_start(out=out_ap, in_=in_ap),
                       reads=reads, writes=writes, name=name, dma=True)

    M_DMA = 8
    TRUST_SAME = True

    def finalize(self):
        for e in ENGINES:
            for o in self.ops[e]:
                for d in o.deps:
                    if d.is_dma or not (d.eng == e == "pe"):
                        d.needed = True
        M = self.M_DMA
        for e in ENGINES:
            cnt = dcnt = 0
            for o in self.ops[e]:
                if o.is_dma:
                    o.inc_no = dcnt
                    dcnt += 1
                elif o.needed:
                    cnt += 1
                    o.inc_no = cnt
        for e in ENGINES:
            watermark = {}
            for o in self.ops[e]:
                waits = {}
                for d in o.deps:
                    if d.is_dma:
                        key = ("q", d.eng, d.inc_no % M)
                        val = 16 * (d.inc_no // M + 1)
                    else:
                        if d.eng == e and e == "pe":
                            continue
                        key = ("e", d.eng)
                        val = d.inc_no
                    if waits.get(key, -1) < val:
                        waits[key] = val
                if o.is_dma:
                    key = ("q", o.eng, o.inc_no % M)
                    gen = o.inc_no // M
                    if gen > 0 and waits.get(key, -1) < 16 * gen:
                        waits[key] = 16 * gen
                o.waits = tuple((k, v) for k, v in waits.items()
                                if watermark.get(k, 0) < v)
                for k, v in o.waits:
                    watermark[k] = v


def _lightning_consts():
    H, C = 16, 128
    start = 2.0 ** (-2.0 ** (-(math.log2(H) - 3)))
    slopes = np.array([start ** (i + 1) for i in range(H)], dtype=np.float32)
    lam = np.exp(-slopes).astype(np.float32)
    pos = np.arange(C, dtype=np.float32)
    decay_q = lam[:, None] ** (pos + 1.0)
    decay_k = lam[:, None] ** (C - 1.0 - pos)
    diff = pos[:, None] - pos[None, :]
    dmask = np.where(diff >= 0, lam[:, None, None] ** np.maximum(diff, 0), 0.0)
    dmaskT = dmask.transpose(0, 2, 1)
    lam_c = (lam ** C).astype(np.float32)
    dqb = np.broadcast_to(decay_q[:, None, :], (H, C, C)).transpose(1, 0, 2)
    return {
        "lt_dmaskT": np.ascontiguousarray(dmaskT.transpose(1, 0, 2)
                                          ).reshape(C, H * C).astype(BF),
        "lt_dq_bcast": np.ascontiguousarray(dqb).reshape(C, H * C).astype(BF),
        "lt_dk_cols": np.ascontiguousarray(decay_k.T).astype(np.float32),
    }, [float(v) for v in lam_c]

LT_CONSTS, LT_LAMC = _lightning_consts()

def col_chunks(v):
    return np.ascontiguousarray(np.asarray(v).reshape(-1, 128).T).astype(np.float32)

def host_prepare(inputs):
    g = {}
    f32 = np.float32
    g["ident"] = np.eye(128, dtype=f32)
    g["ident_bf"] = np.eye(128, dtype=f32).astype(BF)
    g["ones_row"] = np.ones((1, 128), dtype=f32)
    g["ones_row_bf"] = np.ones((1, 128), dtype=f32).astype(BF)
    g["ones_col_bf"] = np.ones((128, 1), dtype=f32).astype(BF)
    onesel = np.zeros((128, 16), dtype=f32)
    for blk in range(4):
        onesel[:, blk * 4 + blk] = 1.0
    g["onesel"] = onesel.astype(BF)
    tri = np.tril(np.ones((128, 128), bool))
    g["diag_mask"] = np.where(tri, 0.0, -1e30).astype(BF)
    g["diag_maskT"] = np.where(tri.T, 0.0, -1e30).astype(BF)
    g.update(LT_CONSTS)

    i = inputs
    sw = np.asarray(i["L0_ffn_select_weight"])
    lu = np.asarray(i["L0_ffn_lora_update"])
    ld = np.asarray(i["L0_ffn_lora_degen"])
    g["L0f_WallT"] = np.ascontiguousarray(sw.transpose(1, 0, 2).reshape(2048, 64))
    g["L0f_smdT"] = np.ascontiguousarray(np.asarray(i["L0_ffn_select_mean_diff"]).T)
    g["L0f_updT"] = np.ascontiguousarray(lu.transpose(1, 0, 2).reshape(2048, 64))
    g["L0f_dgnT"] = np.ascontiguousarray(ld.transpose(0, 2, 1).reshape(64, 2048))
    g["L0f_biasw"] = np.ascontiguousarray(np.asarray(i["L0_ffn_bias_w"]))
    g["L0f_dwT"] = np.ascontiguousarray(np.asarray(i["L0_ffn_debias_w"]).T)
    g["L0f_slope"] = np.asarray(i["L0_ffn_imp_slope"]).reshape(8, 1).astype(f32)
    g["L0f_center"] = np.asarray(i["L0_ffn_imp_center"]).reshape(8, 1).astype(f32)
    E8 = np.zeros((64, 8), f32)
    for n in range(8):
        E8[n * 8:(n + 1) * 8, n] = 1.0
    g["E8mask"] = E8
    g["E8bf"] = E8.astype(BF)
    g["P8bf"] = np.ascontiguousarray(E8.T).astype(BF)
    g["pen8"] = (np.arange(8, dtype=f32) * 1e-9).reshape(8, 1)
    g["L1f_WqT"] = np.ascontiguousarray(np.asarray(i["L1_ffn_Wq"]).T)
    g["L1f_WkT"] = np.ascontiguousarray(np.asarray(i["L1_ffn_Wk"]).T)
    g["L1f_WvT"] = np.ascontiguousarray(np.asarray(i["L1_ffn_Wv"]).T)
    g["L1f_Wo"] = np.ascontiguousarray(np.asarray(i["L1_ffn_Wo"]))
    g["L1f_A"] = np.ascontiguousarray(np.asarray(i["L1_ffn_A"]))
    g["L1f_B"] = np.ascontiguousarray(np.asarray(i["L1_ffn_B"]))
    g["L1f_WinT"] = np.ascontiguousarray(np.asarray(i["L1_ffn_W_in_base"]).T)
    g["L1f_WupT"] = np.ascontiguousarray(np.asarray(i["L1_ffn_W_up_base"]).T)
    g["L1f_WdnT"] = np.ascontiguousarray(np.asarray(i["L1_ffn_W_down"]).T)
    g["L1f_g1"] = col_chunks(i["L1_ffn_ln1_w"])
    g["L1f_g2"] = col_chunks(i["L1_ffn_ln2_w"])
    g["L2a_Wq"] = np.ascontiguousarray(np.asarray(i["L2_attn_Wq"]))
    g["L2a_Wkv"] = np.ascontiguousarray(np.asarray(i["L2_attn_Wkv"]))
    g["L2a_Wout"] = np.ascontiguousarray(np.asarray(i["L2_attn_Wout"]))
    rotP = np.zeros((128, 128), f32)
    for ii in range(64):
        rotP[ii, 64 + ii] = -1.0
        rotP[64 + ii, ii] = 1.0
    g["l2_rotP"] = rotP.astype(BF)
    inv_freq = 1.0 / 10000.0 ** (np.arange(0, 128, 2, dtype=f32) / 128.0)
    fr = np.arange(T, dtype=f32)[:, None] * inv_freq[None, :]
    posf = np.concatenate([fr, fr], -1)
    g["l2_cos"] = np.ascontiguousarray(np.cos(posf).T).astype(f32)
    g["l2_sin"] = np.ascontiguousarray(np.sin(posf).T).astype(f32)
    g["L2f_WcT"] = np.ascontiguousarray(np.asarray(i["L2_ffn_Wc"]).T)
    g["L2f_genes"] = np.ascontiguousarray(np.asarray(i["L2_ffn_genes"]))
    g["L2f_WdT"] = np.ascontiguousarray(np.asarray(i["L2_ffn_Wd"]).T)
    g["L2f_WuT"] = np.ascontiguousarray(np.asarray(i["L2_ffn_Wu"]).T)
    g["L2f_gw"] = col_chunks(i["L2_ffn_norm_w"])
    g["L2f_temp"] = np.asarray(i["L2_ffn_temperature"]).reshape(1, 1).astype(f32)
    g["L2f_scale"] = np.asarray(i["L2_ffn_scale"]).reshape(1, 1).astype(f32)
    for nm, key in (("L3a_wqd", "L3_attn_wq_down"), ("L3a_wqn", "L3_attn_wq_nope"),
                    ("L3a_wqr", "L3_attn_wq_rope"), ("L3a_wkvd", "L3_attn_wkv_down"),
                    ("L3a_wvu", "L3_attn_wv_up"), ("L3a_wkn", "L3_attn_wk_nope"),
                    ("L3a_wkr", "L3_attn_wk_rope"), ("L3a_wo", "L3_attn_wo")):
        g[nm] = np.ascontiguousarray(np.asarray(i[key]))
    g["L3f_WgT"] = np.ascontiguousarray(np.asarray(i["L3_ffn_w_gate"]).T)
    g["L3f_WuT"] = np.ascontiguousarray(np.asarray(i["L3_ffn_w_up"]).T)
    g["L3f_WdT"] = np.ascontiguousarray(np.asarray(i["L3_ffn_w_down"]).T)
    rotP3 = np.zeros((128, 128), f32)
    for ii in range(0, 128, 2):
        rotP3[ii, ii + 1] = 1.0
        rotP3[ii + 1, ii] = 1.0
    g["l3_rotP"] = rotP3.astype(BF)
    DR = 64
    inv3 = 1.0 / 10000.0 ** (np.arange(0, DR, 2, dtype=f32) / DR)
    fr3 = np.arange(T, dtype=f32)[:, None] * inv3[None, :]
    cos3 = np.cos(fr3); sin3 = np.sin(fr3)
    crow = np.repeat(cos3, 2, axis=1)
    srow = np.empty_like(crow)
    srow[:, 0::2] = -sin3
    srow[:, 1::2] = sin3
    g["l3_cos"] = np.ascontiguousarray(np.tile(crow, (1, 2)).T).astype(f32)
    g["l3_sin"] = np.ascontiguousarray(np.tile(srow, (1, 2)).T).astype(f32)
    for L in ("L0", "L1"):
        g[f"{L}a_wqkvT"] = np.ascontiguousarray(np.asarray(i[f"{L}_attn_w_qkv"]).T)
        g[f"{L}a_bqkv"] = col_chunks(i[f"{L}_attn_b_qkv"])
        g[f"{L}a_woutT"] = np.ascontiguousarray(np.asarray(i[f"{L}_attn_w_out"]).T)
        g[f"{L}a_bout"] = col_chunks(i[f"{L}_attn_b_out"])
    return g


class Builder:
    def __init__(self, nc, ctx, host):
        self.nc = nc
        self.ctx = ctx
        self.host = host
        self.S = Sched()
        E = ctx.enter_context
        self.used_inputs = {}
        self.in_bufs = {}
        self.sb_cache = {}

        self.xT_in = self.inp("xT", (D, T), F32)
        self.yT = nc.dram_tensor("yT", [D, T], F32, kind="ExternalOutput").ap()
        self.yT_b = Buf("yT")
        self.xhbm = [nc.dram_tensor(f"xs{j}", [D, T], F32, kind="Internal").ap()
                     for j in range(2)]
        self.xhbm_b = [[Buf(f"xs{j}_{k}") for k in range(NK)] for j in range(2)]
        self.cur = 0
        self.cur_ap = self.xT_in
        self.cur_b = [self.in_bufs["xT"]] * NK

        self.biga = E(nc.sbuf_tensor("biga", [128, 49152], BF16))
        self.biga_b = [Buf(f"bg{k}") for k in range(24)]
        self.fwA = E(nc.sbuf_tensor("fwA", [128, 4096], F32))
        self.fw = [self.fwA[:, 0:2048], self.fwA[:, 2048:4096]]
        self.fw_b = [Buf(f"fw{j}") for j in range(2)]
        self.SR = E(nc.sbuf_tensor("SR", [80, 10240], F32))
        self.rows_b = [Buf(f"row{j}") for j in range(5)]
        self.sr32_b = [Buf(f"sr32_{j}") for j in range(5)]
        self.sr64_b = [Buf(f"sr64_{j}") for j in range(5)]
        self.pqs = [E(nc.sbuf_tensor(f"pqs{j}", [128, 512], F32)) for j in range(3)]
        self.pqs_b = [Buf(f"pqs{j}") for j in range(3)]
        self.cols = E(nc.sbuf_tensor("colsa", [128, 512], F32))
        self.cols_b = [Buf(f"cols{j}") for j in range(4)]
        self.s4 = E(nc.sbuf_tensor("s4", [4, 3584], F32))
        self.s4_b = [Buf(f"s4_{j}") for j in range(8)]
        self.wslab = [E(nc.sbuf_tensor(f"wsl{j}", [128, 2048], BF16)) for j in range(2)]
        self.wslab_b = [Buf(f"wsl{j}") for j in range(2)]
        self.stg = [E(nc.sbuf_tensor(f"stg{j}", [128, 512], BF16)) for j in range(4)]
        self.stg_b = [Buf(f"stg{j}") for j in range(4)]
        self.tmp = [E(nc.sbuf_tensor(f"tmp{j}", [128, 512], F32)) for j in range(2)]
        self.tmp_b = [Buf(f"tmp{j}") for j in range(2)]
        self.small = [E(nc.sbuf_tensor(f"sm{j}", [128, 128], F32)) for j in range(4)]
        self.small_b = [Buf(f"sm{j}") for j in range(4)]
        self.smallb = [E(nc.sbuf_tensor(f"sb{j}", [128, 128], BF16)) for j in range(10)]
        self.dmy = E(nc.sbuf_tensor("dmy", [1, 8], F32))
        self.dmy_b = Buf("dmy")
        self.smallb_b = [Buf(f"sb{j}") for j in range(10)]

        self.pb = [E(nc.psum_tensor(f"pb{j}", [128, 512], F32)) for j in range(6)]
        self.pb_b = [Buf(f"pb{j}") for j in range(6)]
        self.pbf = [E(nc.psum_tensor(f"pbf{j}", [128, 1024], BF16)) for j in range(2)]
        self.pbf_b = [Buf(f"pbf{j}") for j in range(2)]

        self.sems = {e: E(nc.semaphore(f"sem_{e}")) for e in ("pe", "act", "dve", "pool")}
        self.qsems = {(e, j): E(nc.semaphore(f"q_{e}{j}"))
                      for e in ("sp", "act", "pool") for j in range(Sched.M_DMA)}

        self.scr_row = [nc.dram_tensor(f"scrr{j}", [1, T], F32, kind="Internal").ap()
                        for j in range(4)]
        self.scr_row_b = [Buf(f"scrr{j}") for j in range(4)]
        self.qkvT = nc.dram_tensor("qkvT", [3 * D, T], BF16, kind="Internal").ap()
        self.qkvT_b = Buf("qkvT")
        self.attnT = nc.dram_tensor("attnT", [D, T], BF16, kind="Internal").ap()
        self.attnT_b = Buf("attnT")
        self.hidT = nc.dram_tensor("hidT", [8192, T], BF16, kind="Internal").ap()
        self.hidT_b = Buf("hidT")
        self.vT3 = nc.dram_tensor("vT3", [3072, T], BF16, kind="Internal").ap()
        self.vT3_b = Buf("vT3")
        self.attnT3 = nc.dram_tensor("attnT3", [3072, T], BF16, kind="Internal").ap()
        self.attnT3_b = Buf("attnT3")
        self.W2sb = E(nc.sbuf_tensor("W2sb", [128, 256], BF16))
        self.W2sb_b = Buf("W2sb")

    def inp(self, name, shape, dtype):
        ap = self.nc.dram_tensor(name, list(shape), dtype, kind="ExternalInput").ap()
        self.used_inputs[name] = ap
        self.in_bufs[name] = Buf(name)
        return ap

    def win(self, name):
        if name in self.used_inputs:
            return self.used_inputs[name], self.in_bufs[name]
        arr = self.host[name]
        dt = BF16 if arr.dtype == BF else F32
        ap = self.inp(name, arr.shape, dt)
        return ap, self.in_bufs[name]

    def sbconst(self, name):
        if name in self.sb_cache:
            return self.sb_cache[name]
        ap, b = self.win(name)
        arr = self.host[name]
        dt = BF16 if arr.dtype == BF else F32
        tile = self.ctx.enter_context(
            self.nc.sbuf_tensor("c_" + name, list(arr.shape), dt))
        tb = Buf("c_" + name)
        self.S.dma("sp", tile[:], ap, reads=(b,), writes=(tb,), name="ld_" + name)
        self.sb_cache[name] = (tile, tb)
        return tile, tb

    def dump(self, key, ap, bufs, shape, dtype):
        import kernel as _k
        if key not in DEBUG_DUMPS:
            return
        out = self.nc.dram_tensor("dbg_" + key, list(shape), dtype,
                                  kind="ExternalOutput").ap()
        ob = Buf("dbg_" + key)
        self.S.dma("sp", out, ap, reads=tuple(bufs), writes=(ob,),
                   name="dump_" + key)
        self.S.op("sp", lambda e: e.nop(), reads=(ob,), name="dumpw_" + key)

    def next_x(self, last):
        if last:
            return self.yT, [self.yT_b] * NK
        j = 1 - self.cur
        return self.xhbm[j], self.xhbm_b[j]

    def advance_x(self, dst_ap, dst_bufs):
        self.cur = 1 - self.cur
        self.cur_ap = dst_ap
        self.cur_b = dst_bufs

    def biga_slot(self, k):
        return self.biga[:, k * 2048:(k + 1) * 2048], self.biga_b[k]

    def row(self, j):
        return self.SR[0:1, j * 2048:(j + 1) * 2048], self.rows_b[j]

    def sr32(self, j, p=16):
        return self.SR[32:32 + p, j * 2048:(j + 1) * 2048], self.sr32_b[j]

    def sr64(self, j, p=16):
        return self.SR[64:64 + p, j * 2048:(j + 1) * 2048], self.sr64_b[j]

    def cols_to_row(self, cols_ap, cols_buf, n, row_idx, tag):
        S = self.S
        identf, identf_b = self.sbconst("ident")
        dst, dst_b = self.row(row_idx)
        pbk, pbk_b = self.pb[4], self.pb_b[4]
        for g in range(0, n, 4):
            for tt in range(g, min(g + 4, n)):
                S.op("pe", lambda e, tt=tt: e.transpose(
                        pbk[0:1, (tt % 4) * 128:(tt % 4) * 128 + 128],
                        cols_ap[:, tt:tt + 1], identf[:]),
                     reads=(cols_buf, identf_b), writes=(pbk_b,),
                     name=f"{tag}c2r{tt}")
            w = min(g + 4, n) - g
            S.op("dve", lambda e, g=g, w=w: e.tensor_copy(
                    dst[0:1, g * 128:(g + w) * 128], pbk[0:1, 0:w * 128]),
                 reads=(pbk_b,), writes=(dst_b,), name=f"{tag}c2re{g}")

    def s4s(self, j):
        return self.s4[0:4, j * 512:(j + 1) * 512], self.s4_b[j]

    def stats_pass(self, tag):
        S = self.S
        onesel, onesel_b = self.sbconst("onesel")
        psum_s, psum_s_b = self.pb[4], self.pb_b[4]
        psum_q, psum_q_b = self.pb[5], self.pb_b[5]
        for k in range(NK):
            xs, xs_b = self.fw[k % 2], self.fw_b[k % 2]
            S.dma("sp", xs, self.cur_ap[k * 128:(k + 1) * 128, :],
                  reads=(self.cur_b[k],), writes=(xs_b,), name=f"{tag}ldx{k}")
            xbf, xbf_b = self.biga_slot(k)
            S.op("pool", lambda e, xs=xs, xbf=xbf: e.tensor_copy(xbf, xs),
                 reads=(xs_b,), writes=(xbf_b,), name=f"{tag}cast{k}")
            sqs, sqs_b = self.biga_slot(16 + (k % 2))
            S.op("act", lambda e, xs=xs, sqs=sqs: e.activation(sqs, xs, AF.Square),
                 reads=(xs_b,), writes=(sqs_b,), name=f"{tag}sq{k}")
            for blk in range(4):
                st = (k == 0 and blk == 0)
                sp = (k == NK - 1 and blk == 3)
                S.op("pe", lambda e, blk=blk, xbf=xbf, st=st, sp=sp, o=onesel:
                     e.matmul(psum_s[0:4, :], o[:, blk * 4:blk * 4 + 4],
                              xbf[:, blk * 512:(blk + 1) * 512], start=st, stop=sp),
                     reads=(xbf_b, onesel_b), writes=(psum_s_b,), name=f"{tag}ms{k}")
                S.op("pe", lambda e, blk=blk, sqs=sqs, st=st, sp=sp, o=onesel:
                     e.matmul(psum_q[0:4, :], o[:, blk * 4:blk * 4 + 4],
                              sqs[:, blk * 512:(blk + 1) * 512], start=st, stop=sp),
                     reads=(sqs_b, onesel_b), writes=(psum_q_b,), name=f"{tag}mq{k}")
        sm, sm_b = self.s4s(0)
        sq, sq_b = self.s4s(1)
        S.op("dve", lambda e, sm=sm: e.tensor_copy(sm, psum_s[0:4, :]),
             reads=(psum_s_b,), writes=(sm_b,), name=f"{tag}evs")
        S.op("dve", lambda e, sq=sq: e.tensor_copy(sq, psum_q[0:4, :]),
             reads=(psum_q_b,), writes=(sq_b,), name=f"{tag}evq")

    def derive_mu_r(self, tag):
        S = self.S
        mu, mu_b = self.s4s(2)
        var, var_b = self.s4s(3)
        r, r_b = self.s4s(4)
        t0, t0_b = self.s4s(5)
        sm, sm_b = self.s4s(0)
        sq, sq_b = self.s4s(1)
        S.op("dve", lambda e: e.tensor_scalar_mul(mu, sm, 1.0 / D),
             reads=(sm_b,), writes=(mu_b,), name=f"{tag}mu")
        S.op("dve", lambda e: e.tensor_scalar_mul(t0, sq, 1.0 / D),
             reads=(sq_b,), writes=(t0_b,), name=f"{tag}ex2")
        S.op("dve", lambda e: e.tensor_mul(var, mu, mu),
             reads=(mu_b,), writes=(var_b,), name=f"{tag}mu2")
        S.op("dve", lambda e: e.tensor_sub(var, t0, var),
             reads=(t0_b, var_b), writes=(var_b,), name=f"{tag}var")
        S.op("dve", lambda e: e.tensor_scalar_add(t0, var, 1e-5),
             reads=(var_b,), writes=(t0_b,), name=f"{tag}ve")
        S.op("act", lambda e: e.activation(t0, t0, AF.Sqrt),
             reads=(t0_b,), writes=(t0_b,), name=f"{tag}sqrt")
        S.op("dve", lambda e: e.reciprocal(r, t0),
             reads=(t0_b,), writes=(r_b,), name=f"{tag}r")

    def to_row(self, s4_idx, row_idx, tag):
        S = self.S
        src, src_b = self.s4s(s4_idx)
        scr, scr_b = self.scr_row[row_idx], self.scr_row_b[row_idx]
        dst, dst_b = self.row(row_idx)
        scr4 = scr.rearrange("o (a b) -> (o a) b", a=4, b=512)
        S.dma("sp", scr4, src, reads=(src_b,), writes=(scr_b,), name=f"{tag}r2d{row_idx}")
        S.dma("sp", dst, scr, reads=(scr_b,), writes=(dst_b,), name=f"{tag}d2r{row_idx}")

    def make_s_ms_rows(self, tag, kind):
        S = self.S
        self.derive_mu_r(tag)
        mu, mu_b = self.s4s(2)
        r, r_b = self.s4s(4)
        t1, t1_b = self.s4s(6)
        if kind == "ln":
            S.op("dve", lambda e: e.tensor_mul(t1, mu, r),
                 reads=(mu_b, r_b), writes=(t1_b,), name=f"{tag}ms")
            self.to_row(4, 0, tag)
            self.to_row(6, 1, tag)
        else:
            raise NotImplementedError(kind)

    def xhat_inplace(self, tag):
        S = self.S
        ones_row, ones_row_b = self.sbconst("ones_row")
        srow, srow_b = self.row(0)
        msrow, msrow_b = self.row(1)
        for tb in range(NTB):
            bs, bs_b = self.pb[4], self.pb_b[4]
            bm, bm_b = self.pb[5], self.pb_b[5]
            sl = slice(tb * 512, (tb + 1) * 512)
            S.op("pe", lambda e, sl=sl, o=ones_row: e.matmul(
                    bs[:, :], o[:], srow[0:1, sl], start=True, stop=True),
                 reads=(srow_b, ones_row_b), writes=(bs_b,), name=f"{tag}bcs{tb}")
            S.op("pe", lambda e, sl=sl, o=ones_row: e.matmul(
                    bm[:, :], o[:], msrow[0:1, sl], start=True, stop=True),
                 reads=(msrow_b, ones_row_b), writes=(bm_b,), name=f"{tag}bcm{tb}")
            for k in range(NK):
                xbf, xbf_b = self.biga_slot(k)
                tj, tj_b = self.tmp[k % 2], self.tmp_b[k % 2]
                S.op("dve", lambda e, xbf=xbf, sl=sl, tj=tj: e.tensor_mul(
                        tj[:], xbf[:, sl], bs[:, :]),
                     reads=(xbf_b, bs_b), writes=(tj_b,), name=f"{tag}xh1_{k}_{tb}")
                S.op("dve", lambda e, xbf=xbf, sl=sl, tj=tj: e.tensor_sub(
                        xbf[:, sl], tj[:], bm[:, :]),
                     reads=(tj_b, bm_b), writes=(xbf_b,), name=f"{tag}xh2_{k}_{tb}")

    def proj(self, tag, wT_name, o_lo, o_hi, K_total, evac, rhs_slot=None,
             after_oc=None, extra_mm=None, o_w_off=0):
        S = self.S
        wT, wT_b = self.win(wT_name)
        KT = K_total // 128
        if rhs_slot is None:
            rhs_slot = self.biga_slot
        wview = wT.rearrange("(c p) o -> p c o", p=128)
        nslab = (KT + 15) // 16
        for oc in range(o_lo, o_hi):
            slabs = []
            for si in range(nslab):
                j = (oc * nslab + si) % 2
                wsl, wsl_b = self.wslab[j], self.wslab_b[j]
                c0, c1 = si * 16, min((si + 1) * 16, KT)
                dst = wsl[:].rearrange("p (c o) -> p c o", o=128)[:, 0:c1 - c0, :]
                src = wview[:, c0:c1,
                            (oc - o_w_off) * 128:(oc - o_w_off + 1) * 128]
                S.dma("pool", dst, src, reads=(wT_b,), writes=(wsl_b,),
                      name=f"{tag}wld{oc}_{si}")
                slabs.append((wsl, wsl_b, c0, c1))
            for half in range(2):
                g = (oc * 2 + half) % 2
                banks = [(self.pb[2 * g], self.pb_b[2 * g]),
                         (self.pb[2 * g + 1], self.pb_b[2 * g + 1])]
                for (wsl, wsl_b, c0, c1) in slabs:
                    for k in range(c0, c1):
                        lhsT = wsl[:, (k - c0) * 128:(k - c0 + 1) * 128]
                        rap, rb = rhs_slot(k)
                        for t2 in range(2):
                            tb = half * 2 + t2
                            st = (k == 0)
                            sp = (k == KT - 1 and extra_mm is None)
                            pa, pab = banks[t2]
                            S.op("pe", lambda e, lhsT=lhsT, rap=rap, tb=tb, st=st,
                                 sp=sp, pa=pa: e.matmul(
                                     pa[:, :], lhsT, rap[:, tb * 512:(tb + 1) * 512],
                                     start=st, stop=sp),
                                 reads=(wsl_b, rb), writes=(pab,),
                                 name=f"{tag}mm{oc}_{k}_{tb}")
                for t2 in range(2):
                    if extra_mm is not None:
                        extra_mm(oc, half * 2 + t2, banks[t2][0], banks[t2][1])
                    evac(oc, half * 2 + t2, banks[t2][0], banks[t2][1])
            if after_oc is not None:
                after_oc(oc)

    def slot_lightning(self, L, last):
        tag = f"{L}a"
        S = self.S
        nc = self.nc
        dst_ap, dst_b = self.next_x(last)
        self.stats_pass(tag)
        self.make_s_ms_rows(tag, "ln")
        self.xhat_inplace(tag)

        bq, bq_b = self.sbconst(f"{L}a_bqkv")
        def evac_qkv(oc, tb, pa, pab):
            stj, stj_b = self.stg[tb % 2], self.stg_b[tb % 2]
            S.op("dve", lambda e, oc=oc, pa=pa, stj=stj: e.tensor_scalar(
                    stj[:], pa[:, :], bq[:, oc:oc + 1], None, ALU.add),
                 reads=(pab, bq_b), writes=(stj_b,), name=f"{tag}eq{oc}_{tb}")
            S.dma("sp", self.qkvT[oc * 128:(oc + 1) * 128, tb * 512:(tb + 1) * 512],
                  stj[:], reads=(stj_b,), writes=(self.qkvT_b,),
                  name=f"{tag}sq{oc}_{tb}")

        dmaskT, dmaskT_b = self.sbconst("lt_dmaskT")
        dqb_c, dqb_b = self.sbconst("lt_dq_bcast")
        dkc, dkc_b = self.sbconst("lt_dk_cols")
        ones_row, ones_row_b = self.sbconst("ones_row")
        identb, identb_b = self.sbconst("ident_bf")

        def scan_head_ops(h):
            j = h % 2
            QT, QT_b = self.biga_slot(16 + j * 3 + 0)
            KT_, KT_b = self.biga_slot(16 + j * 3 + 1)
            VT, VT_b = self.biga_slot(16 + j * 3 + 2)
            for t_, tb_, row0 in ((QT, QT_b, h), (KT_, KT_b, 16 + h), (VT, VT_b, 32 + h)):
                S.dma("sp", t_, self.qkvT[row0 * 128:(row0 + 1) * 128, :],
                      reads=(self.qkvT_b,), writes=(tb_,), name=f"{tag}h{h}ld")
            pbx, pbx_b = self.pb[4 + j], self.pb_b[4 + j]
            S32, S32_b = self.small[2 + j], self.small_b[2 + j]
            Sbf, Sbf_b = self.smallb[8 + j], self.smallb_b[8 + j]
            S.op("dve", lambda e, S32=S32: e.memset(S32[:], 0.0),
                 writes=(S32_b,), name=f"{tag}h{h}s0")
            atn, atn_b = self.biga_slot(22 + j)
            pv, pv_b = self.pbf[j], self.pbf_b[j]
            for c in range(NTT):
                csl = slice(c * 128, (c + 1) * 128)
                def step1(c=c, csl=csl):
                    S.op("pe", lambda e: e.matmul(
                            pbx[:, 0:128], KT_[:, csl], QT[:, csl],
                            start=True, stop=True),
                         reads=(QT_b, KT_b), writes=(pbx_b,), name=f"{tag}h{h}c{c}sc")
                    S.op("pe", lambda e: e.transpose(
                            pv[:, 0:128], VT[:, csl], identb[:]),
                         reads=(VT_b, identb_b), writes=(pv_b,), name=f"{tag}h{h}c{c}vt")
                    sm_t, sm_b = self.smallb[4 * j], self.smallb_b[4 * j]
                    S.op("dve", lambda e: e.tensor_mul(
                            sm_t[:], pbx[:, 0:128], dmaskT[:, h * 128:(h + 1) * 128]),
                         reads=(pbx_b, dmaskT_b), writes=(sm_b,), name=f"{tag}h{h}c{c}sm")
                    vb_t, vb_b = self.smallb[4 * j + 1], self.smallb_b[4 * j + 1]
                    S.op("dve", lambda e: e.tensor_copy(vb_t[:], pv[:, 0:128]),
                         reads=(pv_b,), writes=(vb_b,), name=f"{tag}h{h}c{c}ve")
                def step2(c=c, csl=csl):
                    sm_t, sm_b = self.smallb[4 * j], self.smallb_b[4 * j]
                    vb_t, vb_b = self.smallb[4 * j + 1], self.smallb_b[4 * j + 1]
                    S.op("pe", lambda e: e.transpose(
                            pv[:, 128:256], KT_[:, csl], identb[:]),
                         reads=(KT_b, identb_b), writes=(pv_b,), name=f"{tag}h{h}c{c}kt")
                    ksg, ksg_b = self.smallb[4 * j + 2], self.smallb_b[4 * j + 2]
                    S.op("dve", lambda e: e.tensor_scalar(
                            ksg[:], pv[:, 128:256], dkc[:, h:h + 1], None, ALU.mult),
                         reads=(pv_b, dkc_b), writes=(ksg_b,), name=f"{tag}h{h}c{c}ke")
                    qs, qs_b = self.smallb[4 * j + 3], self.smallb_b[4 * j + 3]
                    S.op("dve", lambda e: e.tensor_mul(
                            qs[:], QT[:, csl], dqb_c[:, h * 128:(h + 1) * 128]),
                         reads=(QT_b, dqb_b), writes=(qs_b,), name=f"{tag}h{h}c{c}qs")
                def step3(c=c, csl=csl):
                    sm_t, sm_b = self.smallb[4 * j], self.smallb_b[4 * j]
                    vb_t, vb_b = self.smallb[4 * j + 1], self.smallb_b[4 * j + 1]
                    ksg, ksg_b = self.smallb[4 * j + 2], self.smallb_b[4 * j + 2]
                    qs, qs_b = self.smallb[4 * j + 3], self.smallb_b[4 * j + 3]
                    S.op("pe", lambda e: e.matmul(pbx[:, 128:256], vb_t[:], sm_t[:],
                                                  start=True, stop=(c == 0)),
                         reads=(vb_b, sm_b), writes=(pbx_b,), name=f"{tag}h{h}c{c}o1")
                    if c > 0:
                        S.op("pe", lambda e: e.matmul(pbx[:, 128:256], Sbf[:], qs[:],
                                                      start=False, stop=True),
                             reads=(qs_b, Sbf_b), writes=(pbx_b,), name=f"{tag}h{h}c{c}o2")
                    S.op("pe", lambda e: e.matmul(pbx[:, 256:384], ksg[:], vb_t[:],
                                                  start=True, stop=True),
                         reads=(ksg_b, vb_b), writes=(pbx_b,), name=f"{tag}h{h}c{c}su")
                def step4(c=c, csl=csl):
                    S.op("dve", lambda e: e.tensor_copy(atn[:, csl], pbx[:, 128:256]),
                         reads=(pbx_b,), writes=(atn_b,), name=f"{tag}h{h}c{c}oe")
                    S.op("dve", lambda e: e.scalar_tensor_tensor(
                            S32[:], S32[:], LT_LAMC[h], pbx[:, 256:384],
                            ALU.mult, ALU.add),
                         reads=(S32_b, pbx_b), writes=(S32_b,), name=f"{tag}h{h}c{c}sd")
                    if c < NTT - 1:
                        S.op("act", lambda e: e.activation(Sbf[:], S32[:], AF.Copy),
                             reads=(S32_b,), writes=(Sbf_b,), name=f"{tag}h{h}c{c}sb")
                yield step1
                yield step2
                yield step3
                yield step4
            def final():
                S.dma("sp", self.attnT[h * 128:(h + 1) * 128, :], atn,
                      reads=(atn_b,), writes=(self.attnT_b,), name=f"{tag}h{h}st")
                if h == 0 and "atn0" in DEBUG_DUMPS:
                    out0 = self.nc.dram_tensor("dbg_atn0", [128, T], BF16,
                                               kind="ExternalOutput").ap()
                    ob0 = Buf("dbg_atn0")
                    S.dma("act", out0, atn, reads=(atn_b,), writes=(ob0,),
                          name="dump_atn0")
                    S.op("sp", lambda e: e.nop(), reads=(ob0,), name="dump_atn0w")
            yield final

        def scan_pipeline(prime):
            nxt = [0]
            active = []
            def start():
                if nxt[0] < 16:
                    active.append(scan_head_ops(nxt[0]))
                    nxt[0] += 1
            start()
            for _ in range(prime):
                try:
                    next(active[0])()
                except StopIteration:
                    break
            start()
            while active:
                for g in list(active):
                    try:
                        next(g)()
                    except StopIteration:
                        active.remove(g)
                        start()

        scan_started = [False]
        def after_oc(oc):
            if oc == 47:
                scan_pipeline(8)

        self.dump("xhat", self.biga[:, 0:32768], self.biga_b[0:16],
                  (128, 32768), BF16)
        self.dump("rows", self.SR[0:1, 0:4096], self.rows_b[0:2], (1, 4096), F32)
        self.proj(tag + "qkv", f"{L}a_wqkvT", 0, 48, D, evac_qkv, after_oc=after_oc)
        self.dump("qkvT", self.qkvT, [self.qkvT_b], (3 * D, T), BF16)
        self.dump("attnT", self.attnT, [self.attnT_b], (D, T), BF16)

        def attn_slot(k):
            ap, b = self.biga_slot(k)
            return ap, b
        for k in range(NK):
            ap, b = self.biga_slot(k)
            S.dma("sp", ap, self.attnT[k * 128:(k + 1) * 128, :],
                  reads=(self.attnT_b,), writes=(b,), name=f"{tag}lda{k}")

        bo, bo_b = self.sbconst(f"{L}a_bout")
        xloaded = {}
        def evac_wout(oc, tb, pa, pab):
            if oc not in xloaded:
                xs, xs_b = self.fw[oc % 2], self.fw_b[oc % 2]
                S.dma("sp", xs, self.cur_ap[oc * 128:(oc + 1) * 128, :],
                      reads=(self.cur_b[oc],), writes=(xs_b,), name=f"{tag}rx{oc}")
                xloaded[oc] = (xs, xs_b)
            xs, xs_b = xloaded[oc]
            sl = slice(tb * 512, (tb + 1) * 512)
            S.op("dve", lambda e, pa=pa, oc=oc, sl=sl, xs=xs: e.scalar_tensor_tensor(
                    xs[:, sl], pa[:, :], bo[:, oc:oc + 1], xs[:, sl], ALU.add, ALU.add),
                 reads=(pab, bo_b, xs_b), writes=(xs_b,), name=f"{tag}ro{oc}_{tb}")
            if tb == 3:
                S.dma("sp", dst_ap[oc * 128:(oc + 1) * 128, :], xs,
                      reads=(xs_b,), writes=(dst_b[oc],), name=f"{tag}wx{oc}")

        self.proj(tag + "wo", f"{L}a_woutT", 0, 16, D, evac_wout)
        self.advance_x(dst_ap, dst_b)


    def slot_l0ffn(self, last):
        tag = "L0f"
        S = self.S
        dst_ap, dst_b = self.next_x(last)
        self.stats_pass(tag)
        self.make_s_ms_rows(tag, "ln")
        var, var_b = self.s4s(3)
        r, r_b = self.s4s(4)
        t2, t2_b = self.s4s(6)
        S.op("dve", lambda e: e.tensor_mul(t2, var, r),
             reads=(var_b, r_b), writes=(t2_b,), name=f"{tag}n1")
        S.op("dve", lambda e: e.tensor_mul(t2, t2, r),
             reads=(t2_b, r_b), writes=(t2_b,), name=f"{tag}n2")
        S.op("dve", lambda e: e.tensor_scalar_mul(t2, t2, float(D)),
             reads=(t2_b,), writes=(t2_b,), name=f"{tag}n3")
        self.to_row(6, 2, tag)
        self.xhat_inplace(tag)

        wsl0, wsl0_b = self.wslab[0], self.wslab_b[0]
        wsl1, wsl1_b = self.wslab[1], self.wslab_b[1]
        WallT, WallT_b = self.win("L0f_WallT")
        updT, updT_b = self.win("L0f_updT")
        smdT, smdT_b = self.win("L0f_smdT")
        dwT, dwT_b = self.win("L0f_dwT")
        dgnT, dgnT_b = self.win("L0f_dgnT")
        biasw, biasw_b = self.win("L0f_biasw")
        def slabload(dst, dstb, src, srcb, o, nm):
            sv = src.rearrange("(c p) o -> p c o", p=128)
            S.dma("pool", dst.rearrange("p (c o) -> p c o", o=o), sv,
                  reads=(srcb,), writes=(dstb,), name=nm)
        slabload(wsl0[:, 0:1024], wsl0_b, WallT, WallT_b, 64, f"{tag}lw")
        slabload(wsl0[:, 1024:2048], wsl0_b, updT, updT_b, 64, f"{tag}lu")
        smd_sb, smd_sb_b = self.biga_slot(16)
        slabload(smd_sb[:, 0:128], smd_sb_b, smdT, smdT_b, 8, f"{tag}ls")
        dw_sb, dw_sb_b = self.biga_slot(17)
        slabload(dw_sb[:, 0:128], dw_sb_b, dwT, dwT_b, 8, f"{tag}ld")
        dgn_sb, dgn_sb_b = self.biga_slot(18)
        S.dma("pool", dgn_sb[0:64, :], dgnT, reads=(dgnT_b,), writes=(dgn_sb_b,),
              name=f"{tag}lg")
        bw_sb, bw_sb_b = self.biga_slot(19)
        S.dma("pool", bw_sb[0:8, :], biasw, reads=(biasw_b,), writes=(bw_sb_b,),
              name=f"{tag}lb")
        oh8bf, oh8bf_b = self.biga_slot(20)
        z_sb, z_sb_b = self.biga_slot(21)
        E8m, E8m_b = self.sbconst("E8mask")
        E8b, E8b_b = self.sbconst("E8bf")
        P8b, P8b_b = self.sbconst("P8bf")
        pen8, pen8_b = self.sbconst("pen8")
        onesc, onesc_b = self.sbconst("ones_col_bf")
        ones_row, ones_row_b = self.sbconst("ones_row")
        identf, identf_b = self.sbconst("ident")

        pb0, pb0_b = self.pb[0], self.pb_b[0]
        pb1, pb1_b = self.pb[1], self.pb_b[1]
        pb2, pb2_b = self.pb[2], self.pb_b[2]
        pb3, pb3_b = self.pb[3], self.pb_b[3]
        pb4, pb4_b = self.pb[4], self.pb_b[4]
        pb5, pb5_b = self.pb[5], self.pb_b[5]

        sq_sb, sq_sb_b = self.stg[1], self.stg_b[1]
        S.op("act", lambda e: e.activation(sq_sb[:, 0:128], smd_sb[:, 0:128],
                                           AF.Square),
             reads=(smd_sb_b,), writes=(sq_sb_b,), name=f"{tag}sqm")
        for c in range(NK):
            S.op("pe", lambda e, c=c: e.matmul(
                    pb4[0:8, 0:1], sq_sb[:, c * 8:(c + 1) * 8], onesc[:],
                    start=(c == 0), stop=(c == NK - 1)),
                 reads=(sq_sb_b, onesc_b), writes=(pb4_b,), name=f"{tag}mq{c}")
        musq_col = self.small[1]
        musq_col_b = self.small_b[1]
        S.op("dve", lambda e: e.tensor_copy(musq_col[0:8, 2:3], pb4[0:8, 0:1]),
             reads=(pb4_b,), writes=(musq_col_b,), name=f"{tag}mqe")
        slp, slp_b = self.small[0], self.small_b[0]
        S.dma("sp", slp[0:8, 4:5], self.win("L0f_slope")[0],
              reads=(slope_cb if False else self.in_bufs["L0f_slope"],),
              writes=(slp_b,), name=f"{tag}lsl")
        S.dma("sp", slp[0:8, 5:6], self.win("L0f_center")[0],
              reads=(self.in_bufs["L0f_center"],), writes=(slp_b,), name=f"{tag}lce")
        S.op("dve", lambda e: e.tensor_mul(slp[0:8, 6:7], slp[0:8, 4:5],
                                           slp[0:8, 5:6]),
             reads=(slp_b,), writes=(slp_b,), name=f"{tag}gb1")
        S.op("dve", lambda e: e.tensor_scalar_mul(slp[0:8, 6:7], slp[0:8, 6:7], -1.0),
             reads=(slp_b,), writes=(slp_b,), name=f"{tag}gb2")

        def diag64(lhs_sl, rhs_sl, rhs_b, out_sl, nm):
            for c in range(NK):
                S.op("pe", lambda e, c=c: e.matmul(
                        pb5[0:64, 0:8], wsl0[:, lhs_sl + c * 64: lhs_sl + (c + 1) * 64],
                        rhs_sl[:, c * 8:(c + 1) * 8],
                        start=(c == 0), stop=(c == NK - 1)),
                     reads=(wsl0_b, rhs_b), writes=(pb5_b,), name=f"{nm}{c}")
            S.op("dve", lambda e: e.tensor_mul(self.tmp[0][0:64, 0:8],
                                               pb5[0:64, 0:8], E8m[:]),
                 reads=(pb5_b, E8m_b), writes=(self.tmp_b[0],), name=f"{nm}m")
            S.op("dve", lambda e: e.reduce_sum(musq_col[0:64, out_sl:out_sl + 1],
                                               self.tmp[0][0:64, 0:8], axis=AX.X),
                 reads=(self.tmp_b[0],), writes=(musq_col_b,), name=f"{nm}r")
        diag64(0, smd_sb, smd_sb_b, 0, f"{tag}muw")
        diag64(1024, dw_sb, dw_sb_b, 1, f"{tag}mdb")

        umu, umu_b = self.sr32(0, 8)
        for tb in range(NTB):
            for c in range(NK):
                xh, xh_b = self.biga_slot(c)
                S.op("pe", lambda e, c=c, tb=tb, xh=xh: e.matmul(
                        pb4[0:8, :], smd_sb[:, c * 8:(c + 1) * 8],
                        xh[:, tb * 512:(tb + 1) * 512],
                        start=(c == 0), stop=(c == NK - 1)),
                     reads=(smd_sb_b, xh_b), writes=(pb4_b,), name=f"{tag}um{tb}_{c}")
            S.op("dve", lambda e, tb=tb: e.tensor_copy(
                    umu[0:8, tb * 512:(tb + 1) * 512], pb4[0:8, :]),
                 reads=(pb4_b,), writes=(umu_b,), name=f"{tag}ume{tb}")

        nsqr, nsqr_b = self.row(2)
        d2, d2_b = self.sr32(1, 8)
        score, score_b = self.sr32(2, 8)
        for tb in range(NTB):
            sl = slice(tb * 512, (tb + 1) * 512)
            for c in range(NK):
                xh, xh_b = self.biga_slot(c)
                S.op("pe", lambda e, c=c, sl=sl, xh=xh: e.matmul(
                        pb5[0:64, :], wsl0[:, c * 64:(c + 1) * 64], xh[:, sl],
                        start=(c == 0), stop=(c == NK - 1)),
                     reads=(wsl0_b, xh_b), writes=(pb5_b,), name=f"{tag}pj{tb}_{c}")
            S.op("dve", lambda e: e.tensor_scalar(
                    self.tmp[0][0:64, :], pb5[0:64, :], musq_col[0:64, 0:1],
                    None, ALU.subtract),
                 reads=(pb5_b, musq_col_b), writes=(self.tmp_b[0],),
                 name=f"{tag}pc{tb}")
            S.op("act", lambda e: e.activation(self.stg[0][0:64, :],
                                               self.tmp[0][0:64, :], AF.Square),
                 reads=(self.tmp_b[0],), writes=(self.stg_b[0],), name=f"{tag}ps{tb}")
            S.op("pe", lambda e: e.matmul(pb4[0:8, :], E8b[:], self.stg[0][0:64, :],
                                          start=True, stop=True),
                 reads=(E8b_b, self.stg_b[0]), writes=(pb4_b,), name=f"{tag}pg{tb}")
            S.op("pe", lambda e, sl=sl: e.matmul(
                    pb3[0:8, :], ones_row[0:1, 0:8], nsqr[0:1, sl],
                    start=True, stop=True),
                 reads=(ones_row_b, nsqr_b), writes=(pb3_b,), name=f"{tag}nb{tb}")
            S.op("dve", lambda e, sl=sl: e.scalar_tensor_tensor(
                    d2[0:8, sl], umu[0:8, sl], -2.0, pb3[0:8, :], ALU.mult, ALU.add),
                 reads=(umu_b, pb3_b), writes=(d2_b,), name=f"{tag}d2{tb}")
            S.op("dve", lambda e, sl=sl: e.tensor_scalar(
                    d2[0:8, sl], d2[0:8, sl], musq_col[0:8, 2:3], None, ALU.add),
                 reads=(d2_b, musq_col_b), writes=(d2_b,), name=f"{tag}d2b{tb}")
            S.op("dve", lambda e, sl=sl: e.reciprocal(d2[0:8, sl], d2[0:8, sl]),
                 reads=(d2_b,), writes=(d2_b,), name=f"{tag}rd{tb}")
            S.op("dve", lambda e, sl=sl: e.tensor_mul(
                    score[0:8, sl], pb4[0:8, :], d2[0:8, sl]),
                 reads=(pb4_b, d2_b), writes=(score_b,), name=f"{tag}sc{tb}")
        g8, g8_b = self.sr32(3, 8)
        S.op("act", lambda e: e.activation(g8[0:8, :], score[0:8, :], AF.Sigmoid,
                                           bias=slp[0:8, 6:7], scale=slp[0:8, 4:5]),
             reads=(score_b, slp_b), writes=(g8_b,), name=f"{tag}gate")
        S.op("dve", lambda e: e.tensor_scalar(g8[0:8, :], g8[0:8, :], pen8[:],
                                              None, ALU.subtract),
             reads=(g8_b, pen8_b), writes=(g8_b,), name=f"{tag}pen")
        GT = self.cols[:, 0:128]
        OH = self.cols[:, 128:256]
        SEL = self.cols[:, 256:384]
        GT_b, OH_b, SEL_b = self.cols_b[0], self.cols_b[1], self.cols_b[2]
        for tt in range(NTT):
            S.op("pe", lambda e, tt=tt: e.transpose(
                    pb4[0:128, 0:8], g8[0:8, tt * 128:(tt + 1) * 128],
                    identf[32:40, 32:40]),
                 reads=(g8_b, identf_b), writes=(pb4_b,), name=f"{tag}gt{tt}")
            S.op("dve", lambda e, tt=tt: e.tensor_copy(
                    GT[:, tt * 8:(tt + 1) * 8], pb4[0:128, 0:8]),
                 reads=(pb4_b,), writes=(GT_b,), name=f"{tag}gte{tt}")
            S.op("dve", lambda e, tt=tt: e.reduce_max(
                    SEL[:, tt:tt + 1], GT[:, tt * 8:(tt + 1) * 8], axis=AX.X),
                 reads=(GT_b,), writes=(SEL_b,), name=f"{tag}mx{tt}")
            S.op("dve", lambda e, tt=tt: e.tensor_scalar(
                    OH[:, tt * 8:(tt + 1) * 8], GT[:, tt * 8:(tt + 1) * 8],
                    SEL[:, tt:tt + 1], None, ALU.is_ge),
                 reads=(GT_b, SEL_b), writes=(OH_b,), name=f"{tag}oh{tt}")
        self.cols_to_row(SEL, SEL_b, 16, 2, tag + "sel")
        selr, selr_b = self.row(2)
        for tt in range(NTT):
            S.op("pe", lambda e, tt=tt: e.transpose(
                    pb4[0:8, 0:128], OH[:, tt * 8:(tt + 1) * 8], identf[:]),
                 reads=(OH_b, identf_b), writes=(pb4_b,), name=f"{tag}oht{tt}")
            S.op("act", lambda e, tt=tt: e.activation(
                    oh8bf[0:8, tt * 128:(tt + 1) * 128], pb4[0:8, 0:128], AF.Copy),
                 reads=(pb4_b,), writes=(oh8bf_b,), name=f"{tag}ohe{tt}")
        srow, srow_b = self.row(0)
        msrow, msrow_b = self.row(1)
        Prow, Prow_b = self.row(3)
        Qrow, Qrow_b = self.row(4)
        S.op("dve", lambda e: e.tensor_scalar(Prow, selr, -1.0, 1.0,
                                              ALU.mult, ALU.add),
             reads=(selr_b,), writes=(Prow_b,), name=f"{tag}p1")
        S.op("dve", lambda e: e.tensor_mul(Qrow, msrow, Prow),
             reads=(msrow_b, Prow_b), writes=(Qrow_b,), name=f"{tag}q1")
        S.op("dve", lambda e: e.tensor_scalar_mul(Qrow, Qrow, -1.0),
             reads=(Qrow_b,), writes=(Qrow_b,), name=f"{tag}q2")
        S.op("dve", lambda e: e.tensor_mul(Prow, srow, Prow),
             reads=(srow_b, Prow_b), writes=(Prow_b,), name=f"{tag}p2")
        S.op("dve", lambda e: e.tensor_scalar_add(Prow, Prow, 1.0),
             reads=(Prow_b,), writes=(Prow_b,), name=f"{tag}p3")

        for tb in range(NTB):
            sl = slice(tb * 512, (tb + 1) * 512)
            for c in range(NK):
                xh, xh_b = self.biga_slot(c)
                S.op("pe", lambda e, c=c, sl=sl, xh=xh: e.matmul(
                        pb5[0:64, :], wsl0[:, 1024 + c * 64:1024 + (c + 1) * 64],
                        xh[:, sl], start=(c == 0), stop=(c == NK - 1)),
                     reads=(wsl0_b, xh_b), writes=(pb5_b,), name=f"{tag}md{tb}_{c}")
            S.op("pe", lambda e, sl=sl: e.matmul(
                    pb4[0:64, :], P8b[:], oh8bf[0:8, sl], start=True, stop=True),
                 reads=(P8b_b, oh8bf_b), writes=(pb4_b,), name=f"{tag}ob{tb}")
            S.op("dve", lambda e: e.tensor_scalar(
                    self.tmp[0][0:64, :], pb5[0:64, :], musq_col[0:64, 1:2],
                    None, ALU.subtract),
                 reads=(pb5_b, musq_col_b), writes=(self.tmp_b[0],),
                 name=f"{tag}mz{tb}")
            S.op("dve", lambda e, sl=sl: e.tensor_mul(
                    z_sb[0:64, sl], self.tmp[0][0:64, :], pb4[0:64, :]),
                 reads=(self.tmp_b[0], pb4_b), writes=(z_sb_b,), name=f"{tag}zz{tb}")

        self.dump("l0f_sr", self.SR[0:80, :],
                  [umu_b, d2_b, score_b, g8_b, selr_b, Prow_b, Qrow_b,
                   srow_b, msrow_b, nsqr_b], (80, 10240), F32)
        self.dump("l0f_oh", oh8bf, [oh8bf_b], (128, 2048), BF16)
        self.dump("l0f_z", z_sb, [z_sb_b], (128, 2048), BF16)
        self.dump("l0f_cols", self.cols[:], [GT_b, OH_b, SEL_b], (128, 512), F32)
        self.dump("l0f_mc", musq_col[:], [musq_col_b], (128, 128), F32)
        for tb in range(NTB):
            sl = slice(tb * 512, (tb + 1) * 512)
            for ri, rowsrc in enumerate(((Prow, Prow_b), (Qrow, Qrow_b),
                                         (selr, selr_b))):
                ra, rb = rowsrc
                S.op("pe", lambda e, ra=ra, sl=sl: e.matmul(
                        pb2[:, :], ones_row[:], ra[0:1, sl], start=True, stop=True),
                     reads=(rb, ones_row_b), writes=(pb2_b,), name=f"{tag}fb{tb}_{ri}")
                S.op("dve", lambda e, ri=ri: e.tensor_copy(self.pqs[ri][:], pb2[:, :]),
                     reads=(pb2_b,), writes=(self.pqs_b[ri],), name=f"{tag}fe{tb}_{ri}")
            for oc in range(NK):
                bank, bank_b = (pb0, pb0_b) if oc % 2 == 0 else (pb1, pb1_b)
                S.op("pe", lambda e, oc=oc, sl=sl, bank=bank: e.matmul(
                        bank[:, :], dgn_sb[0:64, oc * 128:(oc + 1) * 128],
                        z_sb[0:64, sl], start=True, stop=False),
                     reads=(dgn_sb_b, z_sb_b), writes=(bank_b,), name=f"{tag}fm{tb}_{oc}")
                S.op("pe", lambda e, oc=oc, sl=sl, bank=bank: e.matmul(
                        bank[:, :], bw_sb[0:8, oc * 128:(oc + 1) * 128],
                        oh8bf[0:8, sl], start=False, stop=True),
                     reads=(bw_sb_b, oh8bf_b), writes=(bank_b,),
                     name=f"{tag}fn{tb}_{oc}")
                hw = (oc >> 1) & 1
                xs = self.fw[oc % 2][:, hw * 512:(hw + 1) * 512]
                xs_b = self.fw_b[oc % 2]
                S.dma("sp", xs, self.cur_ap[oc * 128:(oc + 1) * 128, sl],
                      reads=(self.cur_b[oc],), writes=(xs_b,), name=f"{tag}fx{tb}_{oc}")
                S.op("dve", lambda e, xs=xs: e.tensor_mul(
                        self.tmp[0][:, :], xs, self.pqs[0][:]),
                     reads=(xs_b, self.pqs_b[0]), writes=(self.tmp_b[0],),
                     name=f"{tag}f1{tb}_{oc}")
                S.op("dve", lambda e: e.tensor_add(
                        self.tmp[0][:, :], self.tmp[0][:, :], self.pqs[1][:]),
                     reads=(self.tmp_b[0], self.pqs_b[1]), writes=(self.tmp_b[0],),
                     name=f"{tag}f2{tb}_{oc}")
                S.op("dve", lambda e, bank=bank: e.tensor_mul(
                        self.tmp[1][:, :], bank[:, :], self.pqs[2][:]),
                     reads=(bank_b, self.pqs_b[2]), writes=(self.tmp_b[1],),
                     name=f"{tag}f3{tb}_{oc}")
                S.op("dve", lambda e, xs=xs: e.tensor_add(
                        xs, self.tmp[0][:, :], self.tmp[1][:, :]),
                     reads=(self.tmp_b[0], self.tmp_b[1]), writes=(xs_b,),
                     name=f"{tag}f4{tb}_{oc}")
                S.dma("sp", dst_ap[oc * 128:(oc + 1) * 128, sl], xs,
                      reads=(xs_b,), writes=(dst_b[oc],), name=f"{tag}fw{tb}_{oc}")
        self.advance_x(dst_ap, dst_b)

    def final_combine(self, tag, dst_ap, dst_b, Arow, Arow_b, Qrow, Qrow_b,
                      mlp_bank_fn):
        S = self.S
        ones_row, ones_row_b = self.sbconst("ones_row")
        pbc, pbc_b = self.pb[4], self.pb_b[4]
        for tb in range(NTB):
            sl = slice(tb * 512, (tb + 1) * 512)
            for ri, (ra, rb) in enumerate(((Arow, Arow_b), (Qrow, Qrow_b))):
                S.op("pe", lambda e, ra=ra, sl=sl: e.matmul(
                        pbc[:, :], ones_row[:], ra[0:1, sl], start=True, stop=True),
                     reads=(rb, ones_row_b), writes=(pbc_b,), name=f"{tag}cb{tb}_{ri}")
                S.op("dve", lambda e, ri=ri: e.tensor_copy(self.pqs[ri][:], pbc[:, :]),
                     reads=(pbc_b,), writes=(self.pqs_b[ri],), name=f"{tag}ce{tb}_{ri}")
            for oc in range(NK):
                bank, bank_b = mlp_bank_fn(tb, oc)
                hw = (oc >> 1) & 1
                xs = self.fw[oc % 2][:, hw * 512:(hw + 1) * 512]
                xs_b = self.fw_b[oc % 2]
                S.dma("sp", xs, self.cur_ap[oc * 128:(oc + 1) * 128, sl],
                      reads=(self.cur_b[oc],), writes=(xs_b,), name=f"{tag}cx{tb}_{oc}")
                S.op("dve", lambda e, xs=xs: e.tensor_mul(
                        self.tmp[0][:, :], xs, self.pqs[0][:]),
                     reads=(xs_b, self.pqs_b[0]), writes=(self.tmp_b[0],),
                     name=f"{tag}c1{tb}_{oc}")
                S.op("dve", lambda e: e.tensor_add(
                        self.tmp[0][:, :], self.tmp[0][:, :], self.pqs[1][:]),
                     reads=(self.tmp_b[0], self.pqs_b[1]), writes=(self.tmp_b[0],),
                     name=f"{tag}c2{tb}_{oc}")
                S.op("dve", lambda e, xs=xs, bank=bank: e.tensor_add(
                        xs, self.tmp[0][:, :], bank[:, :]),
                     reads=(self.tmp_b[0], bank_b), writes=(xs_b,),
                     name=f"{tag}c3{tb}_{oc}")
                S.dma("sp", dst_ap[oc * 128:(oc + 1) * 128, sl], xs,
                      reads=(xs_b,), writes=(dst_b[oc],), name=f"{tag}cw{tb}_{oc}")

    def slot_l1ffn(self, last):
        tag = "L1f"
        S = self.S
        dst_ap, dst_b = self.next_x(last)
        self.stats_pass(tag)
        self.derive_mu_r(tag)
        mu, mu_b = self.s4s(2)
        var, var_b = self.s4s(3)
        r, r_b = self.s4s(4)
        t5, t5_b = self.s4s(5)
        t6, t6_b = self.s4s(6)
        S.op("dve", lambda e: e.tensor_mul(t5, var, r), reads=(var_b, r_b),
             writes=(t5_b,), name=f"{tag}s1")
        S.op("dve", lambda e: e.tensor_mul(t5, t5, r), reads=(t5_b, r_b),
             writes=(t5_b,), name=f"{tag}s2")
        S.op("dve", lambda e: e.tensor_scalar_add(t5, t5, 1e-6), reads=(t5_b,),
             writes=(t5_b,), name=f"{tag}s3")
        S.op("act", lambda e: e.activation(t5, t5, AF.Sqrt), reads=(t5_b,),
             writes=(t5_b,), name=f"{tag}s4")
        S.op("dve", lambda e: e.reciprocal(t5, t5), reads=(t5_b,),
             writes=(t5_b,), name=f"{tag}s5")
        S.op("dve", lambda e: e.tensor_mul(t5, t5, r), reads=(t5_b, r_b),
             writes=(t5_b,), name=f"{tag}s6")
        S.op("dve", lambda e: e.tensor_mul(t6, mu, t5), reads=(mu_b, t5_b),
             writes=(t6_b,), name=f"{tag}s7")
        self.to_row(5, 0, tag)
        self.to_row(6, 1, tag)
        self.xhat_inplace(tag)
        g1, g1_b = self.sbconst("L1f_g1")
        g2, g2_b = self.sbconst("L1f_g2")
        for k in range(NK):
            xh, xh_b = self.biga_slot(k)
            S.op("act", lambda e, xh=xh, k=k: e.activation(
                    xh, xh, AF.Copy, scale=g1[:, k:k + 1]),
                 reads=(xh_b, g1_b), writes=(xh_b,), name=f"{tag}g1_{k}")

        qscale = float(128.0 ** -0.5)
        def evac_qkv(oc, tb, pa, pab):
            stj, stj_b = self.stg[tb % 2], self.stg_b[tb % 2]
            sc = qscale if oc < 16 else 1.0
            S.op("dve", lambda e, pa=pa, stj=stj, sc=sc: e.tensor_scalar_mul(
                    stj[:], pa[:, :], sc),
                 reads=(pab,), writes=(stj_b,), name=f"{tag}eq{oc}_{tb}")
            S.dma("sp", self.qkvT[oc * 128:(oc + 1) * 128, tb * 512:(tb + 1) * 512],
                  stj[:], reads=(stj_b,), writes=(self.qkvT_b,),
                  name=f"{tag}sq{oc}_{tb}")
        self.proj(tag + "q", "L1f_WqT", 0, 16, D, evac_qkv)
        self.proj(tag + "k", "L1f_WkT", 16, 32, D, evac_qkv, o_w_off=16)
        self.proj(tag + "v", "L1f_WvT", 32, 48, D, evac_qkv, o_w_off=32)

        self.dump("l1f_attnT_pre", self.attnT, [self.attnT_b], (D, T), BF16)
        A_ap, A_b = self.win("L1f_A")
        Asb, Asb_b = self.biga_slot(16)
        av = A_ap.rearrange("(c p) o -> p c o", p=128)
        S.dma("pool", Asb[:, 0:256].rearrange("p (c o) -> p c o", o=16), av,
              reads=(A_b,), writes=(Asb_b,), name=f"{tag}lA")
        Wo_ap, Wo_b = self.win("L1f_Wo")
        wov = Wo_ap.rearrange("(c p) o -> p c o", p=128)
        pW, pW_b = self.pb[4], self.pb_b[4]
        for f in range(16):
            j = f % 2
            wsl, wsl_b = self.wslab[j], self.wslab_b[j]
            S.dma("pool", wsl[:].rearrange("p (c o) -> p c o", o=128),
                  wov[:, :, f * 128:(f + 1) * 128],
                  reads=(Wo_b,), writes=(wsl_b,), name=f"{tag}w2l{f}")
            for o in range(16):
                S.op("pe", lambda e, o=o, wsl=wsl: e.matmul(
                        pW[0:128, 0:16], wsl[:, o * 128:(o + 1) * 128],
                        Asb[:, o * 16:(o + 1) * 16],
                        start=(o == 0), stop=(o == 15)),
                     reads=(wsl_b, Asb_b), writes=(pW_b,), name=f"{tag}w2m{f}_{o}")
            S.op("act", lambda e, f=f: e.activation(
                    self.W2sb[:, f * 16:(f + 1) * 16], pW[0:128, 0:16], AF.Copy),
                 reads=(pW_b,), writes=(self.W2sb_b,), name=f"{tag}w2e{f}")

        dmk, dmk_b = self.sbconst("diag_mask")
        dmkT, dmkT_b = self.sbconst("diag_maskT")
        identb, identb_b = self.sbconst("ident_bf")
        ones_row_bf, ones_row_bf_b = self.sbconst("ones_row_bf")

        def attn_head_ops(h):
            j = h % 2
            QT, QT_b = self.biga_slot(16 + j * 3 + 0)
            KT_, KT_b = self.biga_slot(16 + j * 3 + 1)
            VT, VT_b = self.biga_slot(16 + j * 3 + 2)
            for t_, tb_, row0 in ((QT, QT_b, h), (KT_, KT_b, 16 + h),
                                  (VT, VT_b, 32 + h)):
                S.dma("sp", t_, self.qkvT[row0 * 128:(row0 + 1) * 128, :],
                      reads=(self.qkvT_b,), writes=(tb_,), name=f"{tag}h{h}ld")
            psc, psc_b = (self.pb[0], self.pb_b[0]) if j == 0 else (self.pb[2], self.pb_b[2])
            pvx, pvx_b = (self.pb[1], self.pb_b[1]) if j == 0 else (self.pb[3], self.pb_b[3])
            pvt, pvt_b = self.pbf[j], self.pbf_b[j]
            fwj, fwj_b = self.fw[j], self.fw_b[j]
            negm = self.cols[:, j * 128 + 0:j * 128 + 16]
            Zc = self.cols[:, j * 128 + 16:j * 128 + 32]
            lnZ = self.cols[:, j * 128 + 32:j * 128 + 48]
            bcol = self.cols[:, j * 128 + 48:j * 128 + 64]
            colsb = self.cols_b[j]
            atn, atn_b = self.biga_slot(22 + j)
            for c in range(NTT):
                def vstep(c=c):
                    S.op("pe", lambda e, c=c: e.transpose(
                            pvt[:, 0:128], VT[:, c * 128:(c + 1) * 128], identb[:]),
                         reads=(VT_b, identb_b), writes=(pvt_b,), name=f"{tag}h{h}v{c}t")
                    S.op("dve", lambda e, c=c: e.tensor_copy(
                            VT[:, c * 128:(c + 1) * 128], pvt[:, 0:128]),
                         reads=(pvt_b,), writes=(VT_b,), name=f"{tag}h{h}v{c}e")
                yield vstep
            for qt in range(NTT):
                def astep(qt=qt):
                    L = (qt + 1) * 128
                    nb = qt // 4 + 1
                    for jb in range(nb):
                        w = 512 if jb < qt // 4 else (qt % 4 + 1) * 128
                        S.op("pe", lambda e, qt=qt, jb=jb, w=w: e.matmul(
                                psc[:, 0:w], QT[:, qt * 128:(qt + 1) * 128],
                                KT_[:, jb * 512:jb * 512 + w],
                                start=True, stop=True),
                             reads=(QT_b, KT_b), writes=(psc_b,),
                             name=f"{tag}h{h}a{qt}_{jb}")
                        if jb < qt // 4:
                            S.op("dve", lambda e, jb=jb, w=w: e.tensor_copy(
                                    fwj[:, jb * 512:jb * 512 + w], psc[:, 0:w]),
                                 reads=(psc_b,), writes=(fwj_b,),
                                 name=f"{tag}h{h}ae{qt}_{jb}")
                        else:
                            if w > 128:
                                S.op("dve", lambda e, jb=jb, w=w: e.tensor_copy(
                                        fwj[:, jb * 512:jb * 512 + w - 128],
                                        psc[:, 0:w - 128]),
                                     reads=(psc_b,), writes=(fwj_b,),
                                     name=f"{tag}h{h}ap{qt}_{jb}")
                            S.op("dve", lambda e, qt=qt, w=w: e.tensor_add(
                                    fwj[:, qt * 128:(qt + 1) * 128],
                                    psc[:, w - 128:w], dmk[:]),
                                 reads=(psc_b, dmk_b), writes=(fwj_b,),
                                 name=f"{tag}h{h}ad{qt}")
                    S.op("dve", lambda e, qt=qt, L=L: e.reduce_max(
                            negm[:, qt:qt + 1], fwj[:, 0:L], axis=AX.X, negate=True),
                         reads=(fwj_b,), writes=(colsb,), name=f"{tag}h{h}am{qt}")
                    S.op("act", lambda e, qt=qt, L=L: e.activation(
                            fwj[:, 0:L], fwj[:, 0:L], AF.Exp,
                            bias=negm[:, qt:qt + 1], accum_out=Zc[:, qt:qt + 1]),
                         reads=(fwj_b, colsb), writes=(fwj_b, colsb),
                         name=f"{tag}h{h}az{qt}")
                yield astep
            def brows():
                S.op("act", lambda e: e.activation(lnZ, Zc, AF.Ln),
                     reads=(colsb,), writes=(colsb,), name=f"{tag}h{h}ln")
                S.op("dve", lambda e: e.tensor_sub(bcol, negm, lnZ),
                     reads=(colsb,), writes=(colsb,), name=f"{tag}h{h}bc")
                self.cols_to_row(bcol, colsb, 16, 2 if j == 0 else 1, tag + f"h{h}")
                brow, brow_b = self.row(2 if j == 0 else 1)
                hilo = self.SR[0:1, (3 if j == 0 else 4) * 2048:
                               (4 if j == 0 else 5) * 2048].bitcast(BF16)
                hilo_b = self.rows_b[3 if j == 0 else 4]
                S.op("act", lambda e: e.activation(hilo[0:1, 0:2048], brow, AF.Copy),
                     reads=(brow_b,), writes=(hilo_b,), name=f"{tag}h{h}hi")
                S.op("dve", lambda e: e.tensor_sub(self.tmp[j][0:1, 0:512],
                                                   brow[0:1, 0:512],
                                                   hilo[0:1, 0:512]),
                     reads=(brow_b, hilo_b), writes=(self.tmp_b[j],),
                     name=f"{tag}h{h}lo0")
                for pc in range(4):
                    if pc > 0:
                        S.op("dve", lambda e, pc=pc: e.tensor_sub(
                                self.tmp[j][0:1, 0:512],
                                brow[0:1, pc * 512:(pc + 1) * 512],
                                hilo[0:1, pc * 512:(pc + 1) * 512]),
                             reads=(brow_b, hilo_b), writes=(self.tmp_b[j],),
                             name=f"{tag}h{h}lo{pc}")
                    S.op("act", lambda e, pc=pc: e.activation(
                            hilo[0:1, 2048 + pc * 512:2048 + (pc + 1) * 512],
                            self.tmp[j][0:1, 0:512], AF.Copy),
                         reads=(self.tmp_b[j],), writes=(hilo_b,),
                         name=f"{tag}h{h}loc{pc}")
            yield brows
            hilo = self.SR[0:1, (3 if j == 0 else 4) * 2048:
                           (4 if j == 0 else 5) * 2048].bitcast(BF16)
            hilo_b = self.rows_b[3 if j == 0 else 4]
            for jb in range(4):
                for c in range(4 * jb + 4):
                    def bstep(jb=jb, c=c):
                        qsl = slice(jb * 512, (jb + 1) * 512)
                        S.op("pe", lambda e, c=c, qsl=qsl: e.matmul(
                                psc[:, 0:512], KT_[:, c * 128:(c + 1) * 128],
                                QT[:, qsl], start=True, stop=False),
                             reads=(QT_b, KT_b), writes=(psc_b,),
                             name=f"{tag}h{h}b{jb}_{c}")
                        S.op("pe", lambda e, jb=jb: e.matmul(
                                psc[:, 0:512], ones_row_bf[:],
                                hilo[0:1, jb * 512:(jb + 1) * 512],
                                start=False, stop=False),
                             reads=(hilo_b, ones_row_bf_b), writes=(psc_b,),
                             name=f"{tag}h{h}bh{jb}_{c}")
                        S.op("pe", lambda e, jb=jb: e.matmul(
                                psc[:, 0:512], ones_row_bf[:],
                                hilo[0:1, 2048 + jb * 512:2048 + (jb + 1) * 512],
                                start=False, stop=True),
                             reads=(hilo_b, ones_row_bf_b), writes=(psc_b,),
                             name=f"{tag}h{h}bl{jb}_{c}")
                        d = c - 4 * jb
                        pT, pT_b = self.stg[2 * j + c % 2], self.stg_b[2 * j + c % 2]
                        if d >= 0:
                            S.op("dve", lambda e, d=d: e.tensor_add(
                                    psc[:, d * 128:(d + 1) * 128],
                                    psc[:, d * 128:(d + 1) * 128], dmkT[:]),
                                 reads=(psc_b, dmkT_b), writes=(psc_b,),
                                 name=f"{tag}h{h}bm{jb}_{c}")
                            if d > 0:
                                S.op("dve", lambda e, d=d, pT=pT: e.memset(
                                        pT[:, 0:d * 128], 0.0),
                                     writes=(pT_b,), name=f"{tag}h{h}bz{jb}_{c}")
                            S.op("act", lambda e, d=d, pT=pT: e.activation(
                                    pT[:, d * 128:512], psc[:, d * 128:512], AF.Exp),
                                 reads=(psc_b,), writes=(pT_b,),
                                 name=f"{tag}h{h}be{jb}_{c}")
                        else:
                            S.op("act", lambda e, pT=pT: e.activation(
                                    pT[:], psc[:, 0:512], AF.Exp),
                                 reads=(psc_b,), writes=(pT_b,),
                                 name=f"{tag}h{h}be{jb}_{c}")
                        S.op("pe", lambda e, c=c, jb=jb, pT=pT: e.matmul(
                                pvx[:, 0:512], VT[:, c * 128:(c + 1) * 128], pT[:],
                                start=(c == 0), stop=(c == 4 * jb + 3)),
                             reads=(VT_b, pT_b), writes=(pvx_b,),
                             name=f"{tag}h{h}bp{jb}_{c}")
                    yield bstep
                def oevac(jb=jb):
                    S.op("dve", lambda e, jb=jb: e.tensor_copy(
                            atn[:, jb * 512:(jb + 1) * 512], pvx[:, 0:512]),
                         reads=(pvx_b,), writes=(atn_b,), name=f"{tag}h{h}oe{jb}")
                yield oevac
            def final():
                S.dma("sp", self.attnT[h * 128:(h + 1) * 128, :], atn,
                      reads=(atn_b,), writes=(self.attnT_b,), name=f"{tag}h{h}st")
            yield final

        def run_pipeline(nheads, prime):
            nxt = [0]
            active = []
            def start():
                if nxt[0] < nheads:
                    active.append(attn_head_ops(nxt[0]))
                    nxt[0] += 1
            start()
            for _ in range(prime):
                try:
                    next(active[0])()
                except StopIteration:
                    break
            start()
            while active:
                for g in list(active):
                    try:
                        next(g)()
                    except StopIteration:
                        active.remove(g)
                        start()
        run_pipeline(16, 2 * NTT)

        rat, rat_b = self.small[3], self.small_b[3]
        S.op("dve", lambda e: e.reciprocal(rat[:, 0:16], g1[:]),
             reads=(g1_b,), writes=(rat_b,), name=f"{tag}rt1")
        S.op("dve", lambda e: e.tensor_mul(rat[:, 0:16], rat[:, 0:16], g2[:]),
             reads=(rat_b, g2_b), writes=(rat_b,), name=f"{tag}rt2")
        for k in range(NK):
            xh, xh_b = self.biga_slot(k)
            S.op("act", lambda e, xh=xh, k=k: e.activation(
                    xh, xh, AF.Copy, scale=rat[:, k:k + 1]),
                 reads=(xh_b, rat_b), writes=(xh_b,), name=f"{tag}g2_{k}")

        Asb, Asb_b = self.biga_slot(16)
        S.dma("pool", Asb[:, 0:256].rearrange("p (c o) -> p c o", o=16), av,
              reads=(A_b,), writes=(Asb_b,), name=f"{tag}lA2")
        modt, modt_b = self.sr32(0, 16)
        hAt, hAt_b = self.sr32(1, 16)
        pW4, pW4_b = self.pb[4], self.pb_b[4]
        pW5, pW5_b = self.pb[5], self.pb_b[5]
        for tb in range(NTB):
            sl = slice(tb * 512, (tb + 1) * 512)
            for c in range(NK):
                at, at_b2 = self.biga_slot(17 + (c % 2) * 3)
                S.dma("sp", at[:, 0:512], self.attnT[c * 128:(c + 1) * 128, sl],
                      reads=(self.attnT_b,), writes=(at_b2,), name=f"{tag}ml{tb}_{c}")
                S.op("pe", lambda e, c=c, at=at: e.matmul(
                        pW4[0:16, 0:512], self.W2sb[:, c * 16:(c + 1) * 16],
                        at[:, 0:512], start=(c == 0), stop=(c == NK - 1)),
                     reads=(self.W2sb_b, at_b2), writes=(pW4_b,),
                     name=f"{tag}mm{tb}_{c}")
                xh, xh_b = self.biga_slot(c)
                S.op("pe", lambda e, c=c, sl=sl, xh=xh: e.matmul(
                        pW5[0:16, 0:512], Asb[:, c * 16:(c + 1) * 16], xh[:, sl],
                        start=(c == 0), stop=(c == NK - 1)),
                     reads=(Asb_b, xh_b), writes=(pW5_b,), name=f"{tag}ha{tb}_{c}")
            S.op("act", lambda e, sl=sl: e.activation(
                    modt[0:16, sl], pW4[0:16, 0:512], AF.Tanh),
                 reads=(pW4_b,), writes=(modt_b,), name=f"{tag}mt{tb}")
            S.op("dve", lambda e, sl=sl: e.tensor_copy(hAt[0:16, sl], pW5[0:16, 0:512]),
                 reads=(pW5_b,), writes=(hAt_b,), name=f"{tag}he{tb}")
        mB, mB_b = self.biga_slot(23)
        S.op("dve", lambda e: e.tensor_scalar_mul(modt[0:16, :], modt[0:16, :], 0.1),
             reads=(modt_b,), writes=(modt_b,), name=f"{tag}msc")
        S.op("dve", lambda e: e.tensor_mul(mB[0:16, :], modt[0:16, :], hAt[0:16, :]),
             reads=(modt_b, hAt_b), writes=(mB_b,), name=f"{tag}mb")

        self.dump("l1f_mB", mB, [mB_b], (128, 2048), BF16)
        self.dump("l1f_mod", self.SR[0:80, :], [modt_b, hAt_b], (80, 10240), F32)
        WinT, WinT_b = self.win("L1f_WinT")
        WupT, WupT_b = self.win("L1f_WupT")
        B_ap, B_b = self.win("L1f_B")
        winv = WinT.rearrange("(c p) o -> p c o", p=128)
        wupv = WupT.rearrange("(c p) o -> p c o", p=128)
        for oc in range(64):
            wsl0, wsl0_b = self.wslab[0], self.wslab_b[0]
            wsl1, wsl1_b = self.wslab[1], self.wslab_b[1]
            S.dma("pool", wsl0[:].rearrange("p (c o) -> p c o", o=128),
                  winv[:, :, oc * 128:(oc + 1) * 128],
                  reads=(WinT_b,), writes=(wsl0_b,), name=f"{tag}gl{oc}")
            S.dma("pool", wsl1[:].rearrange("p (c o) -> p c o", o=128),
                  wupv[:, :, oc * 128:(oc + 1) * 128],
                  reads=(WupT_b,), writes=(wsl1_b,), name=f"{tag}ul{oc}")
            Bs, Bs_b = self.smallb[6 + oc % 2], self.smallb_b[6 + oc % 2]
            S.dma("pool", Bs[0:16, :], B_ap[:, oc * 128:(oc + 1) * 128],
                  reads=(B_b,), writes=(Bs_b,), name=f"{tag}bl{oc}")
            for half in range(2):
                gbanks = [(self.pb[0], self.pb_b[0]), (self.pb[1], self.pb_b[1])]
                ubanks = [(self.pb[2], self.pb_b[2]), (self.pb[3], self.pb_b[3])]
                for k in range(NK):
                    for t2 in range(2):
                        tb = half * 2 + t2
                        sl = slice(tb * 512, (tb + 1) * 512)
                        xh, xh_b = self.biga_slot(k)
                        gb, gb_b = gbanks[t2]
                        ub, ub_b = ubanks[t2]
                        S.op("pe", lambda e, k=k, sl=sl, xh=xh, gb=gb: e.matmul(
                                gb[:, :], wsl0[:, k * 128:(k + 1) * 128], xh[:, sl],
                                start=(k == 0), stop=False),
                             reads=(wsl0_b, xh_b), writes=(gb_b,),
                             name=f"{tag}gm{oc}_{tb}_{k}")
                        S.op("pe", lambda e, k=k, sl=sl, xh=xh, ub=ub: e.matmul(
                                ub[:, :], wsl1[:, k * 128:(k + 1) * 128], xh[:, sl],
                                start=(k == 0), stop=(k == NK - 1)),
                             reads=(wsl1_b, xh_b), writes=(ub_b,),
                             name=f"{tag}um{oc}_{tb}_{k}")
                for t2 in range(2):
                    tb = half * 2 + t2
                    sl = slice(tb * 512, (tb + 1) * 512)
                    gb, gb_b = gbanks[t2]
                    ub, ub_b = ubanks[t2]
                    S.op("pe", lambda e, sl=sl, gb=gb, Bs=Bs: e.matmul(
                            gb[:, :], Bs[0:16, :], mB[0:16, sl],
                            start=False, stop=True),
                         reads=(Bs_b, mB_b), writes=(gb_b,), name=f"{tag}gb{oc}_{tb}")
                    sg, sg_b = self.stg[t2], self.stg_b[t2]
                    hd, hd_b = self.stg[2 + t2], self.stg_b[2 + t2]
                    S.op("act", lambda e, gb=gb, sg=sg: e.activation(
                            sg[:], gb[:, :], AF.Silu),
                         reads=(gb_b,), writes=(sg_b,), name=f"{tag}si{oc}_{tb}")
                    S.op("dve", lambda e, ub=ub, sg=sg, hd=hd: e.tensor_mul(
                            hd[:], ub[:, :], sg[:]),
                         reads=(ub_b, sg_b), writes=(hd_b,), name=f"{tag}hd{oc}_{tb}")
                    S.dma("sp", self.hidT[oc * 128:(oc + 1) * 128, sl], hd[:],
                          reads=(hd_b,), writes=(self.hidT_b,),
                          name=f"{tag}hs{oc}_{tb}")

        S.op("dve", lambda e: e.tensor_mul(t6, mu, r), reads=(mu_b, r_b),
             writes=(t6_b,), name=f"{tag}qf1")
        S.op("dve", lambda e: e.tensor_scalar_mul(t6, t6, -1.0), reads=(t6_b,),
             writes=(t6_b,), name=f"{tag}qf2")
        self.to_row(4, 0, tag + "A")
        self.to_row(6, 1, tag + "Q")
        Arow, Arow_b = self.row(0)
        Qrow, Qrow_b = self.row(1)
        S.op("dve", lambda e: e.tensor_scalar_add(Arow, Arow, 1.0),
             reads=(Arow_b,), writes=(Arow_b,), name=f"{tag}A1")
        WdnT, WdnT_b = self.win("L1f_WdnT")
        wdv = WdnT.rearrange("(c p) o -> p c o", p=128)

        def hid_chunk(k):
            sslot, sb = self.biga_slot(k // 4)
            return sslot[:, (k % 4) * 512:(k % 4 + 1) * 512], sb

        def mlp_bank(tb, oc):
            sl = slice(tb * 512, (tb + 1) * 512)
            if oc == 0:
                for sslot in range(NK):
                    ba, bb = self.biga_slot(sslot)
                    src = self.hidT.rearrange("(c p) t -> p c t", p=128)[
                        :, sslot * 4:(sslot + 1) * 4, sl]
                    S.dma("sp", ba.rearrange("p (c t) -> p c t", t=512)[:, 0:4, :],
                          src, reads=(self.hidT_b,), writes=(bb,),
                          name=f"{tag}dl{tb}_{sslot}")
            bank, bank_b = (self.pb[0], self.pb_b[0]) if oc % 2 == 0 else \
                           (self.pb[1], self.pb_b[1])
            for si in range(4):
                j2 = (tb * 64 + oc * 4 + si) % 2
                wsl, wsl_b = self.wslab[j2], self.wslab_b[j2]
                S.dma("pool", wsl[:].rearrange("p (c o) -> p c o", o=128),
                      wdv[:, si * 16:(si + 1) * 16, oc * 128:(oc + 1) * 128],
                      reads=(WdnT_b,), writes=(wsl_b,), name=f"{tag}dw{tb}_{oc}_{si}")
                for kk in range(16):
                    k = si * 16 + kk
                    ha, hb = hid_chunk(k)
                    S.op("pe", lambda e, kk=kk, k=k, ha=ha, wsl=wsl, bank=bank:
                         e.matmul(bank[:, :], wsl[:, kk * 128:(kk + 1) * 128], ha,
                                  start=(k == 0), stop=(k == 63)),
                         reads=(wsl_b, hb), writes=(bank_b,),
                         name=f"{tag}dm{tb}_{oc}_{k}")
            return bank, bank_b

        self.final_combine(tag, dst_ap, dst_b, Arow, Arow_b, Qrow, Qrow_b, mlp_bank)
        self.advance_x(dst_ap, dst_b)


    def rope_evac(self, tag, oc, tb, pa, pab, cosv, cosv_b, sinv, sinv_b,
                  rotP, rotP_b, scale, dst_rows):
        S = self.S
        sl = slice(tb * 512, (tb + 1) * 512)
        q_sb, q_sb_b = self.stg[0 + tb % 2], self.stg_b[0 + tb % 2]
        S.op("dve", lambda e: e.tensor_scalar_mul(q_sb[:], pa[:, :], scale),
             reads=(pab,), writes=(q_sb_b,), name=f"{tag}re{oc}_{tb}")
        prp, prp_b = self.pb[4 + tb % 2], self.pb_b[4 + tb % 2]
        S.op("pe", lambda e: e.matmul(prp[:, 0:512], rotP[:], q_sb[:],
                                      start=True, stop=True),
             reads=(rotP_b, q_sb_b), writes=(prp_b,), name=f"{tag}rp{oc}_{tb}")
        t0, t0_b = self.tmp[tb % 2], self.tmp_b[tb % 2]
        S.op("dve", lambda e: e.tensor_mul(t0[:], q_sb[:], cosv[:, sl]),
             reads=(q_sb_b, cosv_b), writes=(t0_b,), name=f"{tag}rc{oc}_{tb}")
        hd, hd_b = self.stg[2 + tb % 2], self.stg_b[2 + tb % 2]
        S.op("dve", lambda e: e.tensor_mul(hd[:], prp[:, 0:512], sinv[:, sl]),
             reads=(prp_b, sinv_b), writes=(hd_b,), name=f"{tag}rs{oc}_{tb}")
        S.op("dve", lambda e: e.tensor_add(hd[:], t0[:], hd[:]),
             reads=(t0_b, hd_b), writes=(hd_b,), name=f"{tag}ra{oc}_{tb}")
        S.dma("sp", self.qkvT[dst_rows * 128:(dst_rows + 1) * 128, sl], hd[:],
              reads=(hd_b,), writes=(self.qkvT_b,), name=f"{tag}rw{oc}_{tb}")

    def slot_l2attn(self, last):
        tag = "L2a"
        S = self.S
        dst_ap, dst_b = self.next_x(last)
        self.stats_pass(tag)
        self.make_s_ms_rows(tag, "ln")
        self.xhat_inplace(tag)
        cosv = self.biga[:, 16 * 2048:18 * 2048].bitcast(F32)
        sinv = self.biga[:, 18 * 2048:20 * 2048].bitcast(F32)
        cosv_b, sinv_b = self.biga_b[16], self.biga_b[18]
        lc, lc_b = self.win("l2_cos")
        ls, ls_b = self.win("l2_sin")
        S.dma("sp", cosv, lc, reads=(lc_b,), writes=(cosv_b,), name=f"{tag}lc")
        S.dma("sp", sinv, ls, reads=(ls_b,), writes=(sinv_b,), name=f"{tag}ls")
        rotP, rotP_b = self.sbconst("l2_rotP")
        qsc = float(128.0 ** -0.5)

        def evac_q(oc, tb, pa, pab):
            self.rope_evac(tag, oc, tb, pa, pab, cosv, cosv_b, sinv, sinv_b,
                           rotP, rotP_b, qsc, oc)
        self.proj(tag + "q", "L2a_Wq", 0, 16, D, evac_q)

        def evac_kv(oc, tb, pa, pab):
            if oc == 16:
                self.rope_evac(tag, oc, tb, pa, pab, cosv, cosv_b, sinv, sinv_b,
                               rotP, rotP_b, 1.0, 16)
            else:
                stj, stj_b = self.stg[tb % 2], self.stg_b[tb % 2]
                S.op("dve", lambda e, pa=pa, stj=stj: e.tensor_copy(stj[:], pa[:, :]),
                     reads=(pab,), writes=(stj_b,), name=f"{tag}ev{oc}_{tb}")
                S.dma("sp", self.qkvT[17 * 128:18 * 128,
                                      tb * 512:(tb + 1) * 512], stj[:],
                      reads=(stj_b,), writes=(self.qkvT_b,), name=f"{tag}sv{oc}_{tb}")
        self.proj(tag + "kv", "L2a_Wkv", 16, 18, D, evac_kv, o_w_off=16)

        dmk, dmk_b = self.sbconst("diag_mask")
        dmkT, dmkT_b = self.sbconst("diag_maskT")
        identb, identb_b = self.sbconst("ident_bf")
        KT_, KT_b = self.biga_slot(20)
        VT, VT_b = self.biga_slot(21)
        S.dma("sp", KT_, self.qkvT[16 * 128:17 * 128, :],
              reads=(self.qkvT_b,), writes=(KT_b,), name=f"{tag}ldk")
        S.dma("sp", VT, self.qkvT[17 * 128:18 * 128, :],
              reads=(self.qkvT_b,), writes=(VT_b,), name=f"{tag}ldv")
        pvt0, pvt0_b = self.pbf[0], self.pbf_b[0]
        for c in range(NTT):
            S.op("pe", lambda e, c=c: e.transpose(
                    pvt0[:, 0:128], VT[:, c * 128:(c + 1) * 128], identb[:]),
                 reads=(VT_b, identb_b), writes=(pvt0_b,), name=f"{tag}v{c}t")
            S.op("dve", lambda e, c=c: e.tensor_copy(
                    VT[:, c * 128:(c + 1) * 128], pvt0[:, 0:128]),
                 reads=(pvt0_b,), writes=(VT_b,), name=f"{tag}v{c}e")

        def attn_head_ops(h):
            j = h % 2
            QT, QT_b = self.biga_slot(16 + j)
            S.dma("sp", QT, self.qkvT[h * 128:(h + 1) * 128, :],
                  reads=(self.qkvT_b,), writes=(self.biga_b[16 + j],),
                  name=f"{tag}h{h}ld")
            QT_b = self.biga_b[16 + j]
            psc, psc_b = (self.pb[0], self.pb_b[0]) if j == 0 else (self.pb[2], self.pb_b[2])
            pvx, pvx_b = (self.pb[1], self.pb_b[1]) if j == 0 else (self.pb[3], self.pb_b[3])
            fwj, fwj_b = self.fw[j], self.fw_b[j]
            negm = self.cols[:, j * 128 + 0:j * 128 + 16]
            Zc = self.cols[:, j * 128 + 16:j * 128 + 32]
            lnZ = self.cols[:, j * 128 + 32:j * 128 + 48]
            bcol = self.cols[:, j * 128 + 48:j * 128 + 64]
            colsb = self.cols_b[j]
            atn, atn_b = self.biga_slot(22 + j)
            for qt in range(NTT):
                def astep(qt=qt):
                    L = (qt + 1) * 128
                    nb = qt // 4 + 1
                    for jb in range(nb):
                        w = 512 if jb < qt // 4 else (qt % 4 + 1) * 128
                        S.op("pe", lambda e, qt=qt, jb=jb, w=w: e.matmul(
                                psc[:, 0:w], QT[:, qt * 128:(qt + 1) * 128],
                                KT_[:, jb * 512:jb * 512 + w],
                                start=True, stop=True),
                             reads=(QT_b, KT_b), writes=(psc_b,),
                             name=f"{tag}h{h}a{qt}_{jb}")
                        if jb < qt // 4:
                            S.op("dve", lambda e, jb=jb, w=w: e.tensor_copy(
                                    fwj[:, jb * 512:jb * 512 + w], psc[:, 0:w]),
                                 reads=(psc_b,), writes=(fwj_b,),
                                 name=f"{tag}h{h}ae{qt}_{jb}")
                        else:
                            if w > 128:
                                S.op("dve", lambda e, jb=jb, w=w: e.tensor_copy(
                                        fwj[:, jb * 512:jb * 512 + w - 128],
                                        psc[:, 0:w - 128]),
                                     reads=(psc_b,), writes=(fwj_b,),
                                     name=f"{tag}h{h}ap{qt}_{jb}")
                            S.op("dve", lambda e, qt=qt, w=w: e.tensor_add(
                                    fwj[:, qt * 128:(qt + 1) * 128],
                                    psc[:, w - 128:w], dmk[:]),
                                 reads=(psc_b, dmk_b), writes=(fwj_b,),
                                 name=f"{tag}h{h}ad{qt}")
                    S.op("dve", lambda e, qt=qt, L=L: e.reduce_max(
                            negm[:, qt:qt + 1], fwj[:, 0:L], axis=AX.X, negate=True),
                         reads=(fwj_b,), writes=(colsb,), name=f"{tag}h{h}am{qt}")
                    S.op("act", lambda e, qt=qt, L=L: e.activation(
                            fwj[:, 0:L], fwj[:, 0:L], AF.Exp,
                            bias=negm[:, qt:qt + 1], accum_out=Zc[:, qt:qt + 1]),
                         reads=(fwj_b, colsb), writes=(fwj_b, colsb),
                         name=f"{tag}h{h}az{qt}")
                yield astep
            def brows():
                S.op("act", lambda e: e.activation(lnZ, Zc, AF.Ln),
                     reads=(colsb,), writes=(colsb,), name=f"{tag}h{h}ln")
                S.op("dve", lambda e: e.tensor_sub(bcol, negm, lnZ),
                     reads=(colsb,), writes=(colsb,), name=f"{tag}h{h}bc")
                self.cols_to_row(bcol, colsb, 16, 2 if j == 0 else 1, tag + f"h{h}")
                brow, brow_b = self.row(2 if j == 0 else 1)
                hilo = self.SR[0:1, (3 if j == 0 else 4) * 2048:
                               (4 if j == 0 else 5) * 2048].bitcast(BF16)
                hilo_b = self.rows_b[3 if j == 0 else 4]
                S.op("act", lambda e: e.activation(hilo[0:1, 0:2048], brow, AF.Copy),
                     reads=(brow_b,), writes=(hilo_b,), name=f"{tag}h{h}hi")
                for pc in range(4):
                    S.op("dve", lambda e, pc=pc: e.tensor_sub(
                            self.tmp[j][0:1, 0:512],
                            brow[0:1, pc * 512:(pc + 1) * 512],
                            hilo[0:1, pc * 512:(pc + 1) * 512]),
                         reads=(brow_b, hilo_b), writes=(self.tmp_b[j],),
                         name=f"{tag}h{h}lo{pc}")
                    S.op("act", lambda e, pc=pc: e.activation(
                            hilo[0:1, 2048 + pc * 512:2048 + (pc + 1) * 512],
                            self.tmp[j][0:1, 0:512], AF.Copy),
                         reads=(self.tmp_b[j],), writes=(hilo_b,),
                         name=f"{tag}h{h}loc{pc}")
            yield brows
            hilo = self.SR[0:1, (3 if j == 0 else 4) * 2048:
                           (4 if j == 0 else 5) * 2048].bitcast(BF16)
            hilo_b = self.rows_b[3 if j == 0 else 4]
            ones_row_bf, ones_row_bf_b = self.sbconst("ones_row_bf")
            for jb in range(4):
                for c in range(4 * jb + 4):
                    def bstep(jb=jb, c=c):
                        qsl = slice(jb * 512, (jb + 1) * 512)
                        S.op("pe", lambda e, c=c, qsl=qsl: e.matmul(
                                psc[:, 0:512], KT_[:, c * 128:(c + 1) * 128],
                                QT[:, qsl], start=True, stop=False),
                             reads=(QT_b, KT_b), writes=(psc_b,),
                             name=f"{tag}h{h}b{jb}_{c}")
                        S.op("pe", lambda e, jb=jb: e.matmul(
                                psc[:, 0:512], ones_row_bf[:],
                                hilo[0:1, jb * 512:(jb + 1) * 512],
                                start=False, stop=False),
                             reads=(hilo_b, ones_row_bf_b), writes=(psc_b,),
                             name=f"{tag}h{h}bh{jb}_{c}")
                        S.op("pe", lambda e, jb=jb: e.matmul(
                                psc[:, 0:512], ones_row_bf[:],
                                hilo[0:1, 2048 + jb * 512:2048 + (jb + 1) * 512],
                                start=False, stop=True),
                             reads=(hilo_b, ones_row_bf_b), writes=(psc_b,),
                             name=f"{tag}h{h}bl{jb}_{c}")
                        d = c - 4 * jb
                        pT, pT_b = self.stg[2 * j + c % 2], self.stg_b[2 * j + c % 2]
                        if d >= 0:
                            S.op("dve", lambda e, d=d: e.tensor_add(
                                    psc[:, d * 128:(d + 1) * 128],
                                    psc[:, d * 128:(d + 1) * 128], dmkT[:]),
                                 reads=(psc_b, dmkT_b), writes=(psc_b,),
                                 name=f"{tag}h{h}bm{jb}_{c}")
                            if d > 0:
                                S.op("dve", lambda e, d=d, pT=pT: e.memset(
                                        pT[:, 0:d * 128], 0.0),
                                     writes=(pT_b,), name=f"{tag}h{h}bz{jb}_{c}")
                            S.op("act", lambda e, d=d, pT=pT: e.activation(
                                    pT[:, d * 128:512], psc[:, d * 128:512], AF.Exp),
                                 reads=(psc_b,), writes=(pT_b,),
                                 name=f"{tag}h{h}be{jb}_{c}")
                        else:
                            S.op("act", lambda e, pT=pT: e.activation(
                                    pT[:], psc[:, 0:512], AF.Exp),
                                 reads=(psc_b,), writes=(pT_b,),
                                 name=f"{tag}h{h}be{jb}_{c}")
                        S.op("pe", lambda e, c=c, jb=jb, pT=pT: e.matmul(
                                pvx[:, 0:512], VT[:, c * 128:(c + 1) * 128], pT[:],
                                start=(c == 0), stop=(c == 4 * jb + 3)),
                             reads=(VT_b, pT_b), writes=(pvx_b,),
                             name=f"{tag}h{h}bp{jb}_{c}")
                    yield bstep
                def oevac(jb=jb):
                    S.op("dve", lambda e, jb=jb: e.tensor_copy(
                            atn[:, jb * 512:(jb + 1) * 512], pvx[:, 0:512]),
                         reads=(pvx_b,), writes=(atn_b,), name=f"{tag}h{h}oe{jb}")
                yield oevac
            def final():
                S.dma("sp", self.attnT[h * 128:(h + 1) * 128, :], atn,
                      reads=(atn_b,), writes=(self.attnT_b,), name=f"{tag}h{h}st")
            yield final

        def run_pipeline(nheads, prime):
            nxt = [0]
            active = []
            def start():
                if nxt[0] < nheads:
                    active.append(attn_head_ops(nxt[0]))
                    nxt[0] += 1
            start()
            for _ in range(prime):
                try:
                    next(active[0])()
                except StopIteration:
                    break
            start()
            while active:
                for g in list(active):
                    try:
                        next(g)()
                    except StopIteration:
                        active.remove(g)
                        start()
        run_pipeline(16, 2 * NTT)

        for k in range(NK):
            ap, b2 = self.biga_slot(k)
            S.dma("sp", ap, self.attnT[k * 128:(k + 1) * 128, :],
                  reads=(self.attnT_b,), writes=(b2,), name=f"{tag}lda{k}")
        xloaded = {}
        def evac_wout(oc, tb, pa, pab):
            if oc not in xloaded:
                xs, xs_b = self.fw[oc % 2], self.fw_b[oc % 2]
                S.dma("sp", xs, self.cur_ap[oc * 128:(oc + 1) * 128, :],
                      reads=(self.cur_b[oc],), writes=(xs_b,), name=f"{tag}rx{oc}")
                xloaded[oc] = (xs, xs_b)
            xs, xs_b = xloaded[oc]
            sl = slice(tb * 512, (tb + 1) * 512)
            S.op("dve", lambda e, pa=pa, sl=sl, xs=xs: e.tensor_add(
                    xs[:, sl], pa[:, :], xs[:, sl]),
                 reads=(pab, xs_b), writes=(xs_b,), name=f"{tag}ro{oc}_{tb}")
            if tb == 3:
                S.dma("sp", dst_ap[oc * 128:(oc + 1) * 128, :], xs,
                      reads=(xs_b,), writes=(dst_b[oc],), name=f"{tag}wx{oc}")
        self.proj(tag + "wo", "L2a_Wout", 0, 16, D, evac_wout)
        self.advance_x(dst_ap, dst_b)


    def make_rms_rows(self, tag, eps):
        S = self.S
        self.derive_mu_r(tag)
        mu, mu_b = self.s4s(2)
        var, var_b = self.s4s(3)
        r, r_b = self.s4s(4)
        t5, t5_b = self.s4s(5)
        t6, t6_b = self.s4s(6)
        S.op("dve", lambda e: e.tensor_mul(t5, var, r), reads=(var_b, r_b),
             writes=(t5_b,), name=f"{tag}s1")
        S.op("dve", lambda e: e.tensor_mul(t5, t5, r), reads=(t5_b, r_b),
             writes=(t5_b,), name=f"{tag}s2")
        S.op("dve", lambda e, eps=eps: e.tensor_scalar_add(t5, t5, eps),
             reads=(t5_b,), writes=(t5_b,), name=f"{tag}s3")
        S.op("act", lambda e: e.activation(t5, t5, AF.Sqrt), reads=(t5_b,),
             writes=(t5_b,), name=f"{tag}s4")
        S.op("dve", lambda e: e.reciprocal(t5, t5), reads=(t5_b,),
             writes=(t5_b,), name=f"{tag}s5")
        S.op("dve", lambda e: e.tensor_mul(t5, t5, r), reads=(t5_b, r_b),
             writes=(t5_b,), name=f"{tag}s6")
        S.op("dve", lambda e: e.tensor_mul(t6, mu, t5), reads=(mu_b, t5_b),
             writes=(t6_b,), name=f"{tag}s7")
        self.to_row(5, 0, tag)
        self.to_row(6, 1, tag)

    def bval_col(self, tag, name, dstcol, dstcol_b, fn=None):
        S = self.S
        ap, b2 = self.win(name)
        ones_row, ones_row_b = self.sbconst("ones_row")
        sc = self.small[0]
        sc_b = self.small_b[0]
        S.dma("sp", sc[0:1, 12:13], ap, reads=(b2,), writes=(sc_b,),
              name=f"{tag}bv{name}")
        if fn is not None:
            fn(sc, sc_b)
        pbc, pbc_b = self.pb[4], self.pb_b[4]
        S.op("pe", lambda e: e.matmul(pbc[:, 0:1], ones_row[:], sc[0:1, 12:13],
                                      start=True, stop=True),
             reads=(sc_b, ones_row_b), writes=(pbc_b,), name=f"{tag}bm{name}")
        S.op("dve", lambda e: e.tensor_copy(dstcol, pbc[:, 0:1]),
             reads=(pbc_b,), writes=(dstcol_b,), name=f"{tag}be{name}")

    def slot_l2ffn(self, last):
        tag = "L2f"
        S = self.S
        dst_ap, dst_b = self.next_x(last)
        self.stats_pass(tag)
        self.make_rms_rows(tag, float(np.finfo(np.float32).eps))
        self.xhat_inplace(tag)
        gw, gw_b = self.sbconst("L2f_gw")
        for k in range(NK):
            xh, xh_b = self.biga_slot(k)
            S.op("act", lambda e, xh=xh, k=k: e.activation(
                    xh, xh, AF.Copy, scale=gw[:, k:k + 1]),
                 reads=(xh_b, gw_b), writes=(xh_b,), name=f"{tag}gw{k}")
        rt_col = self.small[2][:, 4:5]
        sc_col = self.small[2][:, 5:6]
        sm2_b = self.small_b[2]
        def mktemp(sc, sc_b):
            S.op("dve", lambda e: e.tensor_scalar_max(sc[0:1, 12:13],
                                                      sc[0:1, 12:13], 0.1),
                 reads=(sc_b,), writes=(sc_b,), name=f"{tag}tmx")
            S.op("dve", lambda e: e.reciprocal(sc[0:1, 12:13], sc[0:1, 12:13]),
                 reads=(sc_b,), writes=(sc_b,), name=f"{tag}trc")
        self.bval_col(tag, "L2f_temp", rt_col, sm2_b, fn=mktemp)
        self.bval_col(tag, "L2f_scale", sc_col, sm2_b)

        def evac_lg(oc, tb, pa, pab):
            stj, stj_b = self.stg[tb % 2], self.stg_b[tb % 2]
            S.op("dve", lambda e, pa=pa, stj=stj: e.tensor_scalar(
                    stj[:], pa[:, :], rt_col, None, ALU.mult),
                 reads=(pab, sm2_b), writes=(stj_b,), name=f"{tag}el{oc}_{tb}")
            S.dma("sp", self.hidT[oc * 128:(oc + 1) * 128,
                                  tb * 512:(tb + 1) * 512], stj[:],
                  reads=(stj_b,), writes=(self.hidT_b,), name=f"{tag}sl{oc}_{tb}")
        self.proj(tag + "lg", "L2f_WcT", 0, 32, D, evac_lg)

        genes_ap, genes_b = self.win("L2f_genes")
        identb, identb_b = self.sbconst("ident_bf")
        lg = self.fwA[:, 0:2048].bitcast(BF16)
        zap = self.fwA[:, 2048:4096].bitcast(BF16)
        lg_b, zap_b = self.fw_b[0], self.fw_b[1]
        exA, exA_b = self.biga_slot(16)
        exB, exB_b = self.biga_slot(17)
        m8 = self.smallb[6][:, 0:8]
        m8_b = self.smallb_b[6]
        negm = self.small[2][:, 6:7]
        Zc = self.small[2][:, 7:8]
        for tt in range(NTT):
            for hf in range(2):
                S.op("sp", lambda e, tt=tt, hf=hf: e.dma_start_transpose(
                        lg[:, hf * 2048:(hf + 1) * 2048],
                        self.hidT[hf * 2048:(hf + 1) * 2048,
                                  tt * 128:(tt + 1) * 128]),
                     reads=(self.hidT_b,), writes=(lg_b,), dma=True,
                     name=f"{tag}tt{tt}_{hf}")
            for rnd in range(4):
                src, src_b = (lg, lg_b) if rnd == 0 else (zap, zap_b)
                S.op("dve", lambda e, src=src: e.max(out=m8, in_=src),
                     reads=(src_b,), writes=(m8_b,), name=f"{tag}mx{tt}_{rnd}")
                if rnd == 0:
                    S.op("dve", lambda e: e.tensor_scalar(
                            negm, m8[:, 0:1], -1.0, None, ALU.mult),
                         reads=(m8_b,), writes=(sm2_b,), name=f"{tag}nm{tt}")
                S.op("dve", lambda e, src=src: e.match_replace(
                        out=zap, in_to_replace=m8, in_values=src,
                        imm_value=-1e30),
                     reads=(src_b, m8_b), writes=(zap_b,), name=f"{tag}mr{tt}_{rnd}")
            S.op("act", lambda e: e.activation(lg[:], lg[:], AF.Exp, bias=negm),
                 reads=(lg_b, sm2_b), writes=(lg_b,), name=f"{tag}ex{tt}")
            S.op("dve", lambda e: e.tensor_scalar(zap, zap, -1e29, None, ALU.is_le),
                 reads=(zap_b,), writes=(zap_b,), name=f"{tag}mk{tt}")
            S.op("dve", lambda e: e.scalar_tensor_tensor(
                    lg[:], lg[:], 1.0, zap, ALU.mult, ALU.mult, accum_out=Zc),
                 reads=(lg_b, zap_b), writes=(lg_b, sm2_b), name=f"{tag}wz{tt}")
            S.op("dve", lambda e: e.reciprocal(Zc, Zc),
                 reads=(sm2_b,), writes=(sm2_b,), name=f"{tag}iz{tt}")
            S.op("dve", lambda e: e.tensor_scalar(lg[:], lg[:], Zc, None, ALU.mult),
                 reads=(lg_b, sm2_b), writes=(lg_b,), name=f"{tag}pn{tt}")
            pbf0, pbf0_b = self.pbf[0], self.pbf_b[0]
            pA, pA_b = self.pb[0], self.pb_b[0]
            pB, pB_b = self.pb[1], self.pb_b[1]
            for go in range(32):
                gsl, gsl_b = self.wslab[go % 2], self.wslab_b[go % 2]
                S.dma("pool", gsl[:, 0:256],
                      genes_ap[go * 128:(go + 1) * 128, :],
                      reads=(genes_b,), writes=(gsl_b,), name=f"{tag}gl{tt}_{go}")
                S.op("pe", lambda e, go=go: e.transpose(
                        pbf0[:, 0:128], lg[:, go * 128:(go + 1) * 128], identb[:]),
                     reads=(lg_b, identb_b), writes=(pbf0_b,),
                     name=f"{tag}pt{tt}_{go}")
                pTs, pTs_b = self.smallb[8 + go % 2], self.smallb_b[8 + go % 2]
                S.op("dve", lambda e, pTs=pTs: e.tensor_copy(pTs[:], pbf0[:, 0:128]),
                     reads=(pbf0_b,), writes=(pTs_b,), name=f"{tag}pe{tt}_{go}")
                S.op("pe", lambda e, go=go, gsl=gsl, pTs=pTs: e.matmul(
                        pA[:, 0:128], gsl[:, 0:128], pTs[:],
                        start=(go == 0), stop=(go == 31)),
                     reads=(gsl_b, pTs_b), writes=(pA_b,), name=f"{tag}ea{tt}_{go}")
                S.op("pe", lambda e, go=go, gsl=gsl, pTs=pTs: e.matmul(
                        pB[:, 0:128], gsl[:, 128:256], pTs[:],
                        start=(go == 0), stop=(go == 31)),
                     reads=(gsl_b, pTs_b), writes=(pB_b,), name=f"{tag}eb{tt}_{go}")
            S.op("dve", lambda e, tt=tt: e.tensor_copy(
                    exA[:, tt * 128:(tt + 1) * 128], pA[:, 0:128]),
                 reads=(pA_b,), writes=(exA_b,), name=f"{tag}xa{tt}")
            S.op("dve", lambda e, tt=tt: e.tensor_copy(
                    exB[:, tt * 128:(tt + 1) * 128], pB[:, 0:128]),
                 reads=(pB_b,), writes=(exB_b,), name=f"{tag}xb{tt}")

        WdT, WdT_b = self.win("L2f_WdT")
        WuT, WuT_b = self.win("L2f_WuT")
        wdv = WdT.rearrange("(c p) o -> p c o", p=128)
        wuv = WuT.rearrange("(c p) o -> p c o", p=128)
        for oc in range(NK):
            wsl0, wsl0_b = self.wslab[0], self.wslab_b[0]
            wsl1, wsl1_b = self.wslab[1], self.wslab_b[1]
            S.dma("pool", wsl0[:, 0:256].rearrange("p (c o) -> p c o", o=128),
                  wdv[:, :, oc * 128:(oc + 1) * 128],
                  reads=(WdT_b,), writes=(wsl0_b,), name=f"{tag}dl{oc}")
            S.dma("pool", wsl1[:, 0:256].rearrange("p (c o) -> p c o", o=128),
                  wuv[:, :, oc * 128:(oc + 1) * 128],
                  reads=(WuT_b,), writes=(wsl1_b,), name=f"{tag}ul{oc}")
            for tb in range(NTB):
                sl = slice(tb * 512, (tb + 1) * 512)
                pd, pd_b = self.pb[2], self.pb_b[2]
                pu, pu_b = self.pb[3], self.pb_b[3]
                for ci, (src, src_b) in enumerate(((exA, exA_b), (exB, exB_b))):
                    S.op("pe", lambda e, ci=ci, sl=sl, src=src: e.matmul(
                            pd[:, :], wsl0[:, ci * 128:(ci + 1) * 128], src[:, sl],
                            start=(ci == 0), stop=(ci == 1)),
                         reads=(wsl0_b, src_b), writes=(pd_b,),
                         name=f"{tag}dm{oc}_{tb}_{ci}")
                    S.op("pe", lambda e, ci=ci, sl=sl, src=src: e.matmul(
                            pu[:, :], wsl1[:, ci * 128:(ci + 1) * 128], src[:, sl],
                            start=(ci == 0), stop=(ci == 1)),
                         reads=(wsl1_b, src_b), writes=(pu_b,),
                         name=f"{tag}um{oc}_{tb}_{ci}")
                S.op("act", lambda e: e.activation(self.tmp[0][:], pd[:, :], AF.Tanh),
                     reads=(pd_b,), writes=(self.tmp_b[0],), name=f"{tag}th{oc}_{tb}")
                xh, xh_b = self.biga_slot(oc)
                S.op("dve", lambda e, sl=sl, xh=xh: e.tensor_mul(
                        self.tmp[0][:], self.tmp[0][:], xh[:, sl]),
                     reads=(self.tmp_b[0], xh_b), writes=(self.tmp_b[0],),
                     name=f"{tag}p1{oc}_{tb}")
                S.op("dve", lambda e: e.tensor_mul(
                        self.tmp[1][:], self.tmp[0][:], pu[:, :]),
                     reads=(self.tmp_b[0], pu_b), writes=(self.tmp_b[1],),
                     name=f"{tag}p2{oc}_{tb}")
                hw = (oc >> 1) & 1
                xs = self.fw[oc % 2][:, hw * 512:(hw + 1) * 512]
                xs_b = self.fw_b[oc % 2]
                S.dma("sp", xs, self.cur_ap[oc * 128:(oc + 1) * 128, sl],
                      reads=(self.cur_b[oc],), writes=(xs_b,), name=f"{tag}x{oc}_{tb}")
                S.op("dve", lambda e, xs=xs: e.scalar_tensor_tensor(
                        xs, self.tmp[1][:], sc_col, xs, ALU.mult, ALU.add),
                     reads=(self.tmp_b[1], sm2_b, xs_b), writes=(xs_b,),
                     name=f"{tag}p3{oc}_{tb}")
                S.dma("sp", dst_ap[oc * 128:(oc + 1) * 128, sl], xs,
                      reads=(xs_b,), writes=(dst_b[oc],), name=f"{tag}w{oc}_{tb}")
        self.advance_x(dst_ap, dst_b)


    def slot_l3attn(self, last):
        tag = "L3a"
        S = self.S
        dst_ap, dst_b = self.next_x(last)
        self.stats_pass(tag)
        self.make_s_ms_rows(tag, "ln")
        self.xhat_inplace(tag)
        def evac_lat(oc, tb, pa, pab):
            stj, stj_b = self.stg[tb % 2], self.stg_b[tb % 2]
            S.op("dve", lambda e, pa=pa, stj=stj: e.tensor_copy(stj[:], pa[:, :]),
                 reads=(pab,), writes=(stj_b,), name=f"{tag}el{oc}_{tb}")
            S.dma("sp", self.hidT[oc * 128:(oc + 1) * 128,
                                  tb * 512:(tb + 1) * 512], stj[:],
                  reads=(stj_b,), writes=(self.hidT_b,), name=f"{tag}sl{oc}_{tb}")
        self.proj(tag + "ql", "L3a_wqd", 0, 12, D, evac_lat)
        self.proj(tag + "kv", "L3a_wkvd", 12, 16, D, evac_lat, o_w_off=12)
        for k in range(NK):
            ap, b2 = self.biga_slot(k)
            S.dma("sp", ap, self.hidT[k * 128:(k + 1) * 128, :],
                  reads=(self.hidT_b,), writes=(b2,), name=f"{tag}ll{k}")
        cosv = self.biga[:, 16 * 2048:18 * 2048].bitcast(F32)
        sinv = self.biga[:, 18 * 2048:20 * 2048].bitcast(F32)
        cosv_b, sinv_b = self.biga_b[16], self.biga_b[18]
        lc, lc_b = self.win("l3_cos")
        ls, ls_b = self.win("l3_sin")
        S.dma("sp", cosv, lc, reads=(lc_b,), writes=(cosv_b,), name=f"{tag}lc")
        S.dma("sp", sinv, ls, reads=(ls_b,), writes=(sinv_b,), name=f"{tag}ls")
        rotP, rotP_b = self.sbconst("l3_rotP")
        qsc = float(192.0 ** -0.5)
        ql_slot = lambda k: self.biga_slot(k)
        kv_slot = lambda k: self.biga_slot(12 + k)

        def evac_qc(oc, tb, pa, pab):
            stj, stj_b = self.stg[tb % 2], self.stg_b[tb % 2]
            S.op("dve", lambda e, pa=pa, stj=stj: e.tensor_scalar_mul(
                    stj[:], pa[:, :], qsc),
                 reads=(pab,), writes=(stj_b,), name=f"{tag}eqc{oc}_{tb}")
            S.dma("sp", self.qkvT[oc * 128:(oc + 1) * 128,
                                  tb * 512:(tb + 1) * 512], stj[:],
                  reads=(stj_b,), writes=(self.qkvT_b,), name=f"{tag}sqc{oc}_{tb}")
        self.proj(tag + "qc", "L3a_wqn", 0, 16, 1536, evac_qc, rhs_slot=ql_slot)

        def evac_qr(oc, tb, pa, pab):
            self.rope_evac(tag + "qr", oc, tb, pa, pab, cosv, cosv_b, sinv, sinv_b,
                           rotP, rotP_b, qsc, oc)
        self.proj(tag + "qr", "L3a_wqr", 16, 24, 1536, evac_qr,
                  rhs_slot=ql_slot, o_w_off=16)

        def evac_kc(oc, tb, pa, pab):
            stj, stj_b = self.stg[tb % 2], self.stg_b[tb % 2]
            S.op("dve", lambda e, pa=pa, stj=stj: e.tensor_copy(stj[:], pa[:, :]),
                 reads=(pab,), writes=(stj_b,), name=f"{tag}ekc{oc}_{tb}")
            S.dma("sp", self.qkvT[oc * 128:(oc + 1) * 128,
                                  tb * 512:(tb + 1) * 512], stj[:],
                  reads=(stj_b,), writes=(self.qkvT_b,), name=f"{tag}skc{oc}_{tb}")
        self.proj(tag + "kc", "L3a_wkn", 24, 40, 512, evac_kc,
                  rhs_slot=kv_slot, o_w_off=24)

        def evac_kr(oc, tb, pa, pab):
            self.rope_evac(tag + "kr", oc, tb, pa, pab, cosv, cosv_b, sinv, sinv_b,
                           rotP, rotP_b, 1.0, oc)
        self.proj(tag + "kr", "L3a_wkr", 40, 48, 512, evac_kr,
                  rhs_slot=kv_slot, o_w_off=40)

        def evac_v(oc, tb, pa, pab):
            stj, stj_b = self.stg[tb % 2], self.stg_b[tb % 2]
            S.op("dve", lambda e, pa=pa, stj=stj: e.tensor_copy(stj[:], pa[:, :]),
                 reads=(pab,), writes=(stj_b,), name=f"{tag}ev{oc}_{tb}")
            S.dma("sp", self.vT3[oc * 128:(oc + 1) * 128,
                                 tb * 512:(tb + 1) * 512], stj[:],
                  reads=(stj_b,), writes=(self.vT3_b,), name=f"{tag}sv{oc}_{tb}")
        self.proj(tag + "v", "L3a_wvu", 0, 24, 512, evac_v, rhs_slot=kv_slot)

        dmk, dmk_b = self.sbconst("diag_mask")
        dmkT, dmkT_b = self.sbconst("diag_maskT")
        identb, identb_b = self.sbconst("ident_bf")
        ones_row_bf, ones_row_bf_b = self.sbconst("ones_row_bf")
        for h in range(16):
            QTc, QTc_b = self.biga_slot(0)
            QTr, QTr_b = self.biga_slot(1)
            KTc, KTc_b = self.biga_slot(2)
            KTr, KTr_b = self.biga_slot(3)
            VTc, VTc_b = self.biga_slot(4)
            VTr, VTr_b = self.biga_slot(5)
            VTrm, VTrm_b = self.biga_slot(6)
            atnc, atnc_b = self.biga_slot(7)
            atnr, atnr_b = self.biga_slot(8)
            S.dma("sp", QTc, self.qkvT[h * 128:(h + 1) * 128, :],
                  reads=(self.qkvT_b,), writes=(QTc_b,), name=f"{tag}h{h}l1")
            S.dma("sp", QTr[0:64, :], self.qkvT[2048 + h * 64:2048 + (h + 1) * 64, :],
                  reads=(self.qkvT_b,), writes=(QTr_b,), name=f"{tag}h{h}l2")
            S.dma("sp", KTc, self.qkvT[3072 + h * 128:3072 + (h + 1) * 128, :],
                  reads=(self.qkvT_b,), writes=(KTc_b,), name=f"{tag}h{h}l3")
            S.dma("sp", KTr[0:64, :], self.qkvT[5120 + h * 64:5120 + (h + 1) * 64, :],
                  reads=(self.qkvT_b,), writes=(KTr_b,), name=f"{tag}h{h}l4")
            S.dma("sp", VTc, self.vT3[h * 192:h * 192 + 128, :],
                  reads=(self.vT3_b,), writes=(VTc_b,), name=f"{tag}h{h}l5")
            S.dma("sp", VTr[0:64, :], self.vT3[h * 192 + 128:h * 192 + 192, :],
                  reads=(self.vT3_b,), writes=(VTr_b,), name=f"{tag}h{h}l6")
            pvt0, pvt0_b = self.pbf[0], self.pbf_b[0]
            for c in range(NTT):
                S.op("pe", lambda e, c=c: e.transpose(
                        pvt0[:, 0:128], VTc[:, c * 128:(c + 1) * 128], identb[:]),
                     reads=(VTc_b, identb_b), writes=(pvt0_b,), name=f"{tag}h{h}vc{c}")
                S.op("dve", lambda e, c=c: e.tensor_copy(
                        VTc[:, c * 128:(c + 1) * 128], pvt0[:, 0:128]),
                     reads=(pvt0_b,), writes=(VTc_b,), name=f"{tag}h{h}ve{c}")
                S.op("pe", lambda e, c=c: e.transpose(
                        pvt0[:, 128:192], VTr[0:64, c * 128:(c + 1) * 128],
                        identb[0:64, 0:64]),
                     reads=(VTr_b, identb_b), writes=(pvt0_b,), name=f"{tag}h{h}vr{c}")
                S.op("dve", lambda e, c=c: e.tensor_copy(
                        VTrm[:, c * 64:(c + 1) * 64], pvt0[:, 128:192]),
                     reads=(pvt0_b,), writes=(VTrm_b,), name=f"{tag}h{h}vf{c}")
            psc, psc_b = self.pb[0], self.pb_b[0]
            pvC, pvC_b = self.pb[1], self.pb_b[1]
            pvR, pvR_b = self.pb[2], self.pb_b[2]
            fwj, fwj_b = self.fw[0], self.fw_b[0]
            negm = self.cols[:, 0:16]
            Zc = self.cols[:, 16:32]
            lnZ = self.cols[:, 32:48]
            bcol = self.cols[:, 48:64]
            colsb = self.cols_b[0]
            for qt in range(NTT):
                L = (qt + 1) * 128
                nb = qt // 4 + 1
                for jb in range(nb):
                    w = 512 if jb < qt // 4 else (qt % 4 + 1) * 128
                    S.op("pe", lambda e, qt=qt, jb=jb, w=w: e.matmul(
                            psc[:, 0:w], QTc[:, qt * 128:(qt + 1) * 128],
                            KTc[:, jb * 512:jb * 512 + w], start=True, stop=False),
                         reads=(QTc_b, KTc_b), writes=(psc_b,),
                         name=f"{tag}h{h}a{qt}_{jb}")
                    S.op("pe", lambda e, qt=qt, jb=jb, w=w: e.matmul(
                            psc[:, 0:w], QTr[0:64, qt * 128:(qt + 1) * 128],
                            KTr[0:64, jb * 512:jb * 512 + w], start=False, stop=True),
                         reads=(QTr_b, KTr_b), writes=(psc_b,),
                         name=f"{tag}h{h}a2{qt}_{jb}")
                    if jb < qt // 4:
                        S.op("dve", lambda e, jb=jb, w=w: e.tensor_copy(
                                fwj[:, jb * 512:jb * 512 + w], psc[:, 0:w]),
                             reads=(psc_b,), writes=(fwj_b,),
                             name=f"{tag}h{h}ae{qt}_{jb}")
                    else:
                        if w > 128:
                            S.op("dve", lambda e, jb=jb, w=w: e.tensor_copy(
                                    fwj[:, jb * 512:jb * 512 + w - 128],
                                    psc[:, 0:w - 128]),
                                 reads=(psc_b,), writes=(fwj_b,),
                                 name=f"{tag}h{h}ap{qt}_{jb}")
                        S.op("dve", lambda e, qt=qt, w=w: e.tensor_add(
                                fwj[:, qt * 128:(qt + 1) * 128],
                                psc[:, w - 128:w], dmk[:]),
                             reads=(psc_b, dmk_b), writes=(fwj_b,),
                             name=f"{tag}h{h}ad{qt}")
                S.op("dve", lambda e, qt=qt, L=L: e.reduce_max(
                        negm[:, qt:qt + 1], fwj[:, 0:L], axis=AX.X, negate=True),
                     reads=(fwj_b,), writes=(colsb,), name=f"{tag}h{h}am{qt}")
                S.op("act", lambda e, qt=qt, L=L: e.activation(
                        fwj[:, 0:L], fwj[:, 0:L], AF.Exp,
                        bias=negm[:, qt:qt + 1], accum_out=Zc[:, qt:qt + 1]),
                     reads=(fwj_b, colsb), writes=(fwj_b, colsb),
                     name=f"{tag}h{h}az{qt}")
            S.op("act", lambda e: e.activation(lnZ, Zc, AF.Ln),
                 reads=(colsb,), writes=(colsb,), name=f"{tag}h{h}ln")
            S.op("dve", lambda e: e.tensor_sub(bcol, negm, lnZ),
                 reads=(colsb,), writes=(colsb,), name=f"{tag}h{h}bc")
            self.cols_to_row(bcol, colsb, 16, 2, tag + f"h{h}")
            brow, brow_b = self.row(2)
            hilo = self.SR[0:1, 3 * 2048:4 * 2048].bitcast(BF16)
            hilo_b = self.rows_b[3]
            S.op("act", lambda e: e.activation(hilo[0:1, 0:2048], brow, AF.Copy),
                 reads=(brow_b,), writes=(hilo_b,), name=f"{tag}h{h}hi")
            for pc in range(4):
                S.op("dve", lambda e, pc=pc: e.tensor_sub(
                        self.tmp[0][0:1, 0:512],
                        brow[0:1, pc * 512:(pc + 1) * 512],
                        hilo[0:1, pc * 512:(pc + 1) * 512]),
                     reads=(brow_b, hilo_b), writes=(self.tmp_b[0],),
                     name=f"{tag}h{h}lo{pc}")
                S.op("act", lambda e, pc=pc: e.activation(
                        hilo[0:1, 2048 + pc * 512:2048 + (pc + 1) * 512],
                        self.tmp[0][0:1, 0:512], AF.Copy),
                     reads=(self.tmp_b[0],), writes=(hilo_b,),
                     name=f"{tag}h{h}loc{pc}")
            for jb in range(4):
                for c in range(4 * jb + 4):
                    qsl = slice(jb * 512, (jb + 1) * 512)
                    S.op("pe", lambda e, c=c, qsl=qsl: e.matmul(
                            psc[:, 0:512], KTc[:, c * 128:(c + 1) * 128],
                            QTc[:, qsl], start=True, stop=False),
                         reads=(QTc_b, KTc_b), writes=(psc_b,),
                         name=f"{tag}h{h}b{jb}_{c}")
                    S.op("pe", lambda e, c=c, qsl=qsl: e.matmul(
                            psc[:, 0:512], KTr[0:64, c * 128:(c + 1) * 128],
                            QTr[0:64, qsl], start=False, stop=False),
                         reads=(QTr_b, KTr_b), writes=(psc_b,),
                         name=f"{tag}h{h}b2{jb}_{c}")
                    S.op("pe", lambda e, jb=jb: e.matmul(
                            psc[:, 0:512], ones_row_bf[:],
                            hilo[0:1, jb * 512:(jb + 1) * 512],
                            start=False, stop=False),
                         reads=(hilo_b, ones_row_bf_b), writes=(psc_b,),
                         name=f"{tag}h{h}bh{jb}_{c}")
                    S.op("pe", lambda e, jb=jb: e.matmul(
                            psc[:, 0:512], ones_row_bf[:],
                            hilo[0:1, 2048 + jb * 512:2048 + (jb + 1) * 512],
                            start=False, stop=True),
                         reads=(hilo_b, ones_row_bf_b), writes=(psc_b,),
                         name=f"{tag}h{h}bl{jb}_{c}")
                    d = c - 4 * jb
                    pT, pT_b = self.stg[c % 2], self.stg_b[c % 2]
                    if d >= 0:
                        S.op("dve", lambda e, d=d: e.tensor_add(
                                psc[:, d * 128:(d + 1) * 128],
                                psc[:, d * 128:(d + 1) * 128], dmkT[:]),
                             reads=(psc_b, dmkT_b), writes=(psc_b,),
                             name=f"{tag}h{h}bm{jb}_{c}")
                        if d > 0:
                            S.op("dve", lambda e, d=d, pT=pT: e.memset(
                                    pT[:, 0:d * 128], 0.0),
                                 writes=(pT_b,), name=f"{tag}h{h}bz{jb}_{c}")
                        S.op("act", lambda e, d=d, pT=pT: e.activation(
                                pT[:, d * 128:512], psc[:, d * 128:512], AF.Exp),
                             reads=(psc_b,), writes=(pT_b,), name=f"{tag}h{h}be{jb}_{c}")
                    else:
                        S.op("act", lambda e, pT=pT: e.activation(
                                pT[:], psc[:, 0:512], AF.Exp),
                             reads=(psc_b,), writes=(pT_b,), name=f"{tag}h{h}be{jb}_{c}")
                    S.op("pe", lambda e, c=c, jb=jb, pT=pT: e.matmul(
                            pvC[:, 0:512], VTc[:, c * 128:(c + 1) * 128], pT[:],
                            start=(c == 0), stop=(c == 4 * jb + 3)),
                         reads=(VTc_b, pT_b), writes=(pvC_b,),
                         name=f"{tag}h{h}bp{jb}_{c}")
                    S.op("pe", lambda e, c=c, jb=jb, pT=pT: e.matmul(
                            pvR[0:64, 0:512], VTrm[:, c * 64:(c + 1) * 64], pT[:],
                            start=(c == 0), stop=(c == 4 * jb + 3)),
                         reads=(VTrm_b, pT_b), writes=(pvR_b,),
                         name=f"{tag}h{h}bq{jb}_{c}")
                S.op("dve", lambda e, jb=jb: e.tensor_copy(
                        atnc[:, jb * 512:(jb + 1) * 512], pvC[:, 0:512]),
                     reads=(pvC_b,), writes=(atnc_b,), name=f"{tag}h{h}oe{jb}")
                S.op("dve", lambda e, jb=jb: e.tensor_copy(
                        atnr[0:64, jb * 512:(jb + 1) * 512], pvR[0:64, 0:512]),
                     reads=(pvR_b,), writes=(atnr_b,), name=f"{tag}h{h}of{jb}")
            S.dma("sp", self.attnT3[h * 192:h * 192 + 128, :], atnc,
                  reads=(atnc_b,), writes=(self.attnT3_b,), name=f"{tag}h{h}s1")
            S.dma("sp", self.attnT3[h * 192 + 128:h * 192 + 192, :], atnr[0:64, :],
                  reads=(atnr_b,), writes=(self.attnT3_b,), name=f"{tag}h{h}s2")

        for k in range(24):
            ap, b2 = self.biga_slot(k)
            S.dma("sp", ap, self.attnT3[k * 128:(k + 1) * 128, :],
                  reads=(self.attnT3_b,), writes=(b2,), name=f"{tag}lda{k}")
        xloaded = {}
        def evac_wo(oc, tb, pa, pab):
            if oc not in xloaded:
                xs, xs_b = self.fw[oc % 2], self.fw_b[oc % 2]
                S.dma("sp", xs, self.cur_ap[oc * 128:(oc + 1) * 128, :],
                      reads=(self.cur_b[oc],), writes=(xs_b,), name=f"{tag}rx{oc}")
                xloaded[oc] = (xs, xs_b)
            xs, xs_b = xloaded[oc]
            sl = slice(tb * 512, (tb + 1) * 512)
            S.op("dve", lambda e, pa=pa, sl=sl, xs=xs: e.tensor_add(
                    xs[:, sl], pa[:, :], xs[:, sl]),
                 reads=(pab, xs_b), writes=(xs_b,), name=f"{tag}ro{oc}_{tb}")
            if tb == 3:
                S.dma("sp", dst_ap[oc * 128:(oc + 1) * 128, :], xs,
                      reads=(xs_b,), writes=(dst_b[oc],), name=f"{tag}wx{oc}")
        self.proj(tag + "wo", "L3a_wo", 0, 16, 3072, evac_wo,
                  rhs_slot=lambda k: self.biga_slot(k))
        self.advance_x(dst_ap, dst_b)

    def slot_l3ffn(self, last):
        tag = "L3f"
        S = self.S
        dst_ap, dst_b = self.next_x(last)
        self.stats_pass(tag)
        self.make_s_ms_rows(tag, "ln")
        self.xhat_inplace(tag)
        WgT, WgT_b = self.win("L3f_WgT")
        WuT, WuT_b = self.win("L3f_WuT")
        wgv = WgT.rearrange("(c p) o -> p c o", p=128)
        wuv = WuT.rearrange("(c p) o -> p c o", p=128)
        for oc in range(64):
            wsl0, wsl0_b = self.wslab[0], self.wslab_b[0]
            wsl1, wsl1_b = self.wslab[1], self.wslab_b[1]
            S.dma("pool", wsl0[:].rearrange("p (c o) -> p c o", o=128),
                  wgv[:, :, oc * 128:(oc + 1) * 128],
                  reads=(WgT_b,), writes=(wsl0_b,), name=f"{tag}gl{oc}")
            S.dma("pool", wsl1[:].rearrange("p (c o) -> p c o", o=128),
                  wuv[:, :, oc * 128:(oc + 1) * 128],
                  reads=(WuT_b,), writes=(wsl1_b,), name=f"{tag}ul{oc}")
            for half in range(2):
                gbanks = [(self.pb[0], self.pb_b[0]), (self.pb[1], self.pb_b[1])]
                ubanks = [(self.pb[2], self.pb_b[2]), (self.pb[3], self.pb_b[3])]
                for k in range(NK):
                    for t2 in range(2):
                        tb = half * 2 + t2
                        sl = slice(tb * 512, (tb + 1) * 512)
                        xh, xh_b = self.biga_slot(k)
                        gb, gb_b = gbanks[t2]
                        ub, ub_b = ubanks[t2]
                        S.op("pe", lambda e, k=k, sl=sl, xh=xh, gb=gb: e.matmul(
                                gb[:, :], wsl0[:, k * 128:(k + 1) * 128], xh[:, sl],
                                start=(k == 0), stop=(k == NK - 1)),
                             reads=(wsl0_b, xh_b), writes=(gb_b,),
                             name=f"{tag}gm{oc}_{tb}_{k}")
                        S.op("pe", lambda e, k=k, sl=sl, xh=xh, ub=ub: e.matmul(
                                ub[:, :], wsl1[:, k * 128:(k + 1) * 128], xh[:, sl],
                                start=(k == 0), stop=(k == NK - 1)),
                             reads=(wsl1_b, xh_b), writes=(ub_b,),
                             name=f"{tag}um{oc}_{tb}_{k}")
                for t2 in range(2):
                    tb = half * 2 + t2
                    sl = slice(tb * 512, (tb + 1) * 512)
                    gb, gb_b = gbanks[t2]
                    ub, ub_b = ubanks[t2]
                    sg, sg_b = self.stg[t2], self.stg_b[t2]
                    hd, hd_b = self.stg[2 + t2], self.stg_b[2 + t2]
                    S.op("act", lambda e, gb=gb, sg=sg: e.activation(
                            sg[:], gb[:, :], AF.Silu),
                         reads=(gb_b,), writes=(sg_b,), name=f"{tag}si{oc}_{tb}")
                    S.op("dve", lambda e, ub=ub, sg=sg, hd=hd: e.tensor_mul(
                            hd[:], ub[:, :], sg[:]),
                         reads=(ub_b, sg_b), writes=(hd_b,), name=f"{tag}hd{oc}_{tb}")
                    S.dma("sp", self.hidT[oc * 128:(oc + 1) * 128, sl], hd[:],
                          reads=(hd_b,), writes=(self.hidT_b,),
                          name=f"{tag}hs{oc}_{tb}")
        Arow, Arow_b = self.row(0)
        Qrow, Qrow_b = self.row(1)
        S.op("dve", lambda e: e.memset(Arow, 1.0), writes=(Arow_b,), name=f"{tag}A")
        S.op("dve", lambda e: e.memset(Qrow, 0.0), writes=(Qrow_b,), name=f"{tag}Q")
        WdT, WdT_b = self.win("L3f_WdT")
        wdv = WdT.rearrange("(c p) o -> p c o", p=128)

        def hid_chunk(k):
            sslot, sb2 = self.biga_slot(k // 4)
            return sslot[:, (k % 4) * 512:(k % 4 + 1) * 512], sb2

        def mlp_bank(tb, oc):
            sl = slice(tb * 512, (tb + 1) * 512)
            if oc == 0:
                for sslot in range(NK):
                    ba, bb = self.biga_slot(sslot)
                    src = self.hidT.rearrange("(c p) t -> p c t", p=128)[
                        :, sslot * 4:(sslot + 1) * 4, sl]
                    S.dma("sp", ba.rearrange("p (c t) -> p c t", t=512)[:, 0:4, :],
                          src, reads=(self.hidT_b,), writes=(bb,),
                          name=f"{tag}dl{tb}_{sslot}")
            bank, bank_b = (self.pb[0], self.pb_b[0]) if oc % 2 == 0 else                            (self.pb[1], self.pb_b[1])
            for si in range(4):
                j2 = (tb * 64 + oc * 4 + si) % 2
                wsl, wsl_b = self.wslab[j2], self.wslab_b[j2]
                S.dma("pool", wsl[:].rearrange("p (c o) -> p c o", o=128),
                      wdv[:, si * 16:(si + 1) * 16, oc * 128:(oc + 1) * 128],
                      reads=(WdT_b,), writes=(wsl_b,), name=f"{tag}dw{tb}_{oc}_{si}")
                for kk in range(16):
                    k = si * 16 + kk
                    ha, hb = hid_chunk(k)
                    S.op("pe", lambda e, kk=kk, k=k, ha=ha, wsl=wsl, bank=bank:
                         e.matmul(bank[:, :], wsl[:, kk * 128:(kk + 1) * 128], ha,
                                  start=(k == 0), stop=(k == 63)),
                         reads=(wsl_b, hb), writes=(bank_b,),
                         name=f"{tag}dm{tb}_{oc}_{k}")
            return bank, bank_b

        self.final_combine(tag, dst_ap, dst_b, Arow, Arow_b, Qrow, Qrow_b, mlp_bank)
        self.advance_x(dst_ap, dst_b)



SLOT_ORDER = ["L0_attn", "L0_ffn", "L1_attn", "L1_ffn",
              "L2_attn", "L2_ffn", "L3_attn", "L3_ffn"]
IMPLEMENTED = SLOT_ORDER[:]
DEBUG_DUMPS = []

def build_program(host, slots):
    nc = bass.Bass("TRN2", target_bir_lowering=False)
    with ExitStack() as ctx:
        b = Builder(nc, ctx, host)
        for i, s in enumerate(slots):
            last = (i == len(slots) - 1)
            if s in ("L0_attn", "L1_attn"):
                b.slot_lightning(s[:2], last)
            elif s == "L0_ffn":
                b.slot_l0ffn(last)
            elif s == "L1_ffn":
                b.slot_l1ffn(last)
            elif s == "L2_attn":
                b.slot_l2attn(last)
            elif s == "L2_ffn":
                b.slot_l2ffn(last)
            elif s == "L3_attn":
                b.slot_l3attn(last)
            elif s == "L3_ffn":
                b.slot_l3ffn(last)
            else:
                raise NotImplementedError(s)
        b.S.op("sp", lambda e: e.nop(), reads=(b.yT_b,), name="final")
        b.S.finalize()
        sems, qsems = b.sems, b.qsems
        S = b.S

        def emitf(name, eng):
            for o in S.ops[name]:
                for kv, val in o.waits:
                    if kv[0] == "q":
                        eng.wait_ge(qsems[(kv[1], kv[2])], val)
                    else:
                        eng.wait_ge(sems[kv[1]], val)
                ins = o.fn(eng)
                if o.is_dma:
                    ins.then_inc(qsems[(o.eng, o.inc_no % Sched.M_DMA)], 16)
                elif o.needed:
                    ins.then_inc(sems[o.eng], 1)

        with nc.Block() as blk:
            @blk.tensor
            def _(t):
                emitf("pe", t)
            @blk.scalar
            def _(s):
                emitf("act", s)
            @blk.vector
            def _(v):
                emitf("dve", v)
            @blk.gpsimd
            def _(g):
                emitf("pool", g)
            @blk.sync
            def _(s):
                emitf("sp", s)
        return nc, list(b.used_inputs.keys())


def kernel(**inputs):
    host = host_prepare(inputs)
    slots = [s for s in SLOT_ORDER if s in IMPLEMENTED]
    nc, needed = build_program(host, slots)
    x = np.asarray(inputs["L0_attn_x"])
    in_maps = []
    for c in range(8):
        m = {}
        for nm in needed:
            if nm == "xT":
                m[nm] = np.ascontiguousarray(x[c].T)
            else:
                m[nm] = host[nm]
        in_maps.append(m)
    res = run_bass_kernel_spmd(nc, in_maps, core_ids=list(range(8)))
    out = np.stack([np.ascontiguousarray(r["yT"].T) for r in res.results])
    return out


# version 05
# speedup vs baseline: 1.0329x; 1.0277x over previous
import math
import numpy as np
import ml_dtypes
from contextlib import ExitStack
import concourse.bass as bass
import concourse.mybir as mybir
from concourse.bass_utils import run_bass_kernel_spmd

F32 = mybir.dt.float32
BF16 = mybir.dt.bfloat16
AF = mybir.ActivationFunctionType
ALU = mybir.AluOpType
AX = mybir.AxisListType
BF = ml_dtypes.bfloat16

T = 2048
D = 2048
NK = D // 128
NTB = T // 512
NTT = T // 128

ENGINES = ("pe", "act", "dve", "pool", "sp")


class Buf:
    __slots__ = ("writers", "readers", "name")
    def __init__(self, name=""):
        self.writers = []
        self.readers = []
        self.name = name

class Op:
    __slots__ = ("eng", "fn", "deps", "inc_no", "is_dma", "name", "needed",
                 "waits", "post_drain")
    def __init__(self, eng, fn, deps, is_dma, name):
        self.eng = eng
        self.fn = fn
        self.deps = deps
        self.is_dma = is_dma
        self.name = name
        self.inc_no = None
        self.needed = False
        self.waits = ()
        self.post_drain = False

class Sched:
    def __init__(self):
        self.ops = {e: [] for e in ENGINES}
        self.n = 0

    def op(self, eng, fn, reads=(), writes=(), name="", dma=False,
           post_drain=False, extra=()):
        deps = set(extra)
        for b in reads:
            deps.update(b.writers)
        for b in writes:
            deps.update(b.writers)
            deps.update(b.readers)
        o = Op(eng, fn, deps, dma, name)
        o.post_drain = post_drain
        for b in reads:
            b.readers.append(o)
        for b in writes:
            b.writers = [o]
            b.readers = []
        self.ops[eng].append(o)
        self.n += 1
        return o

    def dma(self, queue, out_ap, in_ap, reads=(), writes=(), name=""):
        return self.op(queue, lambda e: e.dma_start(out=out_ap, in_=in_ap),
                       reads=reads, writes=writes, name=name, dma=True)

    M_DMA = 8
    TRUST_SAME = True

    def finalize(self):
        for e in ENGINES:
            for o in self.ops[e]:
                for d in o.deps:
                    if d.is_dma or not (d.eng == e == "pe"):
                        d.needed = True
        M = self.M_DMA
        for e in ENGINES:
            cnt = dcnt = 0
            for o in self.ops[e]:
                if o.is_dma:
                    o.inc_no = dcnt
                    dcnt += 1
                elif o.needed:
                    cnt += 1
                    o.inc_no = cnt
        for e in ENGINES:
            watermark = {}
            for o in self.ops[e]:
                waits = {}
                for d in o.deps:
                    if d.is_dma:
                        key = ("q", d.eng, d.inc_no % M)
                        val = 16 * (d.inc_no // M + 1)
                    else:
                        if d.eng == e and e == "pe":
                            continue
                        key = ("e", d.eng)
                        val = d.inc_no
                    if waits.get(key, -1) < val:
                        waits[key] = val
                if o.is_dma:
                    key = ("q", o.eng, o.inc_no % M)
                    gen = o.inc_no // M
                    if gen > 0 and waits.get(key, -1) < 16 * gen:
                        waits[key] = 16 * gen
                o.waits = tuple((k, v) for k, v in waits.items()
                                if watermark.get(k, 0) < v)
                for k, v in o.waits:
                    watermark[k] = v


def _lightning_consts():
    H, C = 16, 128
    start = 2.0 ** (-2.0 ** (-(math.log2(H) - 3)))
    slopes = np.array([start ** (i + 1) for i in range(H)], dtype=np.float32)
    lam = np.exp(-slopes).astype(np.float32)
    pos = np.arange(C, dtype=np.float32)
    decay_q = lam[:, None] ** (pos + 1.0)
    decay_k = lam[:, None] ** (C - 1.0 - pos)
    diff = pos[:, None] - pos[None, :]
    dmask = np.where(diff >= 0, lam[:, None, None] ** np.maximum(diff, 0), 0.0)
    dmaskT = dmask.transpose(0, 2, 1)
    lam_c = (lam ** C).astype(np.float32)
    dqb = np.broadcast_to(decay_q[:, None, :], (H, C, C)).transpose(1, 0, 2)
    return {
        "lt_dmaskT": np.ascontiguousarray(dmaskT.transpose(1, 0, 2)
                                          ).reshape(C, H * C).astype(BF),
        "lt_dq_bcast": np.ascontiguousarray(dqb).reshape(C, H * C).astype(BF),
        "lt_dk_cols": np.ascontiguousarray(decay_k.T).astype(np.float32),
    }, [float(v) for v in lam_c]

LT_CONSTS, LT_LAMC = _lightning_consts()

def col_chunks(v):
    return np.ascontiguousarray(np.asarray(v).reshape(-1, 128).T).astype(np.float32)

def host_prepare(inputs):
    g = {}
    f32 = np.float32
    g["ident"] = np.eye(128, dtype=f32)
    g["ident_bf"] = np.eye(128, dtype=f32).astype(BF)
    g["ones_row"] = np.ones((1, 128), dtype=f32)
    g["ones_row_bf"] = np.ones((1, 128), dtype=f32).astype(BF)
    g["ones_col_bf"] = np.ones((128, 1), dtype=f32).astype(BF)
    onesel = np.zeros((128, 16), dtype=f32)
    for blk in range(4):
        onesel[:, blk * 4 + blk] = 1.0
    g["onesel"] = onesel.astype(BF)
    tri = np.tril(np.ones((128, 128), bool))
    g["diag_mask"] = np.where(tri, 0.0, -1e30).astype(BF)
    g["diag_maskT"] = np.where(tri.T, 0.0, -1e30).astype(BF)
    g.update(LT_CONSTS)

    i = inputs
    sw = np.asarray(i["L0_ffn_select_weight"])
    lu = np.asarray(i["L0_ffn_lora_update"])
    ld = np.asarray(i["L0_ffn_lora_degen"])
    g["L0f_WallT"] = np.ascontiguousarray(sw.transpose(1, 0, 2).reshape(2048, 64))
    g["L0f_smdT"] = np.ascontiguousarray(np.asarray(i["L0_ffn_select_mean_diff"]).T)
    g["L0f_updT"] = np.ascontiguousarray(lu.transpose(1, 0, 2).reshape(2048, 64))
    g["L0f_dgnT"] = np.ascontiguousarray(ld.transpose(0, 2, 1).reshape(64, 2048))
    g["L0f_biasw"] = np.ascontiguousarray(np.asarray(i["L0_ffn_bias_w"]))
    g["L0f_dwT"] = np.ascontiguousarray(np.asarray(i["L0_ffn_debias_w"]).T)
    g["L0f_slope"] = np.asarray(i["L0_ffn_imp_slope"]).reshape(8, 1).astype(f32)
    g["L0f_center"] = np.asarray(i["L0_ffn_imp_center"]).reshape(8, 1).astype(f32)
    E8 = np.zeros((64, 8), f32)
    for n in range(8):
        E8[n * 8:(n + 1) * 8, n] = 1.0
    g["E8mask"] = E8
    g["E8bf"] = E8.astype(BF)
    g["P8bf"] = np.ascontiguousarray(E8.T).astype(BF)
    g["pen8"] = (np.arange(8, dtype=f32) * 1e-9).reshape(8, 1)
    g["L1f_WqT"] = np.ascontiguousarray(np.asarray(i["L1_ffn_Wq"]).T)
    g["L1f_WkT"] = np.ascontiguousarray(np.asarray(i["L1_ffn_Wk"]).T)
    g["L1f_WvT"] = np.ascontiguousarray(np.asarray(i["L1_ffn_Wv"]).T)
    g["L1f_Wo"] = np.ascontiguousarray(np.asarray(i["L1_ffn_Wo"]))
    g["L1f_A"] = np.ascontiguousarray(np.asarray(i["L1_ffn_A"]))
    g["L1f_B"] = np.ascontiguousarray(np.asarray(i["L1_ffn_B"]))
    g["L1f_WinT"] = np.ascontiguousarray(np.asarray(i["L1_ffn_W_in_base"]).T)
    g["L1f_WupT"] = np.ascontiguousarray(np.asarray(i["L1_ffn_W_up_base"]).T)
    g["L1f_WdnT"] = np.ascontiguousarray(np.asarray(i["L1_ffn_W_down"]).T)
    g["L1f_g1"] = col_chunks(i["L1_ffn_ln1_w"])
    g["L1f_g2"] = col_chunks(i["L1_ffn_ln2_w"])
    g["L2a_Wq"] = np.ascontiguousarray(np.asarray(i["L2_attn_Wq"]))
    g["L2a_Wkv"] = np.ascontiguousarray(np.asarray(i["L2_attn_Wkv"]))
    g["L2a_Wout"] = np.ascontiguousarray(np.asarray(i["L2_attn_Wout"]))
    rotP = np.zeros((128, 128), f32)
    for ii in range(64):
        rotP[ii, 64 + ii] = -1.0
        rotP[64 + ii, ii] = 1.0
    g["l2_rotP"] = rotP.astype(BF)
    inv_freq = 1.0 / 10000.0 ** (np.arange(0, 128, 2, dtype=f32) / 128.0)
    fr = np.arange(T, dtype=f32)[:, None] * inv_freq[None, :]
    posf = np.concatenate([fr, fr], -1)
    g["l2_cos"] = np.ascontiguousarray(np.cos(posf).T).astype(f32)
    g["l2_sin"] = np.ascontiguousarray(np.sin(posf).T).astype(f32)
    g["L2f_WcT"] = np.ascontiguousarray(np.asarray(i["L2_ffn_Wc"]).T)
    g["L2f_genes"] = np.ascontiguousarray(np.asarray(i["L2_ffn_genes"]))
    g["L2f_WdT"] = np.ascontiguousarray(np.asarray(i["L2_ffn_Wd"]).T)
    g["L2f_WuT"] = np.ascontiguousarray(np.asarray(i["L2_ffn_Wu"]).T)
    g["L2f_gw"] = col_chunks(i["L2_ffn_norm_w"])
    g["L2f_temp"] = np.asarray(i["L2_ffn_temperature"]).reshape(1, 1).astype(f32)
    g["L2f_scale"] = np.asarray(i["L2_ffn_scale"]).reshape(1, 1).astype(f32)
    for nm, key in (("L3a_wqd", "L3_attn_wq_down"), ("L3a_wqn", "L3_attn_wq_nope"),
                    ("L3a_wqr", "L3_attn_wq_rope"), ("L3a_wkvd", "L3_attn_wkv_down"),
                    ("L3a_wvu", "L3_attn_wv_up"), ("L3a_wkn", "L3_attn_wk_nope"),
                    ("L3a_wkr", "L3_attn_wk_rope"), ("L3a_wo", "L3_attn_wo")):
        g[nm] = np.ascontiguousarray(np.asarray(i[key]))
    g["L3f_WgT"] = np.ascontiguousarray(np.asarray(i["L3_ffn_w_gate"]).T)
    g["L3f_WuT"] = np.ascontiguousarray(np.asarray(i["L3_ffn_w_up"]).T)
    g["L3f_WdT"] = np.ascontiguousarray(np.asarray(i["L3_ffn_w_down"]).T)
    rotP3 = np.zeros((128, 128), f32)
    for ii in range(0, 128, 2):
        rotP3[ii, ii + 1] = 1.0
        rotP3[ii + 1, ii] = 1.0
    g["l3_rotP"] = rotP3.astype(BF)
    DR = 64
    inv3 = 1.0 / 10000.0 ** (np.arange(0, DR, 2, dtype=f32) / DR)
    fr3 = np.arange(T, dtype=f32)[:, None] * inv3[None, :]
    cos3 = np.cos(fr3); sin3 = np.sin(fr3)
    crow = np.repeat(cos3, 2, axis=1)
    srow = np.empty_like(crow)
    srow[:, 0::2] = -sin3
    srow[:, 1::2] = sin3
    g["l3_cos"] = np.ascontiguousarray(np.tile(crow, (1, 2)).T).astype(f32)
    g["l3_sin"] = np.ascontiguousarray(np.tile(srow, (1, 2)).T).astype(f32)
    for L in ("L0", "L1"):
        g[f"{L}a_wqkvT"] = np.ascontiguousarray(np.asarray(i[f"{L}_attn_w_qkv"]).T)
        g[f"{L}a_bqkv"] = col_chunks(i[f"{L}_attn_b_qkv"])
        g[f"{L}a_woutT"] = np.ascontiguousarray(np.asarray(i[f"{L}_attn_w_out"]).T)
        g[f"{L}a_bout"] = col_chunks(i[f"{L}_attn_b_out"])
    return g


class Builder:
    def __init__(self, nc, ctx, host):
        self.nc = nc
        self.ctx = ctx
        self.host = host
        self.S = Sched()
        E = ctx.enter_context
        self.used_inputs = {}
        self.in_bufs = {}
        self.sb_cache = {}

        self.xT_in = self.inp("xT", (D, T), F32)
        self.yT = nc.dram_tensor("yT", [D, T], F32, kind="ExternalOutput").ap()
        self.yT_b = Buf("yT")
        self.xhbm = [nc.dram_tensor(f"xs{j}", [D, T], F32, kind="Internal").ap()
                     for j in range(2)]
        self.xhbm_b = [[Buf(f"xs{j}_{k}") for k in range(NK)] for j in range(2)]
        self.cur = 0
        self.cur_ap = self.xT_in
        self.cur_b = [self.in_bufs["xT"]] * NK

        self.biga = E(nc.sbuf_tensor("biga", [128, 49152], BF16))
        self.biga_b = [Buf(f"bg{k}") for k in range(24)]
        self.fwA = E(nc.sbuf_tensor("fwA", [128, 4096], F32))
        self.fw = [self.fwA[:, 0:2048], self.fwA[:, 2048:4096]]
        self.fw_b = [Buf(f"fw{j}") for j in range(2)]
        self.SR = E(nc.sbuf_tensor("SR", [80, 10240], F32))
        self.rows_b = [Buf(f"row{j}") for j in range(5)]
        self.sr32_b = [Buf(f"sr32_{j}") for j in range(5)]
        self.sr64_b = [Buf(f"sr64_{j}") for j in range(5)]
        self.pqs = [E(nc.sbuf_tensor(f"pqs{j}", [128, 512], F32)) for j in range(3)]
        self.pqs_b = [Buf(f"pqs{j}") for j in range(3)]
        self.cols = E(nc.sbuf_tensor("colsa", [128, 512], F32))
        self.cols_b = [Buf(f"cols{j}") for j in range(4)]
        self.s4 = E(nc.sbuf_tensor("s4", [4, 3584], F32))
        self.s4_b = [Buf(f"s4_{j}") for j in range(8)]
        self.wslab = [E(nc.sbuf_tensor(f"wsl{j}", [128, 2048], BF16)) for j in range(2)]
        self.wslab_b = [Buf(f"wsl{j}") for j in range(2)]
        self.stg = [E(nc.sbuf_tensor(f"stg{j}", [128, 512], BF16)) for j in range(4)]
        self.stg_b = [Buf(f"stg{j}") for j in range(4)]
        self.tmp = [E(nc.sbuf_tensor(f"tmp{j}", [128, 512], F32)) for j in range(2)]
        self.tmp_b = [Buf(f"tmp{j}") for j in range(2)]
        self.small = [E(nc.sbuf_tensor(f"sm{j}", [128, 128], F32)) for j in range(4)]
        self.small_b = [Buf(f"sm{j}") for j in range(4)]
        self.smallb = [E(nc.sbuf_tensor(f"sb{j}", [128, 128], BF16)) for j in range(10)]
        self.dmy = E(nc.sbuf_tensor("dmy", [1, 8], F32))
        self.dmy_b = Buf("dmy")
        self.smallb_b = [Buf(f"sb{j}") for j in range(10)]

        self.pb = [E(nc.psum_tensor(f"pb{j}", [128, 512], F32)) for j in range(6)]
        self.pb_b = [Buf(f"pb{j}") for j in range(6)]
        self.pbf = [E(nc.psum_tensor(f"pbf{j}", [128, 1024], BF16)) for j in range(2)]
        self.pbf_b = [Buf(f"pbf{j}") for j in range(2)]

        self.sems = {e: E(nc.semaphore(f"sem_{e}")) for e in ("pe", "act", "dve", "pool")}
        self.qsems = {(e, j): E(nc.semaphore(f"q_{e}{j}"))
                      for e in ("sp", "act", "pool") for j in range(Sched.M_DMA)}

        self.scr_row = [nc.dram_tensor(f"scrr{j}", [1, T], F32, kind="Internal").ap()
                        for j in range(4)]
        self.scr_row_b = [Buf(f"scrr{j}") for j in range(4)]
        self.qkvT = nc.dram_tensor("qkvT", [3 * D, T], BF16, kind="Internal").ap()
        self.qkvT_b = Buf("qkvT")
        self.attnT = nc.dram_tensor("attnT", [D, T], BF16, kind="Internal").ap()
        self.attnT_b = Buf("attnT")
        self.hidT = nc.dram_tensor("hidT", [8192, T], BF16, kind="Internal").ap()
        self.hidT_b = Buf("hidT")
        self.vT3 = nc.dram_tensor("vT3", [3072, T], BF16, kind="Internal").ap()
        self.vT3_b = Buf("vT3")
        self.attnT3 = nc.dram_tensor("attnT3", [3072, T], BF16, kind="Internal").ap()
        self.attnT3_b = Buf("attnT3")
        self.W2sb = E(nc.sbuf_tensor("W2sb", [128, 256], BF16))
        self.W2sb_b = Buf("W2sb")

    def inp(self, name, shape, dtype):
        ap = self.nc.dram_tensor(name, list(shape), dtype, kind="ExternalInput").ap()
        self.used_inputs[name] = ap
        self.in_bufs[name] = Buf(name)
        return ap

    def win(self, name):
        if name in self.used_inputs:
            return self.used_inputs[name], self.in_bufs[name]
        arr = self.host[name]
        dt = BF16 if arr.dtype == BF else F32
        ap = self.inp(name, arr.shape, dt)
        return ap, self.in_bufs[name]

    def sbconst(self, name):
        if name in self.sb_cache:
            return self.sb_cache[name]
        ap, b = self.win(name)
        arr = self.host[name]
        dt = BF16 if arr.dtype == BF else F32
        tile = self.ctx.enter_context(
            self.nc.sbuf_tensor("c_" + name, list(arr.shape), dt))
        tb = Buf("c_" + name)
        self.S.dma("sp", tile[:], ap, reads=(b,), writes=(tb,), name="ld_" + name)
        self.sb_cache[name] = (tile, tb)
        return tile, tb

    def dump(self, key, ap, bufs, shape, dtype):
        import kernel as _k
        if key not in DEBUG_DUMPS:
            return
        out = self.nc.dram_tensor("dbg_" + key, list(shape), dtype,
                                  kind="ExternalOutput").ap()
        ob = Buf("dbg_" + key)
        self.S.dma("sp", out, ap, reads=tuple(bufs), writes=(ob,),
                   name="dump_" + key)
        self.S.op("sp", lambda e: e.nop(), reads=(ob,), name="dumpw_" + key)

    def next_x(self, last):
        if last:
            return self.yT, [self.yT_b] * NK
        j = 1 - self.cur
        return self.xhbm[j], self.xhbm_b[j]

    def advance_x(self, dst_ap, dst_bufs):
        self.cur = 1 - self.cur
        self.cur_ap = dst_ap
        self.cur_b = dst_bufs

    def biga_slot(self, k):
        return self.biga[:, k * 2048:(k + 1) * 2048], self.biga_b[k]

    def row(self, j):
        return self.SR[0:1, j * 2048:(j + 1) * 2048], self.rows_b[j]

    def sr32(self, j, p=16):
        return self.SR[32:32 + p, j * 2048:(j + 1) * 2048], self.sr32_b[j]

    def sr64(self, j, p=16):
        return self.SR[64:64 + p, j * 2048:(j + 1) * 2048], self.sr64_b[j]

    def cols_to_row(self, cols_ap, cols_buf, n, row_idx, tag):
        S = self.S
        identf, identf_b = self.sbconst("ident")
        dst, dst_b = self.row(row_idx)
        pbk, pbk_b = self.pb[4], self.pb_b[4]
        for g in range(0, n, 4):
            for tt in range(g, min(g + 4, n)):
                S.op("pe", lambda e, tt=tt: e.transpose(
                        pbk[0:1, (tt % 4) * 128:(tt % 4) * 128 + 128],
                        cols_ap[:, tt:tt + 1], identf[:]),
                     reads=(cols_buf, identf_b), writes=(pbk_b,),
                     name=f"{tag}c2r{tt}")
            w = min(g + 4, n) - g
            S.op("dve", lambda e, g=g, w=w: e.tensor_copy(
                    dst[0:1, g * 128:(g + w) * 128], pbk[0:1, 0:w * 128]),
                 reads=(pbk_b,), writes=(dst_b,), name=f"{tag}c2re{g}")

    def s4s(self, j):
        return self.s4[0:4, j * 512:(j + 1) * 512], self.s4_b[j]

    def stats_pass(self, tag):
        S = self.S
        onesel, onesel_b = self.sbconst("onesel")
        psum_s, psum_s_b = self.pb[4], self.pb_b[4]
        psum_q, psum_q_b = self.pb[5], self.pb_b[5]
        for k in range(NK):
            xs, xs_b = self.fw[k % 2], self.fw_b[k % 2]
            S.dma("sp", xs, self.cur_ap[k * 128:(k + 1) * 128, :],
                  reads=(self.cur_b[k],), writes=(xs_b,), name=f"{tag}ldx{k}")
            xbf, xbf_b = self.biga_slot(k)
            S.op("pool", lambda e, xs=xs, xbf=xbf: e.tensor_copy(xbf, xs),
                 reads=(xs_b,), writes=(xbf_b,), name=f"{tag}cast{k}")
            sqs, sqs_b = self.biga_slot(16 + (k % 2))
            S.op("act", lambda e, xs=xs, sqs=sqs: e.activation(sqs, xs, AF.Square),
                 reads=(xs_b,), writes=(sqs_b,), name=f"{tag}sq{k}")
            for blk in range(4):
                st = (k == 0 and blk == 0)
                sp = (k == NK - 1 and blk == 3)
                S.op("pe", lambda e, blk=blk, xbf=xbf, st=st, sp=sp, o=onesel:
                     e.matmul(psum_s[0:4, :], o[:, blk * 4:blk * 4 + 4],
                              xbf[:, blk * 512:(blk + 1) * 512], start=st, stop=sp),
                     reads=(xbf_b, onesel_b), writes=(psum_s_b,), name=f"{tag}ms{k}")
                S.op("pe", lambda e, blk=blk, sqs=sqs, st=st, sp=sp, o=onesel:
                     e.matmul(psum_q[0:4, :], o[:, blk * 4:blk * 4 + 4],
                              sqs[:, blk * 512:(blk + 1) * 512], start=st, stop=sp),
                     reads=(sqs_b, onesel_b), writes=(psum_q_b,), name=f"{tag}mq{k}")
        sm, sm_b = self.s4s(0)
        sq, sq_b = self.s4s(1)
        S.op("dve", lambda e, sm=sm: e.tensor_copy(sm, psum_s[0:4, :]),
             reads=(psum_s_b,), writes=(sm_b,), name=f"{tag}evs")
        S.op("dve", lambda e, sq=sq: e.tensor_copy(sq, psum_q[0:4, :]),
             reads=(psum_q_b,), writes=(sq_b,), name=f"{tag}evq")

    def derive_mu_r(self, tag):
        S = self.S
        mu, mu_b = self.s4s(2)
        var, var_b = self.s4s(3)
        r, r_b = self.s4s(4)
        t0, t0_b = self.s4s(5)
        sm, sm_b = self.s4s(0)
        sq, sq_b = self.s4s(1)
        S.op("dve", lambda e: e.tensor_scalar_mul(mu, sm, 1.0 / D),
             reads=(sm_b,), writes=(mu_b,), name=f"{tag}mu")
        S.op("dve", lambda e: e.tensor_scalar_mul(t0, sq, 1.0 / D),
             reads=(sq_b,), writes=(t0_b,), name=f"{tag}ex2")
        S.op("dve", lambda e: e.tensor_mul(var, mu, mu),
             reads=(mu_b,), writes=(var_b,), name=f"{tag}mu2")
        S.op("dve", lambda e: e.tensor_sub(var, t0, var),
             reads=(t0_b, var_b), writes=(var_b,), name=f"{tag}var")
        S.op("dve", lambda e: e.tensor_scalar_add(t0, var, 1e-5),
             reads=(var_b,), writes=(t0_b,), name=f"{tag}ve")
        S.op("act", lambda e: e.activation(t0, t0, AF.Sqrt),
             reads=(t0_b,), writes=(t0_b,), name=f"{tag}sqrt")
        S.op("dve", lambda e: e.reciprocal(r, t0),
             reads=(t0_b,), writes=(r_b,), name=f"{tag}r")

    def to_row(self, s4_idx, row_idx, tag):
        S = self.S
        src, src_b = self.s4s(s4_idx)
        scr, scr_b = self.scr_row[row_idx], self.scr_row_b[row_idx]
        dst, dst_b = self.row(row_idx)
        scr4 = scr.rearrange("o (a b) -> (o a) b", a=4, b=512)
        S.dma("sp", scr4, src, reads=(src_b,), writes=(scr_b,), name=f"{tag}r2d{row_idx}")
        S.dma("sp", dst, scr, reads=(scr_b,), writes=(dst_b,), name=f"{tag}d2r{row_idx}")

    def make_s_ms_rows(self, tag, kind):
        S = self.S
        self.derive_mu_r(tag)
        mu, mu_b = self.s4s(2)
        r, r_b = self.s4s(4)
        t1, t1_b = self.s4s(6)
        if kind == "ln":
            S.op("dve", lambda e: e.tensor_mul(t1, mu, r),
                 reads=(mu_b, r_b), writes=(t1_b,), name=f"{tag}ms")
            self.to_row(4, 0, tag)
            self.to_row(6, 1, tag)
        else:
            raise NotImplementedError(kind)

    def xhat_inplace(self, tag):
        S = self.S
        ones_row, ones_row_b = self.sbconst("ones_row")
        srow, srow_b = self.row(0)
        msrow, msrow_b = self.row(1)
        for tb in range(NTB):
            bs, bs_b = self.pb[4], self.pb_b[4]
            bm, bm_b = self.pb[5], self.pb_b[5]
            sl = slice(tb * 512, (tb + 1) * 512)
            S.op("pe", lambda e, sl=sl, o=ones_row: e.matmul(
                    bs[:, :], o[:], srow[0:1, sl], start=True, stop=True),
                 reads=(srow_b, ones_row_b), writes=(bs_b,), name=f"{tag}bcs{tb}")
            S.op("pe", lambda e, sl=sl, o=ones_row: e.matmul(
                    bm[:, :], o[:], msrow[0:1, sl], start=True, stop=True),
                 reads=(msrow_b, ones_row_b), writes=(bm_b,), name=f"{tag}bcm{tb}")
            for k in range(NK):
                xbf, xbf_b = self.biga_slot(k)
                tj, tj_b = self.tmp[k % 2], self.tmp_b[k % 2]
                S.op("dve", lambda e, xbf=xbf, sl=sl, tj=tj: e.tensor_mul(
                        tj[:], xbf[:, sl], bs[:, :]),
                     reads=(xbf_b, bs_b), writes=(tj_b,), name=f"{tag}xh1_{k}_{tb}")
                S.op("dve", lambda e, xbf=xbf, sl=sl, tj=tj: e.tensor_sub(
                        xbf[:, sl], tj[:], bm[:, :]),
                     reads=(tj_b, bm_b), writes=(xbf_b,), name=f"{tag}xh2_{k}_{tb}")

    def proj(self, tag, wT_name, o_lo, o_hi, K_total, evac, rhs_slot=None,
             after_oc=None, extra_mm=None, o_w_off=0):
        S = self.S
        wT, wT_b = self.win(wT_name)
        KT = K_total // 128
        if rhs_slot is None:
            rhs_slot = self.biga_slot
        wview = wT.rearrange("(c p) o -> p c o", p=128)
        nslab = (KT + 15) // 16
        for oc in range(o_lo, o_hi):
            slabs = []
            for si in range(nslab):
                j = (oc * nslab + si) % 2
                wsl, wsl_b = self.wslab[j], self.wslab_b[j]
                c0, c1 = si * 16, min((si + 1) * 16, KT)
                dst = wsl[:].rearrange("p (c o) -> p c o", o=128)[:, 0:c1 - c0, :]
                src = wview[:, c0:c1,
                            (oc - o_w_off) * 128:(oc - o_w_off + 1) * 128]
                S.dma("pool", dst, src, reads=(wT_b,), writes=(wsl_b,),
                      name=f"{tag}wld{oc}_{si}")
                slabs.append((wsl, wsl_b, c0, c1))
            for half in range(2):
                g = (oc * 2 + half) % 2
                banks = [(self.pb[2 * g], self.pb_b[2 * g]),
                         (self.pb[2 * g + 1], self.pb_b[2 * g + 1])]
                for (wsl, wsl_b, c0, c1) in slabs:
                    for k in range(c0, c1):
                        lhsT = wsl[:, (k - c0) * 128:(k - c0 + 1) * 128]
                        rap, rb = rhs_slot(k)
                        for t2 in range(2):
                            tb = half * 2 + t2
                            st = (k == 0)
                            sp = (k == KT - 1 and extra_mm is None)
                            pa, pab = banks[t2]
                            S.op("pe", lambda e, lhsT=lhsT, rap=rap, tb=tb, st=st,
                                 sp=sp, pa=pa: e.matmul(
                                     pa[:, :], lhsT, rap[:, tb * 512:(tb + 1) * 512],
                                     start=st, stop=sp),
                                 reads=(wsl_b, rb), writes=(pab,),
                                 name=f"{tag}mm{oc}_{k}_{tb}")
                for t2 in range(2):
                    if extra_mm is not None:
                        extra_mm(oc, half * 2 + t2, banks[t2][0], banks[t2][1])
                    evac(oc, half * 2 + t2, banks[t2][0], banks[t2][1])
            if after_oc is not None:
                after_oc(oc)

    def slot_lightning(self, L, last):
        tag = f"{L}a"
        S = self.S
        nc = self.nc
        dst_ap, dst_b = self.next_x(last)
        self.stats_pass(tag)
        self.make_s_ms_rows(tag, "ln")
        self.xhat_inplace(tag)

        bq, bq_b = self.sbconst(f"{L}a_bqkv")
        def evac_qkv(oc, tb, pa, pab):
            stj, stj_b = self.stg[tb % 2], self.stg_b[tb % 2]
            S.op("dve", lambda e, oc=oc, pa=pa, stj=stj: e.tensor_scalar(
                    stj[:], pa[:, :], bq[:, oc:oc + 1], None, ALU.add),
                 reads=(pab, bq_b), writes=(stj_b,), name=f"{tag}eq{oc}_{tb}")
            S.dma("sp", self.qkvT[oc * 128:(oc + 1) * 128, tb * 512:(tb + 1) * 512],
                  stj[:], reads=(stj_b,), writes=(self.qkvT_b,),
                  name=f"{tag}sq{oc}_{tb}")

        dmaskT, dmaskT_b = self.sbconst("lt_dmaskT")
        dqb_c, dqb_b = self.sbconst("lt_dq_bcast")
        dkc, dkc_b = self.sbconst("lt_dk_cols")
        ones_row, ones_row_b = self.sbconst("ones_row")
        identb, identb_b = self.sbconst("ident_bf")

        def scan_head_ops(h):
            j = h % 2
            QT, QT_b = self.biga_slot(16 + j * 3 + 0)
            KT_, KT_b = self.biga_slot(16 + j * 3 + 1)
            VT, VT_b = self.biga_slot(16 + j * 3 + 2)
            for t_, tb_, row0 in ((QT, QT_b, h), (KT_, KT_b, 16 + h), (VT, VT_b, 32 + h)):
                S.dma("sp", t_, self.qkvT[row0 * 128:(row0 + 1) * 128, :],
                      reads=(self.qkvT_b,), writes=(tb_,), name=f"{tag}h{h}ld")
            pbx, pbx_b = self.pb[4 + j], self.pb_b[4 + j]
            S32, S32_b = self.small[2 + j], self.small_b[2 + j]
            Sbf, Sbf_b = self.smallb[8 + j], self.smallb_b[8 + j]
            S.op("dve", lambda e, S32=S32: e.memset(S32[:], 0.0),
                 writes=(S32_b,), name=f"{tag}h{h}s0")
            atn, atn_b = self.biga_slot(22 + j)
            pv, pv_b = self.pbf[j], self.pbf_b[j]
            for c in range(NTT):
                csl = slice(c * 128, (c + 1) * 128)
                def step1(c=c, csl=csl):
                    S.op("pe", lambda e: e.matmul(
                            pbx[:, 0:128], KT_[:, csl], QT[:, csl],
                            start=True, stop=True),
                         reads=(QT_b, KT_b), writes=(pbx_b,), name=f"{tag}h{h}c{c}sc")
                    S.op("pe", lambda e: e.transpose(
                            pv[:, 0:128], VT[:, csl], identb[:]),
                         reads=(VT_b, identb_b), writes=(pv_b,), name=f"{tag}h{h}c{c}vt")
                    sm_t, sm_b = self.smallb[4 * j], self.smallb_b[4 * j]
                    S.op("dve", lambda e: e.tensor_mul(
                            sm_t[:], pbx[:, 0:128], dmaskT[:, h * 128:(h + 1) * 128]),
                         reads=(pbx_b, dmaskT_b), writes=(sm_b,), name=f"{tag}h{h}c{c}sm")
                    vb_t, vb_b = self.smallb[4 * j + 1], self.smallb_b[4 * j + 1]
                    S.op("dve", lambda e: e.tensor_copy(vb_t[:], pv[:, 0:128]),
                         reads=(pv_b,), writes=(vb_b,), name=f"{tag}h{h}c{c}ve")
                def step2(c=c, csl=csl):
                    sm_t, sm_b = self.smallb[4 * j], self.smallb_b[4 * j]
                    vb_t, vb_b = self.smallb[4 * j + 1], self.smallb_b[4 * j + 1]
                    S.op("pe", lambda e: e.transpose(
                            pv[:, 128:256], KT_[:, csl], identb[:]),
                         reads=(KT_b, identb_b), writes=(pv_b,), name=f"{tag}h{h}c{c}kt")
                    ksg, ksg_b = self.smallb[4 * j + 2], self.smallb_b[4 * j + 2]
                    S.op("dve", lambda e: e.tensor_scalar(
                            ksg[:], pv[:, 128:256], dkc[:, h:h + 1], None, ALU.mult),
                         reads=(pv_b, dkc_b), writes=(ksg_b,), name=f"{tag}h{h}c{c}ke")
                    qs, qs_b = self.smallb[4 * j + 3], self.smallb_b[4 * j + 3]
                    S.op("dve", lambda e: e.tensor_mul(
                            qs[:], QT[:, csl], dqb_c[:, h * 128:(h + 1) * 128]),
                         reads=(QT_b, dqb_b), writes=(qs_b,), name=f"{tag}h{h}c{c}qs")
                def step3(c=c, csl=csl):
                    sm_t, sm_b = self.smallb[4 * j], self.smallb_b[4 * j]
                    vb_t, vb_b = self.smallb[4 * j + 1], self.smallb_b[4 * j + 1]
                    ksg, ksg_b = self.smallb[4 * j + 2], self.smallb_b[4 * j + 2]
                    qs, qs_b = self.smallb[4 * j + 3], self.smallb_b[4 * j + 3]
                    S.op("pe", lambda e: e.matmul(pbx[:, 128:256], vb_t[:], sm_t[:],
                                                  start=True, stop=(c == 0)),
                         reads=(vb_b, sm_b), writes=(pbx_b,), name=f"{tag}h{h}c{c}o1")
                    if c > 0:
                        S.op("pe", lambda e: e.matmul(pbx[:, 128:256], Sbf[:], qs[:],
                                                      start=False, stop=True),
                             reads=(qs_b, Sbf_b), writes=(pbx_b,), name=f"{tag}h{h}c{c}o2")
                    S.op("pe", lambda e: e.matmul(pbx[:, 256:384], ksg[:], vb_t[:],
                                                  start=True, stop=True),
                         reads=(ksg_b, vb_b), writes=(pbx_b,), name=f"{tag}h{h}c{c}su")
                def step4(c=c, csl=csl):
                    S.op("dve", lambda e: e.tensor_copy(atn[:, csl], pbx[:, 128:256]),
                         reads=(pbx_b,), writes=(atn_b,), name=f"{tag}h{h}c{c}oe")
                    S.op("dve", lambda e: e.scalar_tensor_tensor(
                            S32[:], S32[:], LT_LAMC[h], pbx[:, 256:384],
                            ALU.mult, ALU.add),
                         reads=(S32_b, pbx_b), writes=(S32_b,), name=f"{tag}h{h}c{c}sd")
                    if c < NTT - 1:
                        S.op("act", lambda e: e.activation(Sbf[:], S32[:], AF.Copy),
                             reads=(S32_b,), writes=(Sbf_b,), name=f"{tag}h{h}c{c}sb")
                yield step1
                yield step2
                yield step3
                yield step4
            def final():
                S.dma("sp", self.attnT[h * 128:(h + 1) * 128, :], atn,
                      reads=(atn_b,), writes=(self.attnT_b,), name=f"{tag}h{h}st")
                if h == 0 and "atn0" in DEBUG_DUMPS:
                    out0 = self.nc.dram_tensor("dbg_atn0", [128, T], BF16,
                                               kind="ExternalOutput").ap()
                    ob0 = Buf("dbg_atn0")
                    S.dma("act", out0, atn, reads=(atn_b,), writes=(ob0,),
                          name="dump_atn0")
                    S.op("sp", lambda e: e.nop(), reads=(ob0,), name="dump_atn0w")
            yield final

        def scan_pipeline(prime):
            nxt = [0]
            active = []
            def start():
                if nxt[0] < 16:
                    active.append(scan_head_ops(nxt[0]))
                    nxt[0] += 1
            start()
            for _ in range(prime):
                try:
                    next(active[0])()
                except StopIteration:
                    break
            start()
            while active:
                for g in list(active):
                    try:
                        next(g)()
                    except StopIteration:
                        active.remove(g)
                        start()

        scan_started = [False]
        def after_oc(oc):
            if oc == 47:
                scan_pipeline(8)

        self.dump("xhat", self.biga[:, 0:32768], self.biga_b[0:16],
                  (128, 32768), BF16)
        self.dump("rows", self.SR[0:1, 0:4096], self.rows_b[0:2], (1, 4096), F32)
        self.proj(tag + "qkv", f"{L}a_wqkvT", 0, 48, D, evac_qkv, after_oc=after_oc)
        self.dump("qkvT", self.qkvT, [self.qkvT_b], (3 * D, T), BF16)
        self.dump("attnT", self.attnT, [self.attnT_b], (D, T), BF16)

        def attn_slot(k):
            ap, b = self.biga_slot(k)
            return ap, b
        for k in range(NK):
            ap, b = self.biga_slot(k)
            S.dma("sp", ap, self.attnT[k * 128:(k + 1) * 128, :],
                  reads=(self.attnT_b,), writes=(b,), name=f"{tag}lda{k}")

        bo, bo_b = self.sbconst(f"{L}a_bout")
        xloaded = {}
        def evac_wout(oc, tb, pa, pab):
            if oc not in xloaded:
                xs, xs_b = self.fw[oc % 2], self.fw_b[oc % 2]
                S.dma("sp", xs, self.cur_ap[oc * 128:(oc + 1) * 128, :],
                      reads=(self.cur_b[oc],), writes=(xs_b,), name=f"{tag}rx{oc}")
                xloaded[oc] = (xs, xs_b)
            xs, xs_b = xloaded[oc]
            sl = slice(tb * 512, (tb + 1) * 512)
            S.op("dve", lambda e, pa=pa, oc=oc, sl=sl, xs=xs: e.scalar_tensor_tensor(
                    xs[:, sl], pa[:, :], bo[:, oc:oc + 1], xs[:, sl], ALU.add, ALU.add),
                 reads=(pab, bo_b, xs_b), writes=(xs_b,), name=f"{tag}ro{oc}_{tb}")
            if tb == 3:
                S.dma("sp", dst_ap[oc * 128:(oc + 1) * 128, :], xs,
                      reads=(xs_b,), writes=(dst_b[oc],), name=f"{tag}wx{oc}")

        self.proj(tag + "wo", f"{L}a_woutT", 0, 16, D, evac_wout)
        self.advance_x(dst_ap, dst_b)


    def slot_l0ffn(self, last):
        tag = "L0f"
        S = self.S
        dst_ap, dst_b = self.next_x(last)
        self.stats_pass(tag)
        self.make_s_ms_rows(tag, "ln")
        var, var_b = self.s4s(3)
        r, r_b = self.s4s(4)
        t2, t2_b = self.s4s(6)
        S.op("dve", lambda e: e.tensor_mul(t2, var, r),
             reads=(var_b, r_b), writes=(t2_b,), name=f"{tag}n1")
        S.op("dve", lambda e: e.tensor_mul(t2, t2, r),
             reads=(t2_b, r_b), writes=(t2_b,), name=f"{tag}n2")
        S.op("dve", lambda e: e.tensor_scalar_mul(t2, t2, float(D)),
             reads=(t2_b,), writes=(t2_b,), name=f"{tag}n3")
        self.to_row(6, 2, tag)
        self.xhat_inplace(tag)

        wsl0, wsl0_b = self.wslab[0], self.wslab_b[0]
        wsl1, wsl1_b = self.wslab[1], self.wslab_b[1]
        WallT, WallT_b = self.win("L0f_WallT")
        updT, updT_b = self.win("L0f_updT")
        smdT, smdT_b = self.win("L0f_smdT")
        dwT, dwT_b = self.win("L0f_dwT")
        dgnT, dgnT_b = self.win("L0f_dgnT")
        biasw, biasw_b = self.win("L0f_biasw")
        def slabload(dst, dstb, src, srcb, o, nm):
            sv = src.rearrange("(c p) o -> p c o", p=128)
            S.dma("pool", dst.rearrange("p (c o) -> p c o", o=o), sv,
                  reads=(srcb,), writes=(dstb,), name=nm)
        slabload(wsl0[:, 0:1024], wsl0_b, WallT, WallT_b, 64, f"{tag}lw")
        slabload(wsl0[:, 1024:2048], wsl0_b, updT, updT_b, 64, f"{tag}lu")
        smd_sb, smd_sb_b = self.biga_slot(16)
        slabload(smd_sb[:, 0:128], smd_sb_b, smdT, smdT_b, 8, f"{tag}ls")
        dw_sb, dw_sb_b = self.biga_slot(17)
        slabload(dw_sb[:, 0:128], dw_sb_b, dwT, dwT_b, 8, f"{tag}ld")
        dgn_sb, dgn_sb_b = self.biga_slot(18)
        S.dma("pool", dgn_sb[0:64, :], dgnT, reads=(dgnT_b,), writes=(dgn_sb_b,),
              name=f"{tag}lg")
        bw_sb, bw_sb_b = self.biga_slot(19)
        S.dma("pool", bw_sb[0:8, :], biasw, reads=(biasw_b,), writes=(bw_sb_b,),
              name=f"{tag}lb")
        oh8bf, oh8bf_b = self.biga_slot(20)
        z_sb, z_sb_b = self.biga_slot(21)
        E8m, E8m_b = self.sbconst("E8mask")
        E8b, E8b_b = self.sbconst("E8bf")
        P8b, P8b_b = self.sbconst("P8bf")
        pen8, pen8_b = self.sbconst("pen8")
        onesc, onesc_b = self.sbconst("ones_col_bf")
        ones_row, ones_row_b = self.sbconst("ones_row")
        identf, identf_b = self.sbconst("ident")

        pb0, pb0_b = self.pb[0], self.pb_b[0]
        pb1, pb1_b = self.pb[1], self.pb_b[1]
        pb2, pb2_b = self.pb[2], self.pb_b[2]
        pb3, pb3_b = self.pb[3], self.pb_b[3]
        pb4, pb4_b = self.pb[4], self.pb_b[4]
        pb5, pb5_b = self.pb[5], self.pb_b[5]

        sq_sb, sq_sb_b = self.stg[1], self.stg_b[1]
        S.op("act", lambda e: e.activation(sq_sb[:, 0:128], smd_sb[:, 0:128],
                                           AF.Square),
             reads=(smd_sb_b,), writes=(sq_sb_b,), name=f"{tag}sqm")
        for c in range(NK):
            S.op("pe", lambda e, c=c: e.matmul(
                    pb4[0:8, 0:1], sq_sb[:, c * 8:(c + 1) * 8], onesc[:],
                    start=(c == 0), stop=(c == NK - 1)),
                 reads=(sq_sb_b, onesc_b), writes=(pb4_b,), name=f"{tag}mq{c}")
        musq_col = self.small[1]
        musq_col_b = self.small_b[1]
        S.op("dve", lambda e: e.tensor_copy(musq_col[0:8, 2:3], pb4[0:8, 0:1]),
             reads=(pb4_b,), writes=(musq_col_b,), name=f"{tag}mqe")
        slp, slp_b = self.small[0], self.small_b[0]
        S.dma("sp", slp[0:8, 4:5], self.win("L0f_slope")[0],
              reads=(slope_cb if False else self.in_bufs["L0f_slope"],),
              writes=(slp_b,), name=f"{tag}lsl")
        S.dma("sp", slp[0:8, 5:6], self.win("L0f_center")[0],
              reads=(self.in_bufs["L0f_center"],), writes=(slp_b,), name=f"{tag}lce")
        S.op("dve", lambda e: e.tensor_mul(slp[0:8, 6:7], slp[0:8, 4:5],
                                           slp[0:8, 5:6]),
             reads=(slp_b,), writes=(slp_b,), name=f"{tag}gb1")
        S.op("dve", lambda e: e.tensor_scalar_mul(slp[0:8, 6:7], slp[0:8, 6:7], -1.0),
             reads=(slp_b,), writes=(slp_b,), name=f"{tag}gb2")

        def diag64(lhs_sl, rhs_sl, rhs_b, out_sl, nm):
            for c in range(NK):
                S.op("pe", lambda e, c=c: e.matmul(
                        pb5[0:64, 0:8], wsl0[:, lhs_sl + c * 64: lhs_sl + (c + 1) * 64],
                        rhs_sl[:, c * 8:(c + 1) * 8],
                        start=(c == 0), stop=(c == NK - 1)),
                     reads=(wsl0_b, rhs_b), writes=(pb5_b,), name=f"{nm}{c}")
            S.op("dve", lambda e: e.tensor_mul(self.tmp[0][0:64, 0:8],
                                               pb5[0:64, 0:8], E8m[:]),
                 reads=(pb5_b, E8m_b), writes=(self.tmp_b[0],), name=f"{nm}m")
            S.op("dve", lambda e: e.reduce_sum(musq_col[0:64, out_sl:out_sl + 1],
                                               self.tmp[0][0:64, 0:8], axis=AX.X),
                 reads=(self.tmp_b[0],), writes=(musq_col_b,), name=f"{nm}r")
        diag64(0, smd_sb, smd_sb_b, 0, f"{tag}muw")
        diag64(1024, dw_sb, dw_sb_b, 1, f"{tag}mdb")

        umu, umu_b = self.sr32(0, 8)
        for tb in range(NTB):
            for c in range(NK):
                xh, xh_b = self.biga_slot(c)
                S.op("pe", lambda e, c=c, tb=tb, xh=xh: e.matmul(
                        pb4[0:8, :], smd_sb[:, c * 8:(c + 1) * 8],
                        xh[:, tb * 512:(tb + 1) * 512],
                        start=(c == 0), stop=(c == NK - 1)),
                     reads=(smd_sb_b, xh_b), writes=(pb4_b,), name=f"{tag}um{tb}_{c}")
            S.op("dve", lambda e, tb=tb: e.tensor_copy(
                    umu[0:8, tb * 512:(tb + 1) * 512], pb4[0:8, :]),
                 reads=(pb4_b,), writes=(umu_b,), name=f"{tag}ume{tb}")

        nsqr, nsqr_b = self.row(2)
        d2, d2_b = self.sr32(1, 8)
        score, score_b = self.sr32(2, 8)
        for tb in range(NTB):
            sl = slice(tb * 512, (tb + 1) * 512)
            for c in range(NK):
                xh, xh_b = self.biga_slot(c)
                S.op("pe", lambda e, c=c, sl=sl, xh=xh: e.matmul(
                        pb5[0:64, :], wsl0[:, c * 64:(c + 1) * 64], xh[:, sl],
                        start=(c == 0), stop=(c == NK - 1)),
                     reads=(wsl0_b, xh_b), writes=(pb5_b,), name=f"{tag}pj{tb}_{c}")
            S.op("dve", lambda e: e.tensor_scalar(
                    self.tmp[0][0:64, :], pb5[0:64, :], musq_col[0:64, 0:1],
                    None, ALU.subtract),
                 reads=(pb5_b, musq_col_b), writes=(self.tmp_b[0],),
                 name=f"{tag}pc{tb}")
            S.op("act", lambda e: e.activation(self.stg[0][0:64, :],
                                               self.tmp[0][0:64, :], AF.Square),
                 reads=(self.tmp_b[0],), writes=(self.stg_b[0],), name=f"{tag}ps{tb}")
            S.op("pe", lambda e: e.matmul(pb4[0:8, :], E8b[:], self.stg[0][0:64, :],
                                          start=True, stop=True),
                 reads=(E8b_b, self.stg_b[0]), writes=(pb4_b,), name=f"{tag}pg{tb}")
            S.op("pe", lambda e, sl=sl: e.matmul(
                    pb3[0:8, :], ones_row[0:1, 0:8], nsqr[0:1, sl],
                    start=True, stop=True),
                 reads=(ones_row_b, nsqr_b), writes=(pb3_b,), name=f"{tag}nb{tb}")
            S.op("dve", lambda e, sl=sl: e.scalar_tensor_tensor(
                    d2[0:8, sl], umu[0:8, sl], -2.0, pb3[0:8, :], ALU.mult, ALU.add),
                 reads=(umu_b, pb3_b), writes=(d2_b,), name=f"{tag}d2{tb}")
            S.op("dve", lambda e, sl=sl: e.tensor_scalar(
                    d2[0:8, sl], d2[0:8, sl], musq_col[0:8, 2:3], None, ALU.add),
                 reads=(d2_b, musq_col_b), writes=(d2_b,), name=f"{tag}d2b{tb}")
            S.op("dve", lambda e, sl=sl: e.reciprocal(d2[0:8, sl], d2[0:8, sl]),
                 reads=(d2_b,), writes=(d2_b,), name=f"{tag}rd{tb}")
            S.op("dve", lambda e, sl=sl: e.tensor_mul(
                    score[0:8, sl], pb4[0:8, :], d2[0:8, sl]),
                 reads=(pb4_b, d2_b), writes=(score_b,), name=f"{tag}sc{tb}")
        g8, g8_b = self.sr32(3, 8)
        S.op("act", lambda e: e.activation(g8[0:8, :], score[0:8, :], AF.Sigmoid,
                                           bias=slp[0:8, 6:7], scale=slp[0:8, 4:5]),
             reads=(score_b, slp_b), writes=(g8_b,), name=f"{tag}gate")
        S.op("dve", lambda e: e.tensor_scalar(g8[0:8, :], g8[0:8, :], pen8[:],
                                              None, ALU.subtract),
             reads=(g8_b, pen8_b), writes=(g8_b,), name=f"{tag}pen")
        GT = self.cols[:, 0:128]
        OH = self.cols[:, 128:256]
        SEL = self.cols[:, 256:384]
        GT_b, OH_b, SEL_b = self.cols_b[0], self.cols_b[1], self.cols_b[2]
        for tt in range(NTT):
            S.op("pe", lambda e, tt=tt: e.transpose(
                    pb4[0:128, 0:8], g8[0:8, tt * 128:(tt + 1) * 128],
                    identf[32:40, 32:40]),
                 reads=(g8_b, identf_b), writes=(pb4_b,), name=f"{tag}gt{tt}")
            S.op("dve", lambda e, tt=tt: e.tensor_copy(
                    GT[:, tt * 8:(tt + 1) * 8], pb4[0:128, 0:8]),
                 reads=(pb4_b,), writes=(GT_b,), name=f"{tag}gte{tt}")
            S.op("dve", lambda e, tt=tt: e.reduce_max(
                    SEL[:, tt:tt + 1], GT[:, tt * 8:(tt + 1) * 8], axis=AX.X),
                 reads=(GT_b,), writes=(SEL_b,), name=f"{tag}mx{tt}")
            S.op("dve", lambda e, tt=tt: e.tensor_scalar(
                    OH[:, tt * 8:(tt + 1) * 8], GT[:, tt * 8:(tt + 1) * 8],
                    SEL[:, tt:tt + 1], None, ALU.is_ge),
                 reads=(GT_b, SEL_b), writes=(OH_b,), name=f"{tag}oh{tt}")
        self.cols_to_row(SEL, SEL_b, 16, 2, tag + "sel")
        selr, selr_b = self.row(2)
        for tt in range(NTT):
            S.op("pe", lambda e, tt=tt: e.transpose(
                    pb4[0:8, 0:128], OH[:, tt * 8:(tt + 1) * 8], identf[:]),
                 reads=(OH_b, identf_b), writes=(pb4_b,), name=f"{tag}oht{tt}")
            S.op("act", lambda e, tt=tt: e.activation(
                    oh8bf[0:8, tt * 128:(tt + 1) * 128], pb4[0:8, 0:128], AF.Copy),
                 reads=(pb4_b,), writes=(oh8bf_b,), name=f"{tag}ohe{tt}")
        srow, srow_b = self.row(0)
        msrow, msrow_b = self.row(1)
        Prow, Prow_b = self.row(3)
        Qrow, Qrow_b = self.row(4)
        S.op("dve", lambda e: e.tensor_scalar(Prow, selr, -1.0, 1.0,
                                              ALU.mult, ALU.add),
             reads=(selr_b,), writes=(Prow_b,), name=f"{tag}p1")
        S.op("dve", lambda e: e.tensor_mul(Qrow, msrow, Prow),
             reads=(msrow_b, Prow_b), writes=(Qrow_b,), name=f"{tag}q1")
        S.op("dve", lambda e: e.tensor_scalar_mul(Qrow, Qrow, -1.0),
             reads=(Qrow_b,), writes=(Qrow_b,), name=f"{tag}q2")
        S.op("dve", lambda e: e.tensor_mul(Prow, srow, Prow),
             reads=(srow_b, Prow_b), writes=(Prow_b,), name=f"{tag}p2")
        S.op("dve", lambda e: e.tensor_scalar_add(Prow, Prow, 1.0),
             reads=(Prow_b,), writes=(Prow_b,), name=f"{tag}p3")

        for tb in range(NTB):
            sl = slice(tb * 512, (tb + 1) * 512)
            for c in range(NK):
                xh, xh_b = self.biga_slot(c)
                S.op("pe", lambda e, c=c, sl=sl, xh=xh: e.matmul(
                        pb5[0:64, :], wsl0[:, 1024 + c * 64:1024 + (c + 1) * 64],
                        xh[:, sl], start=(c == 0), stop=(c == NK - 1)),
                     reads=(wsl0_b, xh_b), writes=(pb5_b,), name=f"{tag}md{tb}_{c}")
            S.op("pe", lambda e, sl=sl: e.matmul(
                    pb4[0:64, :], P8b[:], oh8bf[0:8, sl], start=True, stop=True),
                 reads=(P8b_b, oh8bf_b), writes=(pb4_b,), name=f"{tag}ob{tb}")
            S.op("dve", lambda e: e.tensor_scalar(
                    self.tmp[0][0:64, :], pb5[0:64, :], musq_col[0:64, 1:2],
                    None, ALU.subtract),
                 reads=(pb5_b, musq_col_b), writes=(self.tmp_b[0],),
                 name=f"{tag}mz{tb}")
            S.op("dve", lambda e, sl=sl: e.tensor_mul(
                    z_sb[0:64, sl], self.tmp[0][0:64, :], pb4[0:64, :]),
                 reads=(self.tmp_b[0], pb4_b), writes=(z_sb_b,), name=f"{tag}zz{tb}")

        self.dump("l0f_sr", self.SR[0:80, :],
                  [umu_b, d2_b, score_b, g8_b, selr_b, Prow_b, Qrow_b,
                   srow_b, msrow_b, nsqr_b], (80, 10240), F32)
        self.dump("l0f_oh", oh8bf, [oh8bf_b], (128, 2048), BF16)
        self.dump("l0f_z", z_sb, [z_sb_b], (128, 2048), BF16)
        self.dump("l0f_cols", self.cols[:], [GT_b, OH_b, SEL_b], (128, 512), F32)
        self.dump("l0f_mc", musq_col[:], [musq_col_b], (128, 128), F32)
        for tb in range(NTB):
            sl = slice(tb * 512, (tb + 1) * 512)
            for ri, rowsrc in enumerate(((Prow, Prow_b), (Qrow, Qrow_b),
                                         (selr, selr_b))):
                ra, rb = rowsrc
                S.op("pe", lambda e, ra=ra, sl=sl: e.matmul(
                        pb2[:, :], ones_row[:], ra[0:1, sl], start=True, stop=True),
                     reads=(rb, ones_row_b), writes=(pb2_b,), name=f"{tag}fb{tb}_{ri}")
                S.op("dve", lambda e, ri=ri: e.tensor_copy(self.pqs[ri][:], pb2[:, :]),
                     reads=(pb2_b,), writes=(self.pqs_b[ri],), name=f"{tag}fe{tb}_{ri}")
            for oc in range(NK):
                bank, bank_b = (pb0, pb0_b) if oc % 2 == 0 else (pb1, pb1_b)
                S.op("pe", lambda e, oc=oc, sl=sl, bank=bank: e.matmul(
                        bank[:, :], dgn_sb[0:64, oc * 128:(oc + 1) * 128],
                        z_sb[0:64, sl], start=True, stop=False),
                     reads=(dgn_sb_b, z_sb_b), writes=(bank_b,), name=f"{tag}fm{tb}_{oc}")
                S.op("pe", lambda e, oc=oc, sl=sl, bank=bank: e.matmul(
                        bank[:, :], bw_sb[0:8, oc * 128:(oc + 1) * 128],
                        oh8bf[0:8, sl], start=False, stop=True),
                     reads=(bw_sb_b, oh8bf_b), writes=(bank_b,),
                     name=f"{tag}fn{tb}_{oc}")
                hw = (oc >> 1) & 1
                xs = self.fw[oc % 2][:, hw * 512:(hw + 1) * 512]
                xs_b = self.fw_b[oc % 2]
                S.dma("sp", xs, self.cur_ap[oc * 128:(oc + 1) * 128, sl],
                      reads=(self.cur_b[oc],), writes=(xs_b,), name=f"{tag}fx{tb}_{oc}")
                S.op("dve", lambda e, xs=xs: e.tensor_mul(
                        self.tmp[0][:, :], xs, self.pqs[0][:]),
                     reads=(xs_b, self.pqs_b[0]), writes=(self.tmp_b[0],),
                     name=f"{tag}f1{tb}_{oc}")
                S.op("dve", lambda e: e.tensor_add(
                        self.tmp[0][:, :], self.tmp[0][:, :], self.pqs[1][:]),
                     reads=(self.tmp_b[0], self.pqs_b[1]), writes=(self.tmp_b[0],),
                     name=f"{tag}f2{tb}_{oc}")
                S.op("dve", lambda e, bank=bank: e.tensor_mul(
                        self.tmp[1][:, :], bank[:, :], self.pqs[2][:]),
                     reads=(bank_b, self.pqs_b[2]), writes=(self.tmp_b[1],),
                     name=f"{tag}f3{tb}_{oc}")
                S.op("dve", lambda e, xs=xs: e.tensor_add(
                        xs, self.tmp[0][:, :], self.tmp[1][:, :]),
                     reads=(self.tmp_b[0], self.tmp_b[1]), writes=(xs_b,),
                     name=f"{tag}f4{tb}_{oc}")
                S.dma("sp", dst_ap[oc * 128:(oc + 1) * 128, sl], xs,
                      reads=(xs_b,), writes=(dst_b[oc],), name=f"{tag}fw{tb}_{oc}")
        self.advance_x(dst_ap, dst_b)

    def final_combine(self, tag, dst_ap, dst_b, Arow, Arow_b, Qrow, Qrow_b,
                      mlp_bank_fn):
        S = self.S
        ones_row, ones_row_b = self.sbconst("ones_row")
        pbc, pbc_b = self.pb[4], self.pb_b[4]
        for tb in range(NTB):
            sl = slice(tb * 512, (tb + 1) * 512)
            for ri, (ra, rb) in enumerate(((Arow, Arow_b), (Qrow, Qrow_b))):
                S.op("pe", lambda e, ra=ra, sl=sl: e.matmul(
                        pbc[:, :], ones_row[:], ra[0:1, sl], start=True, stop=True),
                     reads=(rb, ones_row_b), writes=(pbc_b,), name=f"{tag}cb{tb}_{ri}")
                S.op("dve", lambda e, ri=ri: e.tensor_copy(self.pqs[ri][:], pbc[:, :]),
                     reads=(pbc_b,), writes=(self.pqs_b[ri],), name=f"{tag}ce{tb}_{ri}")
            for oc in range(NK):
                bank, bank_b = mlp_bank_fn(tb, oc)
                hw = (oc >> 1) & 1
                xs = self.fw[oc % 2][:, hw * 512:(hw + 1) * 512]
                xs_b = self.fw_b[oc % 2]
                S.dma("sp", xs, self.cur_ap[oc * 128:(oc + 1) * 128, sl],
                      reads=(self.cur_b[oc],), writes=(xs_b,), name=f"{tag}cx{tb}_{oc}")
                S.op("dve", lambda e, xs=xs: e.tensor_mul(
                        self.tmp[0][:, :], xs, self.pqs[0][:]),
                     reads=(xs_b, self.pqs_b[0]), writes=(self.tmp_b[0],),
                     name=f"{tag}c1{tb}_{oc}")
                S.op("dve", lambda e: e.tensor_add(
                        self.tmp[0][:, :], self.tmp[0][:, :], self.pqs[1][:]),
                     reads=(self.tmp_b[0], self.pqs_b[1]), writes=(self.tmp_b[0],),
                     name=f"{tag}c2{tb}_{oc}")
                S.op("dve", lambda e, xs=xs, bank=bank: e.tensor_add(
                        xs, self.tmp[0][:, :], bank[:, :]),
                     reads=(self.tmp_b[0], bank_b), writes=(xs_b,),
                     name=f"{tag}c3{tb}_{oc}")
                S.dma("sp", dst_ap[oc * 128:(oc + 1) * 128, sl], xs,
                      reads=(xs_b,), writes=(dst_b[oc],), name=f"{tag}cw{tb}_{oc}")

    def slot_l1ffn(self, last):
        tag = "L1f"
        S = self.S
        dst_ap, dst_b = self.next_x(last)
        self.stats_pass(tag)
        self.derive_mu_r(tag)
        mu, mu_b = self.s4s(2)
        var, var_b = self.s4s(3)
        r, r_b = self.s4s(4)
        t5, t5_b = self.s4s(5)
        t6, t6_b = self.s4s(6)
        S.op("dve", lambda e: e.tensor_mul(t5, var, r), reads=(var_b, r_b),
             writes=(t5_b,), name=f"{tag}s1")
        S.op("dve", lambda e: e.tensor_mul(t5, t5, r), reads=(t5_b, r_b),
             writes=(t5_b,), name=f"{tag}s2")
        S.op("dve", lambda e: e.tensor_scalar_add(t5, t5, 1e-6), reads=(t5_b,),
             writes=(t5_b,), name=f"{tag}s3")
        S.op("act", lambda e: e.activation(t5, t5, AF.Sqrt), reads=(t5_b,),
             writes=(t5_b,), name=f"{tag}s4")
        S.op("dve", lambda e: e.reciprocal(t5, t5), reads=(t5_b,),
             writes=(t5_b,), name=f"{tag}s5")
        S.op("dve", lambda e: e.tensor_mul(t5, t5, r), reads=(t5_b, r_b),
             writes=(t5_b,), name=f"{tag}s6")
        S.op("dve", lambda e: e.tensor_mul(t6, mu, t5), reads=(mu_b, t5_b),
             writes=(t6_b,), name=f"{tag}s7")
        self.to_row(5, 0, tag)
        self.to_row(6, 1, tag)
        self.xhat_inplace(tag)
        g1, g1_b = self.sbconst("L1f_g1")
        g2, g2_b = self.sbconst("L1f_g2")
        for k in range(NK):
            xh, xh_b = self.biga_slot(k)
            S.op("act", lambda e, xh=xh, k=k: e.activation(
                    xh, xh, AF.Copy, scale=g1[:, k:k + 1]),
                 reads=(xh_b, g1_b), writes=(xh_b,), name=f"{tag}g1_{k}")

        qscale = float(128.0 ** -0.5)
        def evac_qkv(oc, tb, pa, pab):
            stj, stj_b = self.stg[tb % 2], self.stg_b[tb % 2]
            sc = qscale if oc < 16 else 1.0
            S.op("dve", lambda e, pa=pa, stj=stj, sc=sc: e.tensor_scalar_mul(
                    stj[:], pa[:, :], sc),
                 reads=(pab,), writes=(stj_b,), name=f"{tag}eq{oc}_{tb}")
            S.dma("sp", self.qkvT[oc * 128:(oc + 1) * 128, tb * 512:(tb + 1) * 512],
                  stj[:], reads=(stj_b,), writes=(self.qkvT_b,),
                  name=f"{tag}sq{oc}_{tb}")
        self.proj(tag + "q", "L1f_WqT", 0, 16, D, evac_qkv)
        self.proj(tag + "k", "L1f_WkT", 16, 32, D, evac_qkv, o_w_off=16)
        self.proj(tag + "v", "L1f_WvT", 32, 48, D, evac_qkv, o_w_off=32)

        self.dump("l1f_attnT_pre", self.attnT, [self.attnT_b], (D, T), BF16)
        A_ap, A_b = self.win("L1f_A")
        Asb, Asb_b = self.biga_slot(16)
        av = A_ap.rearrange("(c p) o -> p c o", p=128)
        S.dma("pool", Asb[:, 0:256].rearrange("p (c o) -> p c o", o=16), av,
              reads=(A_b,), writes=(Asb_b,), name=f"{tag}lA")
        Wo_ap, Wo_b = self.win("L1f_Wo")
        wov = Wo_ap.rearrange("(c p) o -> p c o", p=128)
        pW, pW_b = self.pb[4], self.pb_b[4]
        for f in range(16):
            j = f % 2
            wsl, wsl_b = self.wslab[j], self.wslab_b[j]
            S.dma("pool", wsl[:].rearrange("p (c o) -> p c o", o=128),
                  wov[:, :, f * 128:(f + 1) * 128],
                  reads=(Wo_b,), writes=(wsl_b,), name=f"{tag}w2l{f}")
            for o in range(16):
                S.op("pe", lambda e, o=o, wsl=wsl: e.matmul(
                        pW[0:128, 0:16], wsl[:, o * 128:(o + 1) * 128],
                        Asb[:, o * 16:(o + 1) * 16],
                        start=(o == 0), stop=(o == 15)),
                     reads=(wsl_b, Asb_b), writes=(pW_b,), name=f"{tag}w2m{f}_{o}")
            S.op("act", lambda e, f=f: e.activation(
                    self.W2sb[:, f * 16:(f + 1) * 16], pW[0:128, 0:16], AF.Copy),
                 reads=(pW_b,), writes=(self.W2sb_b,), name=f"{tag}w2e{f}")

        dmk, dmk_b = self.sbconst("diag_mask")
        dmkT, dmkT_b = self.sbconst("diag_maskT")
        identb, identb_b = self.sbconst("ident_bf")
        ones_row_bf, ones_row_bf_b = self.sbconst("ones_row_bf")

        def attn_head_ops(h):
            j = h % 2
            QT, QT_b = self.biga_slot(16 + j * 3 + 0)
            KT_, KT_b = self.biga_slot(16 + j * 3 + 1)
            VT, VT_b = self.biga_slot(16 + j * 3 + 2)
            for t_, tb_, row0 in ((QT, QT_b, h), (KT_, KT_b, 16 + h),
                                  (VT, VT_b, 32 + h)):
                S.dma("sp", t_, self.qkvT[row0 * 128:(row0 + 1) * 128, :],
                      reads=(self.qkvT_b,), writes=(tb_,), name=f"{tag}h{h}ld")
            pscs = [(self.pb[0], self.pb_b[0]), (self.pb[4], self.pb_b[4])] \
                if j == 0 else [(self.pb[2], self.pb_b[2]), (self.pb[5], self.pb_b[5])]
            psc, psc_b = pscs[0]
            pvx, pvx_b = (self.pb[1], self.pb_b[1]) if j == 0 else (self.pb[3], self.pb_b[3])
            pvt, pvt_b = self.pbf[j], self.pbf_b[j]
            fwj, fwj_b = self.fw[j], self.fw_b[j]
            negm = self.cols[:, j * 128 + 0:j * 128 + 16]
            Zc = self.cols[:, j * 128 + 16:j * 128 + 32]
            lnZ = self.cols[:, j * 128 + 32:j * 128 + 48]
            bcol = self.cols[:, j * 128 + 48:j * 128 + 64]
            colsb = self.cols_b[j]
            atn, atn_b = self.biga_slot(22 + j)
            for c in range(NTT):
                def vstep(c=c):
                    S.op("pe", lambda e, c=c: e.transpose(
                            pvt[:, 0:128], VT[:, c * 128:(c + 1) * 128], identb[:]),
                         reads=(VT_b, identb_b), writes=(pvt_b,), name=f"{tag}h{h}v{c}t")
                    S.op("dve", lambda e, c=c: e.tensor_copy(
                            VT[:, c * 128:(c + 1) * 128], pvt[:, 0:128]),
                         reads=(pvt_b,), writes=(VT_b,), name=f"{tag}h{h}v{c}e")
                yield vstep
            for qt in range(NTT):
                def astep(qt=qt):
                    L = (qt + 1) * 128
                    nb = qt // 4 + 1
                    for jb in range(nb):
                        psc, psc_b = pscs[(qt * 5 + jb) % 2]
                        w = 512 if jb < qt // 4 else (qt % 4 + 1) * 128
                        S.op("pe", lambda e, qt=qt, jb=jb, w=w, psc=psc: e.matmul(
                                psc[:, 0:w], QT[:, qt * 128:(qt + 1) * 128],
                                KT_[:, jb * 512:jb * 512 + w],
                                start=True, stop=True),
                             reads=(QT_b, KT_b), writes=(psc_b,),
                             name=f"{tag}h{h}a{qt}_{jb}")
                        if jb < qt // 4:
                            S.op("dve", lambda e, jb=jb, w=w, psc=psc: e.tensor_copy(
                                    fwj[:, jb * 512:jb * 512 + w], psc[:, 0:w]),
                                 reads=(psc_b,), writes=(fwj_b,),
                                 name=f"{tag}h{h}ae{qt}_{jb}")
                        else:
                            if w > 128:
                                S.op("dve", lambda e, jb=jb, w=w, psc=psc: e.tensor_copy(
                                        fwj[:, jb * 512:jb * 512 + w - 128],
                                        psc[:, 0:w - 128]),
                                     reads=(psc_b,), writes=(fwj_b,),
                                     name=f"{tag}h{h}ap{qt}_{jb}")
                            S.op("dve", lambda e, qt=qt, w=w, psc=psc: e.tensor_add(
                                    fwj[:, qt * 128:(qt + 1) * 128],
                                    psc[:, w - 128:w], dmk[:]),
                                 reads=(psc_b, dmk_b), writes=(fwj_b,),
                                 name=f"{tag}h{h}ad{qt}")
                    S.op("dve", lambda e, qt=qt, L=L: e.reduce_max(
                            negm[:, qt:qt + 1], fwj[:, 0:L], axis=AX.X, negate=True),
                         reads=(fwj_b,), writes=(colsb,), name=f"{tag}h{h}am{qt}")
                    S.op("act", lambda e, qt=qt, L=L: e.activation(
                            fwj[:, 0:L], fwj[:, 0:L], AF.Exp,
                            bias=negm[:, qt:qt + 1], accum_out=Zc[:, qt:qt + 1]),
                         reads=(fwj_b, colsb), writes=(fwj_b, colsb),
                         name=f"{tag}h{h}az{qt}")
                yield astep
            def brows():
                S.op("act", lambda e: e.activation(lnZ, Zc, AF.Ln),
                     reads=(colsb,), writes=(colsb,), name=f"{tag}h{h}ln")
                S.op("dve", lambda e: e.tensor_sub(bcol, negm, lnZ),
                     reads=(colsb,), writes=(colsb,), name=f"{tag}h{h}bc")
                self.cols_to_row(bcol, colsb, 16, 2 if j == 0 else 1, tag + f"h{h}")
                brow, brow_b = self.row(2 if j == 0 else 1)
                hilo = self.SR[0:1, (3 if j == 0 else 4) * 2048:
                               (4 if j == 0 else 5) * 2048].bitcast(BF16)
                hilo_b = self.rows_b[3 if j == 0 else 4]
                S.op("act", lambda e: e.activation(hilo[0:1, 0:2048], brow, AF.Copy),
                     reads=(brow_b,), writes=(hilo_b,), name=f"{tag}h{h}hi")
                S.op("dve", lambda e: e.tensor_sub(self.tmp[j][0:1, 0:512],
                                                   brow[0:1, 0:512],
                                                   hilo[0:1, 0:512]),
                     reads=(brow_b, hilo_b), writes=(self.tmp_b[j],),
                     name=f"{tag}h{h}lo0")
                for pc in range(4):
                    if pc > 0:
                        S.op("dve", lambda e, pc=pc: e.tensor_sub(
                                self.tmp[j][0:1, 0:512],
                                brow[0:1, pc * 512:(pc + 1) * 512],
                                hilo[0:1, pc * 512:(pc + 1) * 512]),
                             reads=(brow_b, hilo_b), writes=(self.tmp_b[j],),
                             name=f"{tag}h{h}lo{pc}")
                    S.op("act", lambda e, pc=pc: e.activation(
                            hilo[0:1, 2048 + pc * 512:2048 + (pc + 1) * 512],
                            self.tmp[j][0:1, 0:512], AF.Copy),
                         reads=(self.tmp_b[j],), writes=(hilo_b,),
                         name=f"{tag}h{h}loc{pc}")
            yield brows
            hilo = self.SR[0:1, (3 if j == 0 else 4) * 2048:
                           (4 if j == 0 else 5) * 2048].bitcast(BF16)
            hilo_b = self.rows_b[3 if j == 0 else 4]
            for jb in range(4):
                for c in range(4 * jb + 4):
                    def bstep(jb=jb, c=c):
                        qsl = slice(jb * 512, (jb + 1) * 512)
                        psc, psc_b = pscs[c % 2]
                        S.op("pe", lambda e, c=c, qsl=qsl, psc=psc: e.matmul(
                                psc[:, 0:512], KT_[:, c * 128:(c + 1) * 128],
                                QT[:, qsl], start=True, stop=False),
                             reads=(QT_b, KT_b), writes=(psc_b,),
                             name=f"{tag}h{h}b{jb}_{c}")
                        S.op("pe", lambda e, jb=jb, psc=psc: e.matmul(
                                psc[:, 0:512], ones_row_bf[:],
                                hilo[0:1, jb * 512:(jb + 1) * 512],
                                start=False, stop=False),
                             reads=(hilo_b, ones_row_bf_b), writes=(psc_b,),
                             name=f"{tag}h{h}bh{jb}_{c}")
                        S.op("pe", lambda e, jb=jb, psc=psc: e.matmul(
                                psc[:, 0:512], ones_row_bf[:],
                                hilo[0:1, 2048 + jb * 512:2048 + (jb + 1) * 512],
                                start=False, stop=True),
                             reads=(hilo_b, ones_row_bf_b), writes=(psc_b,),
                             name=f"{tag}h{h}bl{jb}_{c}")
                        d = c - 4 * jb
                        pT, pT_b = self.stg[2 * j + c % 2], self.stg_b[2 * j + c % 2]
                        if d >= 0:
                            S.op("dve", lambda e, d=d, psc=psc: e.tensor_add(
                                    psc[:, d * 128:(d + 1) * 128],
                                    psc[:, d * 128:(d + 1) * 128], dmkT[:]),
                                 reads=(psc_b, dmkT_b), writes=(psc_b,),
                                 name=f"{tag}h{h}bm{jb}_{c}")
                            if d > 0:
                                S.op("dve", lambda e, d=d, pT=pT: e.memset(
                                        pT[:, 0:d * 128], 0.0),
                                     writes=(pT_b,), name=f"{tag}h{h}bz{jb}_{c}")
                            S.op("act", lambda e, d=d, pT=pT, psc=psc: e.activation(
                                    pT[:, d * 128:512], psc[:, d * 128:512], AF.Exp),
                                 reads=(psc_b,), writes=(pT_b,),
                                 name=f"{tag}h{h}be{jb}_{c}")
                        else:
                            S.op("act", lambda e, pT=pT, psc=psc: e.activation(
                                    pT[:], psc[:, 0:512], AF.Exp),
                                 reads=(psc_b,), writes=(pT_b,),
                                 name=f"{tag}h{h}be{jb}_{c}")
                        S.op("pe", lambda e, c=c, jb=jb, pT=pT: e.matmul(
                                pvx[:, 0:512], VT[:, c * 128:(c + 1) * 128], pT[:],
                                start=(c == 0), stop=(c == 4 * jb + 3)),
                             reads=(VT_b, pT_b), writes=(pvx_b,),
                             name=f"{tag}h{h}bp{jb}_{c}")
                    yield bstep
                def oevac(jb=jb):
                    S.op("dve", lambda e, jb=jb: e.tensor_copy(
                            atn[:, jb * 512:(jb + 1) * 512], pvx[:, 0:512]),
                         reads=(pvx_b,), writes=(atn_b,), name=f"{tag}h{h}oe{jb}")
                yield oevac
            def final():
                S.dma("sp", self.attnT[h * 128:(h + 1) * 128, :], atn,
                      reads=(atn_b,), writes=(self.attnT_b,), name=f"{tag}h{h}st")
            yield final

        def run_pipeline(nheads, prime):
            nxt = [0]
            active = []
            def start():
                if nxt[0] < nheads:
                    active.append(attn_head_ops(nxt[0]))
                    nxt[0] += 1
            start()
            for _ in range(prime):
                try:
                    next(active[0])()
                except StopIteration:
                    break
            start()
            while active:
                for g in list(active):
                    try:
                        next(g)()
                    except StopIteration:
                        active.remove(g)
                        start()
        run_pipeline(16, 2 * NTT)

        rat, rat_b = self.small[3], self.small_b[3]
        S.op("dve", lambda e: e.reciprocal(rat[:, 0:16], g1[:]),
             reads=(g1_b,), writes=(rat_b,), name=f"{tag}rt1")
        S.op("dve", lambda e: e.tensor_mul(rat[:, 0:16], rat[:, 0:16], g2[:]),
             reads=(rat_b, g2_b), writes=(rat_b,), name=f"{tag}rt2")
        for k in range(NK):
            xh, xh_b = self.biga_slot(k)
            S.op("act", lambda e, xh=xh, k=k: e.activation(
                    xh, xh, AF.Copy, scale=rat[:, k:k + 1]),
                 reads=(xh_b, rat_b), writes=(xh_b,), name=f"{tag}g2_{k}")

        Asb, Asb_b = self.biga_slot(16)
        S.dma("pool", Asb[:, 0:256].rearrange("p (c o) -> p c o", o=16), av,
              reads=(A_b,), writes=(Asb_b,), name=f"{tag}lA2")
        modt, modt_b = self.sr32(0, 16)
        hAt, hAt_b = self.sr32(1, 16)
        pW4, pW4_b = self.pb[4], self.pb_b[4]
        pW5, pW5_b = self.pb[5], self.pb_b[5]
        for tb in range(NTB):
            sl = slice(tb * 512, (tb + 1) * 512)
            for c in range(NK):
                at, at_b2 = self.biga_slot(17 + (c % 2) * 3)
                S.dma("sp", at[:, 0:512], self.attnT[c * 128:(c + 1) * 128, sl],
                      reads=(self.attnT_b,), writes=(at_b2,), name=f"{tag}ml{tb}_{c}")
                S.op("pe", lambda e, c=c, at=at: e.matmul(
                        pW4[0:16, 0:512], self.W2sb[:, c * 16:(c + 1) * 16],
                        at[:, 0:512], start=(c == 0), stop=(c == NK - 1)),
                     reads=(self.W2sb_b, at_b2), writes=(pW4_b,),
                     name=f"{tag}mm{tb}_{c}")
                xh, xh_b = self.biga_slot(c)
                S.op("pe", lambda e, c=c, sl=sl, xh=xh: e.matmul(
                        pW5[0:16, 0:512], Asb[:, c * 16:(c + 1) * 16], xh[:, sl],
                        start=(c == 0), stop=(c == NK - 1)),
                     reads=(Asb_b, xh_b), writes=(pW5_b,), name=f"{tag}ha{tb}_{c}")
            S.op("act", lambda e, sl=sl: e.activation(
                    modt[0:16, sl], pW4[0:16, 0:512], AF.Tanh),
                 reads=(pW4_b,), writes=(modt_b,), name=f"{tag}mt{tb}")
            S.op("dve", lambda e, sl=sl: e.tensor_copy(hAt[0:16, sl], pW5[0:16, 0:512]),
                 reads=(pW5_b,), writes=(hAt_b,), name=f"{tag}he{tb}")
        mB, mB_b = self.biga_slot(23)
        S.op("dve", lambda e: e.tensor_scalar_mul(modt[0:16, :], modt[0:16, :], 0.1),
             reads=(modt_b,), writes=(modt_b,), name=f"{tag}msc")
        S.op("dve", lambda e: e.tensor_mul(mB[0:16, :], modt[0:16, :], hAt[0:16, :]),
             reads=(modt_b, hAt_b), writes=(mB_b,), name=f"{tag}mb")

        self.dump("l1f_mB", mB, [mB_b], (128, 2048), BF16)
        self.dump("l1f_mod", self.SR[0:80, :], [modt_b, hAt_b], (80, 10240), F32)
        WinT, WinT_b = self.win("L1f_WinT")
        WupT, WupT_b = self.win("L1f_WupT")
        B_ap, B_b = self.win("L1f_B")
        winv = WinT.rearrange("(c p) o -> p c o", p=128)
        wupv = WupT.rearrange("(c p) o -> p c o", p=128)
        for oc in range(64):
            wsl0, wsl0_b = self.wslab[0], self.wslab_b[0]
            wsl1, wsl1_b = self.wslab[1], self.wslab_b[1]
            S.dma("pool", wsl0[:].rearrange("p (c o) -> p c o", o=128),
                  winv[:, :, oc * 128:(oc + 1) * 128],
                  reads=(WinT_b,), writes=(wsl0_b,), name=f"{tag}gl{oc}")
            S.dma("pool", wsl1[:].rearrange("p (c o) -> p c o", o=128),
                  wupv[:, :, oc * 128:(oc + 1) * 128],
                  reads=(WupT_b,), writes=(wsl1_b,), name=f"{tag}ul{oc}")
            Bs, Bs_b = self.smallb[6 + oc % 2], self.smallb_b[6 + oc % 2]
            S.dma("pool", Bs[0:16, :], B_ap[:, oc * 128:(oc + 1) * 128],
                  reads=(B_b,), writes=(Bs_b,), name=f"{tag}bl{oc}")
            for half in range(2):
                gbanks = [(self.pb[0], self.pb_b[0]), (self.pb[1], self.pb_b[1])]
                ubanks = [(self.pb[2], self.pb_b[2]), (self.pb[3], self.pb_b[3])]
                for k in range(NK):
                    for t2 in range(2):
                        tb = half * 2 + t2
                        sl = slice(tb * 512, (tb + 1) * 512)
                        xh, xh_b = self.biga_slot(k)
                        gb, gb_b = gbanks[t2]
                        ub, ub_b = ubanks[t2]
                        S.op("pe", lambda e, k=k, sl=sl, xh=xh, gb=gb: e.matmul(
                                gb[:, :], wsl0[:, k * 128:(k + 1) * 128], xh[:, sl],
                                start=(k == 0), stop=False),
                             reads=(wsl0_b, xh_b), writes=(gb_b,),
                             name=f"{tag}gm{oc}_{tb}_{k}")
                        S.op("pe", lambda e, k=k, sl=sl, xh=xh, ub=ub: e.matmul(
                                ub[:, :], wsl1[:, k * 128:(k + 1) * 128], xh[:, sl],
                                start=(k == 0), stop=(k == NK - 1)),
                             reads=(wsl1_b, xh_b), writes=(ub_b,),
                             name=f"{tag}um{oc}_{tb}_{k}")
                for t2 in range(2):
                    tb = half * 2 + t2
                    sl = slice(tb * 512, (tb + 1) * 512)
                    gb, gb_b = gbanks[t2]
                    ub, ub_b = ubanks[t2]
                    S.op("pe", lambda e, sl=sl, gb=gb, Bs=Bs: e.matmul(
                            gb[:, :], Bs[0:16, :], mB[0:16, sl],
                            start=False, stop=True),
                         reads=(Bs_b, mB_b), writes=(gb_b,), name=f"{tag}gb{oc}_{tb}")
                    sg, sg_b = self.stg[t2], self.stg_b[t2]
                    hd, hd_b = self.stg[2 + t2], self.stg_b[2 + t2]
                    S.op("act", lambda e, gb=gb, sg=sg: e.activation(
                            sg[:], gb[:, :], AF.Silu),
                         reads=(gb_b,), writes=(sg_b,), name=f"{tag}si{oc}_{tb}")
                    S.op("dve", lambda e, ub=ub, sg=sg, hd=hd: e.tensor_mul(
                            hd[:], ub[:, :], sg[:]),
                         reads=(ub_b, sg_b), writes=(hd_b,), name=f"{tag}hd{oc}_{tb}")
                    S.dma("sp", self.hidT[oc * 128:(oc + 1) * 128, sl], hd[:],
                          reads=(hd_b,), writes=(self.hidT_b,),
                          name=f"{tag}hs{oc}_{tb}")

        S.op("dve", lambda e: e.tensor_mul(t6, mu, r), reads=(mu_b, r_b),
             writes=(t6_b,), name=f"{tag}qf1")
        S.op("dve", lambda e: e.tensor_scalar_mul(t6, t6, -1.0), reads=(t6_b,),
             writes=(t6_b,), name=f"{tag}qf2")
        self.to_row(4, 0, tag + "A")
        self.to_row(6, 1, tag + "Q")
        Arow, Arow_b = self.row(0)
        Qrow, Qrow_b = self.row(1)
        S.op("dve", lambda e: e.tensor_scalar_add(Arow, Arow, 1.0),
             reads=(Arow_b,), writes=(Arow_b,), name=f"{tag}A1")
        WdnT, WdnT_b = self.win("L1f_WdnT")
        wdv = WdnT.rearrange("(c p) o -> p c o", p=128)

        def hid_chunk(k):
            sslot, sb = self.biga_slot(k // 4)
            return sslot[:, (k % 4) * 512:(k % 4 + 1) * 512], sb

        def mlp_bank(tb, oc):
            sl = slice(tb * 512, (tb + 1) * 512)
            if oc == 0:
                for sslot in range(NK):
                    ba, bb = self.biga_slot(sslot)
                    src = self.hidT.rearrange("(c p) t -> p c t", p=128)[
                        :, sslot * 4:(sslot + 1) * 4, sl]
                    S.dma("sp", ba.rearrange("p (c t) -> p c t", t=512)[:, 0:4, :],
                          src, reads=(self.hidT_b,), writes=(bb,),
                          name=f"{tag}dl{tb}_{sslot}")
            bank, bank_b = (self.pb[0], self.pb_b[0]) if oc % 2 == 0 else \
                           (self.pb[1], self.pb_b[1])
            for si in range(4):
                j2 = (tb * 64 + oc * 4 + si) % 2
                wsl, wsl_b = self.wslab[j2], self.wslab_b[j2]
                S.dma("pool", wsl[:].rearrange("p (c o) -> p c o", o=128),
                      wdv[:, si * 16:(si + 1) * 16, oc * 128:(oc + 1) * 128],
                      reads=(WdnT_b,), writes=(wsl_b,), name=f"{tag}dw{tb}_{oc}_{si}")
                for kk in range(16):
                    k = si * 16 + kk
                    ha, hb = hid_chunk(k)
                    S.op("pe", lambda e, kk=kk, k=k, ha=ha, wsl=wsl, bank=bank:
                         e.matmul(bank[:, :], wsl[:, kk * 128:(kk + 1) * 128], ha,
                                  start=(k == 0), stop=(k == 63)),
                         reads=(wsl_b, hb), writes=(bank_b,),
                         name=f"{tag}dm{tb}_{oc}_{k}")
            return bank, bank_b

        self.final_combine(tag, dst_ap, dst_b, Arow, Arow_b, Qrow, Qrow_b, mlp_bank)
        self.advance_x(dst_ap, dst_b)


    def rope_evac(self, tag, oc, tb, pa, pab, cosv, cosv_b, sinv, sinv_b,
                  rotP, rotP_b, scale, dst_rows):
        S = self.S
        sl = slice(tb * 512, (tb + 1) * 512)
        q_sb, q_sb_b = self.stg[0 + tb % 2], self.stg_b[0 + tb % 2]
        S.op("dve", lambda e: e.tensor_scalar_mul(q_sb[:], pa[:, :], scale),
             reads=(pab,), writes=(q_sb_b,), name=f"{tag}re{oc}_{tb}")
        prp, prp_b = self.pb[4 + tb % 2], self.pb_b[4 + tb % 2]
        S.op("pe", lambda e: e.matmul(prp[:, 0:512], rotP[:], q_sb[:],
                                      start=True, stop=True),
             reads=(rotP_b, q_sb_b), writes=(prp_b,), name=f"{tag}rp{oc}_{tb}")
        t0, t0_b = self.tmp[tb % 2], self.tmp_b[tb % 2]
        S.op("dve", lambda e: e.tensor_mul(t0[:], q_sb[:], cosv[:, sl]),
             reads=(q_sb_b, cosv_b), writes=(t0_b,), name=f"{tag}rc{oc}_{tb}")
        hd, hd_b = self.stg[2 + tb % 2], self.stg_b[2 + tb % 2]
        S.op("dve", lambda e: e.tensor_mul(hd[:], prp[:, 0:512], sinv[:, sl]),
             reads=(prp_b, sinv_b), writes=(hd_b,), name=f"{tag}rs{oc}_{tb}")
        S.op("dve", lambda e: e.tensor_add(hd[:], t0[:], hd[:]),
             reads=(t0_b, hd_b), writes=(hd_b,), name=f"{tag}ra{oc}_{tb}")
        S.dma("sp", self.qkvT[dst_rows * 128:(dst_rows + 1) * 128, sl], hd[:],
              reads=(hd_b,), writes=(self.qkvT_b,), name=f"{tag}rw{oc}_{tb}")

    def slot_l2attn(self, last):
        tag = "L2a"
        S = self.S
        dst_ap, dst_b = self.next_x(last)
        self.stats_pass(tag)
        self.make_s_ms_rows(tag, "ln")
        self.xhat_inplace(tag)
        cosv = self.biga[:, 16 * 2048:18 * 2048].bitcast(F32)
        sinv = self.biga[:, 18 * 2048:20 * 2048].bitcast(F32)
        cosv_b, sinv_b = self.biga_b[16], self.biga_b[18]
        lc, lc_b = self.win("l2_cos")
        ls, ls_b = self.win("l2_sin")
        S.dma("sp", cosv, lc, reads=(lc_b,), writes=(cosv_b,), name=f"{tag}lc")
        S.dma("sp", sinv, ls, reads=(ls_b,), writes=(sinv_b,), name=f"{tag}ls")
        rotP, rotP_b = self.sbconst("l2_rotP")
        qsc = float(128.0 ** -0.5)

        def evac_q(oc, tb, pa, pab):
            self.rope_evac(tag, oc, tb, pa, pab, cosv, cosv_b, sinv, sinv_b,
                           rotP, rotP_b, qsc, oc)
        self.proj(tag + "q", "L2a_Wq", 0, 16, D, evac_q)

        def evac_kv(oc, tb, pa, pab):
            if oc == 16:
                self.rope_evac(tag, oc, tb, pa, pab, cosv, cosv_b, sinv, sinv_b,
                               rotP, rotP_b, 1.0, 16)
            else:
                stj, stj_b = self.stg[tb % 2], self.stg_b[tb % 2]
                S.op("dve", lambda e, pa=pa, stj=stj: e.tensor_copy(stj[:], pa[:, :]),
                     reads=(pab,), writes=(stj_b,), name=f"{tag}ev{oc}_{tb}")
                S.dma("sp", self.qkvT[17 * 128:18 * 128,
                                      tb * 512:(tb + 1) * 512], stj[:],
                      reads=(stj_b,), writes=(self.qkvT_b,), name=f"{tag}sv{oc}_{tb}")
        self.proj(tag + "kv", "L2a_Wkv", 16, 18, D, evac_kv, o_w_off=16)

        dmk, dmk_b = self.sbconst("diag_mask")
        dmkT, dmkT_b = self.sbconst("diag_maskT")
        identb, identb_b = self.sbconst("ident_bf")
        KT_, KT_b = self.biga_slot(20)
        VT, VT_b = self.biga_slot(21)
        S.dma("sp", KT_, self.qkvT[16 * 128:17 * 128, :],
              reads=(self.qkvT_b,), writes=(KT_b,), name=f"{tag}ldk")
        S.dma("sp", VT, self.qkvT[17 * 128:18 * 128, :],
              reads=(self.qkvT_b,), writes=(VT_b,), name=f"{tag}ldv")
        pvt0, pvt0_b = self.pbf[0], self.pbf_b[0]
        for c in range(NTT):
            S.op("pe", lambda e, c=c: e.transpose(
                    pvt0[:, 0:128], VT[:, c * 128:(c + 1) * 128], identb[:]),
                 reads=(VT_b, identb_b), writes=(pvt0_b,), name=f"{tag}v{c}t")
            S.op("dve", lambda e, c=c: e.tensor_copy(
                    VT[:, c * 128:(c + 1) * 128], pvt0[:, 0:128]),
                 reads=(pvt0_b,), writes=(VT_b,), name=f"{tag}v{c}e")

        def attn_head_ops(h):
            j = h % 2
            QT, QT_b = self.biga_slot(16 + j)
            S.dma("sp", QT, self.qkvT[h * 128:(h + 1) * 128, :],
                  reads=(self.qkvT_b,), writes=(self.biga_b[16 + j],),
                  name=f"{tag}h{h}ld")
            QT_b = self.biga_b[16 + j]
            pscs = [(self.pb[0], self.pb_b[0]), (self.pb[4], self.pb_b[4])] \
                if j == 0 else [(self.pb[2], self.pb_b[2]), (self.pb[5], self.pb_b[5])]
            psc, psc_b = pscs[0]
            pvx, pvx_b = (self.pb[1], self.pb_b[1]) if j == 0 else (self.pb[3], self.pb_b[3])
            fwj, fwj_b = self.fw[j], self.fw_b[j]
            negm = self.cols[:, j * 128 + 0:j * 128 + 16]
            Zc = self.cols[:, j * 128 + 16:j * 128 + 32]
            lnZ = self.cols[:, j * 128 + 32:j * 128 + 48]
            bcol = self.cols[:, j * 128 + 48:j * 128 + 64]
            colsb = self.cols_b[j]
            atn, atn_b = self.biga_slot(22 + j)
            for qt in range(NTT):
                def astep(qt=qt):
                    L = (qt + 1) * 128
                    nb = qt // 4 + 1
                    for jb in range(nb):
                        psc, psc_b = pscs[(qt * 5 + jb) % 2]
                        w = 512 if jb < qt // 4 else (qt % 4 + 1) * 128
                        S.op("pe", lambda e, qt=qt, jb=jb, w=w, psc=psc: e.matmul(
                                psc[:, 0:w], QT[:, qt * 128:(qt + 1) * 128],
                                KT_[:, jb * 512:jb * 512 + w],
                                start=True, stop=True),
                             reads=(QT_b, KT_b), writes=(psc_b,),
                             name=f"{tag}h{h}a{qt}_{jb}")
                        if jb < qt // 4:
                            S.op("dve", lambda e, jb=jb, w=w, psc=psc: e.tensor_copy(
                                    fwj[:, jb * 512:jb * 512 + w], psc[:, 0:w]),
                                 reads=(psc_b,), writes=(fwj_b,),
                                 name=f"{tag}h{h}ae{qt}_{jb}")
                        else:
                            if w > 128:
                                S.op("dve", lambda e, jb=jb, w=w, psc=psc: e.tensor_copy(
                                        fwj[:, jb * 512:jb * 512 + w - 128],
                                        psc[:, 0:w - 128]),
                                     reads=(psc_b,), writes=(fwj_b,),
                                     name=f"{tag}h{h}ap{qt}_{jb}")
                            S.op("dve", lambda e, qt=qt, w=w, psc=psc: e.tensor_add(
                                    fwj[:, qt * 128:(qt + 1) * 128],
                                    psc[:, w - 128:w], dmk[:]),
                                 reads=(psc_b, dmk_b), writes=(fwj_b,),
                                 name=f"{tag}h{h}ad{qt}")
                    S.op("dve", lambda e, qt=qt, L=L: e.reduce_max(
                            negm[:, qt:qt + 1], fwj[:, 0:L], axis=AX.X, negate=True),
                         reads=(fwj_b,), writes=(colsb,), name=f"{tag}h{h}am{qt}")
                    S.op("act", lambda e, qt=qt, L=L: e.activation(
                            fwj[:, 0:L], fwj[:, 0:L], AF.Exp,
                            bias=negm[:, qt:qt + 1], accum_out=Zc[:, qt:qt + 1]),
                         reads=(fwj_b, colsb), writes=(fwj_b, colsb),
                         name=f"{tag}h{h}az{qt}")
                yield astep
            def brows():
                S.op("act", lambda e: e.activation(lnZ, Zc, AF.Ln),
                     reads=(colsb,), writes=(colsb,), name=f"{tag}h{h}ln")
                S.op("dve", lambda e: e.tensor_sub(bcol, negm, lnZ),
                     reads=(colsb,), writes=(colsb,), name=f"{tag}h{h}bc")
                self.cols_to_row(bcol, colsb, 16, 2 if j == 0 else 1, tag + f"h{h}")
                brow, brow_b = self.row(2 if j == 0 else 1)
                hilo = self.SR[0:1, (3 if j == 0 else 4) * 2048:
                               (4 if j == 0 else 5) * 2048].bitcast(BF16)
                hilo_b = self.rows_b[3 if j == 0 else 4]
                S.op("act", lambda e: e.activation(hilo[0:1, 0:2048], brow, AF.Copy),
                     reads=(brow_b,), writes=(hilo_b,), name=f"{tag}h{h}hi")
                for pc in range(4):
                    S.op("dve", lambda e, pc=pc: e.tensor_sub(
                            self.tmp[j][0:1, 0:512],
                            brow[0:1, pc * 512:(pc + 1) * 512],
                            hilo[0:1, pc * 512:(pc + 1) * 512]),
                         reads=(brow_b, hilo_b), writes=(self.tmp_b[j],),
                         name=f"{tag}h{h}lo{pc}")
                    S.op("act", lambda e, pc=pc: e.activation(
                            hilo[0:1, 2048 + pc * 512:2048 + (pc + 1) * 512],
                            self.tmp[j][0:1, 0:512], AF.Copy),
                         reads=(self.tmp_b[j],), writes=(hilo_b,),
                         name=f"{tag}h{h}loc{pc}")
            yield brows
            hilo = self.SR[0:1, (3 if j == 0 else 4) * 2048:
                           (4 if j == 0 else 5) * 2048].bitcast(BF16)
            hilo_b = self.rows_b[3 if j == 0 else 4]
            ones_row_bf, ones_row_bf_b = self.sbconst("ones_row_bf")
            for jb in range(4):
                for c in range(4 * jb + 4):
                    def bstep(jb=jb, c=c):
                        qsl = slice(jb * 512, (jb + 1) * 512)
                        psc, psc_b = pscs[c % 2]
                        S.op("pe", lambda e, c=c, qsl=qsl, psc=psc: e.matmul(
                                psc[:, 0:512], KT_[:, c * 128:(c + 1) * 128],
                                QT[:, qsl], start=True, stop=False),
                             reads=(QT_b, KT_b), writes=(psc_b,),
                             name=f"{tag}h{h}b{jb}_{c}")
                        S.op("pe", lambda e, jb=jb, psc=psc: e.matmul(
                                psc[:, 0:512], ones_row_bf[:],
                                hilo[0:1, jb * 512:(jb + 1) * 512],
                                start=False, stop=False),
                             reads=(hilo_b, ones_row_bf_b), writes=(psc_b,),
                             name=f"{tag}h{h}bh{jb}_{c}")
                        S.op("pe", lambda e, jb=jb, psc=psc: e.matmul(
                                psc[:, 0:512], ones_row_bf[:],
                                hilo[0:1, 2048 + jb * 512:2048 + (jb + 1) * 512],
                                start=False, stop=True),
                             reads=(hilo_b, ones_row_bf_b), writes=(psc_b,),
                             name=f"{tag}h{h}bl{jb}_{c}")
                        d = c - 4 * jb
                        pT, pT_b = self.stg[2 * j + c % 2], self.stg_b[2 * j + c % 2]
                        if d >= 0:
                            S.op("dve", lambda e, d=d, psc=psc: e.tensor_add(
                                    psc[:, d * 128:(d + 1) * 128],
                                    psc[:, d * 128:(d + 1) * 128], dmkT[:]),
                                 reads=(psc_b, dmkT_b), writes=(psc_b,),
                                 name=f"{tag}h{h}bm{jb}_{c}")
                            if d > 0:
                                S.op("dve", lambda e, d=d, pT=pT: e.memset(
                                        pT[:, 0:d * 128], 0.0),
                                     writes=(pT_b,), name=f"{tag}h{h}bz{jb}_{c}")
                            S.op("act", lambda e, d=d, pT=pT, psc=psc: e.activation(
                                    pT[:, d * 128:512], psc[:, d * 128:512], AF.Exp),
                                 reads=(psc_b,), writes=(pT_b,),
                                 name=f"{tag}h{h}be{jb}_{c}")
                        else:
                            S.op("act", lambda e, pT=pT, psc=psc: e.activation(
                                    pT[:], psc[:, 0:512], AF.Exp),
                                 reads=(psc_b,), writes=(pT_b,),
                                 name=f"{tag}h{h}be{jb}_{c}")
                        S.op("pe", lambda e, c=c, jb=jb, pT=pT: e.matmul(
                                pvx[:, 0:512], VT[:, c * 128:(c + 1) * 128], pT[:],
                                start=(c == 0), stop=(c == 4 * jb + 3)),
                             reads=(VT_b, pT_b), writes=(pvx_b,),
                             name=f"{tag}h{h}bp{jb}_{c}")
                    yield bstep
                def oevac(jb=jb):
                    S.op("dve", lambda e, jb=jb: e.tensor_copy(
                            atn[:, jb * 512:(jb + 1) * 512], pvx[:, 0:512]),
                         reads=(pvx_b,), writes=(atn_b,), name=f"{tag}h{h}oe{jb}")
                yield oevac
            def final():
                S.dma("sp", self.attnT[h * 128:(h + 1) * 128, :], atn,
                      reads=(atn_b,), writes=(self.attnT_b,), name=f"{tag}h{h}st")
            yield final

        def run_pipeline(nheads, prime):
            nxt = [0]
            active = []
            def start():
                if nxt[0] < nheads:
                    active.append(attn_head_ops(nxt[0]))
                    nxt[0] += 1
            start()
            for _ in range(prime):
                try:
                    next(active[0])()
                except StopIteration:
                    break
            start()
            while active:
                for g in list(active):
                    try:
                        next(g)()
                    except StopIteration:
                        active.remove(g)
                        start()
        run_pipeline(16, 2 * NTT)

        for k in range(NK):
            ap, b2 = self.biga_slot(k)
            S.dma("sp", ap, self.attnT[k * 128:(k + 1) * 128, :],
                  reads=(self.attnT_b,), writes=(b2,), name=f"{tag}lda{k}")
        xloaded = {}
        def evac_wout(oc, tb, pa, pab):
            if oc not in xloaded:
                xs, xs_b = self.fw[oc % 2], self.fw_b[oc % 2]
                S.dma("sp", xs, self.cur_ap[oc * 128:(oc + 1) * 128, :],
                      reads=(self.cur_b[oc],), writes=(xs_b,), name=f"{tag}rx{oc}")
                xloaded[oc] = (xs, xs_b)
            xs, xs_b = xloaded[oc]
            sl = slice(tb * 512, (tb + 1) * 512)
            S.op("dve", lambda e, pa=pa, sl=sl, xs=xs: e.tensor_add(
                    xs[:, sl], pa[:, :], xs[:, sl]),
                 reads=(pab, xs_b), writes=(xs_b,), name=f"{tag}ro{oc}_{tb}")
            if tb == 3:
                S.dma("sp", dst_ap[oc * 128:(oc + 1) * 128, :], xs,
                      reads=(xs_b,), writes=(dst_b[oc],), name=f"{tag}wx{oc}")
        self.proj(tag + "wo", "L2a_Wout", 0, 16, D, evac_wout)
        self.advance_x(dst_ap, dst_b)


    def make_rms_rows(self, tag, eps):
        S = self.S
        self.derive_mu_r(tag)
        mu, mu_b = self.s4s(2)
        var, var_b = self.s4s(3)
        r, r_b = self.s4s(4)
        t5, t5_b = self.s4s(5)
        t6, t6_b = self.s4s(6)
        S.op("dve", lambda e: e.tensor_mul(t5, var, r), reads=(var_b, r_b),
             writes=(t5_b,), name=f"{tag}s1")
        S.op("dve", lambda e: e.tensor_mul(t5, t5, r), reads=(t5_b, r_b),
             writes=(t5_b,), name=f"{tag}s2")
        S.op("dve", lambda e, eps=eps: e.tensor_scalar_add(t5, t5, eps),
             reads=(t5_b,), writes=(t5_b,), name=f"{tag}s3")
        S.op("act", lambda e: e.activation(t5, t5, AF.Sqrt), reads=(t5_b,),
             writes=(t5_b,), name=f"{tag}s4")
        S.op("dve", lambda e: e.reciprocal(t5, t5), reads=(t5_b,),
             writes=(t5_b,), name=f"{tag}s5")
        S.op("dve", lambda e: e.tensor_mul(t5, t5, r), reads=(t5_b, r_b),
             writes=(t5_b,), name=f"{tag}s6")
        S.op("dve", lambda e: e.tensor_mul(t6, mu, t5), reads=(mu_b, t5_b),
             writes=(t6_b,), name=f"{tag}s7")
        self.to_row(5, 0, tag)
        self.to_row(6, 1, tag)

    def bval_col(self, tag, name, dstcol, dstcol_b, fn=None):
        S = self.S
        ap, b2 = self.win(name)
        ones_row, ones_row_b = self.sbconst("ones_row")
        sc = self.small[0]
        sc_b = self.small_b[0]
        S.dma("sp", sc[0:1, 12:13], ap, reads=(b2,), writes=(sc_b,),
              name=f"{tag}bv{name}")
        if fn is not None:
            fn(sc, sc_b)
        pbc, pbc_b = self.pb[4], self.pb_b[4]
        S.op("pe", lambda e: e.matmul(pbc[:, 0:1], ones_row[:], sc[0:1, 12:13],
                                      start=True, stop=True),
             reads=(sc_b, ones_row_b), writes=(pbc_b,), name=f"{tag}bm{name}")
        S.op("dve", lambda e: e.tensor_copy(dstcol, pbc[:, 0:1]),
             reads=(pbc_b,), writes=(dstcol_b,), name=f"{tag}be{name}")

    def slot_l2ffn(self, last):
        tag = "L2f"
        S = self.S
        dst_ap, dst_b = self.next_x(last)
        self.stats_pass(tag)
        self.make_rms_rows(tag, float(np.finfo(np.float32).eps))
        self.xhat_inplace(tag)
        gw, gw_b = self.sbconst("L2f_gw")
        for k in range(NK):
            xh, xh_b = self.biga_slot(k)
            S.op("act", lambda e, xh=xh, k=k: e.activation(
                    xh, xh, AF.Copy, scale=gw[:, k:k + 1]),
                 reads=(xh_b, gw_b), writes=(xh_b,), name=f"{tag}gw{k}")
        rt_col = self.small[2][:, 4:5]
        sc_col = self.small[2][:, 5:6]
        sm2_b = self.small_b[2]
        def mktemp(sc, sc_b):
            S.op("dve", lambda e: e.tensor_scalar_max(sc[0:1, 12:13],
                                                      sc[0:1, 12:13], 0.1),
                 reads=(sc_b,), writes=(sc_b,), name=f"{tag}tmx")
            S.op("dve", lambda e: e.reciprocal(sc[0:1, 12:13], sc[0:1, 12:13]),
                 reads=(sc_b,), writes=(sc_b,), name=f"{tag}trc")
        self.bval_col(tag, "L2f_temp", rt_col, sm2_b, fn=mktemp)
        self.bval_col(tag, "L2f_scale", sc_col, sm2_b)

        def evac_lg(oc, tb, pa, pab):
            stj, stj_b = self.stg[tb % 2], self.stg_b[tb % 2]
            S.op("dve", lambda e, pa=pa, stj=stj: e.tensor_scalar(
                    stj[:], pa[:, :], rt_col, None, ALU.mult),
                 reads=(pab, sm2_b), writes=(stj_b,), name=f"{tag}el{oc}_{tb}")
            S.dma("sp", self.hidT[oc * 128:(oc + 1) * 128,
                                  tb * 512:(tb + 1) * 512], stj[:],
                  reads=(stj_b,), writes=(self.hidT_b,), name=f"{tag}sl{oc}_{tb}")
        self.proj(tag + "lg", "L2f_WcT", 0, 32, D, evac_lg)

        genes_ap, genes_b = self.win("L2f_genes")
        identb, identb_b = self.sbconst("ident_bf")
        lg = self.fwA[:, 0:2048].bitcast(BF16)
        zap = self.fwA[:, 2048:4096].bitcast(BF16)
        lg_b, zap_b = self.fw_b[0], self.fw_b[1]
        exA, exA_b = self.biga_slot(16)
        exB, exB_b = self.biga_slot(17)
        m8 = self.smallb[6][:, 0:8]
        m8_b = self.smallb_b[6]
        negm = self.small[2][:, 6:7]
        Zc = self.small[2][:, 7:8]
        for tt in range(NTT):
            for hf in range(2):
                S.op("sp", lambda e, tt=tt, hf=hf: e.dma_start_transpose(
                        lg[:, hf * 2048:(hf + 1) * 2048],
                        self.hidT[hf * 2048:(hf + 1) * 2048,
                                  tt * 128:(tt + 1) * 128]),
                     reads=(self.hidT_b,), writes=(lg_b,), dma=True,
                     name=f"{tag}tt{tt}_{hf}")
            for rnd in range(4):
                src, src_b = (lg, lg_b) if rnd == 0 else (zap, zap_b)
                S.op("dve", lambda e, src=src: e.max(out=m8, in_=src),
                     reads=(src_b,), writes=(m8_b,), name=f"{tag}mx{tt}_{rnd}")
                if rnd == 0:
                    S.op("dve", lambda e: e.tensor_scalar(
                            negm, m8[:, 0:1], -1.0, None, ALU.mult),
                         reads=(m8_b,), writes=(sm2_b,), name=f"{tag}nm{tt}")
                S.op("dve", lambda e, src=src: e.match_replace(
                        out=zap, in_to_replace=m8, in_values=src,
                        imm_value=-1e30),
                     reads=(src_b, m8_b), writes=(zap_b,), name=f"{tag}mr{tt}_{rnd}")
            S.op("act", lambda e: e.activation(lg[:], lg[:], AF.Exp, bias=negm),
                 reads=(lg_b, sm2_b), writes=(lg_b,), name=f"{tag}ex{tt}")
            S.op("dve", lambda e: e.tensor_scalar(zap, zap, -1e29, None, ALU.is_le),
                 reads=(zap_b,), writes=(zap_b,), name=f"{tag}mk{tt}")
            S.op("dve", lambda e: e.scalar_tensor_tensor(
                    lg[:], lg[:], 1.0, zap, ALU.mult, ALU.mult, accum_out=Zc),
                 reads=(lg_b, zap_b), writes=(lg_b, sm2_b), name=f"{tag}wz{tt}")
            S.op("dve", lambda e: e.reciprocal(Zc, Zc),
                 reads=(sm2_b,), writes=(sm2_b,), name=f"{tag}iz{tt}")
            S.op("dve", lambda e: e.tensor_scalar(lg[:], lg[:], Zc, None, ALU.mult),
                 reads=(lg_b, sm2_b), writes=(lg_b,), name=f"{tag}pn{tt}")
            pbf0, pbf0_b = self.pbf[0], self.pbf_b[0]
            pA, pA_b = self.pb[0], self.pb_b[0]
            pB, pB_b = self.pb[1], self.pb_b[1]
            for go in range(32):
                gsl, gsl_b = self.wslab[go % 2], self.wslab_b[go % 2]
                S.dma("pool", gsl[:, 0:256],
                      genes_ap[go * 128:(go + 1) * 128, :],
                      reads=(genes_b,), writes=(gsl_b,), name=f"{tag}gl{tt}_{go}")
                S.op("pe", lambda e, go=go: e.transpose(
                        pbf0[:, 0:128], lg[:, go * 128:(go + 1) * 128], identb[:]),
                     reads=(lg_b, identb_b), writes=(pbf0_b,),
                     name=f"{tag}pt{tt}_{go}")
                pTs, pTs_b = self.smallb[8 + go % 2], self.smallb_b[8 + go % 2]
                S.op("dve", lambda e, pTs=pTs: e.tensor_copy(pTs[:], pbf0[:, 0:128]),
                     reads=(pbf0_b,), writes=(pTs_b,), name=f"{tag}pe{tt}_{go}")
                S.op("pe", lambda e, go=go, gsl=gsl, pTs=pTs: e.matmul(
                        pA[:, 0:128], gsl[:, 0:128], pTs[:],
                        start=(go == 0), stop=(go == 31)),
                     reads=(gsl_b, pTs_b), writes=(pA_b,), name=f"{tag}ea{tt}_{go}")
                S.op("pe", lambda e, go=go, gsl=gsl, pTs=pTs: e.matmul(
                        pB[:, 0:128], gsl[:, 128:256], pTs[:],
                        start=(go == 0), stop=(go == 31)),
                     reads=(gsl_b, pTs_b), writes=(pB_b,), name=f"{tag}eb{tt}_{go}")
            S.op("dve", lambda e, tt=tt: e.tensor_copy(
                    exA[:, tt * 128:(tt + 1) * 128], pA[:, 0:128]),
                 reads=(pA_b,), writes=(exA_b,), name=f"{tag}xa{tt}")
            S.op("dve", lambda e, tt=tt: e.tensor_copy(
                    exB[:, tt * 128:(tt + 1) * 128], pB[:, 0:128]),
                 reads=(pB_b,), writes=(exB_b,), name=f"{tag}xb{tt}")

        WdT, WdT_b = self.win("L2f_WdT")
        WuT, WuT_b = self.win("L2f_WuT")
        wdv = WdT.rearrange("(c p) o -> p c o", p=128)
        wuv = WuT.rearrange("(c p) o -> p c o", p=128)
        for oc in range(NK):
            wsl0, wsl0_b = self.wslab[0], self.wslab_b[0]
            wsl1, wsl1_b = self.wslab[1], self.wslab_b[1]
            S.dma("pool", wsl0[:, 0:256].rearrange("p (c o) -> p c o", o=128),
                  wdv[:, :, oc * 128:(oc + 1) * 128],
                  reads=(WdT_b,), writes=(wsl0_b,), name=f"{tag}dl{oc}")
            S.dma("pool", wsl1[:, 0:256].rearrange("p (c o) -> p c o", o=128),
                  wuv[:, :, oc * 128:(oc + 1) * 128],
                  reads=(WuT_b,), writes=(wsl1_b,), name=f"{tag}ul{oc}")
            for tb in range(NTB):
                sl = slice(tb * 512, (tb + 1) * 512)
                pd, pd_b = self.pb[2], self.pb_b[2]
                pu, pu_b = self.pb[3], self.pb_b[3]
                for ci, (src, src_b) in enumerate(((exA, exA_b), (exB, exB_b))):
                    S.op("pe", lambda e, ci=ci, sl=sl, src=src: e.matmul(
                            pd[:, :], wsl0[:, ci * 128:(ci + 1) * 128], src[:, sl],
                            start=(ci == 0), stop=(ci == 1)),
                         reads=(wsl0_b, src_b), writes=(pd_b,),
                         name=f"{tag}dm{oc}_{tb}_{ci}")
                    S.op("pe", lambda e, ci=ci, sl=sl, src=src: e.matmul(
                            pu[:, :], wsl1[:, ci * 128:(ci + 1) * 128], src[:, sl],
                            start=(ci == 0), stop=(ci == 1)),
                         reads=(wsl1_b, src_b), writes=(pu_b,),
                         name=f"{tag}um{oc}_{tb}_{ci}")
                S.op("act", lambda e: e.activation(self.tmp[0][:], pd[:, :], AF.Tanh),
                     reads=(pd_b,), writes=(self.tmp_b[0],), name=f"{tag}th{oc}_{tb}")
                xh, xh_b = self.biga_slot(oc)
                S.op("dve", lambda e, sl=sl, xh=xh: e.tensor_mul(
                        self.tmp[0][:], self.tmp[0][:], xh[:, sl]),
                     reads=(self.tmp_b[0], xh_b), writes=(self.tmp_b[0],),
                     name=f"{tag}p1{oc}_{tb}")
                S.op("dve", lambda e: e.tensor_mul(
                        self.tmp[1][:], self.tmp[0][:], pu[:, :]),
                     reads=(self.tmp_b[0], pu_b), writes=(self.tmp_b[1],),
                     name=f"{tag}p2{oc}_{tb}")
                hw = (oc >> 1) & 1
                xs = self.fw[oc % 2][:, hw * 512:(hw + 1) * 512]
                xs_b = self.fw_b[oc % 2]
                S.dma("sp", xs, self.cur_ap[oc * 128:(oc + 1) * 128, sl],
                      reads=(self.cur_b[oc],), writes=(xs_b,), name=f"{tag}x{oc}_{tb}")
                S.op("dve", lambda e, xs=xs: e.scalar_tensor_tensor(
                        xs, self.tmp[1][:], sc_col, xs, ALU.mult, ALU.add),
                     reads=(self.tmp_b[1], sm2_b, xs_b), writes=(xs_b,),
                     name=f"{tag}p3{oc}_{tb}")
                S.dma("sp", dst_ap[oc * 128:(oc + 1) * 128, sl], xs,
                      reads=(xs_b,), writes=(dst_b[oc],), name=f"{tag}w{oc}_{tb}")
        self.advance_x(dst_ap, dst_b)


    def slot_l3attn(self, last):
        tag = "L3a"
        S = self.S
        dst_ap, dst_b = self.next_x(last)
        self.stats_pass(tag)
        self.make_s_ms_rows(tag, "ln")
        self.xhat_inplace(tag)
        def evac_lat(oc, tb, pa, pab):
            stj, stj_b = self.stg[tb % 2], self.stg_b[tb % 2]
            S.op("dve", lambda e, pa=pa, stj=stj: e.tensor_copy(stj[:], pa[:, :]),
                 reads=(pab,), writes=(stj_b,), name=f"{tag}el{oc}_{tb}")
            S.dma("sp", self.hidT[oc * 128:(oc + 1) * 128,
                                  tb * 512:(tb + 1) * 512], stj[:],
                  reads=(stj_b,), writes=(self.hidT_b,), name=f"{tag}sl{oc}_{tb}")
        self.proj(tag + "ql", "L3a_wqd", 0, 12, D, evac_lat)
        self.proj(tag + "kv", "L3a_wkvd", 12, 16, D, evac_lat, o_w_off=12)
        for k in range(NK):
            ap, b2 = self.biga_slot(k)
            S.dma("sp", ap, self.hidT[k * 128:(k + 1) * 128, :],
                  reads=(self.hidT_b,), writes=(b2,), name=f"{tag}ll{k}")
        cosv = self.biga[:, 16 * 2048:18 * 2048].bitcast(F32)
        sinv = self.biga[:, 18 * 2048:20 * 2048].bitcast(F32)
        cosv_b, sinv_b = self.biga_b[16], self.biga_b[18]
        lc, lc_b = self.win("l3_cos")
        ls, ls_b = self.win("l3_sin")
        S.dma("sp", cosv, lc, reads=(lc_b,), writes=(cosv_b,), name=f"{tag}lc")
        S.dma("sp", sinv, ls, reads=(ls_b,), writes=(sinv_b,), name=f"{tag}ls")
        rotP, rotP_b = self.sbconst("l3_rotP")
        qsc = float(192.0 ** -0.5)
        ql_slot = lambda k: self.biga_slot(k)
        kv_slot = lambda k: self.biga_slot(12 + k)

        def evac_qc(oc, tb, pa, pab):
            stj, stj_b = self.stg[tb % 2], self.stg_b[tb % 2]
            S.op("dve", lambda e, pa=pa, stj=stj: e.tensor_scalar_mul(
                    stj[:], pa[:, :], qsc),
                 reads=(pab,), writes=(stj_b,), name=f"{tag}eqc{oc}_{tb}")
            S.dma("sp", self.qkvT[oc * 128:(oc + 1) * 128,
                                  tb * 512:(tb + 1) * 512], stj[:],
                  reads=(stj_b,), writes=(self.qkvT_b,), name=f"{tag}sqc{oc}_{tb}")
        self.proj(tag + "qc", "L3a_wqn", 0, 16, 1536, evac_qc, rhs_slot=ql_slot)

        def evac_qr(oc, tb, pa, pab):
            self.rope_evac(tag + "qr", oc, tb, pa, pab, cosv, cosv_b, sinv, sinv_b,
                           rotP, rotP_b, qsc, oc)
        self.proj(tag + "qr", "L3a_wqr", 16, 24, 1536, evac_qr,
                  rhs_slot=ql_slot, o_w_off=16)

        def evac_kc(oc, tb, pa, pab):
            stj, stj_b = self.stg[tb % 2], self.stg_b[tb % 2]
            S.op("dve", lambda e, pa=pa, stj=stj: e.tensor_copy(stj[:], pa[:, :]),
                 reads=(pab,), writes=(stj_b,), name=f"{tag}ekc{oc}_{tb}")
            S.dma("sp", self.qkvT[oc * 128:(oc + 1) * 128,
                                  tb * 512:(tb + 1) * 512], stj[:],
                  reads=(stj_b,), writes=(self.qkvT_b,), name=f"{tag}skc{oc}_{tb}")
        self.proj(tag + "kc", "L3a_wkn", 24, 40, 512, evac_kc,
                  rhs_slot=kv_slot, o_w_off=24)

        def evac_kr(oc, tb, pa, pab):
            self.rope_evac(tag + "kr", oc, tb, pa, pab, cosv, cosv_b, sinv, sinv_b,
                           rotP, rotP_b, 1.0, oc)
        self.proj(tag + "kr", "L3a_wkr", 40, 48, 512, evac_kr,
                  rhs_slot=kv_slot, o_w_off=40)

        def evac_v(oc, tb, pa, pab):
            stj, stj_b = self.stg[tb % 2], self.stg_b[tb % 2]
            S.op("dve", lambda e, pa=pa, stj=stj: e.tensor_copy(stj[:], pa[:, :]),
                 reads=(pab,), writes=(stj_b,), name=f"{tag}ev{oc}_{tb}")
            S.dma("sp", self.vT3[oc * 128:(oc + 1) * 128,
                                 tb * 512:(tb + 1) * 512], stj[:],
                  reads=(stj_b,), writes=(self.vT3_b,), name=f"{tag}sv{oc}_{tb}")
        self.proj(tag + "v", "L3a_wvu", 0, 24, 512, evac_v, rhs_slot=kv_slot)

        dmk, dmk_b = self.sbconst("diag_mask")
        dmkT, dmkT_b = self.sbconst("diag_maskT")
        identb, identb_b = self.sbconst("ident_bf")
        ones_row_bf, ones_row_bf_b = self.sbconst("ones_row_bf")
        for h in range(16):
            QTc, QTc_b = self.biga_slot(0)
            QTr, QTr_b = self.biga_slot(1)
            KTc, KTc_b = self.biga_slot(2)
            KTr, KTr_b = self.biga_slot(3)
            VTc, VTc_b = self.biga_slot(4)
            VTr, VTr_b = self.biga_slot(5)
            VTrm, VTrm_b = self.biga_slot(6)
            atnc, atnc_b = self.biga_slot(7)
            atnr, atnr_b = self.biga_slot(8)
            S.dma("sp", QTc, self.qkvT[h * 128:(h + 1) * 128, :],
                  reads=(self.qkvT_b,), writes=(QTc_b,), name=f"{tag}h{h}l1")
            S.dma("sp", QTr[0:64, :], self.qkvT[2048 + h * 64:2048 + (h + 1) * 64, :],
                  reads=(self.qkvT_b,), writes=(QTr_b,), name=f"{tag}h{h}l2")
            S.dma("sp", KTc, self.qkvT[3072 + h * 128:3072 + (h + 1) * 128, :],
                  reads=(self.qkvT_b,), writes=(KTc_b,), name=f"{tag}h{h}l3")
            S.dma("sp", KTr[0:64, :], self.qkvT[5120 + h * 64:5120 + (h + 1) * 64, :],
                  reads=(self.qkvT_b,), writes=(KTr_b,), name=f"{tag}h{h}l4")
            S.dma("sp", VTc, self.vT3[h * 192:h * 192 + 128, :],
                  reads=(self.vT3_b,), writes=(VTc_b,), name=f"{tag}h{h}l5")
            S.dma("sp", VTr[0:64, :], self.vT3[h * 192 + 128:h * 192 + 192, :],
                  reads=(self.vT3_b,), writes=(VTr_b,), name=f"{tag}h{h}l6")
            pvt0, pvt0_b = self.pbf[0], self.pbf_b[0]
            for c in range(NTT):
                S.op("pe", lambda e, c=c: e.transpose(
                        pvt0[:, 0:128], VTc[:, c * 128:(c + 1) * 128], identb[:]),
                     reads=(VTc_b, identb_b), writes=(pvt0_b,), name=f"{tag}h{h}vc{c}")
                S.op("dve", lambda e, c=c: e.tensor_copy(
                        VTc[:, c * 128:(c + 1) * 128], pvt0[:, 0:128]),
                     reads=(pvt0_b,), writes=(VTc_b,), name=f"{tag}h{h}ve{c}")
                S.op("pe", lambda e, c=c: e.transpose(
                        pvt0[:, 128:192], VTr[0:64, c * 128:(c + 1) * 128],
                        identb[0:64, 0:64]),
                     reads=(VTr_b, identb_b), writes=(pvt0_b,), name=f"{tag}h{h}vr{c}")
                S.op("dve", lambda e, c=c: e.tensor_copy(
                        VTrm[:, c * 64:(c + 1) * 64], pvt0[:, 128:192]),
                     reads=(pvt0_b,), writes=(VTrm_b,), name=f"{tag}h{h}vf{c}")
            pscs3 = [(self.pb[0], self.pb_b[0]), (self.pb[3], self.pb_b[3])]
            pvC, pvC_b = self.pb[1], self.pb_b[1]
            pvR, pvR_b = self.pb[2], self.pb_b[2]
            fwj, fwj_b = self.fw[0], self.fw_b[0]
            negm = self.cols[:, 0:16]
            Zc = self.cols[:, 16:32]
            lnZ = self.cols[:, 32:48]
            bcol = self.cols[:, 48:64]
            colsb = self.cols_b[0]
            for qt in range(NTT):
                L = (qt + 1) * 128
                nb = qt // 4 + 1
                for jb in range(nb):
                    psc, psc_b = pscs3[(qt * 5 + jb) % 2]
                    w = 512 if jb < qt // 4 else (qt % 4 + 1) * 128
                    S.op("pe", lambda e, qt=qt, jb=jb, w=w, psc=psc: e.matmul(
                            psc[:, 0:w], QTc[:, qt * 128:(qt + 1) * 128],
                            KTc[:, jb * 512:jb * 512 + w], start=True, stop=False),
                         reads=(QTc_b, KTc_b), writes=(psc_b,),
                         name=f"{tag}h{h}a{qt}_{jb}")
                    S.op("pe", lambda e, qt=qt, jb=jb, w=w, psc=psc: e.matmul(
                            psc[:, 0:w], QTr[0:64, qt * 128:(qt + 1) * 128],
                            KTr[0:64, jb * 512:jb * 512 + w], start=False, stop=True),
                         reads=(QTr_b, KTr_b), writes=(psc_b,),
                         name=f"{tag}h{h}a2{qt}_{jb}")
                    if jb < qt // 4:
                        S.op("dve", lambda e, jb=jb, w=w, psc=psc: e.tensor_copy(
                                fwj[:, jb * 512:jb * 512 + w], psc[:, 0:w]),
                             reads=(psc_b,), writes=(fwj_b,),
                             name=f"{tag}h{h}ae{qt}_{jb}")
                    else:
                        if w > 128:
                            S.op("dve", lambda e, jb=jb, w=w, psc=psc: e.tensor_copy(
                                    fwj[:, jb * 512:jb * 512 + w - 128],
                                    psc[:, 0:w - 128]),
                                 reads=(psc_b,), writes=(fwj_b,),
                                 name=f"{tag}h{h}ap{qt}_{jb}")
                        S.op("dve", lambda e, qt=qt, w=w, psc=psc: e.tensor_add(
                                fwj[:, qt * 128:(qt + 1) * 128],
                                psc[:, w - 128:w], dmk[:]),
                             reads=(psc_b, dmk_b), writes=(fwj_b,),
                             name=f"{tag}h{h}ad{qt}")
                S.op("dve", lambda e, qt=qt, L=L: e.reduce_max(
                        negm[:, qt:qt + 1], fwj[:, 0:L], axis=AX.X, negate=True),
                     reads=(fwj_b,), writes=(colsb,), name=f"{tag}h{h}am{qt}")
                S.op("act", lambda e, qt=qt, L=L: e.activation(
                        fwj[:, 0:L], fwj[:, 0:L], AF.Exp,
                        bias=negm[:, qt:qt + 1], accum_out=Zc[:, qt:qt + 1]),
                     reads=(fwj_b, colsb), writes=(fwj_b, colsb),
                     name=f"{tag}h{h}az{qt}")
            S.op("act", lambda e: e.activation(lnZ, Zc, AF.Ln),
                 reads=(colsb,), writes=(colsb,), name=f"{tag}h{h}ln")
            S.op("dve", lambda e: e.tensor_sub(bcol, negm, lnZ),
                 reads=(colsb,), writes=(colsb,), name=f"{tag}h{h}bc")
            self.cols_to_row(bcol, colsb, 16, 2, tag + f"h{h}")
            brow, brow_b = self.row(2)
            hilo = self.SR[0:1, 3 * 2048:4 * 2048].bitcast(BF16)
            hilo_b = self.rows_b[3]
            S.op("act", lambda e: e.activation(hilo[0:1, 0:2048], brow, AF.Copy),
                 reads=(brow_b,), writes=(hilo_b,), name=f"{tag}h{h}hi")
            for pc in range(4):
                S.op("dve", lambda e, pc=pc: e.tensor_sub(
                        self.tmp[0][0:1, 0:512],
                        brow[0:1, pc * 512:(pc + 1) * 512],
                        hilo[0:1, pc * 512:(pc + 1) * 512]),
                     reads=(brow_b, hilo_b), writes=(self.tmp_b[0],),
                     name=f"{tag}h{h}lo{pc}")
                S.op("act", lambda e, pc=pc: e.activation(
                        hilo[0:1, 2048 + pc * 512:2048 + (pc + 1) * 512],
                        self.tmp[0][0:1, 0:512], AF.Copy),
                     reads=(self.tmp_b[0],), writes=(hilo_b,),
                     name=f"{tag}h{h}loc{pc}")
            for jb in range(4):
                for c in range(4 * jb + 4):
                    qsl = slice(jb * 512, (jb + 1) * 512)
                    psc, psc_b = pscs3[c % 2]
                    S.op("pe", lambda e, c=c, qsl=qsl, psc=psc: e.matmul(
                            psc[:, 0:512], KTc[:, c * 128:(c + 1) * 128],
                            QTc[:, qsl], start=True, stop=False),
                         reads=(QTc_b, KTc_b), writes=(psc_b,),
                         name=f"{tag}h{h}b{jb}_{c}")
                    S.op("pe", lambda e, c=c, qsl=qsl, psc=psc: e.matmul(
                            psc[:, 0:512], KTr[0:64, c * 128:(c + 1) * 128],
                            QTr[0:64, qsl], start=False, stop=False),
                         reads=(QTr_b, KTr_b), writes=(psc_b,),
                         name=f"{tag}h{h}b2{jb}_{c}")
                    S.op("pe", lambda e, jb=jb, psc=psc: e.matmul(
                            psc[:, 0:512], ones_row_bf[:],
                            hilo[0:1, jb * 512:(jb + 1) * 512],
                            start=False, stop=False),
                         reads=(hilo_b, ones_row_bf_b), writes=(psc_b,),
                         name=f"{tag}h{h}bh{jb}_{c}")
                    S.op("pe", lambda e, jb=jb, psc=psc: e.matmul(
                            psc[:, 0:512], ones_row_bf[:],
                            hilo[0:1, 2048 + jb * 512:2048 + (jb + 1) * 512],
                            start=False, stop=True),
                         reads=(hilo_b, ones_row_bf_b), writes=(psc_b,),
                         name=f"{tag}h{h}bl{jb}_{c}")
                    d = c - 4 * jb
                    pT, pT_b = self.stg[c % 2], self.stg_b[c % 2]
                    if d >= 0:
                        S.op("dve", lambda e, d=d, psc=psc: e.tensor_add(
                                psc[:, d * 128:(d + 1) * 128],
                                psc[:, d * 128:(d + 1) * 128], dmkT[:]),
                             reads=(psc_b, dmkT_b), writes=(psc_b,),
                             name=f"{tag}h{h}bm{jb}_{c}")
                        if d > 0:
                            S.op("dve", lambda e, d=d, pT=pT: e.memset(
                                    pT[:, 0:d * 128], 0.0),
                                 writes=(pT_b,), name=f"{tag}h{h}bz{jb}_{c}")
                        S.op("act", lambda e, d=d, pT=pT, psc=psc: e.activation(
                                pT[:, d * 128:512], psc[:, d * 128:512], AF.Exp),
                             reads=(psc_b,), writes=(pT_b,), name=f"{tag}h{h}be{jb}_{c}")
                    else:
                        S.op("act", lambda e, pT=pT, psc=psc: e.activation(
                                pT[:], psc[:, 0:512], AF.Exp),
                             reads=(psc_b,), writes=(pT_b,), name=f"{tag}h{h}be{jb}_{c}")
                    S.op("pe", lambda e, c=c, jb=jb, pT=pT: e.matmul(
                            pvC[:, 0:512], VTc[:, c * 128:(c + 1) * 128], pT[:],
                            start=(c == 0), stop=(c == 4 * jb + 3)),
                         reads=(VTc_b, pT_b), writes=(pvC_b,),
                         name=f"{tag}h{h}bp{jb}_{c}")
                    S.op("pe", lambda e, c=c, jb=jb, pT=pT: e.matmul(
                            pvR[0:64, 0:512], VTrm[:, c * 64:(c + 1) * 64], pT[:],
                            start=(c == 0), stop=(c == 4 * jb + 3)),
                         reads=(VTrm_b, pT_b), writes=(pvR_b,),
                         name=f"{tag}h{h}bq{jb}_{c}")
                S.op("dve", lambda e, jb=jb: e.tensor_copy(
                        atnc[:, jb * 512:(jb + 1) * 512], pvC[:, 0:512]),
                     reads=(pvC_b,), writes=(atnc_b,), name=f"{tag}h{h}oe{jb}")
                S.op("dve", lambda e, jb=jb: e.tensor_copy(
                        atnr[0:64, jb * 512:(jb + 1) * 512], pvR[0:64, 0:512]),
                     reads=(pvR_b,), writes=(atnr_b,), name=f"{tag}h{h}of{jb}")
            S.dma("sp", self.attnT3[h * 192:h * 192 + 128, :], atnc,
                  reads=(atnc_b,), writes=(self.attnT3_b,), name=f"{tag}h{h}s1")
            S.dma("sp", self.attnT3[h * 192 + 128:h * 192 + 192, :], atnr[0:64, :],
                  reads=(atnr_b,), writes=(self.attnT3_b,), name=f"{tag}h{h}s2")

        for k in range(24):
            ap, b2 = self.biga_slot(k)
            S.dma("sp", ap, self.attnT3[k * 128:(k + 1) * 128, :],
                  reads=(self.attnT3_b,), writes=(b2,), name=f"{tag}lda{k}")
        xloaded = {}
        def evac_wo(oc, tb, pa, pab):
            if oc not in xloaded:
                xs, xs_b = self.fw[oc % 2], self.fw_b[oc % 2]
                S.dma("sp", xs, self.cur_ap[oc * 128:(oc + 1) * 128, :],
                      reads=(self.cur_b[oc],), writes=(xs_b,), name=f"{tag}rx{oc}")
                xloaded[oc] = (xs, xs_b)
            xs, xs_b = xloaded[oc]
            sl = slice(tb * 512, (tb + 1) * 512)
            S.op("dve", lambda e, pa=pa, sl=sl, xs=xs: e.tensor_add(
                    xs[:, sl], pa[:, :], xs[:, sl]),
                 reads=(pab, xs_b), writes=(xs_b,), name=f"{tag}ro{oc}_{tb}")
            if tb == 3:
                S.dma("sp", dst_ap[oc * 128:(oc + 1) * 128, :], xs,
                      reads=(xs_b,), writes=(dst_b[oc],), name=f"{tag}wx{oc}")
        self.proj(tag + "wo", "L3a_wo", 0, 16, 3072, evac_wo,
                  rhs_slot=lambda k: self.biga_slot(k))
        self.advance_x(dst_ap, dst_b)

    def slot_l3ffn(self, last):
        tag = "L3f"
        S = self.S
        dst_ap, dst_b = self.next_x(last)
        self.stats_pass(tag)
        self.make_s_ms_rows(tag, "ln")
        self.xhat_inplace(tag)
        WgT, WgT_b = self.win("L3f_WgT")
        WuT, WuT_b = self.win("L3f_WuT")
        wgv = WgT.rearrange("(c p) o -> p c o", p=128)
        wuv = WuT.rearrange("(c p) o -> p c o", p=128)
        for oc in range(64):
            wsl0, wsl0_b = self.wslab[0], self.wslab_b[0]
            wsl1, wsl1_b = self.wslab[1], self.wslab_b[1]
            S.dma("pool", wsl0[:].rearrange("p (c o) -> p c o", o=128),
                  wgv[:, :, oc * 128:(oc + 1) * 128],
                  reads=(WgT_b,), writes=(wsl0_b,), name=f"{tag}gl{oc}")
            S.dma("pool", wsl1[:].rearrange("p (c o) -> p c o", o=128),
                  wuv[:, :, oc * 128:(oc + 1) * 128],
                  reads=(WuT_b,), writes=(wsl1_b,), name=f"{tag}ul{oc}")
            for half in range(2):
                gbanks = [(self.pb[0], self.pb_b[0]), (self.pb[1], self.pb_b[1])]
                ubanks = [(self.pb[2], self.pb_b[2]), (self.pb[3], self.pb_b[3])]
                for k in range(NK):
                    for t2 in range(2):
                        tb = half * 2 + t2
                        sl = slice(tb * 512, (tb + 1) * 512)
                        xh, xh_b = self.biga_slot(k)
                        gb, gb_b = gbanks[t2]
                        ub, ub_b = ubanks[t2]
                        S.op("pe", lambda e, k=k, sl=sl, xh=xh, gb=gb: e.matmul(
                                gb[:, :], wsl0[:, k * 128:(k + 1) * 128], xh[:, sl],
                                start=(k == 0), stop=(k == NK - 1)),
                             reads=(wsl0_b, xh_b), writes=(gb_b,),
                             name=f"{tag}gm{oc}_{tb}_{k}")
                        S.op("pe", lambda e, k=k, sl=sl, xh=xh, ub=ub: e.matmul(
                                ub[:, :], wsl1[:, k * 128:(k + 1) * 128], xh[:, sl],
                                start=(k == 0), stop=(k == NK - 1)),
                             reads=(wsl1_b, xh_b), writes=(ub_b,),
                             name=f"{tag}um{oc}_{tb}_{k}")
                for t2 in range(2):
                    tb = half * 2 + t2
                    sl = slice(tb * 512, (tb + 1) * 512)
                    gb, gb_b = gbanks[t2]
                    ub, ub_b = ubanks[t2]
                    sg, sg_b = self.stg[t2], self.stg_b[t2]
                    hd, hd_b = self.stg[2 + t2], self.stg_b[2 + t2]
                    S.op("act", lambda e, gb=gb, sg=sg: e.activation(
                            sg[:], gb[:, :], AF.Silu),
                         reads=(gb_b,), writes=(sg_b,), name=f"{tag}si{oc}_{tb}")
                    S.op("dve", lambda e, ub=ub, sg=sg, hd=hd: e.tensor_mul(
                            hd[:], ub[:, :], sg[:]),
                         reads=(ub_b, sg_b), writes=(hd_b,), name=f"{tag}hd{oc}_{tb}")
                    S.dma("sp", self.hidT[oc * 128:(oc + 1) * 128, sl], hd[:],
                          reads=(hd_b,), writes=(self.hidT_b,),
                          name=f"{tag}hs{oc}_{tb}")
        Arow, Arow_b = self.row(0)
        Qrow, Qrow_b = self.row(1)
        S.op("dve", lambda e: e.memset(Arow, 1.0), writes=(Arow_b,), name=f"{tag}A")
        S.op("dve", lambda e: e.memset(Qrow, 0.0), writes=(Qrow_b,), name=f"{tag}Q")
        WdT, WdT_b = self.win("L3f_WdT")
        wdv = WdT.rearrange("(c p) o -> p c o", p=128)

        def hid_chunk(k):
            sslot, sb2 = self.biga_slot(k // 4)
            return sslot[:, (k % 4) * 512:(k % 4 + 1) * 512], sb2

        def mlp_bank(tb, oc):
            sl = slice(tb * 512, (tb + 1) * 512)
            if oc == 0:
                for sslot in range(NK):
                    ba, bb = self.biga_slot(sslot)
                    src = self.hidT.rearrange("(c p) t -> p c t", p=128)[
                        :, sslot * 4:(sslot + 1) * 4, sl]
                    S.dma("sp", ba.rearrange("p (c t) -> p c t", t=512)[:, 0:4, :],
                          src, reads=(self.hidT_b,), writes=(bb,),
                          name=f"{tag}dl{tb}_{sslot}")
            bank, bank_b = (self.pb[0], self.pb_b[0]) if oc % 2 == 0 else                            (self.pb[1], self.pb_b[1])
            for si in range(4):
                j2 = (tb * 64 + oc * 4 + si) % 2
                wsl, wsl_b = self.wslab[j2], self.wslab_b[j2]
                S.dma("pool", wsl[:].rearrange("p (c o) -> p c o", o=128),
                      wdv[:, si * 16:(si + 1) * 16, oc * 128:(oc + 1) * 128],
                      reads=(WdT_b,), writes=(wsl_b,), name=f"{tag}dw{tb}_{oc}_{si}")
                for kk in range(16):
                    k = si * 16 + kk
                    ha, hb = hid_chunk(k)
                    S.op("pe", lambda e, kk=kk, k=k, ha=ha, wsl=wsl, bank=bank:
                         e.matmul(bank[:, :], wsl[:, kk * 128:(kk + 1) * 128], ha,
                                  start=(k == 0), stop=(k == 63)),
                         reads=(wsl_b, hb), writes=(bank_b,),
                         name=f"{tag}dm{tb}_{oc}_{k}")
            return bank, bank_b

        self.final_combine(tag, dst_ap, dst_b, Arow, Arow_b, Qrow, Qrow_b, mlp_bank)
        self.advance_x(dst_ap, dst_b)



SLOT_ORDER = ["L0_attn", "L0_ffn", "L1_attn", "L1_ffn",
              "L2_attn", "L2_ffn", "L3_attn", "L3_ffn"]
IMPLEMENTED = SLOT_ORDER[:]
DEBUG_DUMPS = []

def build_program(host, slots):
    nc = bass.Bass("TRN2", target_bir_lowering=False)
    with ExitStack() as ctx:
        b = Builder(nc, ctx, host)
        for i, s in enumerate(slots):
            last = (i == len(slots) - 1)
            if s in ("L0_attn", "L1_attn"):
                b.slot_lightning(s[:2], last)
            elif s == "L0_ffn":
                b.slot_l0ffn(last)
            elif s == "L1_ffn":
                b.slot_l1ffn(last)
            elif s == "L2_attn":
                b.slot_l2attn(last)
            elif s == "L2_ffn":
                b.slot_l2ffn(last)
            elif s == "L3_attn":
                b.slot_l3attn(last)
            elif s == "L3_ffn":
                b.slot_l3ffn(last)
            else:
                raise NotImplementedError(s)
        b.S.op("sp", lambda e: e.nop(), reads=(b.yT_b,), name="final")
        b.S.finalize()
        sems, qsems = b.sems, b.qsems
        S = b.S

        def emitf(name, eng):
            for o in S.ops[name]:
                for kv, val in o.waits:
                    if kv[0] == "q":
                        eng.wait_ge(qsems[(kv[1], kv[2])], val)
                    else:
                        eng.wait_ge(sems[kv[1]], val)
                ins = o.fn(eng)
                if o.is_dma:
                    ins.then_inc(qsems[(o.eng, o.inc_no % Sched.M_DMA)], 16)
                elif o.needed:
                    ins.then_inc(sems[o.eng], 1)

        with nc.Block() as blk:
            @blk.tensor
            def _(t):
                emitf("pe", t)
            @blk.scalar
            def _(s):
                emitf("act", s)
            @blk.vector
            def _(v):
                emitf("dve", v)
            @blk.gpsimd
            def _(g):
                emitf("pool", g)
            @blk.sync
            def _(s):
                emitf("sp", s)
        return nc, list(b.used_inputs.keys())


def kernel(**inputs):
    host = host_prepare(inputs)
    slots = [s for s in SLOT_ORDER if s in IMPLEMENTED]
    nc, needed = build_program(host, slots)
    x = np.asarray(inputs["L0_attn_x"])
    in_maps = []
    for c in range(8):
        m = {}
        for nm in needed:
            if nm == "xT":
                m[nm] = np.ascontiguousarray(x[c].T)
            else:
                m[nm] = host[nm]
        in_maps.append(m)
    res = run_bass_kernel_spmd(nc, in_maps, core_ids=list(range(8)))
    out = np.stack([np.ascontiguousarray(r["yT"].T) for r in res.results])
    return out
